# version 03
# baseline (speedup 1.0000x reference)
import numpy as np
import ml_dtypes
from contextlib import ExitStack

import concourse.bass as bass
import concourse.mybir as mybir
from concourse.bass_utils import run_bass_kernel_spmd

F32 = mybir.dt.float32
BF16 = mybir.dt.bfloat16
AF = mybir.ActivationFunctionType
ALU = mybir.AluOpType
AX = mybir.AxisListType
BF = ml_dtypes.bfloat16

T = 2048
D = 1024
NCH = 8
NT = 16
NEG = -1.0e30

ARENA_BF_COLS = 36864
ARENA_F32_COLS = 5120

ENGSTEP = {"pe": 1, "ve": 1, "se": 1, "gp": 16, "sp": 16}
NCH_GP = 12
NCH_SP = 6



def _lay_feat(v):
    f = v.shape[0]
    return np.ascontiguousarray(v.reshape(f // 128, 128).T)


def _host_constants():
    c = {}
    c["ident_bf"] = np.eye(128, dtype=BF)
    c["ident_f32"] = np.eye(128, dtype=np.float32)
    c["ones_bf"] = np.ones((128, 512), dtype=BF)
    c["ones_f32"] = np.ones((128, 128), dtype=np.float32)

    m0 = np.full((128, 4 * 512), NEG, dtype=np.float32)
    w = np.arange(512)[None, :]
    for parity in range(2):
        for b0 in range(2):
            v = parity + 2 * b0
            ql = 128 * parity + np.arange(128)[:, None]
            valid = (w >= ql) & (w <= ql + 256)
            if b0:
                valid &= (w >= 256)
            m0[:, v * 512:(v + 1) * 512][valid] = 0.0
    c["mask0"] = m0.astype(BF)

    mc = np.zeros((128, 4 * 512), dtype=np.float32)
    qf = np.arange(512)[None, :]
    kp = np.arange(128)[:, None]
    for d in range(4):
        mc[:, d * 512:(d + 1) * 512] = (qf >= kp + 128 * d).astype(np.float32)
    c["maskc"] = mc.astype(BF)

    c["pos"] = np.broadcast_to(
        np.arange(T, dtype=np.float32)[None, :], (128, T)).copy()

    QK = 1024
    SCALE_BASE = 512.0
    i2 = np.arange(0, QK, 2, dtype=np.float64)
    base_scale = (i2 + 0.4 * QK) / (1.4 * QK)
    t = np.arange(T, dtype=np.float64)
    scale = base_scale[None, :] ** (t[:, None] / SCALE_BASE)
    inv_freq = 1.0 / (10000.0 ** (np.arange(QK // 2, dtype=np.float64) / (QK // 2)))
    sinu = t[:, None] * inv_freq[None, :]
    sn, cs = np.sin(sinu), np.cos(sinu)
    rep = lambda a: np.repeat(a, 2, axis=1)
    sgn = np.where(np.arange(QK) % 2 == 0, -1.0, 1.0)[None, :]
    c["csq"] = np.ascontiguousarray((rep(cs * scale)).T).astype(BF)
    c["snq"] = np.ascontiguousarray((rep(sn * scale) * sgn).T).astype(BF)
    c["csk"] = np.ascontiguousarray((rep(cs / scale)).T).astype(BF)
    c["snk"] = np.ascontiguousarray((rep(sn / scale) * sgn).T).astype(BF)

    bas = np.zeros((8, 8 * 128), dtype=np.float32)
    for e in range(8):
        bas[e, e * 128:(e + 1) * 128] = 1.0
    c["basis_bf"] = bas.astype(BF)

    mcc = np.zeros((128, 8), dtype=np.float32)
    mcc[:, 0] = 1e-5
    mcc[:, 1] = 1e-6
    mcc[:, 2] = -4.0
    c["misc_cols"] = mcc
    return c


_CONST_CACHE = None


def _prep_shared(inputs):
    global _CONST_CACHE
    if _CONST_CACHE is None:
        _CONST_CACHE = _host_constants()
    s = dict(_CONST_CACHE)
    f32 = np.float32
    asC = lambda a: np.ascontiguousarray(np.asarray(a, f32))
    s["qwT"] = asC(np.asarray(inputs["L0_attn_q_w"]).T)
    s["kwT"] = asC(np.asarray(inputs["L0_attn_k_w"]).T)
    s["vwT"] = asC(np.asarray(inputs["L0_attn_v_w"]).T)
    s["owT"] = asC(np.asarray(inputs["L0_attn_o_w"]).T)
    s["bias_rows"] = np.stack([
        np.asarray(inputs["L0_attn_q_b"], f32),
        np.asarray(inputs["L0_attn_k_b"], f32),
        np.asarray(inputs["L0_attn_v_b"], f32)], axis=0)
    s["l0bias"] = np.concatenate(
        [_lay_feat(np.asarray(inputs[k], f32)) for k in
         ("L0_attn_q_b", "L0_attn_k_b", "L0_attn_v_b", "L0_attn_o_b")],
        axis=1)
    s["w1"] = asC(inputs["L0_ffn_W1"])
    w2 = np.asarray(inputs["L0_ffn_W2"], f32)
    w2r = w2.reshape(8, 32, 128, 8, 128)
    s["w2lay"] = np.ascontiguousarray(
        w2r.transpose(0, 3, 2, 1, 4).reshape(8, 8, 128, 32 * 128))
    s["b1lay"] = np.concatenate(
        [_lay_feat(np.asarray(inputs["L0_ffn_b1"], f32)[e]) for e in range(8)],
        axis=1)
    s["b2"] = asC(inputs["L0_ffn_b2"])
    s["wr"] = asC(inputs["L0_ffn_Wr"])
    s["br_bc"] = np.broadcast_to(
        np.asarray(inputs["L0_ffn_br"], f32)[None, :], (128, 8)).copy()
    s["wqkv"] = asC(inputs["L1_attn_W_qkv"])
    s["wg"] = asC(inputs["L1_attn_W_g"])
    s["wproj1"] = asC(inputs["L1_attn_W_proj"])
    s["decay"] = np.asarray(inputs["L1_attn_decay"], f32).reshape(1, 8).copy()
    s["wfcT"] = asC(np.asarray(inputs["L1_ffn_W_fc"]).T)
    s["wprojT"] = asC(np.asarray(inputs["L1_ffn_W_proj"]).T)
    s["lnw_lay"] = _lay_feat(np.asarray(inputs["L1_ffn_ln_w"], f32))
    s["lnb_lay"] = _lay_feat(np.asarray(inputs["L1_ffn_ln_b"], f32))
    return s



class Prog:

    def __init__(self, strict="none"):
        self.strict = strict
        self.strict_engs = {"none": (), "vese": ("ve", "se"),
                            "all": ("pe", "ve", "se")}[strict]
        self.semkeys = (["pe", "ve", "se"]
                        + [f"gp{i}" for i in range(NCH_GP)]
                        + [f"sp{i}" for i in range(NCH_SP)])
        self.stream = {k: [] for k in ENGSTEP}
        self.cnt = {k: 0 for k in self.semkeys}
        self._last = {k: None for k in ENGSTEP}
        self._rr = {"gp": 0, "sp": 0}

    @staticmethod
    def _flat(deps):
        out = []
        for d in deps:
            if d is None:
                continue
            if isinstance(d, list):
                out.extend(t for t in d if t is not None)
            else:
                out.append(d)
        return out

    def op(self, eng, fn, deps=(), sig=False):
        assert eng in ENGSTEP and (not sig or eng in ("pe", "ve", "se"))
        toks = self._flat(deps)
        entry = [toks, fn, None]
        self.stream[eng].append(entry)
        self._last[eng] = entry
        if eng in self.strict_engs:
            self.sig(eng)
        if sig:
            return self.sig(eng)
        return None

    def dma(self, eng, fn, deps=()):
        assert eng in ("gp", "sp")
        ch = f"{eng}{self._rr[eng]}"
        self._rr[eng] = (self._rr[eng] + 1) % (NCH_GP if eng == "gp" else NCH_SP)
        toks = self._flat(deps)
        if self.cnt[ch]:
            toks.append((ch, self.cnt[ch]))
        self.cnt[ch] += 16
        entry = [toks, fn, (ch, 16)]
        self.stream[eng].append(entry)
        self._last[eng] = entry
        return (ch, self.cnt[ch])

    def sig(self, eng):
        entry = self._last[eng]
        assert entry is not None, f"no op to sign on {eng}"
        if entry[2] is None:
            self.cnt[eng] += 1
            entry[2] = (eng, 1)
        return (eng, self.cnt[eng])

    def front(self, eng):
        if eng in ("pe", "ve", "se"):
            if self._last[eng] is None:
                return []
            return [self.sig(eng)]
        pre = eng
        return [(k, self.cnt[k]) for k in self.semkeys
                if k.startswith(pre) and self.cnt[k]]

    def barrier(self, engs=("pe", "ve", "se", "gp", "sp")):
        toks = []
        for e in engs:
            toks.extend(self.front(e))
        return toks

    def replay(self, nc, sems):
        hw = {k: {k2: 0 for k2 in self.semkeys} for k in ENGSTEP}
        blockmap = {"pe": "tensor", "ve": "vector", "se": "scalar",
                    "gp": "gpsimd", "sp": "sync"}
        with nc.Block() as block:
            for eng in ENGSTEP:
                entries = self.stream[eng]
                if not entries:
                    continue

                def make_body(eng, entries):
                    def body(e):
                        mywait = hw[eng]
                        mycount = 0
                        for toks, fn, sig in entries:
                            for (tkey, tval) in toks:
                                if tkey == eng:
                                    continue
                                if tval > mywait[tkey]:
                                    e.wait_ge(sems[tkey], tval)
                                    mywait[tkey] = tval
                            if eng in self.strict_engs and mycount:
                                e.wait_ge(sems[eng], mycount)
                            inst = fn(e)
                            if sig is not None:
                                assert inst is not None
                                inst.then_inc(sems[sig[0]], sig[1])
                                mycount += sig[1] if sig[0] == eng else 0
                    return body

                getattr(block, blockmap[eng])(make_body(eng, entries))


class Pool:

    def __init__(self, tensor, cols, name):
        self.t = tensor
        self.cols = cols
        self.off = 0
        self.name = name
        self.marks = []

    def alloc(self, n):
        n = (n + 7) & ~7
        assert self.off + n <= self.cols, \
            f"arena {self.name} overflow: {self.off}+{n} > {self.cols}"
        o = self.off
        self.off += n
        return self.t[:, o:o + n]

    def alloc_rows(self, rows, n):
        n = (n + 7) & ~7
        assert self.off + n <= self.cols
        o = self.off
        self.off += n
        return self.t[0:rows, o:o + n]

    def push(self):
        self.marks.append(self.off)

    def pop(self):
        self.off = self.marks.pop()

    def reset(self):
        self.off = 0


class PsumMgr:

    def __init__(self):
        self.tok = {}

    def gate(self, name):
        return self.tok.get(name)

    def done(self, name, token):
        self.tok[name] = token


class Ctx:
    pass



def CH(t, c, t0=0, n=T, stride=T):
    return t[:, c * stride + t0: c * stride + t0 + n]


def emit_load_x(K):
    p = K.p
    K.fpool.push()
    stage = K.fpool.alloc(2 * 1024)
    slot_tok = [None, None]
    bank_tok = [None, None]
    for i in range(NT):
        s = i % 2
        st = stage[:, s * 1024:(s + 1) * 1024]
        dtok = p.dma("sp", (lambda st, i: lambda e: e.dma_start(
            st, K.x_in[i * 128:(i + 1) * 128, :]))(st, i),
            deps=[slot_tok[s]] + (K.consts_all if i == 0 else []))
        for b in range(2):
            ps = K.psA if b == 0 else K.psB
            for j in range(4):
                c = 4 * b + j
                p.op("pe", (lambda st, c, ps, j: lambda e: e.transpose(
                    ps[:, j * 128:(j + 1) * 128],
                    st[:, c * 128:(c + 1) * 128], K.ident_f32))(st, c, ps, j),
                    deps=[dtok, bank_tok[b]] + (K.consts_all if i == 0 else []))
            ptok = p.sig("pe")
            if b == 1:
                slot_tok[s] = ptok
            dst = K.xres.rearrange("p (c t) -> p c t", t=T)[
                :, 4 * b:4 * b + 4, i * 128:(i + 1) * 128]
            psv = ps.rearrange("p (c t) -> p c t", t=128)
            eng = "ve" if b == 0 else "se"
            if eng == "ve":
                fn = (lambda dst, psv: lambda e: e.tensor_copy(dst, psv))(dst, psv)
            else:
                fn = (lambda dst, psv: lambda e: e.copy(dst, psv))(dst, psv)
            bank_tok[b] = p.op(eng, fn, deps=[ptok], sig=True)
    K.x_ready = p.barrier(("pe", "ve", "se"))
    K.final_deps = K.x_ready
    K.fpool.pop()


def emit_store(K):
    p = K.p
    K.fpool.push()
    obuf = K.fpool.alloc(2 * 1024)
    slot_store_tok = [None, None]
    bank_tok = [None, None]
    pre = list(K.final_deps) if hasattr(K, "final_deps") else []
    stok = None
    for i in range(NT):
        s = i % 2
        ob = obuf[:, s * 1024:(s + 1) * 1024]
        evac_toks = []
        for b in range(2):
            ps = K.psA if b == 0 else K.psB
            for j in range(4):
                c = 4 * b + j
                p.op("pe", (lambda c, i, ps, j: lambda e: e.transpose(
                    ps[:, j * 128:(j + 1) * 128],
                    CH(K.xres, c, i * 128, 128), K.ident_f32))(c, i, ps, j),
                    deps=pre + [bank_tok[b]])
            ptok = p.sig("pe")
            dst = ob[:, b * 512:(b + 1) * 512]
            psv = ps[:, 0:512]
            eng = "ve" if b == 0 else "se"
            if eng == "ve":
                fn = (lambda dst, psv: lambda e: e.tensor_copy(dst, psv))(dst, psv)
            else:
                fn = (lambda dst, psv: lambda e: e.copy(dst, psv))(dst, psv)
            etok = p.op(eng, fn, deps=[ptok, slot_store_tok[s]], sig=True)
            evac_toks.append(etok)
            bank_tok[b] = etok
        pre = []
        stok = p.dma("sp", (lambda ob, i: lambda e: e.dma_start(
            K.out[i * 128:(i + 1) * 128, :], ob))(ob, i),
            deps=evac_toks)
        slot_store_tok[s] = stok




def build_nc(upto="full", strict="vese"):
    nc = bass.Bass("TRN2", target_bir_lowering=False)
    K = Ctx()
    K.nc = nc
    K.p = Prog(strict=strict)
    K.ps = PsumMgr()
    K.upto = upto

    f32 = F32
    dt_ = nc.dram_tensor
    K.x_in = dt_("x_in", [T, D], f32, kind="ExternalInput").ap()
    K.out = dt_("out", [T, D], f32, kind="ExternalOutput").ap()

    ins = {}
    for name, shape, dty in [
        ("ident_bf", [128, 128], BF16), ("ident_f32", [128, 128], f32),
        ("ones_bf", [128, 512], BF16), ("ones_f32", [128, 128], f32),
        ("mask0", [128, 2048], BF16), ("maskc", [128, 2048], BF16),
        ("pos", [128, T], f32),
        ("csq", [D, T], BF16), ("snq", [D, T], BF16),
        ("csk", [D, T], BF16), ("snk", [D, T], BF16),
        ("basis_bf", [8, 1024], BF16),
        ("qwT", [D, D], f32), ("kwT", [D, D], f32),
        ("vwT", [D, D], f32), ("owT", [D, D], f32),
        ("l0bias", [128, 32], f32), ("bias_rows", [3, D], f32),
        ("misc_cols", [128, 8], f32),
        ("w1", [8, D, 4096], f32), ("w2lay", [8, 8, 128, 4096], f32),
        ("b1lay", [128, 256], f32), ("b2", [8, D], f32),
        ("wr", [D, 8], f32), ("br_bc", [128, 8], f32),
        ("wqkv", [D, 4096], f32), ("wg", [D, 2048], f32),
        ("wproj1", [2048, D], f32), ("decay", [1, 8], f32),
        ("wfcT", [D, 4096], f32), ("wprojT", [4096, D], f32),
        ("lnw_lay", [128, 32], f32), ("lnb_lay", [128, 32], f32),
    ]:
        ins[name] = dt_(name, shape, dty, kind="ExternalInput").ap()
    K.ins = ins
    K.wqfc_sc = dt_("wqfc_sc", [D, 4096], BF16, kind="Internal").ap()
    K.wqproj_sc = dt_("wqproj_sc", [8, 128, 32, 128], BF16, kind="Internal").ap()

    with ExitStack() as ctx:
        E = ctx.enter_context
        K.xres = E(nc.sbuf_tensor("xres", [128, NCH * T], f32))
        K.xn = E(nc.sbuf_tensor("xn", [128, NCH * T], BF16))
        K.ident_bf = E(nc.sbuf_tensor("ident_bf_s", [128, 128], BF16))[:]
        K.ident_f32 = E(nc.sbuf_tensor("ident_f32_s", [128, 128], f32))[:]
        K.ones_bf = E(nc.sbuf_tensor("ones_bf_s", [128, 512], BF16))[:]
        K.ones_f32 = E(nc.sbuf_tensor("ones_f32_s", [128, 128], f32))[:]
        K.l0bias = E(nc.sbuf_tensor("l0bias_s", [128, 32], f32))[:]
        brow_t = E(nc.sbuf_tensor("brow_bf", [65, D], BF16))
        K.brow_bf = brow_t[:]
        K.brow_dst = bass.AP(brow_t, 0, [[32 * D, 3], [1, D]])
        K.misc = E(nc.sbuf_tensor("misc_cols_s", [128, 8], f32))[:]
        K.b1lay = E(nc.sbuf_tensor("b1lay_s", [128, 256], f32))[:]
        K.b2bf = E(nc.sbuf_tensor("b2bf_s", [8, 1024], BF16))[:]
        K.wr_s = E(nc.sbuf_tensor("wr_s", [128, 64], f32))[:]
        K.br_s = E(nc.sbuf_tensor("br_s", [128, 8], f32))[:]
        K.basis = E(nc.sbuf_tensor("basis_s", [8, 1024], BF16))[:]
        K.lnw = E(nc.sbuf_tensor("lnw_s", [128, 32], f32))[:]
        K.lnb = E(nc.sbuf_tensor("lnb_s", [128, 32], f32))[:]
        K.decay_s = E(nc.sbuf_tensor("decay_s", [1, 8], f32))[:]
        K.lnd = E(nc.sbuf_tensor("lnd_s", [128, 16], f32))[:]
        K.stat_col = E(nc.sbuf_tensor("stat_col", [128, 64], f32))[:]
        K.smalls = E(nc.sbuf_tensor("smalls", [128, 64], f32))[:]
        K.coef = E(nc.sbuf_tensor("coef_s", [128, 128], f32))[:]
        K.coefT_bf = E(nc.sbuf_tensor("coefT_bf", [8, T], BF16))[:]
        K.s1s0_bf = E(nc.sbuf_tensor("s1s0_bf", [1, 2048], BF16))[:]

        arena_bf = E(nc.sbuf_tensor("arena_bf", [128, ARENA_BF_COLS], BF16))
        arena_f32 = E(nc.sbuf_tensor("arena_f32", [128, ARENA_F32_COLS], f32))
        K.bpool = Pool(arena_bf, ARENA_BF_COLS, "bf")
        K.fpool = Pool(arena_f32, ARENA_F32_COLS, "f32")

        K.psA = E(nc.psum_tensor("psA", [128, 512], f32))[:]
        K.psB = E(nc.psum_tensor("psB", [128, 512], f32))[:]
        K.psC = E(nc.psum_tensor("psC", [128, 512], f32))[:]
        K.psD = E(nc.psum_tensor("psD", [128, 512], f32))[:]
        K.psE = E(nc.psum_tensor("psE", [128, 512], f32))[:]
        K.psF = E(nc.psum_tensor("psF", [128, 512], f32))[:]
        K.psG = E(nc.psum_tensor("psG", [128, 512], f32))[:]
        K.psH = E(nc.psum_tensor("psH", [128, 512], f32))[:]

        sems = {k: E(nc.semaphore(f"sem_{k}")) for k in K.p.semkeys}
        K.sems = sems

        emit_program(K)
        K.p.replay(nc, sems)
    return nc


def emit_consts_load(K):
    p = K.p
    toks = []
    for name, dst in [
        ("ident_bf", K.ident_bf), ("ident_f32", K.ident_f32),
        ("ones_bf", K.ones_bf), ("ones_f32", K.ones_f32),
        ("l0bias", K.l0bias), ("b1lay", K.b1lay),
        ("br_bc", K.br_s), ("basis_bf", K.basis),
        ("lnw_lay", K.lnw), ("lnb_lay", K.lnb), ("decay", K.decay_s),
        ("misc_cols", K.misc),
    ]:
        toks.append(p.dma("sp", (lambda name, dst: lambda e: e.dma_start(
            dst, K.ins[name][:]))(name, dst)))
    src = K.ins["wr"].rearrange("(c p) e -> p c e", p=128)
    dstv = K.wr_s.rearrange("p (c e) -> p c e", e=8)
    toks.append(p.dma("sp", lambda e: e.dma_start(dstv, src)))
    toks.append(p.dma("gp", lambda e: e.dma_start(K.brow_dst, K.ins["bias_rows"][:])))
    b2tok = p.dma("gp", lambda e: e.dma_start(K.b2bf, K.ins["b2"][:]))
    toks.append(b2tok)
    K.consts_ready = toks[-1]
    K.consts_all = toks


def emit_program(K):
    emit_consts_load(K)
    emit_load_x(K)
    stages = ["io", "ln0", "l0attn", "moe", "ln1", "l1attn", "full"]
    lvl = stages.index(K.upto)
    if lvl >= 1:
        emit_ln(K, layer=0)
    if lvl >= 3:
        emit_router(K)
    if lvl >= 2:
        emit_l0_attn(K)
    if lvl >= 3:
        emit_moe(K)
    if lvl >= 4:
        emit_ln(K, layer=1)
    if lvl >= 5:
        emit_l1_attn(K)
    if lvl >= 6:
        emit_l1_ffn(K)
    emit_store(K)
    p = K.p
    p.op("sp", lambda e: e.nop(), deps=p.front("sp"))
    if any(p.cnt[k] for k in p.semkeys if k.startswith("gp")):
        p.op("gp", lambda e: e.nop(), deps=p.front("gp"))



def _kernel_impl(inputs, upto="full"):
    x = np.asarray(inputs["L0_attn_x"], np.float32)
    shared = _prep_shared(inputs)
    nc = build_nc(upto=upto)
    in_maps = []
    for core in range(8):
        m = dict(shared)
        m["x_in"] = np.ascontiguousarray(x[core])
        in_maps.append(m)
    res = run_bass_kernel_spmd(nc, in_maps, core_ids=list(range(8)))
    outs = [np.asarray(r["out"], np.float32) for r in res.results]
    return np.stack(outs, axis=0)


def kernel(**inputs):
    return _kernel_impl(inputs, upto="full")


def emit_ln(K, layer):
    p, ps = K.p, K.ps
    pre = list(K.final_deps)
    K.fpool.push(); K.bpool.push()
    xbf = K.bpool.alloc(8 * 512)
    sq = K.bpool.alloc(2 * 512)
    rows = K.fpool.alloc(4 * 512)
    bcast = K.fpool.alloc(2 * 512)
    tmp = K.fpool.alloc(512)
    mu_row = rows[0:1, 0:512]
    v_row = rows[0:1, 512:1024]
    r_row = rows[0:1, 1024:1536]
    nmr_row = rows[0:1, 1536:2048]
    r_bc = bcast[:, 0:512]
    nmr_bc = bcast[:, 512:1024]
    psMU = K.psC[0:1, 0:512]
    psVAR = K.psD[0:1, 0:512]

    xbf_war = None
    sq_war = [None, None]
    rows_war = None
    bc_war = None
    for g in range(4):
        t0 = 512 * g
        mm_toks = []
        for c in range(8):
            cpt = p.op("ve", (lambda c, t0: lambda e: e.tensor_copy(
                xbf[:, c * 512:(c + 1) * 512], CH(K.xres, c, t0, 512)))(c, t0),
                deps=pre + [xbf_war], sig=True)
            sqt = p.op("se", (lambda c, t0: lambda e: e.activation(
                sq[:, (c % 2) * 512:(c % 2 + 1) * 512],
                CH(K.xres, c, t0, 512), AF.Square))(c, t0),
                deps=pre + [sq_war[c % 2]], sig=True)
            p.op("pe", (lambda c: lambda e: e.matmul(
                psMU, K.ones_bf[:, 0:1], xbf[:, c * 512:(c + 1) * 512],
                start=(c == 0), stop=(c == 7)))(c),
                deps=[cpt] + ([ps.gate("C")] if c == 0 else []))
            p.op("pe", (lambda c: lambda e: e.matmul(
                psVAR, K.ones_bf[:, 0:1], sq[:, (c % 2) * 512:(c % 2 + 1) * 512],
                start=(c == 0), stop=(c == 7)))(c),
                deps=[sqt] + ([ps.gate("D")] if c == 0 else []))
            sq_war[c % 2] = p.sig("pe")
        mmtok = p.sig("pe")
        xbf_war = mmtok
        inv = 1.0 / 1024.0
        t1 = p.op("ve", lambda e: e.tensor_scalar_mul(mu_row, psMU, inv),
                  deps=[mmtok, rows_war], sig=True)
        t2 = p.op("ve", lambda e: e.tensor_scalar_mul(v_row, psVAR, inv),
                  deps=[mmtok], sig=True)
        ps.done("C", t1)
        ps.done("D", t2)
        t3 = p.op("se", lambda e: e.activation(r_row, mu_row, AF.Square),
                  deps=[t1, rows_war], sig=True)
        t4 = p.op("ve", lambda e: e.tensor_sub(v_row, v_row, r_row),
                  deps=[t3], sig=True)
        t5 = p.op("se", lambda e: e.activation(
            r_row, v_row, AF.Sqrt, bias=K.misc[0:1, 0:1]), deps=[t4], sig=True)
        t6 = p.op("ve", lambda e: e.reciprocal(r_row, r_row), deps=[t5], sig=True)
        t7 = p.op("ve", lambda e: e.scalar_tensor_tensor(
            nmr_row, mu_row, -1.0, r_row, ALU.mult, ALU.mult),
            deps=[t6], sig=True)

        if layer == 0:
            trdeps = [t7, ps.gate("A")]
            for j in range(4):
                p.op("pe", (lambda j: lambda e: e.transpose(
                    K.psA[:, j:j + 1], r_row[0:1, j * 128:(j + 1) * 128],
                    K.ident_f32[0:1, 0:1]))(j), deps=trdeps)
                trdeps = []
                p.op("pe", (lambda j: lambda e: e.transpose(
                    K.psA[:, 4 + j:5 + j], nmr_row[0:1, j * 128:(j + 1) * 128],
                    K.ident_f32[0:1, 0:1]))(j))
            ttok = p.sig("pe")
            c1 = p.op("ve", (lambda g: lambda e: e.tensor_copy(
                K.stat_col[:, g * 4:(g + 1) * 4], K.psA[:, 0:4]))(g),
                deps=[ttok], sig=True)
            c2 = p.op("ve", (lambda g: lambda e: e.tensor_copy(
                K.stat_col[:, 16 + g * 4:16 + (g + 1) * 4], K.psA[:, 4:8]))(g),
                deps=[ttok], sig=True)
            ps.done("A", c2)

        p.op("pe", lambda e: e.matmul(
            K.psE[:, 0:512], K.ones_f32[0:1, :], r_row, start=True, stop=True),
            deps=[t7, ps.gate("E")])
        bE = p.sig("pe")
        p.op("pe", lambda e: e.matmul(
            K.psF[:, 0:512], K.ones_f32[0:1, :], nmr_row, start=True, stop=True),
            deps=[t7, ps.gate("F")])
        bF = p.sig("pe")
        rows_war = bF
        e1 = p.op("ve", lambda e: e.tensor_copy(r_bc, K.psE[:, 0:512]),
                  deps=[bE, bc_war], sig=True)
        e2 = p.op("se", lambda e: e.copy(nmr_bc, K.psF[:, 0:512]),
                  deps=[bF, bc_war], sig=True)
        ps.done("E", e1)
        ps.done("F", e2)
        atoks = []
        for c in range(8):
            p.op("ve", (lambda c, t0: lambda e: e.tensor_mul(
                tmp, CH(K.xres, c, t0, 512), r_bc))(c, t0),
                deps=pre + [e1, e2])
            at = p.op("ve", (lambda c, t0: lambda e: e.tensor_add(
                CH(K.xn, c, t0, 512), tmp, nmr_bc))(c, t0), sig=True)
            atoks.append(at)
        bc_war = atoks[-1]
        pre = []
    K.final_deps = p.barrier(("pe", "ve", "se"))
    K.fpool.pop(); K.bpool.pop()
    if K.upto == "ln0":
        for c in range(8):
            p.op("ve", (lambda c: lambda e: e.tensor_copy(
                CH(K.xres, c, 0, T), CH(K.xn, c, 0, T)))(c),
                deps=list(K.final_deps), sig=True)
        K.final_deps = p.barrier(("ve",))


def ld_mini(K, dst2d, src_ap, col0, ncols, deps=()):
    src = src_ap.rearrange("(kc p) f -> p kc f", p=128)[:, :, col0:col0 + ncols]
    dstv = dst2d.rearrange("p (kc f) -> p kc f", f=ncols)
    return K.p.dma("gp", (lambda dstv, src: lambda e: e.dma_start(dstv, src))(dstv, src),
                   deps=deps)


def emit_l0_attn(K):
    p, ps = K.p, K.ps
    pre = list(K.final_deps)
    K.fpool.push(); K.bpool.push()
    mask0 = K.bpool.alloc(2048)
    kring = K.bpool.alloc(8 * 1024)
    vring = K.bpool.alloc(8 * 1024)
    qg = K.bpool.alloc(4 * 1024)
    attnT = K.bpool.alloc(8 * 512)
    wk = K.bpool.alloc(4 * 1024)
    wv = K.bpool.alloc(2 * 2048)
    wtsT = K.bpool.alloc(3 * 512)
    wtsf = K.fpool.alloc(3 * 512)
    wtse = K.fpool.alloc(3 * 512)

    mtok = p.dma("sp", lambda e: e.dma_start(mask0, K.ins["mask0"][:]), deps=pre)

    wkslot_war = [None] * 4
    wvslot_war = [None] * 2
    wk_seq = 0
    attn_pe = [None]
    oproj_pe = None
    wts_war = {"f": [None] * 3, "bf": [None] * 3, "T": [None] * 3}
    smalls_war = [None] * 3
    QCH = {"q": K.ins["qwT"], "k": K.ins["kwT"], "o": K.ins["owT"]}

    for g in range(4):
        t0 = 512 * g
        rc0 = (512 * g) % 1024

        for c in range(8):
            slot = wk_seq % 4
            wk_seq += 1
            sl = wk[:, slot * 1024:(slot + 1) * 1024]
            wt = ld_mini(K, sl, K.ins["kwT"], c * 128, 128,
                         deps=[wkslot_war[slot]] + (pre if g == 0 else []))
            bank = "C" if c % 2 == 0 else "D"
            psb = K.psC if c % 2 == 0 else K.psD
            dep0 = [wt, ps.gate(bank)] + (pre if g == 0 else [])
            for kc in range(8):
                p.op("pe", (lambda sl, kc, c, t0, psb: lambda e: e.matmul(
                    psb[:, 0:512], sl[:, kc * 128:(kc + 1) * 128],
                    CH(K.xn, kc, t0, 512), start=(kc == 0), stop=False))(
                        sl, kc, c, t0, psb),
                    deps=dep0 if kc == 0 else [])
            p.op("pe", (lambda c, psb: lambda e: e.matmul(
                psb[:, 0:512], K.brow_bf[32:33, c * 128:(c + 1) * 128],
                K.ones_bf[32:33, 0:512], start=False, stop=True))(c, psb))
            mmtok = p.sig("pe")
            wkslot_war[slot] = mmtok
            eng = "ve" if c % 2 == 0 else "se"
            dst = kring[:, c * 1024 + rc0: c * 1024 + rc0 + 512]
            if eng == "ve":
                fn = (lambda dst, psb: lambda e: e.tensor_copy(dst, psb[:, 0:512]))(dst, psb)
            else:
                fn = (lambda dst, psb: lambda e: e.copy(dst, psb[:, 0:512]))(dst, psb)
            et = p.op(eng, fn, deps=[mmtok, attn_pe[0]], sig=True)
            ps.done(bank, et)

        for u in range(4):
            su = u % 2
            sl = wv[:, su * 2048:(su + 1) * 2048]
            wt = ld_mini(K, sl, K.ins["vwT"], u * 256, 256,
                         deps=[wvslot_war[su]] + (pre if g == 0 else []))
            for i in range(4):
                gi = 4 * g + i
                s = gi % 8
                bank = "E" if (u * 4 + i) % 2 == 0 else "F"
                psb = K.psE if bank == "E" else K.psF
                dep0 = [wt, ps.gate(bank)]
                for kc in range(8):
                    p.op("pe", (lambda sl, kc, gi, psb: lambda e: e.matmul(
                        psb[:, 0:256], CH(K.xn, kc, gi * 128, 128),
                        sl[:, kc * 256:(kc + 1) * 256],
                        start=(kc == 0), stop=False))(sl, kc, gi, psb),
                        deps=dep0 if kc == 0 else [])
                p.op("pe", (lambda u, psb: lambda e: e.matmul(
                    psb[:, 0:256], K.ones_bf[64:65, 0:128],
                    K.brow_bf[64:65, u * 256:(u + 1) * 256],
                    start=False, stop=True))(u, psb))
                mmtok = p.sig("pe")
                if i == 3:
                    wvslot_war[su] = mmtok
                eng = "ve" if i % 2 == 0 else "se"
                dst = vring[:, s * 1024 + u * 256: s * 1024 + u * 256 + 256]
                if eng == "ve":
                    fn = (lambda dst, psb: lambda e: e.tensor_copy(dst, psb[:, 0:256]))(dst, psb)
                else:
                    fn = (lambda dst, psb: lambda e: e.copy(dst, psb[:, 0:256]))(dst, psb)
                et = p.op(eng, fn, deps=[mmtok, attn_pe[0]], sig=True)
                ps.done(bank, et)

        for c in range(8):
            slot = wk_seq % 4
            wk_seq += 1
            sl = wk[:, slot * 1024:(slot + 1) * 1024]
            wt = ld_mini(K, sl, K.ins["qwT"], c * 128, 128, deps=[wkslot_war[slot]])
            bank = "C" if c % 2 == 0 else "D"
            psb = K.psC if c % 2 == 0 else K.psD
            for i in range(4):
                gi = 4 * g + i
                reg = psb[:, i * 128:(i + 1) * 128]
                dep0 = [wt, ps.gate(bank)] if i == 0 else []
                for kc in range(8):
                    p.op("pe", (lambda sl, kc, gi, reg: lambda e: e.matmul(
                        reg, sl[:, kc * 128:(kc + 1) * 128],
                        CH(K.xn, kc, gi * 128, 128),
                        start=(kc == 0), stop=False))(sl, kc, gi, reg),
                        deps=dep0)
                p.op("pe", (lambda c, reg: lambda e: e.matmul(
                    reg, K.brow_bf[0:1, c * 128:(c + 1) * 128],
                    K.ones_bf[0:1, 0:128], start=False, stop=True))(c, reg))
            mmtok = p.sig("pe")
            wkslot_war[slot] = mmtok
            dstv = qg.rearrange("p (i f) -> p i f", f=1024)[:, :, c * 128:(c + 1) * 128]
            psv = psb.rearrange("p (i f) -> p i f", f=128)
            eng = "ve" if c % 2 == 0 else "se"
            if eng == "ve":
                fn = (lambda dstv, psv: lambda e: e.tensor_copy(dstv, psv))(dstv, psv)
            else:
                fn = (lambda dstv, psv: lambda e: e.copy(dstv, psv))(dstv, psv)
            et = p.op(eng, fn, deps=[mmtok, attn_pe[0]], sig=True)
            ps.done(bank, et)

        filldone = p.barrier(("ve", "se"))
        new_attn_pe = None
        for i in range(4):
            qi = 4 * g + i
            b = qi // 2
            parity = qi % 2
            mv = parity + 2 * (1 if b == 0 else 0)
            msl = mask0[:, mv * 512:(mv + 1) * 512]
            rc = (256 * b - 256) % 1024
            for h in range(16):
                it = i * 16 + h
                sbank = ("C", "D", "A")[it % 3]
                psS = (K.psC, K.psD, K.psA)[it % 3]
                ch = h // 2
                r0 = 64 * (h % 2)
                qsl = qg[r0:r0 + 64, i * 1024 + ch * 128: i * 1024 + (ch + 1) * 128]
                krow = lambda a, b_: kring[r0:r0 + 64, ch * 1024 + a: ch * 1024 + b_]
                sdep = [mtok, ps.gate(sbank)] + (filldone if it == 0 else [])
                if b == 0:
                    p.op("pe", (lambda qsl, psS, k1: lambda e: e.matmul(
                        psS[:, 256:512], qsl, k1, start=True, stop=True))(
                            qsl, psS, krow(0, 256)), deps=sdep)
                    stok = p.sig("pe")
                    mz = p.op("ve", (lambda psS: lambda e: e.memset(
                        psS[:, 0:256], 0.0))(psS), deps=[stok], sig=True)
                    stok = mz
                elif rc == 768:
                    p.op("pe", (lambda qsl, psS, k1: lambda e: e.matmul(
                        psS[:, 0:256], qsl, k1, start=True, stop=True))(
                            qsl, psS, krow(768, 1024)), deps=sdep)
                    p.op("pe", (lambda qsl, psS, k2: lambda e: e.matmul(
                        psS[:, 256:512], qsl, k2, start=True, stop=True))(
                            qsl, psS, krow(0, 256)))
                    stok = p.sig("pe")
                else:
                    p.op("pe", (lambda qsl, psS, k1: lambda e: e.matmul(
                        psS[:, 0:512], qsl, k1, start=True, stop=True))(
                            qsl, psS, krow(rc, rc + 512)), deps=sdep)
                    stok = p.sig("pe")
                wf = wtsf[:, (it % 3) * 512:(it % 3 + 1) * 512]
                wb = wtse[:, (it % 3) * 512:(it % 3 + 1) * 512]
                wT = wtsT[:, (it % 3) * 512:(it % 3 + 1) * 512]
                t1 = p.op("ve", (lambda wf, psS, msl: lambda e: e.scalar_tensor_tensor(
                    wf, psS[:, 0:512], 0.125, msl, ALU.mult, ALU.add))(wf, psS, msl),
                    deps=[stok, wts_war["f"][it % 3]], sig=True)
                ps.done(sbank, t1)
                scol = K.smalls[:, (it % 3) * 4 + 1:(it % 3) * 4 + 2]
                rcol = K.smalls[:, (it % 3) * 4 + 2:(it % 3) * 4 + 3]
                t3 = p.op("se", (lambda wb, wf, scol: lambda e: e.activation(
                    wb, wf, AF.Exp, bias=K.misc[:, 2:3], accum_out=scol))(wb, wf, scol),
                    deps=[t1, wts_war["bf"][it % 3], smalls_war[it % 3]], sig=True)
                wts_war["f"][it % 3] = t3
                smalls_war[it % 3] = t3
                t4 = p.op("ve", (lambda rcol, scol: lambda e: e.reciprocal(
                    rcol, scol))(rcol, scol), deps=[t3], sig=True)
                t5 = p.op("se", (lambda wb, rcol: lambda e: e.activation(
                    wb, wb, AF.Copy, scale=rcol))(wb, rcol), deps=[t4], sig=True)
                valid = [2, 3] if b == 0 else [0, 1, 2, 3]
                tbank = ("G", "H", "B")[it % 3]
                psT = (K.psG, K.psH, K.psB)[it % 3]
                tdep = [t5, ps.gate(tbank)]
                for kc in valid:
                    p.op("pe", (lambda psT, wb, kc: lambda e: e.transpose(
                        psT[:, kc * 128:(kc + 1) * 128],
                        wb[:, kc * 128:(kc + 1) * 128], K.ident_f32))(psT, wb, kc),
                        deps=tdep)
                    tdep = []
                ttok = p.sig("pe")
                wts_war["bf"][it % 2] = ttok
                lo = valid[0] * 128
                hi = (valid[-1] + 1) * 128
                eng = "ve" if it % 2 == 0 else "se"
                if eng == "ve":
                    fn = (lambda wT, psT, lo, hi: lambda e: e.tensor_copy(
                        wT[:, lo:hi], psT[:, lo:hi]))(wT, psT, lo, hi)
                else:
                    fn = (lambda wT, psT, lo, hi: lambda e: e.copy(
                        wT[:, lo:hi], psT[:, lo:hi]))(wT, psT, lo, hi)
                t6 = p.op(eng, fn, deps=[ttok, wts_war["T"][it % 3]], sig=True)
                ps.done(tbank, t6)
                pbank = "E" if (h // 2) % 2 == 0 else "F"
                psP = K.psE if (h // 2) % 2 == 0 else K.psF
                avdep = [t6] + ([ps.gate(pbank)] if h % 2 == 0 else [])
                for kidx, kc in enumerate(valid):
                    gkc = 2 * b - 2 + kc
                    s = gkc % 8
                    vsl = vring[:, s * 1024 + h * 64: s * 1024 + h * 64 + 64]
                    p.op("pe", (lambda psP, r0, vsl, wT, kc, st, sp: lambda e: e.matmul(
                        psP[r0:r0 + 64, 0:128], vsl, wT[:, kc * 128:(kc + 1) * 128],
                        start=st, stop=sp))(
                            psP, r0, vsl, wT, kc, kc == valid[0], kc == valid[-1]),
                        deps=avdep if kidx == 0 else [])
                avtok = p.sig("pe")
                wts_war["T"][it % 3] = avtok
                new_attn_pe = avtok
                if h % 2 == 1:
                    dst = attnT[:, ch * 512 + i * 128: ch * 512 + (i + 1) * 128]
                    eng = "ve" if ch % 2 == 0 else "se"
                    if eng == "ve":
                        fn = (lambda dst, psP: lambda e: e.tensor_copy(
                            dst, psP[:, 0:128]))(dst, psP)
                    else:
                        fn = (lambda dst, psP: lambda e: e.copy(
                            dst, psP[:, 0:128]))(dst, psP)
                    pv = p.op(eng, fn, deps=[avtok, oproj_pe], sig=True)
                    ps.done(pbank, pv)
        attn_pe[0] = new_attn_pe
        attn_evacs = p.barrier(("ve", "se"))

        for c in range(8):
            slot = wk_seq % 4
            wk_seq += 1
            sl = wk[:, slot * 1024:(slot + 1) * 1024]
            wt = ld_mini(K, sl, K.ins["owT"], c * 128, 128, deps=[wkslot_war[slot]])
            bank = "C" if c % 2 == 0 else "D"
            psb = K.psC if c % 2 == 0 else K.psD
            dep0 = [wt, ps.gate(bank)] + (attn_evacs if c == 0 else [])
            for kc in range(8):
                p.op("pe", (lambda sl, kc, psb, attnT: lambda e: e.matmul(
                    psb[:, 0:512], sl[:, kc * 128:(kc + 1) * 128],
                    attnT[:, kc * 512:(kc + 1) * 512],
                    start=(kc == 0), stop=(kc == 7)))(sl, kc, psb, attnT),
                    deps=dep0 if kc == 0 else [])
            mmtok = p.sig("pe")
            wkslot_war[slot] = mmtok
            ob = K.l0bias[:, 24 + c:25 + c]
            et = p.op("ve", (lambda c, t0, psb, ob: lambda e: e.scalar_tensor_tensor(
                CH(K.xres, c, t0, 512), psb[:, 0:512], ob,
                CH(K.xres, c, t0, 512), ALU.add, ALU.add))(c, t0, psb, ob),
                deps=[mmtok], sig=True)
            ps.done(bank, et)
        oproj_pe = p.sig("pe")

    K.final_deps = p.barrier(("pe", "ve", "se"))
    K.fpool.pop(); K.bpool.pop()




def emit_router(K):
    p, ps = K.p, K.ps
    pre = list(K.final_deps)
    K.fpool.push()
    lg = K.fpool.alloc(8)
    lg2 = K.fpool.alloc(8)
    s_bc = K.fpool.alloc(8)
    dep0 = pre + [ps.gate("C")]
    for c in range(8):
        p.op("pe", (lambda c: lambda e: e.matmul(
            K.psC[0:1, 0:8], K.ones_f32[:, 0:1], K.wr_s[:, c * 8:(c + 1) * 8],
            start=(c == 0), stop=(c == 7)))(c), deps=dep0 if c == 0 else [])
    t = p.sig("pe")
    sv = p.op("ve", lambda e: e.tensor_copy(K.smalls[0:1, 32:40], K.psC[0:1, 0:8]),
              deps=[t], sig=True)
    ps.done("C", sv)
    p.op("pe", lambda e: e.matmul(K.psD[:, 0:8], K.ones_f32[0:1, :],
                                  K.smalls[0:1, 32:40], start=True, stop=True),
         deps=[sv, ps.gate("D")])
    t = p.sig("pe")
    sb = p.op("ve", lambda e: e.tensor_copy(s_bc, K.psD[:, 0:8]), deps=[t], sig=True)
    ps.done("D", sb)
    for i in range(16):
        dep0 = [ps.gate("C")]
        for c in range(8):
            p.op("pe", (lambda c, i: lambda e: e.matmul(
                K.psC[:, 0:8], CH(K.xres, c, i * 128, 128),
                K.wr_s[:, c * 8:(c + 1) * 8],
                start=(c == 0), stop=(c == 7)))(c, i), deps=dep0 if c == 0 else [])
        mmt = p.sig("pe")
        t1 = p.op("ve", (lambda i: lambda e: e.tensor_single_scalar(
            lg, K.psC[:, 0:8], K.stat_col[:, i:i + 1], ALU.mult))(i),
            deps=[mmt, sb], sig=True)
        ps.done("C", t1)
        p.op("ve", (lambda i: lambda e: e.scalar_tensor_tensor(
            lg2, s_bc, K.stat_col[:, 16 + i:17 + i], lg, ALU.mult, ALU.add))(i))
        p.op("ve", lambda e: e.tensor_add(lg, lg2, K.br_s))
        m = K.smalls[:, 8:9]
        nm = K.smalls[:, 9:10]
        ss = K.smalls[:, 10:11]
        rs = K.smalls[:, 11:12]
        p.op("ve", lambda e: e.reduce_max(m, lg, axis=AX.X))
        t5 = p.op("ve", lambda e: e.tensor_single_scalar(lg2, lg, m, ALU.is_ge),
                  sig=True)
        t6 = p.op("ve", lambda e: e.tensor_scalar_mul(nm, m, -1.0), sig=True)
        t7 = p.op("se", lambda e: e.activation(lg, lg, AF.Exp, bias=nm,
                                               accum_out=ss),
                  deps=[t5, t6], sig=True)
        t8 = p.op("ve", lambda e: e.reciprocal(rs, ss), deps=[t7], sig=True)
        t9 = p.op("ve", (lambda i: lambda e: e.tensor_single_scalar(
            K.coef[:, 8 * i:8 * (i + 1)], lg2, rs, ALU.mult))(i), sig=True)
        p.op("pe", (lambda i: lambda e: e.transpose(
            K.psA[0:8, 0:128], K.coef[:, 8 * i:8 * (i + 1)], K.ident_f32))(i),
            deps=[t9, ps.gate("A")])
        tt = p.sig("pe")
        tc = p.op("ve", (lambda i: lambda e: e.tensor_copy(
            K.coefT_bf[0:8, i * 128:(i + 1) * 128], K.psA[0:8, 0:128]))(i),
            deps=[tt], sig=True)
        ps.done("A", tc)
    K.final_deps = p.barrier(("pe", "ve", "se"))
    K.fpool.pop()


def emit_moe(K):
    p, ps = K.p, K.ps
    pre = list(K.final_deps)
    K.fpool.push(); K.bpool.push()
    h_all = K.bpool.alloc(32 * 512)
    w1b = K.bpool.alloc(2 * 4096)
    w2b = K.bpool.alloc(2 * 4096)
    cbc = K.fpool.alloc(2 * 512)
    tmp = K.fpool.alloc(512)
    w1war = [None, None]
    w2war = [None, None]
    cbcwar = [None, None]
    h_war = None
    u1 = 0
    u2 = 0
    first = pre
    for g in range(4):
        t0 = g * 512
        for e8 in range(8):
            csl = (g * 8 + e8) % 2
            cb = cbc[:, csl * 512:(csl + 1) * 512]
            p.op("pe", (lambda e8, t0: lambda e: e.matmul(
                K.psG[:, 0:512], K.basis[:, e8 * 128:(e8 + 1) * 128],
                K.coefT_bf[0:8, t0:t0 + 512], start=True, stop=True))(e8, t0),
                deps=first + [ps.gate("G")])
            first = []
            bt = p.sig("pe")
            cbt = p.op("ve", (lambda cb: lambda e: e.tensor_copy(
                cb, K.psG[:, 0:512]))(cb), deps=[bt, cbcwar[csl]], sig=True)
            ps.done("G", cbt)
            for fb in range(8):
                slot = u1 % 2
                u1 += 1
                w1u = w1b[:, slot * 4096:(slot + 1) * 4096]
                src = K.ins["w1"][e8].rearrange(
                    "(kc p) f -> p kc f", p=128)[:, :, fb * 512:(fb + 1) * 512]
                dstv = w1u.rearrange("p (kc f) -> p kc f", f=512)
                wt = p.dma("gp", (lambda dstv, src: lambda e: e.dma_start(
                    dstv, src))(dstv, src), deps=[w1war[slot]] + pre)
                for fc in range(4):
                    hidx = fb * 4 + fc
                    bank = "C" if hidx % 2 == 0 else "D"
                    psb = K.psC if hidx % 2 == 0 else K.psD
                    dep0 = [wt, ps.gate(bank)]
                    for kc in range(8):
                        p.op("pe", (lambda w1u, kc, fc, t0, psb: lambda e: e.matmul(
                            psb[:, 0:512], w1u[:, kc * 512 + fc * 128: kc * 512 + (fc + 1) * 128],
                            CH(K.xn, kc, t0, 512),
                            start=(kc == 0), stop=(kc == 7)))(w1u, kc, fc, t0, psb),
                            deps=dep0 if kc == 0 else [])
                    mmtok = p.sig("pe")
                    if fc == 3:
                        w1war[slot] = mmtok
                    et = p.op("se", (lambda hidx, e8, psb: lambda e: e.activation(
                        h_all[:, hidx * 512:(hidx + 1) * 512], psb[:, 0:512],
                        AF.Gelu, bias=K.b1lay[:, e8 * 32 + hidx:e8 * 32 + hidx + 1]))(
                            hidx, e8, psb),
                        deps=[mmtok, h_war], sig=True)
                    ps.done(bank, et)
            hdone = p.barrier(("se",))
            for dc in range(8):
                slot = u2 % 2
                u2 += 1
                w2u = w2b[:, slot * 4096:(slot + 1) * 4096]
                src = K.ins["w2lay"][e8][dc]
                wt = p.dma("gp", (lambda w2u, src: lambda e: e.dma_start(
                    w2u, src))(w2u, src), deps=[w2war[slot]] + pre)
                bank = "E" if dc % 2 == 0 else "F"
                psb = K.psE if dc % 2 == 0 else K.psF
                dep0 = [wt, ps.gate(bank)] + hdone
                for fk in range(32):
                    p.op("pe", (lambda w2u, fk, psb: lambda e: e.matmul(
                        psb[:, 0:512], w2u[:, fk * 128:(fk + 1) * 128],
                        h_all[:, fk * 512:(fk + 1) * 512],
                        start=(fk == 0), stop=(fk == 31)))(w2u, fk, psb),
                        deps=dep0 if fk == 0 else [])
                mmtok = p.sig("pe")
                w2war[slot] = mmtok
                tm = p.op("ve", (lambda psb, cb: lambda e: e.tensor_mul(
                    tmp, psb[:, 0:512], cb))(psb, cb), deps=[mmtok, cbt], sig=True)
                ta = p.op("ve", (lambda dc, t0: lambda e: e.tensor_add(
                    CH(K.xres, dc, t0, 512), CH(K.xres, dc, t0, 512), tmp))(dc, t0),
                    sig=True)
                ps.done(bank, ta)
                if dc == 7:
                    cbcwar[csl] = tm
            h_war = p.sig("pe")
        for dc in range(8):
            bank = "A" if dc % 2 == 0 else "B"
            psb = K.psA if dc % 2 == 0 else K.psB
            p.op("pe", (lambda dc, t0, psb: lambda e: e.matmul(
                psb[:, 0:512], K.b2bf[0:8, dc * 128:(dc + 1) * 128],
                K.coefT_bf[0:8, t0:t0 + 512], start=True, stop=True))(dc, t0, psb),
                deps=[ps.gate(bank)])
            mmtok = p.sig("pe")
            ta = p.op("ve", (lambda dc, t0, psb: lambda e: e.tensor_add(
                CH(K.xres, dc, t0, 512), CH(K.xres, dc, t0, 512),
                psb[:, 0:512]))(dc, t0, psb), deps=[mmtok], sig=True)
            ps.done(bank, ta)
    K.final_deps = p.barrier(("pe", "ve", "se"))
    K.fpool.pop(); K.bpool.pop()




def emit_l1_attn(K):
    p, ps = K.p, K.ps
    pre = list(K.final_deps)
    K.fpool.push(); K.bpool.push()
    maskc = K.bpool.alloc(2048)
    v_h = K.bpool.alloc(16 * 256)
    kd_h = K.bpool.alloc(2048)
    qd = K.bpool.alloc(2 * 512)
    wch = K.bpool.alloc(2 * 512)
    ret = K.bpool.alloc(2 * 1024)
    gate = K.bpool.alloc(2 * 1024)
    wqu = K.bpool.alloc(4 * 2048)
    xps = K.bpool.alloc(4 * 512)
    sqb = K.bpool.alloc(2 * 512)
    kr = K.bpool.alloc(1024).bitcast(F32)
    ks = K.bpool.alloc(1024).bitcast(F32)
    rt1 = K.bpool.alloc(1024).bitcast(F32)
    possl = K.fpool.alloc(2 * 512)
    dkt = K.fpool.alloc(2 * 512)
    rows = K.fpool.alloc(4 * 512)
    bcast = K.fpool.alloc(2 * 512)
    mu_row = rows[0:1, 0:512]
    v_row = rows[0:1, 512:1024]
    r_row = rows[0:1, 1024:1536]
    nmr_row = rows[0:1, 1536:2048]
    r_bc = bcast[:, 0:512]
    nmr_bc = bcast[:, 512:1024]
    SWAP = []
    for i in range(16):
        SWAP += [2 * i + 1, 2 * i]
    CINV = 1.0 / np.sqrt(128.0)

    mtok = p.dma("sp", lambda e: e.dma_start(maskc, K.ins["maskc"][:]), deps=pre)

    lr = K.smalls[0:1, 48:56]
    t = p.op("se", lambda e: e.activation(lr, K.decay_s, AF.Ln),
             deps=pre, sig=True)
    p.op("pe", lambda e: e.matmul(K.psA[:, 0:8], K.ones_f32[0:1, :], lr,
                                  start=True, stop=True),
         deps=[t, ps.gate("A")])
    t = p.sig("pe")
    t1 = p.op("ve", lambda e: e.tensor_copy(K.lnd[:, 0:8], K.psA[:, 0:8]),
              deps=[t], sig=True)
    ps.done("A", t1)
    t2 = p.op("ve", lambda e: e.tensor_scalar_mul(K.lnd[:, 8:16],
                                                  K.lnd[:, 0:8], -1.0), sig=True)
    lnd_done = t2

    useq = [0]
    uwar = [None, None, None, None]
    xwar = [None, None]
    poswar = [None, None]
    dktwar = [None, None]
    retwar = [None, None]
    gatewar = [None, None]
    sq_war = None
    qdwar = [None, None]
    wchwar = [None, None]
    kdwar = None
    vwar = None

    def wunit(cols0, ncols, deps):
        slot = useq[0] % 4
        useq[0] += 1
        sl = wqu[:, slot * 2048: slot * 2048 + 8 * ncols]
        src = K.ins["wqkv"].rearrange("(kc p) f -> p kc f", p=128)[
            :, :, cols0:cols0 + ncols]
        dstv = sl.rearrange("p (kc f) -> p kc f", f=ncols)
        tok = p.dma("gp", (lambda dstv, src: lambda e: e.dma_start(dstv, src))(
            dstv, src), deps=[uwar[slot]] + deps)
        return slot, sl, tok

    def wunit_from(name, rows0, nrows, cols0, ncols, deps):
        slot = useq[0] % 4
        useq[0] += 1
        sl = wqu[:, slot * 2048: slot * 2048 + (nrows // 128) * ncols]
        src = K.ins[name][rows0:rows0 + nrows].rearrange(
            "(kc p) f -> p kc f", p=128)[:, :, cols0:cols0 + ncols]
        dstv = sl.rearrange("p (kc f) -> p kc f", f=ncols)
        tok = p.dma("gp", (lambda dstv, src: lambda e: e.dma_start(dstv, src))(
            dstv, src), deps=[uwar[slot]] + deps)
        return slot, sl, tok

    def rotary_decay(g, h, raw_bank, psb, dst, qk, dep0, dstwar=None):
        sl = g % 2
        cs_s = xps[:, sl * 1024: sl * 1024 + 512]
        sn_s = xps[:, sl * 1024 + 512: sl * 1024 + 1024]
        csn = "csq" if qk == "q" else "csk"
        snn = "snq" if qk == "q" else "snk"
        ct = p.dma("sp", (lambda cs_s, csn, h, g: lambda e: e.dma_start(
            cs_s, K.ins[csn][128 * h:128 * (h + 1), 512 * g:512 * (g + 1)]))(
                cs_s, csn, h, g), deps=[xwar[sl]])
        st_ = p.dma("sp", (lambda sn_s, snn, h, g: lambda e: e.dma_start(
            sn_s, K.ins[snn][128 * h:128 * (h + 1), 512 * g:512 * (g + 1)]))(
                sn_s, snn, h, g), deps=[xwar[sl]])
        pt = p.dma("sp", (lambda g: lambda e: e.dma_start(
            possl[:, (g % 2) * 512:(g % 2 + 1) * 512],
            K.ins["pos"][:, 512 * g:512 * (g + 1)]))(g), deps=[poswar[g % 2]])
        pos_s = possl[:, (g % 2) * 512:(g % 2 + 1) * 512]
        e1 = p.op("ve", (lambda psb: lambda e: e.tensor_copy(kr, psb[:, 0:512]))(psb),
                  deps=dep0, sig=True)
        ps.done(raw_bank, e1)
        p.op("ve", lambda e: e.stream_shuffle(ks, kr, SWAP))
        p.op("ve", (lambda cs_s: lambda e: e.tensor_mul(rt1, kr, cs_s))(cs_s),
             deps=[ct])
        p.op("ve", (lambda sn_s: lambda e: e.tensor_mul(ks, ks, sn_s))(sn_s),
             deps=[st_])
        s1 = p.op("ve", lambda e: e.tensor_add(rt1, rt1, ks), sig=True)
        lcol = K.lnd[:, h:h + 1] if qk == "q" else K.lnd[:, 8 + h:9 + h]
        dk_s = dkt[:, (g % 2) * 512:(g % 2 + 1) * 512]
        d1 = p.op("se", (lambda dk_s, pos_s, lcol: lambda e: e.activation(
            dk_s, pos_s, AF.Exp, scale=lcol))(dk_s, pos_s, lcol),
            deps=[pt, lnd_done, dktwar[g % 2]], sig=True)
        poswar[g % 2] = d1
        o1 = p.op("ve", (lambda dst, dk_s: lambda e: e.tensor_mul(
            dst, rt1, dk_s))(dst, dk_s), deps=[s1, d1, dstwar], sig=True)
        dktwar[g % 2] = o1
        xwar[sl] = o1
        return o1

    for h in range(8):
        slot, ku, ktok = wunit(1024 + 128 * h, 128, pre if h == 0 else [])
        kd_toks = []
        for g in range(4):
            bank = "C" if g % 2 == 0 else "D"
            psb = K.psC if g % 2 == 0 else K.psD
            dep0 = [ktok, ps.gate(bank), kdwar]
            for kc in range(8):
                p.op("pe", (lambda ku, kc, g, psb: lambda e: e.matmul(
                    psb[:, 0:512], ku[:, kc * 128:(kc + 1) * 128],
                    CH(K.xn, kc, g * 512, 512),
                    start=(kc == 0), stop=(kc == 7)))(ku, kc, g, psb),
                    deps=dep0 if kc == 0 else [])
            mmtok = p.sig("pe")
            if g == 3:
                uwar[slot] = mmtok
            kd_toks.append(rotary_decay(
                g, h, bank, psb, kd_h[:, g * 512:(g + 1) * 512], "k", [mmtok],
                dstwar=kdwar))
        slot, vu, vtok = wunit(2048 + 256 * h, 256, [])
        v_toks = []
        for gi in range(16):
            bank = "E" if gi % 2 == 0 else "F"
            psb = K.psE if gi % 2 == 0 else K.psF
            dep0 = [vtok, ps.gate(bank), vwar]
            for kc in range(8):
                p.op("pe", (lambda vu, kc, gi, psb: lambda e: e.matmul(
                    psb[:, 0:256], CH(K.xn, kc, gi * 128, 128),
                    vu[:, kc * 256:(kc + 1) * 256],
                    start=(kc == 0), stop=(kc == 7)))(vu, kc, gi, psb),
                    deps=dep0 if kc == 0 else [])
            mmtok = p.sig("pe")
            if gi == 15:
                uwar[slot] = mmtok
            eng = "ve" if gi % 2 == 0 else "se"
            dst = v_h[:, gi * 256:(gi + 1) * 256]
            if eng == "ve":
                fn = (lambda dst, psb: lambda e: e.tensor_copy(dst, psb[:, 0:256]))(dst, psb)
            else:
                fn = (lambda dst, psb: lambda e: e.copy(dst, psb[:, 0:256]))(dst, psb)
            et = p.op(eng, fn, deps=[mmtok], sig=True)
            ps.done(bank, et)
            v_toks.append(et)
        sqg, qu, qtok = wunit(128 * h, 128, [])
        sgg, gu, gtok = wunit_from("wg", 0, 1024, 256 * h, 256, [])
        spg, pu, ptok = wunit_from("wproj1", 256 * h, 256, 0, 1024, [])
        q_used = None
        g_used = None
        p_used = None
        for g in range(4):
            t0 = 512 * g
            bank = "C" if g % 2 == 0 else "D"
            psb = K.psC if g % 2 == 0 else K.psD
            dep0 = [qtok, ps.gate(bank), qdwar[g % 2]]
            for kc in range(8):
                p.op("pe", (lambda qu, kc, t0, psb: lambda e: e.matmul(
                    psb[:, 0:512], qu[:, kc * 128:(kc + 1) * 128],
                    CH(K.xn, kc, t0, 512),
                    start=(kc == 0), stop=(kc == 7)))(qu, kc, t0, psb),
                    deps=dep0 if kc == 0 else [])
            mmtok = p.sig("pe")
            q_used = mmtok
            qsl = qd[:, (g % 2) * 512:(g % 2 + 1) * 512]
            qdone = rotary_decay(g, h, bank, psb, qsl, "q", [mmtok],
                                 dstwar=qdwar[g % 2])
            nkc = 4 * g + 4
            avdep0 = [ps.gate("G")]
            avdep1 = [ps.gate("H")]
            av_last = None
            ets = [None] * nkc
            def emit_scores(kc):
                bank = "C" if kc % 2 == 0 else "D"
                psb = K.psC if kc % 2 == 0 else K.psD
                p.op("pe", (lambda kc, qsl, psb: lambda e: e.matmul(
                    psb[:, 0:512], kd_h[:, kc * 128:(kc + 1) * 128], qsl,
                    start=True, stop=True))(kc, qsl, psb),
                    deps=[qdone, kd_toks[kc // 4], ps.gate(bank)])
                smt = p.sig("pe")
                wsl = wch[:, (kc % 2) * 512:(kc % 2 + 1) * 512]
                d = kc - 4 * g
                if d >= 0:
                    et = p.op("ve", (lambda wsl, psb, d: lambda e: e.scalar_tensor_tensor(
                        wsl, psb[:, 0:512], CINV,
                        maskc[:, d * 512:(d + 1) * 512], ALU.mult, ALU.mult))(
                            wsl, psb, d), deps=[smt, mtok, wchwar[kc % 2]], sig=True)
                else:
                    et = p.op("ve", (lambda wsl, psb: lambda e: e.tensor_single_scalar(
                        wsl, psb[:, 0:512], CINV, ALU.mult))(wsl, psb),
                        deps=[smt, wchwar[kc % 2]], sig=True)
                ps.done(bank, et)
                ets[kc] = (et, wsl)
            emit_scores(0)
            for kc in range(nkc):
                if kc + 1 < nkc:
                    emit_scores(kc + 1)
                et, wsl = ets[kc]
                for dvc in range(2):
                    psav = K.psG if dvc == 0 else K.psH
                    vsl = v_h[:, kc * 256 + dvc * 128: kc * 256 + (dvc + 1) * 128]
                    dep = [et, v_toks[kc]] + (avdep0 if dvc == 0 else avdep1)
                    p.op("pe", (lambda psav, vsl, wsl, st, sp: lambda e: e.matmul(
                        psav[:, 0:512], vsl, wsl, start=st, stop=sp))(
                            psav, vsl, wsl, kc == 0, kc == nkc - 1),
                        deps=dep)
                    if dvc == 0:
                        avdep0 = []
                    else:
                        avdep1 = []
                av_last = p.sig("pe")
                wchwar[kc % 2] = av_last
            qdwar[g % 2] = av_last
            rsl = ret[:, (g % 2) * 1024:(g % 2 + 1) * 1024]
            e1 = p.op("ve", (lambda rsl: lambda e: e.tensor_copy(
                rsl[:, 0:512], K.psG[:, 0:512]))(rsl),
                deps=[av_last, retwar[g % 2]], sig=True)
            ps.done("G", e1)
            e2 = p.op("se", (lambda rsl: lambda e: e.copy(
                rsl[:, 512:1024], K.psH[:, 0:512]))(rsl),
                deps=[av_last, retwar[g % 2]], sig=True)
            ps.done("H", e2)
            q1 = p.op("se", (lambda rsl: lambda e: e.activation(
                sqb[:, 0:512], rsl[:, 0:512], AF.Square))(rsl),
                deps=[e1, sq_war], sig=True)
            q2 = p.op("se", (lambda rsl: lambda e: e.activation(
                sqb[:, 512:1024], rsl[:, 512:1024], AF.Square))(rsl),
                deps=[e2], sig=True)
            dep0 = [e1, ps.gate("A")]
            p.op("pe", (lambda rsl: lambda e: e.matmul(
                K.psA[0:1, 0:512], K.ones_bf[:, 0:1], rsl[:, 0:512],
                start=True, stop=False))(rsl), deps=dep0)
            p.op("pe", (lambda rsl: lambda e: e.matmul(
                K.psA[0:1, 0:512], K.ones_bf[:, 0:1], rsl[:, 512:1024],
                start=False, stop=True))(rsl), deps=[e2])
            smu = p.sig("pe")
            p.op("pe", lambda e: e.matmul(
                K.psB[0:1, 0:512], K.ones_bf[:, 0:1], sqb[:, 0:512],
                start=True, stop=False), deps=[q1, ps.gate("B")])
            p.op("pe", lambda e: e.matmul(
                K.psB[0:1, 0:512], K.ones_bf[:, 0:1], sqb[:, 512:1024],
                start=False, stop=True), deps=[q2])
            svar = p.sig("pe")
            sq_war = svar
            inv = 1.0 / 256.0
            t1 = p.op("ve", lambda e: e.tensor_scalar_mul(mu_row, K.psA[0:1, 0:512], inv),
                      deps=[smu], sig=True)
            ps.done("A", t1)
            t2 = p.op("ve", lambda e: e.tensor_scalar_mul(v_row, K.psB[0:1, 0:512], inv),
                      deps=[svar], sig=True)
            ps.done("B", t2)
            t3 = p.op("se", lambda e: e.activation(r_row, mu_row, AF.Square),
                      deps=[t1], sig=True)
            t4 = p.op("ve", lambda e: e.tensor_sub(v_row, v_row, r_row),
                      deps=[t3], sig=True)
            t5 = p.op("se", lambda e: e.activation(
                r_row, v_row, AF.Sqrt, bias=K.misc[0:1, 0:1]), deps=[t4], sig=True)
            t6 = p.op("ve", lambda e: e.reciprocal(r_row, r_row), deps=[t5], sig=True)
            t7 = p.op("ve", lambda e: e.scalar_tensor_tensor(
                nmr_row, mu_row, -1.0, r_row, ALU.mult, ALU.mult),
                deps=[t6], sig=True)
            p.op("pe", lambda e: e.matmul(
                K.psA[:, 0:512], K.ones_f32[0:1, :], r_row, start=True, stop=True),
                deps=[t7, ps.gate("A")])
            bE = p.sig("pe")
            p.op("pe", lambda e: e.matmul(
                K.psB[:, 0:512], K.ones_f32[0:1, :], nmr_row, start=True, stop=True),
                deps=[t7, ps.gate("B")])
            bF = p.sig("pe")
            e3 = p.op("ve", lambda e: e.tensor_copy(r_bc, K.psA[:, 0:512]),
                      deps=[bE], sig=True)
            ps.done("A", e3)
            e4 = p.op("se", lambda e: e.copy(nmr_bc, K.psB[:, 0:512]),
                      deps=[bF], sig=True)
            ps.done("B", e4)
            gsl = gate[:, (g % 2) * 1024:(g % 2 + 1) * 1024]
            for u2 in range(2):
                bank = "E" if u2 == 0 else "F"
                psb = K.psE if u2 == 0 else K.psF
                dep0 = [gtok, ps.gate(bank), gatewar[g % 2]]
                for kc in range(8):
                    p.op("pe", (lambda gu, kc, u2, t0, psb: lambda e: e.matmul(
                        psb[:, 0:512], gu[:, kc * 256 + u2 * 128: kc * 256 + (u2 + 1) * 128],
                        CH(K.xn, kc, t0, 512),
                        start=(kc == 0), stop=(kc == 7)))(gu, kc, u2, t0, psb),
                        deps=dep0 if kc == 0 else [])
                mmtok = p.sig("pe")
                g_used = mmtok
                et = p.op("se", (lambda gsl, u2, psb: lambda e: e.activation(
                    gsl[:, u2 * 512:(u2 + 1) * 512], psb[:, 0:512], AF.Silu))(
                        gsl, u2, psb), deps=[mmtok], sig=True)
                ps.done(bank, et)
            gate_done = p.sig("se")
            ptoks = []
            for u2 in range(2):
                p.op("ve", (lambda rsl, u2: lambda e: e.tensor_mul(
                    rt1, rsl[:, u2 * 512:(u2 + 1) * 512], r_bc))(rsl, u2),
                    deps=[e3, e4, gate_done])
                p.op("ve", (lambda rsl, u2: lambda e: e.tensor_add(
                    rsl[:, u2 * 512:(u2 + 1) * 512], rt1, nmr_bc))(rsl, u2))
                pt2 = p.op("ve", (lambda gsl, rsl, u2: lambda e: e.tensor_mul(
                    gsl[:, u2 * 512:(u2 + 1) * 512],
                    rsl[:, u2 * 512:(u2 + 1) * 512],
                    gsl[:, u2 * 512:(u2 + 1) * 512]))(gsl, rsl, u2), sig=True)
                ptoks.append(pt2)
            retwar[g % 2] = ptoks[-1]
            for dc in range(8):
                bank = "C" if dc % 2 == 0 else "D"
                psb = K.psC if dc % 2 == 0 else K.psD
                dep0 = [ptok, ps.gate(bank)] + ptoks
                for kcp in range(2):
                    p.op("pe", (lambda pu, kcp, dc, gsl, psb: lambda e: e.matmul(
                        psb[:, 0:512], pu[:, kcp * 1024 + dc * 128: kcp * 1024 + (dc + 1) * 128],
                        gsl[:, kcp * 512:(kcp + 1) * 512],
                        start=(kcp == 0), stop=(kcp == 1)))(pu, kcp, dc, gsl, psb),
                        deps=dep0 if kcp == 0 else [])
                mmtok = p.sig("pe")
                p_used = mmtok
                gatewar[g % 2] = mmtok
                et = p.op("ve", (lambda dc, t0, psb: lambda e: e.tensor_add(
                    CH(K.xres, dc, t0, 512), CH(K.xres, dc, t0, 512),
                    psb[:, 0:512]))(dc, t0, psb), deps=[mmtok], sig=True)
                ps.done(bank, et)
        uwar[sqg] = q_used
        uwar[sgg] = g_used
        uwar[spg] = p_used
        kdwar = p.sig("pe")
        vwar = kdwar
    K.final_deps = p.barrier(("pe", "ve", "se"))
    K.fpool.pop(); K.bpool.pop()




def emit_l1_ffn(K):
    p, ps = K.p, K.ps
    pre = list(K.final_deps)
    K.fpool.push(); K.bpool.push()

    qch = K.fpool.alloc(4 * 1024)
    acc = K.smalls[:, 12:13]
    accw = K.smalls[:, 13:14]
    thr = K.smalls[:, 14:15]
    nthr = K.smalls[:, 15:16]
    qbf = K.bpool.alloc(4 * 1024)
    lnw_bf = K.bpool.alloc(32)
    lnb_bf = K.bpool.alloc(32)
    rt_q = K.bpool.alloc(4096).bitcast(F32)
    p.op("ve", lambda e: e.tensor_copy(lnw_bf, K.lnw), deps=pre, sig=True)
    lnbf_tok = p.op("ve", lambda e: e.tensor_copy(lnb_bf, K.lnb),
                    deps=pre, sig=True)

    def quant_weight(name, nrows, ncols, chcols, scname, fold_lnw, stats):
        nch_r = nrows // 128
        nch_c = ncols // chcols
        nslot = 4
        chunks = [(r, c) for r in range(nch_r) for c in range(nch_c)]
        zt = p.op("ve", lambda e: e.memset(acc, 0.0), deps=pre, sig=True)
        war = [zt] * nslot
        for idx, (r, c) in enumerate(chunks):
            sl = qch[:, (idx % nslot) * 1024:(idx % nslot) * 1024 + chcols]
            dt = p.dma("sp", (lambda sl, r, c: lambda e: e.dma_start(
                sl, K.ins[name][r * 128:(r + 1) * 128,
                                c * chcols:(c + 1) * chcols]))(sl, r, c),
                deps=[war[idx % nslot]] + pre)
            t1 = p.op("ve", (lambda sl: lambda e: e.reduce_sum(
                accw, sl, axis=AX.X, apply_absolute_value=True))(sl),
                deps=[dt], sig=True)
            war[idx % nslot] = t1
            t2 = p.op("ve", lambda e: e.tensor_add(acc, acc, accw), sig=True)
        p.op("pe", lambda e: e.matmul(K.psA[0:1, 0:1], K.ones_f32[:, 0:1], acc,
                                      start=True, stop=True),
             deps=[t2, ps.gate("A")])
        gt = p.sig("pe")
        n_el = float(nrows * ncols)
        t3 = p.op("ve", lambda e: e.tensor_scalar_mul(
            K.smalls[0:1, 56:57], K.psA[0:1, 0:1], 1.0 / n_el), deps=[gt], sig=True)
        ps.done("A", t3)
        p.op("ve", lambda e: e.tensor_scalar_add(
            K.smalls[0:1, 56:57], K.smalls[0:1, 56:57], 1e-6))
        t4 = p.op("ve", lambda e: e.tensor_scalar_mul(
            K.smalls[0:1, 56:57], K.smalls[0:1, 56:57], 0.5), sig=True)
        p.op("pe", lambda e: e.matmul(K.psA[:, 0:1], K.ones_f32[0:1, :],
                                      K.smalls[0:1, 56:57], start=True, stop=True),
             deps=[t4, ps.gate("A")])
        bt = p.sig("pe")
        t5 = p.op("ve", lambda e: e.tensor_copy(thr, K.psA[:, 0:1]),
                  deps=[bt], sig=True)
        ps.done("A", t5)
        t6 = p.op("ve", lambda e: e.tensor_scalar_mul(nthr, thr, -1.0), sig=True)
        war = [t6] * nslot
        qwar = [None] * nslot
        stat_started = False
        for idx, (r, c) in enumerate(chunks):
            sl = qch[:, (idx % nslot) * 1024:(idx % nslot) * 1024 + chcols]
            qsl = qbf[:, (idx % nslot) * 1024:(idx % nslot) * 1024 + chcols]
            dt = p.dma("sp", (lambda sl, r, c: lambda e: e.dma_start(
                sl, K.ins[name][r * 128:(r + 1) * 128,
                                c * chcols:(c + 1) * chcols]))(sl, r, c),
                deps=[war[idx % nslot]])
            t1 = p.op("ve", (lambda qsl, sl: lambda e: e.tensor_single_scalar(
                qsl, sl, thr, ALU.is_ge))(qsl, sl),
                deps=[dt, qwar[idx % nslot], t6], sig=True)
            t2 = p.op("ve", (lambda sl, cc: lambda e: e.tensor_single_scalar(
                rt_q[:, 0:cc], sl, nthr, ALU.is_le))(sl, chcols), sig=True)
            war[idx % nslot] = t2
            t3 = p.op("ve", (lambda qsl, cc: lambda e: e.tensor_sub(
                qsl, qsl, rt_q[:, 0:cc]))(qsl, chcols), sig=True)
            if stats:
                dep0 = [t3, lnbf_tok] + ([ps.gate("E"), ps.gate("F"),
                                          ps.gate("G"), ps.gate("H")]
                                         if not stat_started else [])
                stat_started = True
                p.op("pe", (lambda qsl, r: lambda e: e.matmul(
                    K.psE[0:1, 0:512], lnw_bf[:, r:r + 1], qsl[:, 0:512],
                    start=(r == 0), stop=(r == 31)))(qsl, r), deps=dep0)
                p.op("pe", (lambda qsl, r: lambda e: e.matmul(
                    K.psF[0:1, 0:512], lnw_bf[:, r:r + 1], qsl[:, 512:1024],
                    start=(r == 0), stop=(r == 31)))(qsl, r))
                p.op("pe", (lambda qsl, r: lambda e: e.matmul(
                    K.psG[0:1, 0:512], lnb_bf[:, r:r + 1], qsl[:, 0:512],
                    start=(r == 0), stop=(r == 31)))(qsl, r))
                p.op("pe", (lambda qsl, r: lambda e: e.matmul(
                    K.psH[0:1, 0:512], lnb_bf[:, r:r + 1], qsl[:, 512:1024],
                    start=(r == 0), stop=(r == 31)))(qsl, r))
                stok = p.sig("pe")
            if fold_lnw:
                t4 = p.op("ve", (lambda qsl, r: lambda e: e.tensor_single_scalar(
                    qsl, qsl, K.lnw[:, r:r + 1], ALU.mult))(qsl, r),
                    deps=[stok] if stats else [], sig=True)
            if scname == "fc":
                dst = K.wqfc_sc[r * 128:(r + 1) * 128, c * chcols:(c + 1) * chcols]
                qv = qsl
            else:
                dst = K.wqproj_sc.rearrange("a p k d -> p a k d")[:, :, r, :]
                qv = qsl.rearrange("p (a d) -> p a d", d=128)
            st = p.dma("sp", (lambda dst, qv: lambda e: e.dma_start(
                dst, qv))(dst, qv), deps=[p.sig("ve")])
            qwar[idx % nslot] = st
        return p.front("sp")

    fc_done = quant_weight("wfcT", 1024, 4096, 1024, "fc", False, False)
    pr_done = quant_weight("wprojT", 4096, 1024, 1024, "proj", True, True)
    e1 = p.op("ve", lambda e: e.tensor_copy(K.s1s0_bf[0:1, 0:512], K.psE[0:1, 0:512]), sig=True)
    ps.done("E", e1)
    e2 = p.op("ve", lambda e: e.tensor_copy(K.s1s0_bf[0:1, 512:1024], K.psF[0:1, 0:512]), sig=True)
    ps.done("F", e2)
    e3 = p.op("ve", lambda e: e.tensor_copy(K.s1s0_bf[0:1, 1024:1536], K.psG[0:1, 0:512]), sig=True)
    ps.done("G", e3)
    e4 = p.op("ve", lambda e: e.tensor_copy(K.s1s0_bf[0:1, 1536:2048], K.psH[0:1, 0:512]), sig=True)
    ps.done("H", e4)
    s_done = [e1, e2, e3, e4]
    quant_done = p.barrier(("ve", "se", "sp"))
    K.fpool.pop()
    K.bpool.pop()

    K.fpool.push(); K.bpool.push()
    h_g = K.bpool.alloc(32 * 512)
    w1b = K.bpool.alloc(2 * 4096)
    w2b = K.bpool.alloc(2 * 4096)
    sq = K.bpool.alloc(2 * 512)
    nmr_bf = K.bpool.alloc(512)
    rows = K.fpool.alloc(4 * 512)
    r_bc = K.fpool.alloc(512)
    mu_row = rows[0:1, 0:512]
    v_row = rows[0:1, 512:1024]
    r_row = rows[0:1, 1024:1536]
    nmr_row = rows[0:1, 1536:2048]
    w1war = [None, None]
    w2war = [None, None]
    sq_war = [quant_done, quant_done]
    h_war = quant_done
    rbc_war = None
    nmrbf_war = quant_done
    u1 = 0
    u2 = 0
    for g in range(4):
        t0 = g * 512
        stat_dep = [ps.gate("G")]
        stat_dep2 = [ps.gate("H")]
        for fb in range(8):
            slot = u1 % 2
            u1 += 1
            w1u = w1b[:, slot * 4096:(slot + 1) * 4096]
            src = K.wqfc_sc.rearrange("(kc p) f -> p kc f", p=128)[
                :, :, fb * 512:(fb + 1) * 512]
            dstv = w1u.rearrange("p (kc f) -> p kc f", f=512)
            wt = p.dma("sp", (lambda dstv, src: lambda e: e.dma_start(
                dstv, src))(dstv, src), deps=[w1war[slot]] + fc_done + pre)
            for fc in range(4):
                hidx = fb * 4 + fc
                bank = "C" if hidx % 2 == 0 else "D"
                psb = K.psC if hidx % 2 == 0 else K.psD
                dep0 = [wt, ps.gate(bank)]
                for kc in range(8):
                    p.op("pe", (lambda w1u, kc, fc, t0, psb: lambda e: e.matmul(
                        psb[:, 0:512],
                        w1u[:, kc * 512 + fc * 128: kc * 512 + (fc + 1) * 128],
                        CH(K.xn, kc, t0, 512),
                        start=(kc == 0), stop=(kc == 7)))(w1u, kc, fc, t0, psb),
                        deps=dep0 if kc == 0 else [])
                mmtok = p.sig("pe")
                if fc == 3:
                    w1war[slot] = mmtok
                et = p.op("se", (lambda hidx, psb: lambda e: e.activation(
                    h_g[:, hidx * 512:(hidx + 1) * 512], psb[:, 0:512],
                    AF.Gelu))(hidx, psb), deps=[mmtok, h_war], sig=True)
                ps.done(bank, et)
                q1 = p.op("se", (lambda hidx: lambda e: e.activation(
                    sq[:, (hidx % 2) * 512:(hidx % 2 + 1) * 512],
                    h_g[:, hidx * 512:(hidx + 1) * 512], AF.Square))(hidx),
                    deps=[sq_war[hidx % 2]], sig=True)
                p.op("pe", (lambda hidx: lambda e: e.matmul(
                    K.psG[0:1, 0:512], K.ones_bf[:, 0:1],
                    h_g[:, hidx * 512:(hidx + 1) * 512],
                    start=(hidx == 0), stop=(hidx == 31)))(hidx),
                    deps=[et] + stat_dep)
                stat_dep = []
                p.op("pe", (lambda hidx: lambda e: e.matmul(
                    K.psH[0:1, 0:512], K.ones_bf[:, 0:1],
                    sq[:, (hidx % 2) * 512:(hidx % 2 + 1) * 512],
                    start=(hidx == 0), stop=(hidx == 31)))(hidx),
                    deps=[q1] + stat_dep2)
                stat_dep2 = []
                sq_war[hidx % 2] = p.sig("pe")
        smtok = p.sig("pe")
        inv = 1.0 / 4096.0
        t1 = p.op("ve", lambda e: e.tensor_scalar_mul(mu_row, K.psG[0:1, 0:512], inv),
                  deps=[smtok], sig=True)
        ps.done("G", t1)
        t2 = p.op("ve", lambda e: e.tensor_scalar_mul(v_row, K.psH[0:1, 0:512], inv),
                  deps=[smtok], sig=True)
        ps.done("H", t2)
        t3 = p.op("se", lambda e: e.activation(r_row, mu_row, AF.Square),
                  deps=[t1], sig=True)
        t4 = p.op("ve", lambda e: e.tensor_sub(v_row, v_row, r_row), deps=[t3], sig=True)
        t5 = p.op("se", lambda e: e.activation(r_row, v_row, AF.Sqrt,
                                               bias=K.misc[0:1, 0:1]),
                  deps=[t4], sig=True)
        t6 = p.op("ve", lambda e: e.reciprocal(r_row, r_row), deps=[t5], sig=True)
        t7 = p.op("ve", lambda e: e.scalar_tensor_tensor(
            nmr_row, mu_row, -1.0, r_row, ALU.mult, ALU.mult), deps=[t6], sig=True)
        t8 = p.op("ve", lambda e: e.tensor_copy(nmr_bf[0:1, 0:512], nmr_row),
                  deps=[t7, nmrbf_war], sig=True)
        p.op("pe", lambda e: e.matmul(K.psG[:, 0:512], K.ones_f32[0:1, :], r_row,
                                      start=True, stop=True),
             deps=[t7, ps.gate("G")])
        bt = p.sig("pe")
        t9 = p.op("ve", (lambda: lambda e: e.tensor_copy(r_bc, K.psG[:, 0:512]))(),
                  deps=[bt, rbc_war], sig=True)
        ps.done("G", t9)
        htoks = []
        for fk in range(32):
            ht = p.op("ve", (lambda fk: lambda e: e.tensor_mul(
                h_g[:, fk * 512:(fk + 1) * 512], h_g[:, fk * 512:(fk + 1) * 512],
                r_bc))(fk), deps=[t9, smtok], sig=True)
            htoks.append(ht)
        rbc_war_next = htoks[-1]
        for dc in range(8):
            slot = u2 % 2
            u2 += 1
            w2u = w2b[:, slot * 4096:(slot + 1) * 4096]
            wt = p.dma("sp", (lambda w2u, dc: lambda e: e.dma_start(
                w2u, K.wqproj_sc[dc].flatten_outer_dims() if False
                else K.wqproj_sc[dc].rearrange("p k d -> p (k d)")))(w2u, dc),
                deps=[w2war[slot]] + pr_done)
            bank = "E" if dc % 2 == 0 else "F"
            psb = K.psE if dc % 2 == 0 else K.psF
            dep0 = [wt, ps.gate(bank)]
            for fk in range(32):
                p.op("pe", (lambda w2u, fk, psb: lambda e: e.matmul(
                    psb[:, 0:512], w2u[:, fk * 128:(fk + 1) * 128],
                    h_g[:, fk * 512:(fk + 1) * 512],
                    start=(fk == 0), stop=False))(w2u, fk, psb),
                    deps=(dep0 if fk == 0 else []) + [htoks[fk]])
            p.op("pe", (lambda dc, psb: lambda e: e.matmul(
                psb[:, 0:512], K.s1s0_bf[0:1, dc * 128:(dc + 1) * 128],
                nmr_bf[0:1, 0:512], start=False, stop=False))(dc, psb),
                deps=[t8] + s_done)
            p.op("pe", (lambda dc, psb: lambda e: e.matmul(
                psb[:, 0:512], K.s1s0_bf[0:1, 1024 + dc * 128:1024 + (dc + 1) * 128],
                K.ones_bf[0:1, 0:512], start=False, stop=True))(dc, psb))
            mmtok = p.sig("pe")
            w2war[slot] = mmtok
            et = p.op("ve", (lambda dc, t0, psb: lambda e: e.tensor_add(
                CH(K.xres, dc, t0, 512), CH(K.xres, dc, t0, 512),
                psb[:, 0:512]))(dc, t0, psb), deps=[mmtok], sig=True)
            ps.done(bank, et)
        h_war = p.sig("pe")
        rbc_war = rbc_war_next
        nmrbf_war = h_war
    K.final_deps = p.barrier(("pe", "ve", "se"))
    K.fpool.pop(); K.bpool.pop()




# version 04
# speedup vs baseline: 1.0177x; 1.0177x over previous
import numpy as np
import ml_dtypes
from contextlib import ExitStack

import concourse.bass as bass
import concourse.mybir as mybir
from concourse.bass_utils import run_bass_kernel_spmd

F32 = mybir.dt.float32
BF16 = mybir.dt.bfloat16
AF = mybir.ActivationFunctionType
ALU = mybir.AluOpType
AX = mybir.AxisListType
BF = ml_dtypes.bfloat16

T = 2048
D = 1024
NCH = 8
NT = 16
NEG = -1.0e30

ARENA_BF_COLS = 36864
ARENA_F32_COLS = 5120

ENGSTEP = {"pe": 1, "ve": 1, "se": 1, "gp": 16, "sp": 16}
NCH_GP = 12
NCH_SP = 6



def _lay_feat(v):
    f = v.shape[0]
    return np.ascontiguousarray(v.reshape(f // 128, 128).T)


def _host_constants():
    c = {}
    c["ident_bf"] = np.eye(128, dtype=BF)
    c["ident_f32"] = np.eye(128, dtype=np.float32)
    c["ones_bf"] = np.ones((128, 512), dtype=BF)
    c["ones_f32"] = np.ones((128, 128), dtype=np.float32)

    m0 = np.full((128, 4 * 512), NEG, dtype=np.float32)
    w = np.arange(512)[None, :]
    for parity in range(2):
        for b0 in range(2):
            v = parity + 2 * b0
            ql = 128 * parity + np.arange(128)[:, None]
            valid = (w >= ql) & (w <= ql + 256)
            if b0:
                valid &= (w >= 256)
            m0[:, v * 512:(v + 1) * 512][valid] = 0.0
    c["mask0"] = m0.astype(BF)

    mc = np.zeros((128, 4 * 512), dtype=np.float32)
    qf = np.arange(512)[None, :]
    kp = np.arange(128)[:, None]
    for d in range(4):
        mc[:, d * 512:(d + 1) * 512] = (qf >= kp + 128 * d).astype(np.float32)
    c["maskc"] = mc.astype(BF)

    c["pos"] = np.broadcast_to(
        np.arange(T, dtype=np.float32)[None, :], (128, T)).copy()

    QK = 1024
    SCALE_BASE = 512.0
    i2 = np.arange(0, QK, 2, dtype=np.float64)
    base_scale = (i2 + 0.4 * QK) / (1.4 * QK)
    t = np.arange(T, dtype=np.float64)
    scale = base_scale[None, :] ** (t[:, None] / SCALE_BASE)
    inv_freq = 1.0 / (10000.0 ** (np.arange(QK // 2, dtype=np.float64) / (QK // 2)))
    sinu = t[:, None] * inv_freq[None, :]
    sn, cs = np.sin(sinu), np.cos(sinu)
    rep = lambda a: np.repeat(a, 2, axis=1)
    sgn = np.where(np.arange(QK) % 2 == 0, -1.0, 1.0)[None, :]
    c["csq"] = np.ascontiguousarray((rep(cs * scale)).T).astype(BF)
    c["snq"] = np.ascontiguousarray((rep(sn * scale) * sgn).T).astype(BF)
    c["csk"] = np.ascontiguousarray((rep(cs / scale)).T).astype(BF)
    c["snk"] = np.ascontiguousarray((rep(sn / scale) * sgn).T).astype(BF)

    bas = np.zeros((8, 8 * 128), dtype=np.float32)
    for e in range(8):
        bas[e, e * 128:(e + 1) * 128] = 1.0
    c["basis_bf"] = bas.astype(BF)

    mcc = np.zeros((128, 8), dtype=np.float32)
    mcc[:, 0] = 1e-5
    mcc[:, 1] = 1e-6
    mcc[:, 2] = -4.0
    c["misc_cols"] = mcc
    return c


_CONST_CACHE = None


def _prep_shared(inputs):
    global _CONST_CACHE
    if _CONST_CACHE is None:
        _CONST_CACHE = _host_constants()
    s = dict(_CONST_CACHE)
    f32 = np.float32
    asC = lambda a: np.ascontiguousarray(np.asarray(a, f32))
    s["qwT"] = asC(np.asarray(inputs["L0_attn_q_w"]).T)
    s["kwT"] = asC(np.asarray(inputs["L0_attn_k_w"]).T)
    s["vwT"] = asC(np.asarray(inputs["L0_attn_v_w"]).T)
    s["owT"] = asC(np.asarray(inputs["L0_attn_o_w"]).T)
    s["bias_rows"] = np.stack([
        np.asarray(inputs["L0_attn_q_b"], f32),
        np.asarray(inputs["L0_attn_k_b"], f32),
        np.asarray(inputs["L0_attn_v_b"], f32)], axis=0)
    s["l0bias"] = np.concatenate(
        [_lay_feat(np.asarray(inputs[k], f32)) for k in
         ("L0_attn_q_b", "L0_attn_k_b", "L0_attn_v_b", "L0_attn_o_b")],
        axis=1)
    s["w1"] = asC(inputs["L0_ffn_W1"])
    w2 = np.asarray(inputs["L0_ffn_W2"], f32)
    w2r = w2.reshape(8, 32, 128, 8, 128)
    s["w2lay"] = np.ascontiguousarray(
        w2r.transpose(0, 3, 2, 1, 4).reshape(8, 8, 128, 32 * 128))
    s["b1lay"] = np.concatenate(
        [_lay_feat(np.asarray(inputs["L0_ffn_b1"], f32)[e]) for e in range(8)],
        axis=1)
    s["b2"] = asC(inputs["L0_ffn_b2"])
    s["wr"] = asC(inputs["L0_ffn_Wr"])
    s["br_bc"] = np.broadcast_to(
        np.asarray(inputs["L0_ffn_br"], f32)[None, :], (128, 8)).copy()
    s["wqkv"] = asC(inputs["L1_attn_W_qkv"])
    s["wg"] = asC(inputs["L1_attn_W_g"])
    s["wproj1"] = asC(inputs["L1_attn_W_proj"])
    s["decay"] = np.asarray(inputs["L1_attn_decay"], f32).reshape(1, 8).copy()
    s["wfcT"] = asC(np.asarray(inputs["L1_ffn_W_fc"]).T)
    s["wprojT"] = asC(np.asarray(inputs["L1_ffn_W_proj"]).T)
    s["lnw_lay"] = _lay_feat(np.asarray(inputs["L1_ffn_ln_w"], f32))
    s["lnb_lay"] = _lay_feat(np.asarray(inputs["L1_ffn_ln_b"], f32))
    return s



class Prog:

    def __init__(self, strict="none"):
        self.strict = strict
        self.strict_engs = {"none": (), "vese": ("ve", "se"),
                            "all": ("pe", "ve", "se")}[strict]
        self.semkeys = (["pe", "ve", "se"]
                        + [f"gp{i}" for i in range(NCH_GP)]
                        + [f"sp{i}" for i in range(NCH_SP)])
        self.stream = {k: [] for k in ENGSTEP}
        self.cnt = {k: 0 for k in self.semkeys}
        self._last = {k: None for k in ENGSTEP}
        self._rr = {"gp": 0, "sp": 0}

    @staticmethod
    def _flat(deps):
        out = []
        for d in deps:
            if d is None:
                continue
            if isinstance(d, list):
                out.extend(t for t in d if t is not None)
            else:
                out.append(d)
        return out

    def op(self, eng, fn, deps=(), sig=False):
        assert eng in ENGSTEP and (not sig or eng in ("pe", "ve", "se"))
        toks = self._flat(deps)
        entry = [toks, fn, None]
        self.stream[eng].append(entry)
        self._last[eng] = entry
        if eng in self.strict_engs:
            self.sig(eng)
        if sig:
            return self.sig(eng)
        return None

    def dma(self, eng, fn, deps=()):
        assert eng in ("gp", "sp")
        ch = f"{eng}{self._rr[eng]}"
        self._rr[eng] = (self._rr[eng] + 1) % (NCH_GP if eng == "gp" else NCH_SP)
        toks = self._flat(deps)
        if self.cnt[ch]:
            toks.append((ch, self.cnt[ch]))
        self.cnt[ch] += 16
        entry = [toks, fn, (ch, 16)]
        self.stream[eng].append(entry)
        self._last[eng] = entry
        return (ch, self.cnt[ch])

    def sig(self, eng):
        entry = self._last[eng]
        assert entry is not None, f"no op to sign on {eng}"
        if entry[2] is None:
            self.cnt[eng] += 1
            entry[2] = (eng, 1)
        return (eng, self.cnt[eng])

    def front(self, eng):
        if eng in ("pe", "ve", "se"):
            if self._last[eng] is None:
                return []
            return [self.sig(eng)]
        pre = eng
        return [(k, self.cnt[k]) for k in self.semkeys
                if k.startswith(pre) and self.cnt[k]]

    def barrier(self, engs=("pe", "ve", "se", "gp", "sp")):
        toks = []
        for e in engs:
            toks.extend(self.front(e))
        return toks

    def replay(self, nc, sems):
        hw = {k: {k2: 0 for k2 in self.semkeys} for k in ENGSTEP}
        blockmap = {"pe": "tensor", "ve": "vector", "se": "scalar",
                    "gp": "gpsimd", "sp": "sync"}
        with nc.Block() as block:
            for eng in ENGSTEP:
                entries = self.stream[eng]
                if not entries:
                    continue

                def make_body(eng, entries):
                    def body(e):
                        mywait = hw[eng]
                        mycount = 0
                        for toks, fn, sig in entries:
                            for (tkey, tval) in toks:
                                if tkey == eng:
                                    continue
                                if tval > mywait[tkey]:
                                    e.wait_ge(sems[tkey], tval)
                                    mywait[tkey] = tval
                            if eng in self.strict_engs and mycount:
                                e.wait_ge(sems[eng], mycount)
                            inst = fn(e)
                            if sig is not None:
                                assert inst is not None
                                inst.then_inc(sems[sig[0]], sig[1])
                                mycount += sig[1] if sig[0] == eng else 0
                    return body

                getattr(block, blockmap[eng])(make_body(eng, entries))


class Pool:

    def __init__(self, tensor, cols, name):
        self.t = tensor
        self.cols = cols
        self.off = 0
        self.name = name
        self.marks = []

    def alloc(self, n):
        n = (n + 7) & ~7
        assert self.off + n <= self.cols, \
            f"arena {self.name} overflow: {self.off}+{n} > {self.cols}"
        o = self.off
        self.off += n
        return self.t[:, o:o + n]

    def alloc_rows(self, rows, n):
        n = (n + 7) & ~7
        assert self.off + n <= self.cols
        o = self.off
        self.off += n
        return self.t[0:rows, o:o + n]

    def push(self):
        self.marks.append(self.off)

    def pop(self):
        self.off = self.marks.pop()

    def reset(self):
        self.off = 0


class PsumMgr:

    def __init__(self):
        self.tok = {}

    def gate(self, name):
        return self.tok.get(name)

    def done(self, name, token):
        self.tok[name] = token


class Ctx:
    pass



def CH(t, c, t0=0, n=T, stride=T):
    return t[:, c * stride + t0: c * stride + t0 + n]


def emit_load_x(K):
    p = K.p
    K.fpool.push()
    stage = K.fpool.alloc(2 * 1024)
    slot_tok = [None, None]
    bank_tok = [None, None]
    for i in range(NT):
        s = i % 2
        st = stage[:, s * 1024:(s + 1) * 1024]
        dtok = p.dma("sp", (lambda st, i: lambda e: e.dma_start(
            st, K.x_in[i * 128:(i + 1) * 128, :]))(st, i),
            deps=[slot_tok[s]] + (K.consts_all if i == 0 else []))
        for b in range(2):
            ps = K.psA if b == 0 else K.psB
            for j in range(4):
                c = 4 * b + j
                p.op("pe", (lambda st, c, ps, j: lambda e: e.transpose(
                    ps[:, j * 128:(j + 1) * 128],
                    st[:, c * 128:(c + 1) * 128], K.ident_f32))(st, c, ps, j),
                    deps=[dtok, bank_tok[b]] + (K.consts_all if i == 0 else []))
            ptok = p.sig("pe")
            if b == 1:
                slot_tok[s] = ptok
            dst = K.xres.rearrange("p (c t) -> p c t", t=T)[
                :, 4 * b:4 * b + 4, i * 128:(i + 1) * 128]
            psv = ps.rearrange("p (c t) -> p c t", t=128)
            eng = "ve" if b == 0 else "se"
            if eng == "ve":
                fn = (lambda dst, psv: lambda e: e.tensor_copy(dst, psv))(dst, psv)
            else:
                fn = (lambda dst, psv: lambda e: e.copy(dst, psv))(dst, psv)
            bank_tok[b] = p.op(eng, fn, deps=[ptok], sig=True)
    K.x_ready = p.barrier(("pe", "ve", "se"))
    K.final_deps = K.x_ready
    K.fpool.pop()


def emit_store(K):
    p = K.p
    K.fpool.push()
    obuf = K.fpool.alloc(2 * 1024)
    slot_store_tok = [None, None]
    bank_tok = [None, None]
    pre = list(K.final_deps) if hasattr(K, "final_deps") else []
    stok = None
    for i in range(NT):
        s = i % 2
        ob = obuf[:, s * 1024:(s + 1) * 1024]
        evac_toks = []
        for b in range(2):
            ps = K.psA if b == 0 else K.psB
            for j in range(4):
                c = 4 * b + j
                p.op("pe", (lambda c, i, ps, j: lambda e: e.transpose(
                    ps[:, j * 128:(j + 1) * 128],
                    CH(K.xres, c, i * 128, 128), K.ident_f32))(c, i, ps, j),
                    deps=pre + [bank_tok[b]])
            ptok = p.sig("pe")
            dst = ob[:, b * 512:(b + 1) * 512]
            psv = ps[:, 0:512]
            eng = "ve" if b == 0 else "se"
            if eng == "ve":
                fn = (lambda dst, psv: lambda e: e.tensor_copy(dst, psv))(dst, psv)
            else:
                fn = (lambda dst, psv: lambda e: e.copy(dst, psv))(dst, psv)
            etok = p.op(eng, fn, deps=[ptok, slot_store_tok[s]], sig=True)
            evac_toks.append(etok)
            bank_tok[b] = etok
        pre = []
        stok = p.dma("sp", (lambda ob, i: lambda e: e.dma_start(
            K.out[i * 128:(i + 1) * 128, :], ob))(ob, i),
            deps=evac_toks)
        slot_store_tok[s] = stok




def build_nc(upto="full", strict="vese"):
    nc = bass.Bass("TRN2", target_bir_lowering=False)
    K = Ctx()
    K.nc = nc
    K.p = Prog(strict=strict)
    K.ps = PsumMgr()
    K.upto = upto

    f32 = F32
    dt_ = nc.dram_tensor
    K.x_in = dt_("x_in", [T, D], f32, kind="ExternalInput").ap()
    K.out = dt_("out", [T, D], f32, kind="ExternalOutput").ap()

    ins = {}
    for name, shape, dty in [
        ("ident_bf", [128, 128], BF16), ("ident_f32", [128, 128], f32),
        ("ones_bf", [128, 512], BF16), ("ones_f32", [128, 128], f32),
        ("mask0", [128, 2048], BF16), ("maskc", [128, 2048], BF16),
        ("pos", [128, T], f32),
        ("csq", [D, T], BF16), ("snq", [D, T], BF16),
        ("csk", [D, T], BF16), ("snk", [D, T], BF16),
        ("basis_bf", [8, 1024], BF16),
        ("qwT", [D, D], f32), ("kwT", [D, D], f32),
        ("vwT", [D, D], f32), ("owT", [D, D], f32),
        ("l0bias", [128, 32], f32), ("bias_rows", [3, D], f32),
        ("misc_cols", [128, 8], f32),
        ("w1", [8, D, 4096], f32), ("w2lay", [8, 8, 128, 4096], f32),
        ("b1lay", [128, 256], f32), ("b2", [8, D], f32),
        ("wr", [D, 8], f32), ("br_bc", [128, 8], f32),
        ("wqkv", [D, 4096], f32), ("wg", [D, 2048], f32),
        ("wproj1", [2048, D], f32), ("decay", [1, 8], f32),
        ("wfcT", [D, 4096], f32), ("wprojT", [4096, D], f32),
        ("lnw_lay", [128, 32], f32), ("lnb_lay", [128, 32], f32),
    ]:
        ins[name] = dt_(name, shape, dty, kind="ExternalInput").ap()
    K.ins = ins
    K.wqfc_sc = dt_("wqfc_sc", [D, 4096], BF16, kind="Internal").ap()
    K.wqproj_sc = dt_("wqproj_sc", [8, 128, 32, 128], BF16, kind="Internal").ap()

    with ExitStack() as ctx:
        E = ctx.enter_context
        K.xres = E(nc.sbuf_tensor("xres", [128, NCH * T], f32))
        K.xn = E(nc.sbuf_tensor("xn", [128, NCH * T], BF16))
        K.ident_bf = E(nc.sbuf_tensor("ident_bf_s", [128, 128], BF16))[:]
        K.ident_f32 = E(nc.sbuf_tensor("ident_f32_s", [128, 128], f32))[:]
        K.ones_bf = E(nc.sbuf_tensor("ones_bf_s", [128, 512], BF16))[:]
        K.ones_f32 = E(nc.sbuf_tensor("ones_f32_s", [128, 128], f32))[:]
        K.l0bias = E(nc.sbuf_tensor("l0bias_s", [128, 32], f32))[:]
        brow_t = E(nc.sbuf_tensor("brow_bf", [65, D], BF16))
        K.brow_bf = brow_t[:]
        K.brow_dst = bass.AP(brow_t, 0, [[32 * D, 3], [1, D]])
        K.misc = E(nc.sbuf_tensor("misc_cols_s", [128, 8], f32))[:]
        K.b1lay = E(nc.sbuf_tensor("b1lay_s", [128, 256], f32))[:]
        K.b2bf = E(nc.sbuf_tensor("b2bf_s", [8, 1024], BF16))[:]
        K.wr_s = E(nc.sbuf_tensor("wr_s", [128, 64], f32))[:]
        K.br_s = E(nc.sbuf_tensor("br_s", [128, 8], f32))[:]
        K.basis = E(nc.sbuf_tensor("basis_s", [8, 1024], BF16))[:]
        K.lnw = E(nc.sbuf_tensor("lnw_s", [128, 32], f32))[:]
        K.lnb = E(nc.sbuf_tensor("lnb_s", [128, 32], f32))[:]
        K.decay_s = E(nc.sbuf_tensor("decay_s", [1, 8], f32))[:]
        K.lnd = E(nc.sbuf_tensor("lnd_s", [128, 16], f32))[:]
        K.stat_col = E(nc.sbuf_tensor("stat_col", [128, 64], f32))[:]
        K.smalls = E(nc.sbuf_tensor("smalls", [128, 64], f32))[:]
        K.coef = E(nc.sbuf_tensor("coef_s", [128, 128], f32))[:]
        K.coefT_bf = E(nc.sbuf_tensor("coefT_bf", [8, T], BF16))[:]
        K.s1s0_bf = E(nc.sbuf_tensor("s1s0_bf", [1, 2048], BF16))[:]

        arena_bf = E(nc.sbuf_tensor("arena_bf", [128, ARENA_BF_COLS], BF16))
        arena_f32 = E(nc.sbuf_tensor("arena_f32", [128, ARENA_F32_COLS], f32))
        K.bpool = Pool(arena_bf, ARENA_BF_COLS, "bf")
        K.fpool = Pool(arena_f32, ARENA_F32_COLS, "f32")

        K.psA = E(nc.psum_tensor("psA", [128, 512], f32))[:]
        K.psB = E(nc.psum_tensor("psB", [128, 512], f32))[:]
        K.psC = E(nc.psum_tensor("psC", [128, 512], f32))[:]
        K.psD = E(nc.psum_tensor("psD", [128, 512], f32))[:]
        K.psE = E(nc.psum_tensor("psE", [128, 512], f32))[:]
        K.psF = E(nc.psum_tensor("psF", [128, 512], f32))[:]
        K.psG = E(nc.psum_tensor("psG", [128, 512], f32))[:]
        K.psH = E(nc.psum_tensor("psH", [128, 512], f32))[:]

        sems = {k: E(nc.semaphore(f"sem_{k}")) for k in K.p.semkeys}
        K.sems = sems

        emit_program(K)
        K.p.replay(nc, sems)
    return nc


def emit_consts_load(K):
    p = K.p
    toks = []
    for name, dst in [
        ("ident_bf", K.ident_bf), ("ident_f32", K.ident_f32),
        ("ones_bf", K.ones_bf), ("ones_f32", K.ones_f32),
        ("l0bias", K.l0bias), ("b1lay", K.b1lay),
        ("br_bc", K.br_s), ("basis_bf", K.basis),
        ("lnw_lay", K.lnw), ("lnb_lay", K.lnb), ("decay", K.decay_s),
        ("misc_cols", K.misc),
    ]:
        toks.append(p.dma("sp", (lambda name, dst: lambda e: e.dma_start(
            dst, K.ins[name][:]))(name, dst)))
    src = K.ins["wr"].rearrange("(c p) e -> p c e", p=128)
    dstv = K.wr_s.rearrange("p (c e) -> p c e", e=8)
    toks.append(p.dma("sp", lambda e: e.dma_start(dstv, src)))
    toks.append(p.dma("gp", lambda e: e.dma_start(K.brow_dst, K.ins["bias_rows"][:])))
    b2tok = p.dma("gp", lambda e: e.dma_start(K.b2bf, K.ins["b2"][:]))
    toks.append(b2tok)
    K.consts_ready = toks[-1]
    K.consts_all = toks


def emit_program(K):
    emit_consts_load(K)
    emit_load_x(K)
    stages = ["io", "ln0", "l0attn", "moe", "ln1", "l1attn", "full"]
    lvl = stages.index(K.upto)
    if lvl >= 1:
        emit_ln(K, layer=0)
    if lvl >= 3:
        emit_router(K)
    if lvl >= 2:
        emit_l0_attn(K)
    if lvl >= 3:
        emit_moe(K)
    if lvl >= 4:
        emit_ln(K, layer=1)
    if lvl >= 5:
        emit_l1_attn(K)
    if lvl >= 6:
        emit_l1_ffn(K)
    emit_store(K)
    p = K.p
    p.op("sp", lambda e: e.nop(), deps=p.front("sp"))
    if any(p.cnt[k] for k in p.semkeys if k.startswith("gp")):
        p.op("gp", lambda e: e.nop(), deps=p.front("gp"))



def _kernel_impl(inputs, upto="full"):
    x = np.asarray(inputs["L0_attn_x"], np.float32)
    shared = _prep_shared(inputs)
    nc = build_nc(upto=upto)
    in_maps = []
    for core in range(8):
        m = dict(shared)
        m["x_in"] = np.ascontiguousarray(x[core])
        in_maps.append(m)
    res = run_bass_kernel_spmd(nc, in_maps, core_ids=list(range(8)))
    outs = [np.asarray(r["out"], np.float32) for r in res.results]
    return np.stack(outs, axis=0)


def kernel(**inputs):
    return _kernel_impl(inputs, upto="full")


def emit_ln(K, layer):
    p, ps = K.p, K.ps
    pre = list(K.final_deps)
    K.fpool.push(); K.bpool.push()
    xbf = K.bpool.alloc(8 * 512)
    sq = K.bpool.alloc(2 * 512)
    rows = K.fpool.alloc(4 * 512)
    bcast = K.fpool.alloc(2 * 512)
    tmp = K.fpool.alloc(512)
    mu_row = rows[0:1, 0:512]
    v_row = rows[0:1, 512:1024]
    r_row = rows[0:1, 1024:1536]
    nmr_row = rows[0:1, 1536:2048]
    r_bc = bcast[:, 0:512]
    nmr_bc = bcast[:, 512:1024]
    psMU = K.psC[0:1, 0:512]
    psVAR = K.psD[0:1, 0:512]

    xbf_war = None
    sq_war = [None, None]
    rows_war = None
    bc_war = None
    for g in range(4):
        t0 = 512 * g
        mm_toks = []
        for c in range(8):
            cpt = p.op("ve", (lambda c, t0: lambda e: e.tensor_copy(
                xbf[:, c * 512:(c + 1) * 512], CH(K.xres, c, t0, 512)))(c, t0),
                deps=pre + [xbf_war], sig=True)
            sqt = p.op("se", (lambda c, t0: lambda e: e.activation(
                sq[:, (c % 2) * 512:(c % 2 + 1) * 512],
                CH(K.xres, c, t0, 512), AF.Square))(c, t0),
                deps=pre + [sq_war[c % 2]], sig=True)
            p.op("pe", (lambda c: lambda e: e.matmul(
                psMU, K.ones_bf[:, 0:1], xbf[:, c * 512:(c + 1) * 512],
                start=(c == 0), stop=(c == 7)))(c),
                deps=[cpt] + ([ps.gate("C")] if c == 0 else []))
            p.op("pe", (lambda c: lambda e: e.matmul(
                psVAR, K.ones_bf[:, 0:1], sq[:, (c % 2) * 512:(c % 2 + 1) * 512],
                start=(c == 0), stop=(c == 7)))(c),
                deps=[sqt] + ([ps.gate("D")] if c == 0 else []))
            sq_war[c % 2] = p.sig("pe")
        mmtok = p.sig("pe")
        xbf_war = mmtok
        inv = 1.0 / 1024.0
        t1 = p.op("ve", lambda e: e.tensor_scalar_mul(mu_row, psMU, inv),
                  deps=[mmtok, rows_war], sig=True)
        t2 = p.op("ve", lambda e: e.tensor_scalar_mul(v_row, psVAR, inv),
                  deps=[mmtok], sig=True)
        ps.done("C", t1)
        ps.done("D", t2)
        t3 = p.op("se", lambda e: e.activation(r_row, mu_row, AF.Square),
                  deps=[t1, rows_war], sig=True)
        t4 = p.op("ve", lambda e: e.tensor_sub(v_row, v_row, r_row),
                  deps=[t3], sig=True)
        t5 = p.op("se", lambda e: e.activation(
            r_row, v_row, AF.Sqrt, bias=K.misc[0:1, 0:1]), deps=[t4], sig=True)
        t6 = p.op("ve", lambda e: e.reciprocal(r_row, r_row), deps=[t5], sig=True)
        t7 = p.op("ve", lambda e: e.scalar_tensor_tensor(
            nmr_row, mu_row, -1.0, r_row, ALU.mult, ALU.mult),
            deps=[t6], sig=True)

        if layer == 0:
            trdeps = [t7, ps.gate("A")]
            for j in range(4):
                p.op("pe", (lambda j: lambda e: e.transpose(
                    K.psA[:, j:j + 1], r_row[0:1, j * 128:(j + 1) * 128],
                    K.ident_f32[0:1, 0:1]))(j), deps=trdeps)
                trdeps = []
                p.op("pe", (lambda j: lambda e: e.transpose(
                    K.psA[:, 4 + j:5 + j], nmr_row[0:1, j * 128:(j + 1) * 128],
                    K.ident_f32[0:1, 0:1]))(j))
            ttok = p.sig("pe")
            c1 = p.op("ve", (lambda g: lambda e: e.tensor_copy(
                K.stat_col[:, g * 4:(g + 1) * 4], K.psA[:, 0:4]))(g),
                deps=[ttok], sig=True)
            c2 = p.op("ve", (lambda g: lambda e: e.tensor_copy(
                K.stat_col[:, 16 + g * 4:16 + (g + 1) * 4], K.psA[:, 4:8]))(g),
                deps=[ttok], sig=True)
            ps.done("A", c2)

        p.op("pe", lambda e: e.matmul(
            K.psE[:, 0:512], K.ones_f32[0:1, :], r_row, start=True, stop=True),
            deps=[t7, ps.gate("E")])
        bE = p.sig("pe")
        p.op("pe", lambda e: e.matmul(
            K.psF[:, 0:512], K.ones_f32[0:1, :], nmr_row, start=True, stop=True),
            deps=[t7, ps.gate("F")])
        bF = p.sig("pe")
        rows_war = bF
        e1 = p.op("ve", lambda e: e.tensor_copy(r_bc, K.psE[:, 0:512]),
                  deps=[bE, bc_war], sig=True)
        e2 = p.op("se", lambda e: e.copy(nmr_bc, K.psF[:, 0:512]),
                  deps=[bF, bc_war], sig=True)
        ps.done("E", e1)
        ps.done("F", e2)
        atoks = []
        for c in range(8):
            p.op("ve", (lambda c, t0: lambda e: e.tensor_mul(
                tmp, CH(K.xres, c, t0, 512), r_bc))(c, t0),
                deps=pre + [e1, e2])
            at = p.op("ve", (lambda c, t0: lambda e: e.tensor_add(
                CH(K.xn, c, t0, 512), tmp, nmr_bc))(c, t0), sig=True)
            atoks.append(at)
        bc_war = atoks[-1]
        pre = []
    K.final_deps = p.barrier(("pe", "ve", "se"))
    K.fpool.pop(); K.bpool.pop()
    if K.upto == "ln0":
        for c in range(8):
            p.op("ve", (lambda c: lambda e: e.tensor_copy(
                CH(K.xres, c, 0, T), CH(K.xn, c, 0, T)))(c),
                deps=list(K.final_deps), sig=True)
        K.final_deps = p.barrier(("ve",))


def ld_mini(K, dst2d, src_ap, col0, ncols, deps=()):
    src = src_ap.rearrange("(kc p) f -> p kc f", p=128)[:, :, col0:col0 + ncols]
    dstv = dst2d.rearrange("p (kc f) -> p kc f", f=ncols)
    return K.p.dma("gp", (lambda dstv, src: lambda e: e.dma_start(dstv, src))(dstv, src),
                   deps=deps)


def emit_l0_attn(K):
    p, ps = K.p, K.ps
    pre = list(K.final_deps)
    K.fpool.push(); K.bpool.push()
    mask0 = K.bpool.alloc(2048)
    kring = K.bpool.alloc(8 * 1024)
    vring = K.bpool.alloc(8 * 1024)
    qg = K.bpool.alloc(4 * 1024)
    attnT = K.bpool.alloc(8 * 512)
    wk = K.bpool.alloc(4 * 1024)
    wv = K.bpool.alloc(2 * 2048)
    wtsT = K.bpool.alloc(3 * 512)
    wtsf = K.fpool.alloc(3 * 512)
    wtse = K.fpool.alloc(3 * 512)

    mtok = p.dma("sp", lambda e: e.dma_start(mask0, K.ins["mask0"][:]), deps=pre)

    wkslot_war = [None] * 4
    wvslot_war = [None] * 2
    wk_seq = 0
    attn_pe = [None]
    oproj_pe = None
    wts_war = {"f": [None] * 3, "bf": [None] * 3, "T": [None] * 3}
    smalls_war = [None] * 3
    QCH = {"q": K.ins["qwT"], "k": K.ins["kwT"], "o": K.ins["owT"]}

    for g in range(4):
        t0 = 512 * g
        rc0 = (512 * g) % 1024

        for c in range(8):
            slot = wk_seq % 4
            wk_seq += 1
            sl = wk[:, slot * 1024:(slot + 1) * 1024]
            wt = ld_mini(K, sl, K.ins["kwT"], c * 128, 128,
                         deps=[wkslot_war[slot]] + (pre if g == 0 else []))
            bank = "C" if c % 2 == 0 else "D"
            psb = K.psC if c % 2 == 0 else K.psD
            dep0 = [wt, ps.gate(bank)] + (pre if g == 0 else [])
            for kc in range(8):
                p.op("pe", (lambda sl, kc, c, t0, psb: lambda e: e.matmul(
                    psb[:, 0:512], sl[:, kc * 128:(kc + 1) * 128],
                    CH(K.xn, kc, t0, 512), start=(kc == 0), stop=False))(
                        sl, kc, c, t0, psb),
                    deps=dep0 if kc == 0 else [])
            p.op("pe", (lambda c, psb: lambda e: e.matmul(
                psb[:, 0:512], K.brow_bf[32:33, c * 128:(c + 1) * 128],
                K.ones_bf[32:33, 0:512], start=False, stop=True))(c, psb))
            mmtok = p.sig("pe")
            wkslot_war[slot] = mmtok
            eng = "ve" if c % 2 == 0 else "se"
            dst = kring[:, c * 1024 + rc0: c * 1024 + rc0 + 512]
            if eng == "ve":
                fn = (lambda dst, psb: lambda e: e.tensor_copy(dst, psb[:, 0:512]))(dst, psb)
            else:
                fn = (lambda dst, psb: lambda e: e.copy(dst, psb[:, 0:512]))(dst, psb)
            et = p.op(eng, fn, deps=[mmtok, attn_pe[0]], sig=True)
            ps.done(bank, et)

        for u in range(4):
            su = u % 2
            sl = wv[:, su * 2048:(su + 1) * 2048]
            wt = ld_mini(K, sl, K.ins["vwT"], u * 256, 256,
                         deps=[wvslot_war[su]] + (pre if g == 0 else []))
            for i in range(4):
                gi = 4 * g + i
                s = gi % 8
                bank = "E" if (u * 4 + i) % 2 == 0 else "F"
                psb = K.psE if bank == "E" else K.psF
                dep0 = [wt, ps.gate(bank)]
                for kc in range(8):
                    p.op("pe", (lambda sl, kc, gi, psb: lambda e: e.matmul(
                        psb[:, 0:256], CH(K.xn, kc, gi * 128, 128),
                        sl[:, kc * 256:(kc + 1) * 256],
                        start=(kc == 0), stop=False))(sl, kc, gi, psb),
                        deps=dep0 if kc == 0 else [])
                p.op("pe", (lambda u, psb: lambda e: e.matmul(
                    psb[:, 0:256], K.ones_bf[64:65, 0:128],
                    K.brow_bf[64:65, u * 256:(u + 1) * 256],
                    start=False, stop=True))(u, psb))
                mmtok = p.sig("pe")
                if i == 3:
                    wvslot_war[su] = mmtok
                eng = "ve" if i % 2 == 0 else "se"
                dst = vring[:, s * 1024 + u * 256: s * 1024 + u * 256 + 256]
                if eng == "ve":
                    fn = (lambda dst, psb: lambda e: e.tensor_copy(dst, psb[:, 0:256]))(dst, psb)
                else:
                    fn = (lambda dst, psb: lambda e: e.copy(dst, psb[:, 0:256]))(dst, psb)
                et = p.op(eng, fn, deps=[mmtok, attn_pe[0]], sig=True)
                ps.done(bank, et)

        for c in range(8):
            slot = wk_seq % 4
            wk_seq += 1
            sl = wk[:, slot * 1024:(slot + 1) * 1024]
            wt = ld_mini(K, sl, K.ins["qwT"], c * 128, 128, deps=[wkslot_war[slot]])
            bank = "C" if c % 2 == 0 else "D"
            psb = K.psC if c % 2 == 0 else K.psD
            for i in range(4):
                gi = 4 * g + i
                reg = psb[:, i * 128:(i + 1) * 128]
                dep0 = [wt, ps.gate(bank)] if i == 0 else []
                for kc in range(8):
                    p.op("pe", (lambda sl, kc, gi, reg: lambda e: e.matmul(
                        reg, sl[:, kc * 128:(kc + 1) * 128],
                        CH(K.xn, kc, gi * 128, 128),
                        start=(kc == 0), stop=False))(sl, kc, gi, reg),
                        deps=dep0)
                p.op("pe", (lambda c, reg: lambda e: e.matmul(
                    reg, K.brow_bf[0:1, c * 128:(c + 1) * 128],
                    K.ones_bf[0:1, 0:128], start=False, stop=True))(c, reg))
            mmtok = p.sig("pe")
            wkslot_war[slot] = mmtok
            dstv = qg.rearrange("p (i f) -> p i f", f=1024)[:, :, c * 128:(c + 1) * 128]
            psv = psb.rearrange("p (i f) -> p i f", f=128)
            eng = "ve" if c % 2 == 0 else "se"
            if eng == "ve":
                fn = (lambda dstv, psv: lambda e: e.tensor_copy(dstv, psv))(dstv, psv)
            else:
                fn = (lambda dstv, psv: lambda e: e.copy(dstv, psv))(dstv, psv)
            et = p.op(eng, fn, deps=[mmtok, attn_pe[0]], sig=True)
            ps.done(bank, et)

        filldone = p.barrier(("ve", "se"))
        new_attn_pe = None
        seq = [(i, h) for i in range(4) for h in range(16)]
        st8 = {}

        def emitA(it):
            i, h = seq[it]
            qi = 4 * g + i
            b = qi // 2
            parity = qi % 2
            mv = parity + 2 * (1 if b == 0 else 0)
            msl = mask0[:, mv * 512:(mv + 1) * 512]
            rc = (256 * b - 256) % 1024
            sbank = ("C", "D", "A")[it % 3]
            psS = (K.psC, K.psD, K.psA)[it % 3]
            ch = h // 2
            r0 = 64 * (h % 2)
            qsl = qg[r0:r0 + 64, i * 1024 + ch * 128: i * 1024 + (ch + 1) * 128]
            krow = lambda a, b_: kring[r0:r0 + 64, ch * 1024 + a: ch * 1024 + b_]
            sdep = [mtok, ps.gate(sbank)] + (filldone if it == 0 else [])
            if b == 0:
                p.op("pe", (lambda qsl, psS, k1: lambda e: e.matmul(
                    psS[:, 256:512], qsl, k1, start=True, stop=True))(
                        qsl, psS, krow(0, 256)), deps=sdep)
                stok = p.sig("pe")
                stok = p.op("ve", (lambda psS: lambda e: e.memset(
                    psS[:, 0:256], 0.0))(psS), deps=[stok], sig=True)
            elif rc == 768:
                p.op("pe", (lambda qsl, psS, k1: lambda e: e.matmul(
                    psS[:, 0:256], qsl, k1, start=True, stop=True))(
                        qsl, psS, krow(768, 1024)), deps=sdep)
                p.op("pe", (lambda qsl, psS, k2: lambda e: e.matmul(
                    psS[:, 256:512], qsl, k2, start=True, stop=True))(
                        qsl, psS, krow(0, 256)))
                stok = p.sig("pe")
            else:
                p.op("pe", (lambda qsl, psS, k1: lambda e: e.matmul(
                    psS[:, 0:512], qsl, k1, start=True, stop=True))(
                        qsl, psS, krow(rc, rc + 512)), deps=sdep)
                stok = p.sig("pe")
            wf = wtsf[:, (it % 3) * 512:(it % 3 + 1) * 512]
            wb = wtse[:, (it % 3) * 512:(it % 3 + 1) * 512]
            wT = wtsT[:, (it % 3) * 512:(it % 3 + 1) * 512]
            t1 = p.op("ve", (lambda wf, psS, msl: lambda e: e.scalar_tensor_tensor(
                wf, psS[:, 0:512], 0.125, msl, ALU.mult, ALU.add))(wf, psS, msl),
                deps=[stok, wts_war["f"][it % 3]], sig=True)
            ps.done(sbank, t1)
            scol = K.smalls[:, (it % 3) * 4 + 1:(it % 3) * 4 + 2]
            rcol = K.smalls[:, (it % 3) * 4 + 2:(it % 3) * 4 + 3]
            t3 = p.op("se", (lambda wb, wf, scol: lambda e: e.activation(
                wb, wf, AF.Exp, bias=K.misc[:, 2:3], accum_out=scol))(wb, wf, scol),
                deps=[t1, wts_war["bf"][it % 3], smalls_war[it % 3]], sig=True)
            wts_war["f"][it % 3] = t3
            st8[it] = (wf, wb, wT, scol, rcol, t3)

        def emitRest(it):
            nonlocal new_attn_pe
            i, h = seq[it]
            qi = 4 * g + i
            b = qi // 2
            ch = h // 2
            r0 = 64 * (h % 2)
            wf, wb, wT, scol, rcol, t3 = st8.pop(it)
            t4 = p.op("ve", (lambda rcol, scol: lambda e: e.reciprocal(
                rcol, scol))(rcol, scol), deps=[t3], sig=True)
            smalls_war[it % 3] = t4
            t5 = p.op("ve", (lambda wb, rcol: lambda e: e.tensor_single_scalar(
                wb, wb, rcol, ALU.mult))(wb, rcol), sig=True)
            valid = [2, 3] if b == 0 else [0, 1, 2, 3]
            tbank = ("G", "H", "B")[it % 3]
            psT = (K.psG, K.psH, K.psB)[it % 3]
            tdep = [t5, ps.gate(tbank)]
            for kc in valid:
                p.op("pe", (lambda psT, wb, kc: lambda e: e.transpose(
                    psT[:, kc * 128:(kc + 1) * 128],
                    wb[:, kc * 128:(kc + 1) * 128], K.ident_f32))(psT, wb, kc),
                    deps=tdep)
                tdep = []
            ttok = p.sig("pe")
            wts_war["bf"][it % 3] = ttok
            lo = valid[0] * 128
            hi = (valid[-1] + 1) * 128
            eng = "ve" if it % 2 == 0 else "se"
            if eng == "ve":
                fn = (lambda wT, psT, lo, hi: lambda e: e.tensor_copy(
                    wT[:, lo:hi], psT[:, lo:hi]))(wT, psT, lo, hi)
            else:
                fn = (lambda wT, psT, lo, hi: lambda e: e.copy(
                    wT[:, lo:hi], psT[:, lo:hi]))(wT, psT, lo, hi)
            t6 = p.op(eng, fn, deps=[ttok, wts_war["T"][it % 3]], sig=True)
            ps.done(tbank, t6)
            pbank = "E" if (h // 2) % 2 == 0 else "F"
            psP = K.psE if (h // 2) % 2 == 0 else K.psF
            avdep = [t6] + ([ps.gate(pbank)] if h % 2 == 0 else [])
            for kidx, kc in enumerate(valid):
                gkc = 2 * b - 2 + kc
                s = gkc % 8
                vsl = vring[:, s * 1024 + h * 64: s * 1024 + h * 64 + 64]
                p.op("pe", (lambda psP, r0, vsl, wT, kc, st, sp: lambda e: e.matmul(
                    psP[r0:r0 + 64, 0:128], vsl, wT[:, kc * 128:(kc + 1) * 128],
                    start=st, stop=sp))(
                        psP, r0, vsl, wT, kc, kc == valid[0], kc == valid[-1]),
                    deps=avdep if kidx == 0 else [])
            avtok = p.sig("pe")
            wts_war["T"][it % 3] = avtok
            new_attn_pe = avtok
            if h % 2 == 1:
                dst = attnT[:, ch * 512 + i * 128: ch * 512 + (i + 1) * 128]
                eng = "ve" if ch % 2 == 0 else "se"
                if eng == "ve":
                    fn = (lambda dst, psP: lambda e: e.tensor_copy(
                        dst, psP[:, 0:128]))(dst, psP)
                else:
                    fn = (lambda dst, psP: lambda e: e.copy(
                        dst, psP[:, 0:128]))(dst, psP)
                pv = p.op(eng, fn, deps=[avtok, oproj_pe], sig=True)
                ps.done(pbank, pv)

        for it in range(len(seq) + 1):
            if it < len(seq):
                emitA(it)
            if it >= 1:
                emitRest(it - 1)
        attn_pe[0] = new_attn_pe
        attn_evacs = p.barrier(("ve", "se"))

        for c in range(8):
            slot = wk_seq % 4
            wk_seq += 1
            sl = wk[:, slot * 1024:(slot + 1) * 1024]
            wt = ld_mini(K, sl, K.ins["owT"], c * 128, 128, deps=[wkslot_war[slot]])
            bank = "C" if c % 2 == 0 else "D"
            psb = K.psC if c % 2 == 0 else K.psD
            dep0 = [wt, ps.gate(bank)] + (attn_evacs if c == 0 else [])
            for kc in range(8):
                p.op("pe", (lambda sl, kc, psb, attnT: lambda e: e.matmul(
                    psb[:, 0:512], sl[:, kc * 128:(kc + 1) * 128],
                    attnT[:, kc * 512:(kc + 1) * 512],
                    start=(kc == 0), stop=(kc == 7)))(sl, kc, psb, attnT),
                    deps=dep0 if kc == 0 else [])
            mmtok = p.sig("pe")
            wkslot_war[slot] = mmtok
            ob = K.l0bias[:, 24 + c:25 + c]
            et = p.op("ve", (lambda c, t0, psb, ob: lambda e: e.scalar_tensor_tensor(
                CH(K.xres, c, t0, 512), psb[:, 0:512], ob,
                CH(K.xres, c, t0, 512), ALU.add, ALU.add))(c, t0, psb, ob),
                deps=[mmtok], sig=True)
            ps.done(bank, et)
        oproj_pe = p.sig("pe")

    K.final_deps = p.barrier(("pe", "ve", "se"))
    K.fpool.pop(); K.bpool.pop()




def emit_router(K):
    p, ps = K.p, K.ps
    pre = list(K.final_deps)
    K.fpool.push()
    lg = K.fpool.alloc(8)
    lg2 = K.fpool.alloc(8)
    s_bc = K.fpool.alloc(8)
    dep0 = pre + [ps.gate("C")]
    for c in range(8):
        p.op("pe", (lambda c: lambda e: e.matmul(
            K.psC[0:1, 0:8], K.ones_f32[:, 0:1], K.wr_s[:, c * 8:(c + 1) * 8],
            start=(c == 0), stop=(c == 7)))(c), deps=dep0 if c == 0 else [])
    t = p.sig("pe")
    sv = p.op("ve", lambda e: e.tensor_copy(K.smalls[0:1, 32:40], K.psC[0:1, 0:8]),
              deps=[t], sig=True)
    ps.done("C", sv)
    p.op("pe", lambda e: e.matmul(K.psD[:, 0:8], K.ones_f32[0:1, :],
                                  K.smalls[0:1, 32:40], start=True, stop=True),
         deps=[sv, ps.gate("D")])
    t = p.sig("pe")
    sb = p.op("ve", lambda e: e.tensor_copy(s_bc, K.psD[:, 0:8]), deps=[t], sig=True)
    ps.done("D", sb)
    for i in range(16):
        dep0 = [ps.gate("C")]
        for c in range(8):
            p.op("pe", (lambda c, i: lambda e: e.matmul(
                K.psC[:, 0:8], CH(K.xres, c, i * 128, 128),
                K.wr_s[:, c * 8:(c + 1) * 8],
                start=(c == 0), stop=(c == 7)))(c, i), deps=dep0 if c == 0 else [])
        mmt = p.sig("pe")
        t1 = p.op("ve", (lambda i: lambda e: e.tensor_single_scalar(
            lg, K.psC[:, 0:8], K.stat_col[:, i:i + 1], ALU.mult))(i),
            deps=[mmt, sb], sig=True)
        ps.done("C", t1)
        p.op("ve", (lambda i: lambda e: e.scalar_tensor_tensor(
            lg2, s_bc, K.stat_col[:, 16 + i:17 + i], lg, ALU.mult, ALU.add))(i))
        p.op("ve", lambda e: e.tensor_add(lg, lg2, K.br_s))
        m = K.smalls[:, 8:9]
        nm = K.smalls[:, 9:10]
        ss = K.smalls[:, 10:11]
        rs = K.smalls[:, 11:12]
        p.op("ve", lambda e: e.reduce_max(m, lg, axis=AX.X))
        t5 = p.op("ve", lambda e: e.tensor_single_scalar(lg2, lg, m, ALU.is_ge),
                  sig=True)
        t6 = p.op("ve", lambda e: e.tensor_scalar_mul(nm, m, -1.0), sig=True)
        t7 = p.op("se", lambda e: e.activation(lg, lg, AF.Exp, bias=nm,
                                               accum_out=ss),
                  deps=[t5, t6], sig=True)
        t8 = p.op("ve", lambda e: e.reciprocal(rs, ss), deps=[t7], sig=True)
        t9 = p.op("ve", (lambda i: lambda e: e.tensor_single_scalar(
            K.coef[:, 8 * i:8 * (i + 1)], lg2, rs, ALU.mult))(i), sig=True)
        p.op("pe", (lambda i: lambda e: e.transpose(
            K.psA[0:8, 0:128], K.coef[:, 8 * i:8 * (i + 1)], K.ident_f32))(i),
            deps=[t9, ps.gate("A")])
        tt = p.sig("pe")
        tc = p.op("ve", (lambda i: lambda e: e.tensor_copy(
            K.coefT_bf[0:8, i * 128:(i + 1) * 128], K.psA[0:8, 0:128]))(i),
            deps=[tt], sig=True)
        ps.done("A", tc)
    K.final_deps = p.barrier(("pe", "ve", "se"))
    K.fpool.pop()


def emit_moe(K):
    p, ps = K.p, K.ps
    pre = list(K.final_deps)
    K.fpool.push(); K.bpool.push()
    h_all = K.bpool.alloc(32 * 512)
    w1b = K.bpool.alloc(2 * 4096)
    w2b = K.bpool.alloc(2 * 4096)
    cbc = K.fpool.alloc(2 * 512)
    tmp = K.fpool.alloc(512)
    w1war = [None, None]
    w2war = [None, None]
    cbcwar = [None, None]
    h_war = None
    u1 = 0
    u2 = 0
    first = pre
    for g in range(4):
        t0 = g * 512
        for e8 in range(8):
            csl = (g * 8 + e8) % 2
            cb = cbc[:, csl * 512:(csl + 1) * 512]
            p.op("pe", (lambda e8, t0: lambda e: e.matmul(
                K.psG[:, 0:512], K.basis[:, e8 * 128:(e8 + 1) * 128],
                K.coefT_bf[0:8, t0:t0 + 512], start=True, stop=True))(e8, t0),
                deps=first + [ps.gate("G")])
            first = []
            bt = p.sig("pe")
            cbt = p.op("ve", (lambda cb: lambda e: e.tensor_copy(
                cb, K.psG[:, 0:512]))(cb), deps=[bt, cbcwar[csl]], sig=True)
            ps.done("G", cbt)
            for fb in range(8):
                slot = u1 % 2
                u1 += 1
                w1u = w1b[:, slot * 4096:(slot + 1) * 4096]
                src = K.ins["w1"][e8].rearrange(
                    "(kc p) f -> p kc f", p=128)[:, :, fb * 512:(fb + 1) * 512]
                dstv = w1u.rearrange("p (kc f) -> p kc f", f=512)
                wt = p.dma("gp", (lambda dstv, src: lambda e: e.dma_start(
                    dstv, src))(dstv, src), deps=[w1war[slot]] + pre)
                for fc in range(4):
                    hidx = fb * 4 + fc
                    bank = "C" if hidx % 2 == 0 else "D"
                    psb = K.psC if hidx % 2 == 0 else K.psD
                    dep0 = [wt, ps.gate(bank)]
                    for kc in range(8):
                        p.op("pe", (lambda w1u, kc, fc, t0, psb: lambda e: e.matmul(
                            psb[:, 0:512], w1u[:, kc * 512 + fc * 128: kc * 512 + (fc + 1) * 128],
                            CH(K.xn, kc, t0, 512),
                            start=(kc == 0), stop=(kc == 7)))(w1u, kc, fc, t0, psb),
                            deps=dep0 if kc == 0 else [])
                    mmtok = p.sig("pe")
                    if fc == 3:
                        w1war[slot] = mmtok
                    et = p.op("se", (lambda hidx, e8, psb: lambda e: e.activation(
                        h_all[:, hidx * 512:(hidx + 1) * 512], psb[:, 0:512],
                        AF.Gelu, bias=K.b1lay[:, e8 * 32 + hidx:e8 * 32 + hidx + 1]))(
                            hidx, e8, psb),
                        deps=[mmtok, h_war], sig=True)
                    ps.done(bank, et)
            hdone = p.barrier(("se",))
            for dc in range(8):
                slot = u2 % 2
                u2 += 1
                w2u = w2b[:, slot * 4096:(slot + 1) * 4096]
                src = K.ins["w2lay"][e8][dc]
                wt = p.dma("gp", (lambda w2u, src: lambda e: e.dma_start(
                    w2u, src))(w2u, src), deps=[w2war[slot]] + pre)
                bank = "E" if dc % 2 == 0 else "F"
                psb = K.psE if dc % 2 == 0 else K.psF
                dep0 = [wt, ps.gate(bank)] + hdone
                for fk in range(32):
                    p.op("pe", (lambda w2u, fk, psb: lambda e: e.matmul(
                        psb[:, 0:512], w2u[:, fk * 128:(fk + 1) * 128],
                        h_all[:, fk * 512:(fk + 1) * 512],
                        start=(fk == 0), stop=(fk == 31)))(w2u, fk, psb),
                        deps=dep0 if fk == 0 else [])
                mmtok = p.sig("pe")
                w2war[slot] = mmtok
                tm = p.op("ve", (lambda psb, cb: lambda e: e.tensor_mul(
                    tmp, psb[:, 0:512], cb))(psb, cb), deps=[mmtok, cbt], sig=True)
                ta = p.op("ve", (lambda dc, t0: lambda e: e.tensor_add(
                    CH(K.xres, dc, t0, 512), CH(K.xres, dc, t0, 512), tmp))(dc, t0),
                    sig=True)
                ps.done(bank, ta)
                if dc == 7:
                    cbcwar[csl] = tm
            h_war = p.sig("pe")
        for dc in range(8):
            bank = "A" if dc % 2 == 0 else "B"
            psb = K.psA if dc % 2 == 0 else K.psB
            p.op("pe", (lambda dc, t0, psb: lambda e: e.matmul(
                psb[:, 0:512], K.b2bf[0:8, dc * 128:(dc + 1) * 128],
                K.coefT_bf[0:8, t0:t0 + 512], start=True, stop=True))(dc, t0, psb),
                deps=[ps.gate(bank)])
            mmtok = p.sig("pe")
            ta = p.op("ve", (lambda dc, t0, psb: lambda e: e.tensor_add(
                CH(K.xres, dc, t0, 512), CH(K.xres, dc, t0, 512),
                psb[:, 0:512]))(dc, t0, psb), deps=[mmtok], sig=True)
            ps.done(bank, ta)
    K.final_deps = p.barrier(("pe", "ve", "se"))
    K.fpool.pop(); K.bpool.pop()




def emit_l1_attn(K):
    p, ps = K.p, K.ps
    pre = list(K.final_deps)
    K.fpool.push(); K.bpool.push()
    maskc = K.bpool.alloc(2048)
    v_h = K.bpool.alloc(16 * 256)
    kd_h = K.bpool.alloc(2048)
    qd = K.bpool.alloc(2 * 512)
    wch = K.bpool.alloc(2 * 512)
    ret = K.bpool.alloc(2 * 1024)
    gate = K.bpool.alloc(2 * 1024)
    wqu = K.bpool.alloc(4 * 2048)
    xps = K.bpool.alloc(4 * 512)
    sqb = K.bpool.alloc(2 * 512)
    kr = K.bpool.alloc(1024).bitcast(F32)
    ks = K.bpool.alloc(1024).bitcast(F32)
    rt1 = K.bpool.alloc(1024).bitcast(F32)
    possl = K.fpool.alloc(2 * 512)
    dkt = K.fpool.alloc(2 * 512)
    rows = K.fpool.alloc(4 * 512)
    bcast = K.fpool.alloc(2 * 512)
    mu_row = rows[0:1, 0:512]
    v_row = rows[0:1, 512:1024]
    r_row = rows[0:1, 1024:1536]
    nmr_row = rows[0:1, 1536:2048]
    r_bc = bcast[:, 0:512]
    nmr_bc = bcast[:, 512:1024]
    SWAP = []
    for i in range(16):
        SWAP += [2 * i + 1, 2 * i]
    CINV = 1.0 / np.sqrt(128.0)

    mtok = p.dma("sp", lambda e: e.dma_start(maskc, K.ins["maskc"][:]), deps=pre)

    lr = K.smalls[0:1, 48:56]
    t = p.op("se", lambda e: e.activation(lr, K.decay_s, AF.Ln),
             deps=pre, sig=True)
    p.op("pe", lambda e: e.matmul(K.psA[:, 0:8], K.ones_f32[0:1, :], lr,
                                  start=True, stop=True),
         deps=[t, ps.gate("A")])
    t = p.sig("pe")
    t1 = p.op("ve", lambda e: e.tensor_copy(K.lnd[:, 0:8], K.psA[:, 0:8]),
              deps=[t], sig=True)
    ps.done("A", t1)
    t2 = p.op("ve", lambda e: e.tensor_scalar_mul(K.lnd[:, 8:16],
                                                  K.lnd[:, 0:8], -1.0), sig=True)
    lnd_done = t2

    useq = [0]
    uwar = [None, None, None, None]
    xwar = [None, None]
    poswar = [None, None]
    dktwar = [None, None]
    retwar = [None, None]
    gatewar = [None, None]
    sq_war = None
    qdwar = [None, None]
    wchwar = [None, None]
    kdwar = None
    vwar = None

    def wunit(cols0, ncols, deps):
        slot = useq[0] % 4
        useq[0] += 1
        sl = wqu[:, slot * 2048: slot * 2048 + 8 * ncols]
        src = K.ins["wqkv"].rearrange("(kc p) f -> p kc f", p=128)[
            :, :, cols0:cols0 + ncols]
        dstv = sl.rearrange("p (kc f) -> p kc f", f=ncols)
        tok = p.dma("gp", (lambda dstv, src: lambda e: e.dma_start(dstv, src))(
            dstv, src), deps=[uwar[slot]] + deps)
        return slot, sl, tok

    def wunit_from(name, rows0, nrows, cols0, ncols, deps):
        slot = useq[0] % 4
        useq[0] += 1
        sl = wqu[:, slot * 2048: slot * 2048 + (nrows // 128) * ncols]
        src = K.ins[name][rows0:rows0 + nrows].rearrange(
            "(kc p) f -> p kc f", p=128)[:, :, cols0:cols0 + ncols]
        dstv = sl.rearrange("p (kc f) -> p kc f", f=ncols)
        tok = p.dma("gp", (lambda dstv, src: lambda e: e.dma_start(dstv, src))(
            dstv, src), deps=[uwar[slot]] + deps)
        return slot, sl, tok

    def rotary_decay(g, h, raw_bank, psb, dst, qk, dep0, dstwar=None):
        sl = g % 2
        cs_s = xps[:, sl * 1024: sl * 1024 + 512]
        sn_s = xps[:, sl * 1024 + 512: sl * 1024 + 1024]
        csn = "csq" if qk == "q" else "csk"
        snn = "snq" if qk == "q" else "snk"
        ct = p.dma("sp", (lambda cs_s, csn, h, g: lambda e: e.dma_start(
            cs_s, K.ins[csn][128 * h:128 * (h + 1), 512 * g:512 * (g + 1)]))(
                cs_s, csn, h, g), deps=[xwar[sl]])
        st_ = p.dma("sp", (lambda sn_s, snn, h, g: lambda e: e.dma_start(
            sn_s, K.ins[snn][128 * h:128 * (h + 1), 512 * g:512 * (g + 1)]))(
                sn_s, snn, h, g), deps=[xwar[sl]])
        pt = p.dma("sp", (lambda g: lambda e: e.dma_start(
            possl[:, (g % 2) * 512:(g % 2 + 1) * 512],
            K.ins["pos"][:, 512 * g:512 * (g + 1)]))(g), deps=[poswar[g % 2]])
        pos_s = possl[:, (g % 2) * 512:(g % 2 + 1) * 512]
        e1 = p.op("ve", (lambda psb: lambda e: e.tensor_copy(kr, psb[:, 0:512]))(psb),
                  deps=dep0, sig=True)
        ps.done(raw_bank, e1)
        p.op("ve", lambda e: e.stream_shuffle(ks, kr, SWAP))
        p.op("ve", (lambda cs_s: lambda e: e.tensor_mul(rt1, kr, cs_s))(cs_s),
             deps=[ct])
        p.op("ve", (lambda sn_s: lambda e: e.tensor_mul(ks, ks, sn_s))(sn_s),
             deps=[st_])
        s1 = p.op("ve", lambda e: e.tensor_add(rt1, rt1, ks), sig=True)
        lcol = K.lnd[:, h:h + 1] if qk == "q" else K.lnd[:, 8 + h:9 + h]
        dk_s = dkt[:, (g % 2) * 512:(g % 2 + 1) * 512]
        d1 = p.op("se", (lambda dk_s, pos_s, lcol: lambda e: e.activation(
            dk_s, pos_s, AF.Exp, scale=lcol))(dk_s, pos_s, lcol),
            deps=[pt, lnd_done, dktwar[g % 2]], sig=True)
        poswar[g % 2] = d1
        o1 = p.op("ve", (lambda dst, dk_s: lambda e: e.tensor_mul(
            dst, rt1, dk_s))(dst, dk_s), deps=[s1, d1, dstwar], sig=True)
        dktwar[g % 2] = o1
        xwar[sl] = o1
        return o1

    for h in range(8):
        slot, ku, ktok = wunit(1024 + 128 * h, 128, pre if h == 0 else [])
        kd_toks = []
        for g in range(4):
            bank = "C" if g % 2 == 0 else "D"
            psb = K.psC if g % 2 == 0 else K.psD
            dep0 = [ktok, ps.gate(bank), kdwar]
            for kc in range(8):
                p.op("pe", (lambda ku, kc, g, psb: lambda e: e.matmul(
                    psb[:, 0:512], ku[:, kc * 128:(kc + 1) * 128],
                    CH(K.xn, kc, g * 512, 512),
                    start=(kc == 0), stop=(kc == 7)))(ku, kc, g, psb),
                    deps=dep0 if kc == 0 else [])
            mmtok = p.sig("pe")
            if g == 3:
                uwar[slot] = mmtok
            kd_toks.append(rotary_decay(
                g, h, bank, psb, kd_h[:, g * 512:(g + 1) * 512], "k", [mmtok],
                dstwar=kdwar))
        slot, vu, vtok = wunit(2048 + 256 * h, 256, [])
        v_toks = []
        for gi in range(16):
            bank = "E" if gi % 2 == 0 else "F"
            psb = K.psE if gi % 2 == 0 else K.psF
            dep0 = [vtok, ps.gate(bank), vwar]
            for kc in range(8):
                p.op("pe", (lambda vu, kc, gi, psb: lambda e: e.matmul(
                    psb[:, 0:256], CH(K.xn, kc, gi * 128, 128),
                    vu[:, kc * 256:(kc + 1) * 256],
                    start=(kc == 0), stop=(kc == 7)))(vu, kc, gi, psb),
                    deps=dep0 if kc == 0 else [])
            mmtok = p.sig("pe")
            if gi == 15:
                uwar[slot] = mmtok
            eng = "ve" if gi % 2 == 0 else "se"
            dst = v_h[:, gi * 256:(gi + 1) * 256]
            if eng == "ve":
                fn = (lambda dst, psb: lambda e: e.tensor_copy(dst, psb[:, 0:256]))(dst, psb)
            else:
                fn = (lambda dst, psb: lambda e: e.copy(dst, psb[:, 0:256]))(dst, psb)
            et = p.op(eng, fn, deps=[mmtok], sig=True)
            ps.done(bank, et)
            v_toks.append(et)
        sqg, qu, qtok = wunit(128 * h, 128, [])
        sgg, gu, gtok = wunit_from("wg", 0, 1024, 256 * h, 256, [])
        spg, pu, ptok = wunit_from("wproj1", 256 * h, 256, 0, 1024, [])
        q_used = None
        g_used = None
        p_used = None
        for g in range(4):
            t0 = 512 * g
            bank = "C" if g % 2 == 0 else "D"
            psb = K.psC if g % 2 == 0 else K.psD
            dep0 = [qtok, ps.gate(bank), qdwar[g % 2]]
            for kc in range(8):
                p.op("pe", (lambda qu, kc, t0, psb: lambda e: e.matmul(
                    psb[:, 0:512], qu[:, kc * 128:(kc + 1) * 128],
                    CH(K.xn, kc, t0, 512),
                    start=(kc == 0), stop=(kc == 7)))(qu, kc, t0, psb),
                    deps=dep0 if kc == 0 else [])
            mmtok = p.sig("pe")
            q_used = mmtok
            qsl = qd[:, (g % 2) * 512:(g % 2 + 1) * 512]
            qdone = rotary_decay(g, h, bank, psb, qsl, "q", [mmtok],
                                 dstwar=qdwar[g % 2])
            nkc = 4 * g + 4
            avdep0 = [ps.gate("G")]
            avdep1 = [ps.gate("H")]
            av_last = None
            ets = [None] * nkc
            def emit_scores(kc):
                bank = "C" if kc % 2 == 0 else "D"
                psb = K.psC if kc % 2 == 0 else K.psD
                p.op("pe", (lambda kc, qsl, psb: lambda e: e.matmul(
                    psb[:, 0:512], kd_h[:, kc * 128:(kc + 1) * 128], qsl,
                    start=True, stop=True))(kc, qsl, psb),
                    deps=[qdone, kd_toks[kc // 4], ps.gate(bank)])
                smt = p.sig("pe")
                wsl = wch[:, (kc % 2) * 512:(kc % 2 + 1) * 512]
                d = kc - 4 * g
                if d >= 0:
                    et = p.op("ve", (lambda wsl, psb, d: lambda e: e.scalar_tensor_tensor(
                        wsl, psb[:, 0:512], CINV,
                        maskc[:, d * 512:(d + 1) * 512], ALU.mult, ALU.mult))(
                            wsl, psb, d), deps=[smt, mtok, wchwar[kc % 2]], sig=True)
                else:
                    et = p.op("ve", (lambda wsl, psb: lambda e: e.tensor_single_scalar(
                        wsl, psb[:, 0:512], CINV, ALU.mult))(wsl, psb),
                        deps=[smt, wchwar[kc % 2]], sig=True)
                ps.done(bank, et)
                ets[kc] = (et, wsl)
            emit_scores(0)
            for kc in range(nkc):
                if kc + 1 < nkc:
                    emit_scores(kc + 1)
                et, wsl = ets[kc]
                for dvc in range(2):
                    psav = K.psG if dvc == 0 else K.psH
                    vsl = v_h[:, kc * 256 + dvc * 128: kc * 256 + (dvc + 1) * 128]
                    dep = [et, v_toks[kc]] + (avdep0 if dvc == 0 else avdep1)
                    p.op("pe", (lambda psav, vsl, wsl, st, sp: lambda e: e.matmul(
                        psav[:, 0:512], vsl, wsl, start=st, stop=sp))(
                            psav, vsl, wsl, kc == 0, kc == nkc - 1),
                        deps=dep)
                    if dvc == 0:
                        avdep0 = []
                    else:
                        avdep1 = []
                av_last = p.sig("pe")
                wchwar[kc % 2] = av_last
            qdwar[g % 2] = av_last
            rsl = ret[:, (g % 2) * 1024:(g % 2 + 1) * 1024]
            e1 = p.op("ve", (lambda rsl: lambda e: e.tensor_copy(
                rsl[:, 0:512], K.psG[:, 0:512]))(rsl),
                deps=[av_last, retwar[g % 2]], sig=True)
            ps.done("G", e1)
            e2 = p.op("se", (lambda rsl: lambda e: e.copy(
                rsl[:, 512:1024], K.psH[:, 0:512]))(rsl),
                deps=[av_last, retwar[g % 2]], sig=True)
            ps.done("H", e2)
            q1 = p.op("se", (lambda rsl: lambda e: e.activation(
                sqb[:, 0:512], rsl[:, 0:512], AF.Square))(rsl),
                deps=[e1, sq_war], sig=True)
            q2 = p.op("se", (lambda rsl: lambda e: e.activation(
                sqb[:, 512:1024], rsl[:, 512:1024], AF.Square))(rsl),
                deps=[e2], sig=True)
            dep0 = [e1, ps.gate("A")]
            p.op("pe", (lambda rsl: lambda e: e.matmul(
                K.psA[0:1, 0:512], K.ones_bf[:, 0:1], rsl[:, 0:512],
                start=True, stop=False))(rsl), deps=dep0)
            p.op("pe", (lambda rsl: lambda e: e.matmul(
                K.psA[0:1, 0:512], K.ones_bf[:, 0:1], rsl[:, 512:1024],
                start=False, stop=True))(rsl), deps=[e2])
            smu = p.sig("pe")
            p.op("pe", lambda e: e.matmul(
                K.psB[0:1, 0:512], K.ones_bf[:, 0:1], sqb[:, 0:512],
                start=True, stop=False), deps=[q1, ps.gate("B")])
            p.op("pe", lambda e: e.matmul(
                K.psB[0:1, 0:512], K.ones_bf[:, 0:1], sqb[:, 512:1024],
                start=False, stop=True), deps=[q2])
            svar = p.sig("pe")
            sq_war = svar
            inv = 1.0 / 256.0
            t1 = p.op("ve", lambda e: e.tensor_scalar_mul(mu_row, K.psA[0:1, 0:512], inv),
                      deps=[smu], sig=True)
            ps.done("A", t1)
            t2 = p.op("ve", lambda e: e.tensor_scalar_mul(v_row, K.psB[0:1, 0:512], inv),
                      deps=[svar], sig=True)
            ps.done("B", t2)
            t3 = p.op("se", lambda e: e.activation(r_row, mu_row, AF.Square),
                      deps=[t1], sig=True)
            t4 = p.op("ve", lambda e: e.tensor_sub(v_row, v_row, r_row),
                      deps=[t3], sig=True)
            t5 = p.op("se", lambda e: e.activation(
                r_row, v_row, AF.Sqrt, bias=K.misc[0:1, 0:1]), deps=[t4], sig=True)
            t6 = p.op("ve", lambda e: e.reciprocal(r_row, r_row), deps=[t5], sig=True)
            t7 = p.op("ve", lambda e: e.scalar_tensor_tensor(
                nmr_row, mu_row, -1.0, r_row, ALU.mult, ALU.mult),
                deps=[t6], sig=True)
            p.op("pe", lambda e: e.matmul(
                K.psA[:, 0:512], K.ones_f32[0:1, :], r_row, start=True, stop=True),
                deps=[t7, ps.gate("A")])
            bE = p.sig("pe")
            p.op("pe", lambda e: e.matmul(
                K.psB[:, 0:512], K.ones_f32[0:1, :], nmr_row, start=True, stop=True),
                deps=[t7, ps.gate("B")])
            bF = p.sig("pe")
            e3 = p.op("ve", lambda e: e.tensor_copy(r_bc, K.psA[:, 0:512]),
                      deps=[bE], sig=True)
            ps.done("A", e3)
            e4 = p.op("se", lambda e: e.copy(nmr_bc, K.psB[:, 0:512]),
                      deps=[bF], sig=True)
            ps.done("B", e4)
            gsl = gate[:, (g % 2) * 1024:(g % 2 + 1) * 1024]
            for u2 in range(2):
                bank = "E" if u2 == 0 else "F"
                psb = K.psE if u2 == 0 else K.psF
                dep0 = [gtok, ps.gate(bank), gatewar[g % 2]]
                for kc in range(8):
                    p.op("pe", (lambda gu, kc, u2, t0, psb: lambda e: e.matmul(
                        psb[:, 0:512], gu[:, kc * 256 + u2 * 128: kc * 256 + (u2 + 1) * 128],
                        CH(K.xn, kc, t0, 512),
                        start=(kc == 0), stop=(kc == 7)))(gu, kc, u2, t0, psb),
                        deps=dep0 if kc == 0 else [])
                mmtok = p.sig("pe")
                g_used = mmtok
                et = p.op("se", (lambda gsl, u2, psb: lambda e: e.activation(
                    gsl[:, u2 * 512:(u2 + 1) * 512], psb[:, 0:512], AF.Silu))(
                        gsl, u2, psb), deps=[mmtok], sig=True)
                ps.done(bank, et)
            gate_done = p.sig("se")
            ptoks = []
            for u2 in range(2):
                p.op("ve", (lambda rsl, u2: lambda e: e.tensor_mul(
                    rt1, rsl[:, u2 * 512:(u2 + 1) * 512], r_bc))(rsl, u2),
                    deps=[e3, e4, gate_done])
                p.op("ve", (lambda rsl, u2: lambda e: e.tensor_add(
                    rsl[:, u2 * 512:(u2 + 1) * 512], rt1, nmr_bc))(rsl, u2))
                pt2 = p.op("ve", (lambda gsl, rsl, u2: lambda e: e.tensor_mul(
                    gsl[:, u2 * 512:(u2 + 1) * 512],
                    rsl[:, u2 * 512:(u2 + 1) * 512],
                    gsl[:, u2 * 512:(u2 + 1) * 512]))(gsl, rsl, u2), sig=True)
                ptoks.append(pt2)
            retwar[g % 2] = ptoks[-1]
            for dc in range(8):
                bank = "C" if dc % 2 == 0 else "D"
                psb = K.psC if dc % 2 == 0 else K.psD
                dep0 = [ptok, ps.gate(bank)] + ptoks
                for kcp in range(2):
                    p.op("pe", (lambda pu, kcp, dc, gsl, psb: lambda e: e.matmul(
                        psb[:, 0:512], pu[:, kcp * 1024 + dc * 128: kcp * 1024 + (dc + 1) * 128],
                        gsl[:, kcp * 512:(kcp + 1) * 512],
                        start=(kcp == 0), stop=(kcp == 1)))(pu, kcp, dc, gsl, psb),
                        deps=dep0 if kcp == 0 else [])
                mmtok = p.sig("pe")
                p_used = mmtok
                gatewar[g % 2] = mmtok
                et = p.op("ve", (lambda dc, t0, psb: lambda e: e.tensor_add(
                    CH(K.xres, dc, t0, 512), CH(K.xres, dc, t0, 512),
                    psb[:, 0:512]))(dc, t0, psb), deps=[mmtok], sig=True)
                ps.done(bank, et)
        uwar[sqg] = q_used
        uwar[sgg] = g_used
        uwar[spg] = p_used
        kdwar = p.sig("pe")
        vwar = kdwar
    K.final_deps = p.barrier(("pe", "ve", "se"))
    K.fpool.pop(); K.bpool.pop()




def emit_l1_ffn(K):
    p, ps = K.p, K.ps
    pre = list(K.final_deps)
    K.fpool.push(); K.bpool.push()

    qch = K.fpool.alloc(4 * 1024)
    acc = K.smalls[:, 12:13]
    accw = K.smalls[:, 13:14]
    thr = K.smalls[:, 14:15]
    nthr = K.smalls[:, 15:16]
    qbf = K.bpool.alloc(4 * 1024)
    lnw_bf = K.bpool.alloc(32)
    lnb_bf = K.bpool.alloc(32)
    rt_q = K.bpool.alloc(4096).bitcast(F32)
    p.op("ve", lambda e: e.tensor_copy(lnw_bf, K.lnw), deps=pre, sig=True)
    lnbf_tok = p.op("ve", lambda e: e.tensor_copy(lnb_bf, K.lnb),
                    deps=pre, sig=True)

    def quant_weight(name, nrows, ncols, chcols, scname, fold_lnw, stats):
        nch_r = nrows // 128
        nch_c = ncols // chcols
        nslot = 4
        chunks = [(r, c) for r in range(nch_r) for c in range(nch_c)]
        zt = p.op("ve", lambda e: e.memset(acc, 0.0), deps=pre, sig=True)
        war = [zt] * nslot
        for idx, (r, c) in enumerate(chunks):
            sl = qch[:, (idx % nslot) * 1024:(idx % nslot) * 1024 + chcols]
            dt = p.dma("sp", (lambda sl, r, c: lambda e: e.dma_start(
                sl, K.ins[name][r * 128:(r + 1) * 128,
                                c * chcols:(c + 1) * chcols]))(sl, r, c),
                deps=[war[idx % nslot]] + pre)
            t1 = p.op("ve", (lambda sl: lambda e: e.reduce_sum(
                accw, sl, axis=AX.X, apply_absolute_value=True))(sl),
                deps=[dt], sig=True)
            war[idx % nslot] = t1
            t2 = p.op("ve", lambda e: e.tensor_add(acc, acc, accw), sig=True)
        p.op("pe", lambda e: e.matmul(K.psA[0:1, 0:1], K.ones_f32[:, 0:1], acc,
                                      start=True, stop=True),
             deps=[t2, ps.gate("A")])
        gt = p.sig("pe")
        n_el = float(nrows * ncols)
        t3 = p.op("ve", lambda e: e.tensor_scalar_mul(
            K.smalls[0:1, 56:57], K.psA[0:1, 0:1], 1.0 / n_el), deps=[gt], sig=True)
        ps.done("A", t3)
        p.op("ve", lambda e: e.tensor_scalar_add(
            K.smalls[0:1, 56:57], K.smalls[0:1, 56:57], 1e-6))
        t4 = p.op("ve", lambda e: e.tensor_scalar_mul(
            K.smalls[0:1, 56:57], K.smalls[0:1, 56:57], 0.5), sig=True)
        p.op("pe", lambda e: e.matmul(K.psA[:, 0:1], K.ones_f32[0:1, :],
                                      K.smalls[0:1, 56:57], start=True, stop=True),
             deps=[t4, ps.gate("A")])
        bt = p.sig("pe")
        t5 = p.op("ve", lambda e: e.tensor_copy(thr, K.psA[:, 0:1]),
                  deps=[bt], sig=True)
        ps.done("A", t5)
        t6 = p.op("ve", lambda e: e.tensor_scalar_mul(nthr, thr, -1.0), sig=True)
        war = [t6] * nslot
        qwar = [None] * nslot
        stat_started = False
        for idx, (r, c) in enumerate(chunks):
            sl = qch[:, (idx % nslot) * 1024:(idx % nslot) * 1024 + chcols]
            qsl = qbf[:, (idx % nslot) * 1024:(idx % nslot) * 1024 + chcols]
            dt = p.dma("sp", (lambda sl, r, c: lambda e: e.dma_start(
                sl, K.ins[name][r * 128:(r + 1) * 128,
                                c * chcols:(c + 1) * chcols]))(sl, r, c),
                deps=[war[idx % nslot]])
            t1 = p.op("ve", (lambda qsl, sl: lambda e: e.tensor_single_scalar(
                qsl, sl, thr, ALU.is_ge))(qsl, sl),
                deps=[dt, qwar[idx % nslot], t6], sig=True)
            t2 = p.op("ve", (lambda sl, cc: lambda e: e.tensor_single_scalar(
                rt_q[:, 0:cc], sl, nthr, ALU.is_le))(sl, chcols), sig=True)
            war[idx % nslot] = t2
            t3 = p.op("ve", (lambda qsl, cc: lambda e: e.tensor_sub(
                qsl, qsl, rt_q[:, 0:cc]))(qsl, chcols), sig=True)
            if stats:
                dep0 = [t3, lnbf_tok] + ([ps.gate("E"), ps.gate("F"),
                                          ps.gate("G"), ps.gate("H")]
                                         if not stat_started else [])
                stat_started = True
                p.op("pe", (lambda qsl, r: lambda e: e.matmul(
                    K.psE[0:1, 0:512], lnw_bf[:, r:r + 1], qsl[:, 0:512],
                    start=(r == 0), stop=(r == 31)))(qsl, r), deps=dep0)
                p.op("pe", (lambda qsl, r: lambda e: e.matmul(
                    K.psF[0:1, 0:512], lnw_bf[:, r:r + 1], qsl[:, 512:1024],
                    start=(r == 0), stop=(r == 31)))(qsl, r))
                p.op("pe", (lambda qsl, r: lambda e: e.matmul(
                    K.psG[0:1, 0:512], lnb_bf[:, r:r + 1], qsl[:, 0:512],
                    start=(r == 0), stop=(r == 31)))(qsl, r))
                p.op("pe", (lambda qsl, r: lambda e: e.matmul(
                    K.psH[0:1, 0:512], lnb_bf[:, r:r + 1], qsl[:, 512:1024],
                    start=(r == 0), stop=(r == 31)))(qsl, r))
                stok = p.sig("pe")
            if fold_lnw:
                t4 = p.op("ve", (lambda qsl, r: lambda e: e.tensor_single_scalar(
                    qsl, qsl, K.lnw[:, r:r + 1], ALU.mult))(qsl, r),
                    deps=[stok] if stats else [], sig=True)
            if scname == "fc":
                dst = K.wqfc_sc[r * 128:(r + 1) * 128, c * chcols:(c + 1) * chcols]
                qv = qsl
            else:
                dst = K.wqproj_sc.rearrange("a p k d -> p a k d")[:, :, r, :]
                qv = qsl.rearrange("p (a d) -> p a d", d=128)
            st = p.dma("sp", (lambda dst, qv: lambda e: e.dma_start(
                dst, qv))(dst, qv), deps=[p.sig("ve")])
            qwar[idx % nslot] = st
        return p.front("sp")

    fc_done = quant_weight("wfcT", 1024, 4096, 1024, "fc", False, False)
    pr_done = quant_weight("wprojT", 4096, 1024, 1024, "proj", True, True)
    e1 = p.op("ve", lambda e: e.tensor_copy(K.s1s0_bf[0:1, 0:512], K.psE[0:1, 0:512]), sig=True)
    ps.done("E", e1)
    e2 = p.op("ve", lambda e: e.tensor_copy(K.s1s0_bf[0:1, 512:1024], K.psF[0:1, 0:512]), sig=True)
    ps.done("F", e2)
    e3 = p.op("ve", lambda e: e.tensor_copy(K.s1s0_bf[0:1, 1024:1536], K.psG[0:1, 0:512]), sig=True)
    ps.done("G", e3)
    e4 = p.op("ve", lambda e: e.tensor_copy(K.s1s0_bf[0:1, 1536:2048], K.psH[0:1, 0:512]), sig=True)
    ps.done("H", e4)
    s_done = [e1, e2, e3, e4]
    quant_done = p.barrier(("ve", "se", "sp"))
    K.fpool.pop()
    K.bpool.pop()

    K.fpool.push(); K.bpool.push()
    h_g = K.bpool.alloc(32 * 512)
    w1b = K.bpool.alloc(2 * 4096)
    w2b = K.bpool.alloc(2 * 4096)
    sq = K.bpool.alloc(2 * 512)
    nmr_bf = K.bpool.alloc(512)
    rows = K.fpool.alloc(4 * 512)
    r_bc = K.fpool.alloc(512)
    mu_row = rows[0:1, 0:512]
    v_row = rows[0:1, 512:1024]
    r_row = rows[0:1, 1024:1536]
    nmr_row = rows[0:1, 1536:2048]
    w1war = [None, None]
    w2war = [None, None]
    sq_war = [quant_done, quant_done]
    h_war = quant_done
    rbc_war = None
    nmrbf_war = quant_done
    u1 = 0
    u2 = 0
    for g in range(4):
        t0 = g * 512
        stat_dep = [ps.gate("G")]
        stat_dep2 = [ps.gate("H")]
        for fb in range(8):
            slot = u1 % 2
            u1 += 1
            w1u = w1b[:, slot * 4096:(slot + 1) * 4096]
            src = K.wqfc_sc.rearrange("(kc p) f -> p kc f", p=128)[
                :, :, fb * 512:(fb + 1) * 512]
            dstv = w1u.rearrange("p (kc f) -> p kc f", f=512)
            wt = p.dma("sp", (lambda dstv, src: lambda e: e.dma_start(
                dstv, src))(dstv, src), deps=[w1war[slot]] + fc_done + pre)
            for fc in range(4):
                hidx = fb * 4 + fc
                bank = "C" if hidx % 2 == 0 else "D"
                psb = K.psC if hidx % 2 == 0 else K.psD
                dep0 = [wt, ps.gate(bank)]
                for kc in range(8):
                    p.op("pe", (lambda w1u, kc, fc, t0, psb: lambda e: e.matmul(
                        psb[:, 0:512],
                        w1u[:, kc * 512 + fc * 128: kc * 512 + (fc + 1) * 128],
                        CH(K.xn, kc, t0, 512),
                        start=(kc == 0), stop=(kc == 7)))(w1u, kc, fc, t0, psb),
                        deps=dep0 if kc == 0 else [])
                mmtok = p.sig("pe")
                if fc == 3:
                    w1war[slot] = mmtok
                et = p.op("se", (lambda hidx, psb: lambda e: e.activation(
                    h_g[:, hidx * 512:(hidx + 1) * 512], psb[:, 0:512],
                    AF.Gelu))(hidx, psb), deps=[mmtok, h_war], sig=True)
                ps.done(bank, et)
                q1 = p.op("se", (lambda hidx: lambda e: e.activation(
                    sq[:, (hidx % 2) * 512:(hidx % 2 + 1) * 512],
                    h_g[:, hidx * 512:(hidx + 1) * 512], AF.Square))(hidx),
                    deps=[sq_war[hidx % 2]], sig=True)
                p.op("pe", (lambda hidx: lambda e: e.matmul(
                    K.psG[0:1, 0:512], K.ones_bf[:, 0:1],
                    h_g[:, hidx * 512:(hidx + 1) * 512],
                    start=(hidx == 0), stop=(hidx == 31)))(hidx),
                    deps=[et] + stat_dep)
                stat_dep = []
                p.op("pe", (lambda hidx: lambda e: e.matmul(
                    K.psH[0:1, 0:512], K.ones_bf[:, 0:1],
                    sq[:, (hidx % 2) * 512:(hidx % 2 + 1) * 512],
                    start=(hidx == 0), stop=(hidx == 31)))(hidx),
                    deps=[q1] + stat_dep2)
                stat_dep2 = []
                sq_war[hidx % 2] = p.sig("pe")
        smtok = p.sig("pe")
        inv = 1.0 / 4096.0
        t1 = p.op("ve", lambda e: e.tensor_scalar_mul(mu_row, K.psG[0:1, 0:512], inv),
                  deps=[smtok], sig=True)
        ps.done("G", t1)
        t2 = p.op("ve", lambda e: e.tensor_scalar_mul(v_row, K.psH[0:1, 0:512], inv),
                  deps=[smtok], sig=True)
        ps.done("H", t2)
        t3 = p.op("se", lambda e: e.activation(r_row, mu_row, AF.Square),
                  deps=[t1], sig=True)
        t4 = p.op("ve", lambda e: e.tensor_sub(v_row, v_row, r_row), deps=[t3], sig=True)
        t5 = p.op("se", lambda e: e.activation(r_row, v_row, AF.Sqrt,
                                               bias=K.misc[0:1, 0:1]),
                  deps=[t4], sig=True)
        t6 = p.op("ve", lambda e: e.reciprocal(r_row, r_row), deps=[t5], sig=True)
        t7 = p.op("ve", lambda e: e.scalar_tensor_tensor(
            nmr_row, mu_row, -1.0, r_row, ALU.mult, ALU.mult), deps=[t6], sig=True)
        t8 = p.op("ve", lambda e: e.tensor_copy(nmr_bf[0:1, 0:512], nmr_row),
                  deps=[t7, nmrbf_war], sig=True)
        p.op("pe", lambda e: e.matmul(K.psG[:, 0:512], K.ones_f32[0:1, :], r_row,
                                      start=True, stop=True),
             deps=[t7, ps.gate("G")])
        bt = p.sig("pe")
        t9 = p.op("ve", (lambda: lambda e: e.tensor_copy(r_bc, K.psG[:, 0:512]))(),
                  deps=[bt, rbc_war], sig=True)
        ps.done("G", t9)
        htoks = []
        for fk in range(32):
            ht = p.op("ve", (lambda fk: lambda e: e.tensor_mul(
                h_g[:, fk * 512:(fk + 1) * 512], h_g[:, fk * 512:(fk + 1) * 512],
                r_bc))(fk), deps=[t9, smtok], sig=True)
            htoks.append(ht)
        rbc_war_next = htoks[-1]
        for dc in range(8):
            slot = u2 % 2
            u2 += 1
            w2u = w2b[:, slot * 4096:(slot + 1) * 4096]
            wt = p.dma("sp", (lambda w2u, dc: lambda e: e.dma_start(
                w2u, K.wqproj_sc[dc].flatten_outer_dims() if False
                else K.wqproj_sc[dc].rearrange("p k d -> p (k d)")))(w2u, dc),
                deps=[w2war[slot]] + pr_done)
            bank = "E" if dc % 2 == 0 else "F"
            psb = K.psE if dc % 2 == 0 else K.psF
            dep0 = [wt, ps.gate(bank)]
            for fk in range(32):
                p.op("pe", (lambda w2u, fk, psb: lambda e: e.matmul(
                    psb[:, 0:512], w2u[:, fk * 128:(fk + 1) * 128],
                    h_g[:, fk * 512:(fk + 1) * 512],
                    start=(fk == 0), stop=False))(w2u, fk, psb),
                    deps=(dep0 if fk == 0 else []) + [htoks[fk]])
            p.op("pe", (lambda dc, psb: lambda e: e.matmul(
                psb[:, 0:512], K.s1s0_bf[0:1, dc * 128:(dc + 1) * 128],
                nmr_bf[0:1, 0:512], start=False, stop=False))(dc, psb),
                deps=[t8] + s_done)
            p.op("pe", (lambda dc, psb: lambda e: e.matmul(
                psb[:, 0:512], K.s1s0_bf[0:1, 1024 + dc * 128:1024 + (dc + 1) * 128],
                K.ones_bf[0:1, 0:512], start=False, stop=True))(dc, psb))
            mmtok = p.sig("pe")
            w2war[slot] = mmtok
            et = p.op("ve", (lambda dc, t0, psb: lambda e: e.tensor_add(
                CH(K.xres, dc, t0, 512), CH(K.xres, dc, t0, 512),
                psb[:, 0:512]))(dc, t0, psb), deps=[mmtok], sig=True)
            ps.done(bank, et)
        h_war = p.sig("pe")
        rbc_war = rbc_war_next
        nmrbf_war = h_war
    K.final_deps = p.barrier(("pe", "ve", "se"))
    K.fpool.pop(); K.bpool.pop()




# version 05
# speedup vs baseline: 1.0275x; 1.0095x over previous
import numpy as np
import ml_dtypes
from contextlib import ExitStack

import concourse.bass as bass
import concourse.mybir as mybir
from concourse.bass_utils import run_bass_kernel_spmd

F32 = mybir.dt.float32
BF16 = mybir.dt.bfloat16
AF = mybir.ActivationFunctionType
ALU = mybir.AluOpType
AX = mybir.AxisListType
BF = ml_dtypes.bfloat16

T = 2048
D = 1024
NCH = 8
NT = 16
NEG = -1.0e30

ARENA_BF_COLS = 36864
ARENA_F32_COLS = 5120

ENGSTEP = {"pe": 1, "ve": 1, "se": 1, "gp": 16, "sp": 16}
NCH_GP = 12
NCH_SP = 6



def _lay_feat(v):
    f = v.shape[0]
    return np.ascontiguousarray(v.reshape(f // 128, 128).T)


def _host_constants():
    c = {}
    c["ident_bf"] = np.eye(128, dtype=BF)
    c["ident_f32"] = np.eye(128, dtype=np.float32)
    c["ones_bf"] = np.ones((128, 512), dtype=BF)
    c["ones_f32"] = np.ones((128, 128), dtype=np.float32)

    m0 = np.full((128, 4 * 512), NEG, dtype=np.float32)
    w = np.arange(512)[None, :]
    for parity in range(2):
        for b0 in range(2):
            v = parity + 2 * b0
            ql = 128 * parity + np.arange(128)[:, None]
            valid = (w >= ql) & (w <= ql + 256)
            if b0:
                valid &= (w >= 256)
            m0[:, v * 512:(v + 1) * 512][valid] = 0.0
    c["mask0"] = m0.astype(BF)

    mc = np.zeros((128, 4 * 512), dtype=np.float32)
    qf = np.arange(512)[None, :]
    kp = np.arange(128)[:, None]
    for d in range(4):
        mc[:, d * 512:(d + 1) * 512] = (qf >= kp + 128 * d).astype(np.float32)
    c["maskc"] = mc.astype(BF)

    c["pos"] = np.broadcast_to(
        np.arange(T, dtype=np.float32)[None, :], (128, T)).copy()

    QK = 1024
    SCALE_BASE = 512.0
    i2 = np.arange(0, QK, 2, dtype=np.float64)
    base_scale = (i2 + 0.4 * QK) / (1.4 * QK)
    t = np.arange(T, dtype=np.float64)
    scale = base_scale[None, :] ** (t[:, None] / SCALE_BASE)
    inv_freq = 1.0 / (10000.0 ** (np.arange(QK // 2, dtype=np.float64) / (QK // 2)))
    sinu = t[:, None] * inv_freq[None, :]
    sn, cs = np.sin(sinu), np.cos(sinu)
    rep = lambda a: np.repeat(a, 2, axis=1)
    sgn = np.where(np.arange(QK) % 2 == 0, -1.0, 1.0)[None, :]
    c["csq"] = np.ascontiguousarray((rep(cs * scale)).T).astype(BF)
    c["snq"] = np.ascontiguousarray((rep(sn * scale) * sgn).T).astype(BF)
    c["csk"] = np.ascontiguousarray((rep(cs / scale)).T).astype(BF)
    c["snk"] = np.ascontiguousarray((rep(sn / scale) * sgn).T).astype(BF)

    bas = np.zeros((8, 8 * 128), dtype=np.float32)
    for e in range(8):
        bas[e, e * 128:(e + 1) * 128] = 1.0
    c["basis_bf"] = bas.astype(BF)

    mcc = np.zeros((128, 8), dtype=np.float32)
    mcc[:, 0] = 1e-5
    mcc[:, 1] = 1e-6
    mcc[:, 2] = -4.0
    c["misc_cols"] = mcc
    return c


_CONST_CACHE = None


def _prep_shared(inputs):
    global _CONST_CACHE
    if _CONST_CACHE is None:
        _CONST_CACHE = _host_constants()
    s = dict(_CONST_CACHE)
    f32 = np.float32
    asC = lambda a: np.ascontiguousarray(np.asarray(a, f32))
    s["qwT"] = asC(np.asarray(inputs["L0_attn_q_w"]).T)
    s["kwT"] = asC(np.asarray(inputs["L0_attn_k_w"]).T)
    s["vwT"] = asC(np.asarray(inputs["L0_attn_v_w"]).T)
    s["owT"] = asC(np.asarray(inputs["L0_attn_o_w"]).T)
    s["bias_rows"] = np.stack([
        np.asarray(inputs["L0_attn_q_b"], f32),
        np.asarray(inputs["L0_attn_k_b"], f32),
        np.asarray(inputs["L0_attn_v_b"], f32)], axis=0)
    s["l0bias"] = np.concatenate(
        [_lay_feat(np.asarray(inputs[k], f32)) for k in
         ("L0_attn_q_b", "L0_attn_k_b", "L0_attn_v_b", "L0_attn_o_b")],
        axis=1)
    s["w1"] = asC(inputs["L0_ffn_W1"])
    w2 = np.asarray(inputs["L0_ffn_W2"], f32)
    w2r = w2.reshape(8, 32, 128, 8, 128)
    s["w2lay"] = np.ascontiguousarray(
        w2r.transpose(0, 3, 2, 1, 4).reshape(8, 8, 128, 32 * 128))
    s["b1lay"] = np.concatenate(
        [_lay_feat(np.asarray(inputs["L0_ffn_b1"], f32)[e]) for e in range(8)],
        axis=1)
    s["b2"] = asC(inputs["L0_ffn_b2"])
    s["wr"] = asC(inputs["L0_ffn_Wr"])
    s["br_bc"] = np.broadcast_to(
        np.asarray(inputs["L0_ffn_br"], f32)[None, :], (128, 8)).copy()
    s["wqkv"] = asC(inputs["L1_attn_W_qkv"])
    s["wg"] = asC(inputs["L1_attn_W_g"])
    s["wproj1"] = asC(inputs["L1_attn_W_proj"])
    s["decay"] = np.asarray(inputs["L1_attn_decay"], f32).reshape(1, 8).copy()
    s["wfcT"] = asC(np.asarray(inputs["L1_ffn_W_fc"]).T)
    s["wprojT"] = asC(np.asarray(inputs["L1_ffn_W_proj"]).T)
    s["lnw_lay"] = _lay_feat(np.asarray(inputs["L1_ffn_ln_w"], f32))
    s["lnb_lay"] = _lay_feat(np.asarray(inputs["L1_ffn_ln_b"], f32))
    return s



class Prog:

    def __init__(self, strict="none"):
        self.strict = strict
        self.strict_engs = {"none": (), "vese": ("ve", "se"),
                            "all": ("pe", "ve", "se")}[strict]
        self.semkeys = (["pe", "ve", "se"]
                        + [f"gp{i}" for i in range(NCH_GP)]
                        + [f"sp{i}" for i in range(NCH_SP)])
        self.stream = {k: [] for k in ENGSTEP}
        self.cnt = {k: 0 for k in self.semkeys}
        self._last = {k: None for k in ENGSTEP}
        self._rr = {"gp": 0, "sp": 0}

    @staticmethod
    def _flat(deps):
        out = []
        for d in deps:
            if d is None:
                continue
            if isinstance(d, list):
                out.extend(t for t in d if t is not None)
            else:
                out.append(d)
        return out

    def op(self, eng, fn, deps=(), sig=False):
        assert eng in ENGSTEP and (not sig or eng in ("pe", "ve", "se"))
        toks = self._flat(deps)
        entry = [toks, fn, None]
        self.stream[eng].append(entry)
        self._last[eng] = entry
        if eng in self.strict_engs:
            self.sig(eng)
        if sig:
            return self.sig(eng)
        return None

    def dma(self, eng, fn, deps=()):
        assert eng in ("gp", "sp")
        ch = f"{eng}{self._rr[eng]}"
        self._rr[eng] = (self._rr[eng] + 1) % (NCH_GP if eng == "gp" else NCH_SP)
        toks = self._flat(deps)
        if self.cnt[ch]:
            toks.append((ch, self.cnt[ch]))
        self.cnt[ch] += 16
        entry = [toks, fn, (ch, 16)]
        self.stream[eng].append(entry)
        self._last[eng] = entry
        return (ch, self.cnt[ch])

    def sig(self, eng):
        entry = self._last[eng]
        assert entry is not None, f"no op to sign on {eng}"
        if entry[2] is None:
            self.cnt[eng] += 1
            entry[2] = (eng, 1)
        return (eng, self.cnt[eng])

    def front(self, eng):
        if eng in ("pe", "ve", "se"):
            if self._last[eng] is None:
                return []
            return [self.sig(eng)]
        pre = eng
        return [(k, self.cnt[k]) for k in self.semkeys
                if k.startswith(pre) and self.cnt[k]]

    def barrier(self, engs=("pe", "ve", "se", "gp", "sp")):
        toks = []
        for e in engs:
            toks.extend(self.front(e))
        return toks

    def replay(self, nc, sems):
        hw = {k: {k2: 0 for k2 in self.semkeys} for k in ENGSTEP}
        blockmap = {"pe": "tensor", "ve": "vector", "se": "scalar",
                    "gp": "gpsimd", "sp": "sync"}
        with nc.Block() as block:
            for eng in ENGSTEP:
                entries = self.stream[eng]
                if not entries:
                    continue

                def make_body(eng, entries):
                    def body(e):
                        mywait = hw[eng]
                        mycount = 0
                        for toks, fn, sig in entries:
                            for (tkey, tval) in toks:
                                if tkey == eng:
                                    continue
                                if tval > mywait[tkey]:
                                    e.wait_ge(sems[tkey], tval)
                                    mywait[tkey] = tval
                            if eng in self.strict_engs and mycount:
                                e.wait_ge(sems[eng], mycount)
                            inst = fn(e)
                            if sig is not None:
                                assert inst is not None
                                inst.then_inc(sems[sig[0]], sig[1])
                                mycount += sig[1] if sig[0] == eng else 0
                    return body

                getattr(block, blockmap[eng])(make_body(eng, entries))


class Pool:

    def __init__(self, tensor, cols, name):
        self.t = tensor
        self.cols = cols
        self.off = 0
        self.name = name
        self.marks = []

    def alloc(self, n):
        n = (n + 7) & ~7
        assert self.off + n <= self.cols, \
            f"arena {self.name} overflow: {self.off}+{n} > {self.cols}"
        o = self.off
        self.off += n
        return self.t[:, o:o + n]

    def alloc_rows(self, rows, n):
        n = (n + 7) & ~7
        assert self.off + n <= self.cols
        o = self.off
        self.off += n
        return self.t[0:rows, o:o + n]

    def push(self):
        self.marks.append(self.off)

    def pop(self):
        self.off = self.marks.pop()

    def reset(self):
        self.off = 0


class PsumMgr:

    def __init__(self):
        self.tok = {}

    def gate(self, name):
        return self.tok.get(name)

    def done(self, name, token):
        self.tok[name] = token


class Ctx:
    pass



def CH(t, c, t0=0, n=T, stride=T):
    return t[:, c * stride + t0: c * stride + t0 + n]


def emit_load_x(K):
    p = K.p
    K.fpool.push()
    stage = K.fpool.alloc(2 * 1024)
    slot_tok = [None, None]
    bank_tok = [None, None]
    for i in range(NT):
        s = i % 2
        st = stage[:, s * 1024:(s + 1) * 1024]
        dtok = p.dma("sp", (lambda st, i: lambda e: e.dma_start(
            st, K.x_in[i * 128:(i + 1) * 128, :]))(st, i),
            deps=[slot_tok[s]] + (K.consts_all if i == 0 else []))
        for b in range(2):
            ps = K.psA if b == 0 else K.psB
            for j in range(4):
                c = 4 * b + j
                p.op("pe", (lambda st, c, ps, j: lambda e: e.transpose(
                    ps[:, j * 128:(j + 1) * 128],
                    st[:, c * 128:(c + 1) * 128], K.ident_f32))(st, c, ps, j),
                    deps=[dtok, bank_tok[b]] + (K.consts_all if i == 0 else []))
            ptok = p.sig("pe")
            if b == 1:
                slot_tok[s] = ptok
            dst = K.xres.rearrange("p (c t) -> p c t", t=T)[
                :, 4 * b:4 * b + 4, i * 128:(i + 1) * 128]
            psv = ps.rearrange("p (c t) -> p c t", t=128)
            eng = "ve" if b == 0 else "se"
            if eng == "ve":
                fn = (lambda dst, psv: lambda e: e.tensor_copy(dst, psv))(dst, psv)
            else:
                fn = (lambda dst, psv: lambda e: e.copy(dst, psv))(dst, psv)
            bank_tok[b] = p.op(eng, fn, deps=[ptok], sig=True)
    K.x_ready = p.barrier(("pe", "ve", "se"))
    K.final_deps = K.x_ready
    K.fpool.pop()


def emit_store(K):
    p = K.p
    K.fpool.push()
    obuf = K.fpool.alloc(2 * 1024)
    slot_store_tok = [None, None]
    bank_tok = [None, None]
    pre = list(K.final_deps) if hasattr(K, "final_deps") else []
    stok = None
    for i in range(NT):
        s = i % 2
        ob = obuf[:, s * 1024:(s + 1) * 1024]
        evac_toks = []
        for b in range(2):
            ps = K.psA if b == 0 else K.psB
            for j in range(4):
                c = 4 * b + j
                p.op("pe", (lambda c, i, ps, j: lambda e: e.transpose(
                    ps[:, j * 128:(j + 1) * 128],
                    CH(K.xres, c, i * 128, 128), K.ident_f32))(c, i, ps, j),
                    deps=pre + [bank_tok[b]])
            ptok = p.sig("pe")
            dst = ob[:, b * 512:(b + 1) * 512]
            psv = ps[:, 0:512]
            eng = "ve" if b == 0 else "se"
            if eng == "ve":
                fn = (lambda dst, psv: lambda e: e.tensor_copy(dst, psv))(dst, psv)
            else:
                fn = (lambda dst, psv: lambda e: e.copy(dst, psv))(dst, psv)
            etok = p.op(eng, fn, deps=[ptok, slot_store_tok[s]], sig=True)
            evac_toks.append(etok)
            bank_tok[b] = etok
        pre = []
        stok = p.dma("sp", (lambda ob, i: lambda e: e.dma_start(
            K.out[i * 128:(i + 1) * 128, :], ob))(ob, i),
            deps=evac_toks)
        slot_store_tok[s] = stok




def build_nc(upto="full", strict="vese"):
    nc = bass.Bass("TRN2", target_bir_lowering=False)
    K = Ctx()
    K.nc = nc
    K.p = Prog(strict=strict)
    K.ps = PsumMgr()
    K.upto = upto

    f32 = F32
    dt_ = nc.dram_tensor
    K.x_in = dt_("x_in", [T, D], f32, kind="ExternalInput").ap()
    K.out = dt_("out", [T, D], f32, kind="ExternalOutput").ap()

    ins = {}
    for name, shape, dty in [
        ("ident_bf", [128, 128], BF16), ("ident_f32", [128, 128], f32),
        ("ones_bf", [128, 512], BF16), ("ones_f32", [128, 128], f32),
        ("mask0", [128, 2048], BF16), ("maskc", [128, 2048], BF16),
        ("pos", [128, T], f32),
        ("csq", [D, T], BF16), ("snq", [D, T], BF16),
        ("csk", [D, T], BF16), ("snk", [D, T], BF16),
        ("basis_bf", [8, 1024], BF16),
        ("qwT", [D, D], f32), ("kwT", [D, D], f32),
        ("vwT", [D, D], f32), ("owT", [D, D], f32),
        ("l0bias", [128, 32], f32), ("bias_rows", [3, D], f32),
        ("misc_cols", [128, 8], f32),
        ("w1", [8, D, 4096], f32), ("w2lay", [8, 8, 128, 4096], f32),
        ("b1lay", [128, 256], f32), ("b2", [8, D], f32),
        ("wr", [D, 8], f32), ("br_bc", [128, 8], f32),
        ("wqkv", [D, 4096], f32), ("wg", [D, 2048], f32),
        ("wproj1", [2048, D], f32), ("decay", [1, 8], f32),
        ("wfcT", [D, 4096], f32), ("wprojT", [4096, D], f32),
        ("lnw_lay", [128, 32], f32), ("lnb_lay", [128, 32], f32),
    ]:
        ins[name] = dt_(name, shape, dty, kind="ExternalInput").ap()
    K.ins = ins
    K.wqfc_sc = dt_("wqfc_sc", [D, 4096], BF16, kind="Internal").ap()
    K.wqproj_sc = dt_("wqproj_sc", [8, 128, 32, 128], BF16, kind="Internal").ap()

    with ExitStack() as ctx:
        E = ctx.enter_context
        K.xres = E(nc.sbuf_tensor("xres", [128, NCH * T], f32))
        K.xn = E(nc.sbuf_tensor("xn", [128, NCH * T], BF16))
        K.ident_bf = E(nc.sbuf_tensor("ident_bf_s", [128, 128], BF16))[:]
        K.ident_f32 = E(nc.sbuf_tensor("ident_f32_s", [128, 128], f32))[:]
        K.ones_bf = E(nc.sbuf_tensor("ones_bf_s", [128, 512], BF16))[:]
        K.ones_f32 = E(nc.sbuf_tensor("ones_f32_s", [128, 128], f32))[:]
        K.l0bias = E(nc.sbuf_tensor("l0bias_s", [128, 32], f32))[:]
        brow_t = E(nc.sbuf_tensor("brow_bf", [65, D], BF16))
        K.brow_bf = brow_t[:]
        K.brow_dst = bass.AP(brow_t, 0, [[32 * D, 3], [1, D]])
        K.misc = E(nc.sbuf_tensor("misc_cols_s", [128, 8], f32))[:]
        K.b1lay = E(nc.sbuf_tensor("b1lay_s", [128, 256], f32))[:]
        K.b2bf = E(nc.sbuf_tensor("b2bf_s", [8, 1024], BF16))[:]
        K.wr_s = E(nc.sbuf_tensor("wr_s", [128, 64], f32))[:]
        K.br_s = E(nc.sbuf_tensor("br_s", [128, 8], f32))[:]
        K.basis = E(nc.sbuf_tensor("basis_s", [8, 1024], BF16))[:]
        K.lnw = E(nc.sbuf_tensor("lnw_s", [128, 32], f32))[:]
        K.lnb = E(nc.sbuf_tensor("lnb_s", [128, 32], f32))[:]
        K.decay_s = E(nc.sbuf_tensor("decay_s", [1, 8], f32))[:]
        K.lnd = E(nc.sbuf_tensor("lnd_s", [128, 16], f32))[:]
        K.stat_col = E(nc.sbuf_tensor("stat_col", [128, 64], f32))[:]
        K.smalls = E(nc.sbuf_tensor("smalls", [128, 64], f32))[:]
        K.coef = E(nc.sbuf_tensor("coef_s", [128, 128], f32))[:]
        K.coefT_bf = E(nc.sbuf_tensor("coefT_bf", [8, T], BF16))[:]
        K.s1s0_bf = E(nc.sbuf_tensor("s1s0_bf", [1, 2048], BF16))[:]

        arena_bf = E(nc.sbuf_tensor("arena_bf", [128, ARENA_BF_COLS], BF16))
        arena_f32 = E(nc.sbuf_tensor("arena_f32", [128, ARENA_F32_COLS], f32))
        K.bpool = Pool(arena_bf, ARENA_BF_COLS, "bf")
        K.fpool = Pool(arena_f32, ARENA_F32_COLS, "f32")

        K.psA = E(nc.psum_tensor("psA", [128, 512], f32))[:]
        K.psB = E(nc.psum_tensor("psB", [128, 512], f32))[:]
        K.psC = E(nc.psum_tensor("psC", [128, 512], f32))[:]
        K.psD = E(nc.psum_tensor("psD", [128, 512], f32))[:]
        K.psE = E(nc.psum_tensor("psE", [128, 512], f32))[:]
        K.psF = E(nc.psum_tensor("psF", [128, 512], f32))[:]
        K.psG = E(nc.psum_tensor("psG", [128, 512], f32))[:]
        K.psH = E(nc.psum_tensor("psH", [128, 512], f32))[:]

        sems = {k: E(nc.semaphore(f"sem_{k}")) for k in K.p.semkeys}
        K.sems = sems

        emit_program(K)
        K.p.replay(nc, sems)
    return nc


def emit_consts_load(K):
    p = K.p
    toks = []
    for name, dst in [
        ("ident_bf", K.ident_bf), ("ident_f32", K.ident_f32),
        ("ones_bf", K.ones_bf), ("ones_f32", K.ones_f32),
        ("l0bias", K.l0bias), ("b1lay", K.b1lay),
        ("br_bc", K.br_s), ("basis_bf", K.basis),
        ("lnw_lay", K.lnw), ("lnb_lay", K.lnb), ("decay", K.decay_s),
        ("misc_cols", K.misc),
    ]:
        toks.append(p.dma("sp", (lambda name, dst: lambda e: e.dma_start(
            dst, K.ins[name][:]))(name, dst)))
    src = K.ins["wr"].rearrange("(c p) e -> p c e", p=128)
    dstv = K.wr_s.rearrange("p (c e) -> p c e", e=8)
    toks.append(p.dma("sp", lambda e: e.dma_start(dstv, src)))
    toks.append(p.dma("gp", lambda e: e.dma_start(K.brow_dst, K.ins["bias_rows"][:])))
    b2tok = p.dma("gp", lambda e: e.dma_start(K.b2bf, K.ins["b2"][:]))
    toks.append(b2tok)
    K.consts_ready = toks[-1]
    K.consts_all = toks


def emit_program(K):
    emit_consts_load(K)
    emit_load_x(K)
    stages = ["io", "ln0", "l0attn", "moe", "ln1", "l1attn", "full"]
    lvl = stages.index(K.upto)
    if lvl >= 1:
        emit_ln(K, layer=0)
    if lvl >= 3:
        emit_router(K)
    if lvl >= 2:
        emit_l0_attn(K)
    if lvl >= 3:
        emit_moe(K)
    if lvl >= 4:
        emit_ln(K, layer=1)
    if lvl >= 5:
        emit_l1_attn(K)
    if lvl >= 6:
        emit_l1_ffn(K)
    emit_store(K)
    p = K.p
    p.op("sp", lambda e: e.nop(), deps=p.front("sp"))
    if any(p.cnt[k] for k in p.semkeys if k.startswith("gp")):
        p.op("gp", lambda e: e.nop(), deps=p.front("gp"))



def _kernel_impl(inputs, upto="full"):
    x = np.asarray(inputs["L0_attn_x"], np.float32)
    shared = _prep_shared(inputs)
    nc = build_nc(upto=upto)
    in_maps = []
    for core in range(8):
        m = dict(shared)
        m["x_in"] = np.ascontiguousarray(x[core])
        in_maps.append(m)
    res = run_bass_kernel_spmd(nc, in_maps, core_ids=list(range(8)))
    outs = [np.asarray(r["out"], np.float32) for r in res.results]
    return np.stack(outs, axis=0)


def kernel(**inputs):
    return _kernel_impl(inputs, upto="full")


def emit_ln(K, layer):
    p, ps = K.p, K.ps
    pre = list(K.final_deps)
    K.fpool.push(); K.bpool.push()
    xbf = K.bpool.alloc(8 * 512)
    sq = K.bpool.alloc(2 * 512)
    rows = K.fpool.alloc(4 * 512)
    bcast = K.fpool.alloc(2 * 512)
    tmp = K.fpool.alloc(512)
    mu_row = rows[0:1, 0:512]
    v_row = rows[0:1, 512:1024]
    r_row = rows[0:1, 1024:1536]
    nmr_row = rows[0:1, 1536:2048]
    r_bc = bcast[:, 0:512]
    nmr_bc = bcast[:, 512:1024]
    psMU = K.psC[0:1, 0:512]
    psVAR = K.psD[0:1, 0:512]

    xbf_war = None
    sq_war = [None, None]
    rows_war = None
    bc_war = None
    for g in range(4):
        t0 = 512 * g
        mm_toks = []
        for c in range(8):
            cpt = p.op("ve", (lambda c, t0: lambda e: e.tensor_copy(
                xbf[:, c * 512:(c + 1) * 512], CH(K.xres, c, t0, 512)))(c, t0),
                deps=pre + [xbf_war], sig=True)
            sqt = p.op("se", (lambda c, t0: lambda e: e.activation(
                sq[:, (c % 2) * 512:(c % 2 + 1) * 512],
                CH(K.xres, c, t0, 512), AF.Square))(c, t0),
                deps=pre + [sq_war[c % 2]], sig=True)
            p.op("pe", (lambda c: lambda e: e.matmul(
                psMU, K.ones_bf[:, 0:1], xbf[:, c * 512:(c + 1) * 512],
                start=(c == 0), stop=(c == 7)))(c),
                deps=[cpt] + ([ps.gate("C")] if c == 0 else []))
            p.op("pe", (lambda c: lambda e: e.matmul(
                psVAR, K.ones_bf[:, 0:1], sq[:, (c % 2) * 512:(c % 2 + 1) * 512],
                start=(c == 0), stop=(c == 7)))(c),
                deps=[sqt] + ([ps.gate("D")] if c == 0 else []))
            sq_war[c % 2] = p.sig("pe")
        mmtok = p.sig("pe")
        xbf_war = mmtok
        inv = 1.0 / 1024.0
        t1 = p.op("ve", lambda e: e.tensor_scalar_mul(mu_row, psMU, inv),
                  deps=[mmtok, rows_war], sig=True)
        t2 = p.op("ve", lambda e: e.tensor_scalar_mul(v_row, psVAR, inv),
                  deps=[mmtok], sig=True)
        ps.done("C", t1)
        ps.done("D", t2)
        t3 = p.op("se", lambda e: e.activation(r_row, mu_row, AF.Square),
                  deps=[t1, rows_war], sig=True)
        t4 = p.op("ve", lambda e: e.tensor_sub(v_row, v_row, r_row),
                  deps=[t3], sig=True)
        t5 = p.op("se", lambda e: e.activation(
            r_row, v_row, AF.Sqrt, bias=K.misc[0:1, 0:1]), deps=[t4], sig=True)
        t6 = p.op("ve", lambda e: e.reciprocal(r_row, r_row), deps=[t5], sig=True)
        t7 = p.op("ve", lambda e: e.scalar_tensor_tensor(
            nmr_row, mu_row, -1.0, r_row, ALU.mult, ALU.mult),
            deps=[t6], sig=True)

        if layer == 0:
            trdeps = [t7, ps.gate("A")]
            for j in range(4):
                p.op("pe", (lambda j: lambda e: e.transpose(
                    K.psA[:, j:j + 1], r_row[0:1, j * 128:(j + 1) * 128],
                    K.ident_f32[0:1, 0:1]))(j), deps=trdeps)
                trdeps = []
                p.op("pe", (lambda j: lambda e: e.transpose(
                    K.psA[:, 4 + j:5 + j], nmr_row[0:1, j * 128:(j + 1) * 128],
                    K.ident_f32[0:1, 0:1]))(j))
            ttok = p.sig("pe")
            c1 = p.op("ve", (lambda g: lambda e: e.tensor_copy(
                K.stat_col[:, g * 4:(g + 1) * 4], K.psA[:, 0:4]))(g),
                deps=[ttok], sig=True)
            c2 = p.op("ve", (lambda g: lambda e: e.tensor_copy(
                K.stat_col[:, 16 + g * 4:16 + (g + 1) * 4], K.psA[:, 4:8]))(g),
                deps=[ttok], sig=True)
            ps.done("A", c2)

        p.op("pe", lambda e: e.matmul(
            K.psE[:, 0:512], K.ones_f32[0:1, :], r_row, start=True, stop=True),
            deps=[t7, ps.gate("E")])
        bE = p.sig("pe")
        p.op("pe", lambda e: e.matmul(
            K.psF[:, 0:512], K.ones_f32[0:1, :], nmr_row, start=True, stop=True),
            deps=[t7, ps.gate("F")])
        bF = p.sig("pe")
        rows_war = bF
        e1 = p.op("ve", lambda e: e.tensor_copy(r_bc, K.psE[:, 0:512]),
                  deps=[bE, bc_war], sig=True)
        e2 = p.op("se", lambda e: e.copy(nmr_bc, K.psF[:, 0:512]),
                  deps=[bF, bc_war], sig=True)
        ps.done("E", e1)
        ps.done("F", e2)
        atoks = []
        for c in range(8):
            p.op("ve", (lambda c, t0: lambda e: e.tensor_mul(
                tmp, CH(K.xres, c, t0, 512), r_bc))(c, t0),
                deps=pre + [e1, e2])
            at = p.op("ve", (lambda c, t0: lambda e: e.tensor_add(
                CH(K.xn, c, t0, 512), tmp, nmr_bc))(c, t0), sig=True)
            atoks.append(at)
        bc_war = atoks[-1]
        pre = []
    K.final_deps = p.barrier(("pe", "ve", "se"))
    K.fpool.pop(); K.bpool.pop()
    if K.upto == "ln0":
        for c in range(8):
            p.op("ve", (lambda c: lambda e: e.tensor_copy(
                CH(K.xres, c, 0, T), CH(K.xn, c, 0, T)))(c),
                deps=list(K.final_deps), sig=True)
        K.final_deps = p.barrier(("ve",))


def ld_mini(K, dst2d, src_ap, col0, ncols, deps=()):
    src = src_ap.rearrange("(kc p) f -> p kc f", p=128)[:, :, col0:col0 + ncols]
    dstv = dst2d.rearrange("p (kc f) -> p kc f", f=ncols)
    return K.p.dma("gp", (lambda dstv, src: lambda e: e.dma_start(dstv, src))(dstv, src),
                   deps=deps)


def emit_l0_attn(K):
    p, ps = K.p, K.ps
    pre = list(K.final_deps)
    K.fpool.push(); K.bpool.push()
    mask0 = K.bpool.alloc(2048)
    kring = K.bpool.alloc(8 * 1024)
    vring = K.bpool.alloc(8 * 1024)
    qg = K.bpool.alloc(4 * 1024)
    attnT = K.bpool.alloc(8 * 512)
    wk = K.bpool.alloc(4 * 1024)
    wv = K.bpool.alloc(2 * 2048)
    wtsT = K.bpool.alloc(3 * 512)
    wtsf = K.fpool.alloc(3 * 512)
    wtse = K.fpool.alloc(3 * 512)

    mtok = p.dma("sp", lambda e: e.dma_start(mask0, K.ins["mask0"][:]), deps=pre)

    wkslot_war = [None] * 4
    wvslot_war = [None] * 2
    wk_seq = 0
    attn_pe = [None]
    oproj_pe = None
    wts_war = {"f": [None] * 3, "bf": [None] * 3, "T": [None] * 3}
    smalls_war = [None] * 3
    QCH = {"q": K.ins["qwT"], "k": K.ins["kwT"], "o": K.ins["owT"]}

    for g in range(4):
        t0 = 512 * g
        rc0 = (512 * g) % 1024

        for c in range(8):
            slot = wk_seq % 4
            wk_seq += 1
            sl = wk[:, slot * 1024:(slot + 1) * 1024]
            wt = ld_mini(K, sl, K.ins["kwT"], c * 128, 128,
                         deps=[wkslot_war[slot]] + (pre if g == 0 else []))
            bank = "C" if c % 2 == 0 else "D"
            psb = K.psC if c % 2 == 0 else K.psD
            dep0 = [wt, ps.gate(bank)] + (pre if g == 0 else [])
            for kc in range(8):
                p.op("pe", (lambda sl, kc, c, t0, psb: lambda e: e.matmul(
                    psb[:, 0:512], sl[:, kc * 128:(kc + 1) * 128],
                    CH(K.xn, kc, t0, 512), start=(kc == 0), stop=False))(
                        sl, kc, c, t0, psb),
                    deps=dep0 if kc == 0 else [])
            p.op("pe", (lambda c, psb: lambda e: e.matmul(
                psb[:, 0:512], K.brow_bf[32:33, c * 128:(c + 1) * 128],
                K.ones_bf[32:33, 0:512], start=False, stop=True))(c, psb))
            mmtok = p.sig("pe")
            wkslot_war[slot] = mmtok
            eng = "ve" if c % 2 == 0 else "se"
            dst = kring[:, c * 1024 + rc0: c * 1024 + rc0 + 512]
            if eng == "ve":
                fn = (lambda dst, psb: lambda e: e.tensor_copy(dst, psb[:, 0:512]))(dst, psb)
            else:
                fn = (lambda dst, psb: lambda e: e.copy(dst, psb[:, 0:512]))(dst, psb)
            et = p.op(eng, fn, deps=[mmtok, attn_pe[0]], sig=True)
            ps.done(bank, et)

        for u in range(4):
            su = u % 2
            sl = wv[:, su * 2048:(su + 1) * 2048]
            wt = ld_mini(K, sl, K.ins["vwT"], u * 256, 256,
                         deps=[wvslot_war[su]] + (pre if g == 0 else []))
            for i in range(4):
                gi = 4 * g + i
                s = gi % 8
                bank = "E" if (u * 4 + i) % 2 == 0 else "F"
                psb = K.psE if bank == "E" else K.psF
                dep0 = [wt, ps.gate(bank)]
                for kc in range(8):
                    p.op("pe", (lambda sl, kc, gi, psb: lambda e: e.matmul(
                        psb[:, 0:256], CH(K.xn, kc, gi * 128, 128),
                        sl[:, kc * 256:(kc + 1) * 256],
                        start=(kc == 0), stop=False))(sl, kc, gi, psb),
                        deps=dep0 if kc == 0 else [])
                p.op("pe", (lambda u, psb: lambda e: e.matmul(
                    psb[:, 0:256], K.ones_bf[64:65, 0:128],
                    K.brow_bf[64:65, u * 256:(u + 1) * 256],
                    start=False, stop=True))(u, psb))
                mmtok = p.sig("pe")
                if i == 3:
                    wvslot_war[su] = mmtok
                eng = "ve" if i % 2 == 0 else "se"
                dst = vring[:, s * 1024 + u * 256: s * 1024 + u * 256 + 256]
                if eng == "ve":
                    fn = (lambda dst, psb: lambda e: e.tensor_copy(dst, psb[:, 0:256]))(dst, psb)
                else:
                    fn = (lambda dst, psb: lambda e: e.copy(dst, psb[:, 0:256]))(dst, psb)
                et = p.op(eng, fn, deps=[mmtok, attn_pe[0]], sig=True)
                ps.done(bank, et)

        for c in range(8):
            slot = wk_seq % 4
            wk_seq += 1
            sl = wk[:, slot * 1024:(slot + 1) * 1024]
            wt = ld_mini(K, sl, K.ins["qwT"], c * 128, 128, deps=[wkslot_war[slot]])
            bank = "C" if c % 2 == 0 else "D"
            psb = K.psC if c % 2 == 0 else K.psD
            for i in range(4):
                gi = 4 * g + i
                reg = psb[:, i * 128:(i + 1) * 128]
                dep0 = [wt, ps.gate(bank)] if i == 0 else []
                for kc in range(8):
                    p.op("pe", (lambda sl, kc, gi, reg: lambda e: e.matmul(
                        reg, sl[:, kc * 128:(kc + 1) * 128],
                        CH(K.xn, kc, gi * 128, 128),
                        start=(kc == 0), stop=False))(sl, kc, gi, reg),
                        deps=dep0)
                p.op("pe", (lambda c, reg: lambda e: e.matmul(
                    reg, K.brow_bf[0:1, c * 128:(c + 1) * 128],
                    K.ones_bf[0:1, 0:128], start=False, stop=True))(c, reg))
            mmtok = p.sig("pe")
            wkslot_war[slot] = mmtok
            dstv = qg.rearrange("p (i f) -> p i f", f=1024)[:, :, c * 128:(c + 1) * 128]
            psv = psb.rearrange("p (i f) -> p i f", f=128)
            eng = "ve" if c % 2 == 0 else "se"
            if eng == "ve":
                fn = (lambda dstv, psv: lambda e: e.tensor_copy(dstv, psv))(dstv, psv)
            else:
                fn = (lambda dstv, psv: lambda e: e.copy(dstv, psv))(dstv, psv)
            et = p.op(eng, fn, deps=[mmtok, attn_pe[0]], sig=True)
            ps.done(bank, et)

        filldone = p.barrier(("ve", "se"))
        new_attn_pe = None
        seq = [(i, h) for i in range(4) for h in range(16)]
        st8 = {}

        def emitA(it):
            i, h = seq[it]
            qi = 4 * g + i
            b = qi // 2
            parity = qi % 2
            mv = parity + 2 * (1 if b == 0 else 0)
            msl = mask0[:, mv * 512:(mv + 1) * 512]
            rc = (256 * b - 256) % 1024
            sbank = ("C", "D", "A")[it % 3]
            psS = (K.psC, K.psD, K.psA)[it % 3]
            ch = h // 2
            r0 = 64 * (h % 2)
            qsl = qg[r0:r0 + 64, i * 1024 + ch * 128: i * 1024 + (ch + 1) * 128]
            krow = lambda a, b_: kring[r0:r0 + 64, ch * 1024 + a: ch * 1024 + b_]
            sdep = [mtok, ps.gate(sbank)] + (filldone if it == 0 else [])
            if b == 0:
                p.op("pe", (lambda qsl, psS, k1: lambda e: e.matmul(
                    psS[:, 256:512], qsl, k1, start=True, stop=True))(
                        qsl, psS, krow(0, 256)), deps=sdep)
                stok = p.sig("pe")
                stok = p.op("ve", (lambda psS: lambda e: e.memset(
                    psS[:, 0:256], 0.0))(psS), deps=[stok], sig=True)
            elif rc == 768:
                p.op("pe", (lambda qsl, psS, k1: lambda e: e.matmul(
                    psS[:, 0:256], qsl, k1, start=True, stop=True))(
                        qsl, psS, krow(768, 1024)), deps=sdep)
                p.op("pe", (lambda qsl, psS, k2: lambda e: e.matmul(
                    psS[:, 256:512], qsl, k2, start=True, stop=True))(
                        qsl, psS, krow(0, 256)))
                stok = p.sig("pe")
            else:
                p.op("pe", (lambda qsl, psS, k1: lambda e: e.matmul(
                    psS[:, 0:512], qsl, k1, start=True, stop=True))(
                        qsl, psS, krow(rc, rc + 512)), deps=sdep)
                stok = p.sig("pe")
            wf = wtsf[:, (it % 3) * 512:(it % 3 + 1) * 512]
            wb = wtse[:, (it % 3) * 512:(it % 3 + 1) * 512]
            wT = wtsT[:, (it % 3) * 512:(it % 3 + 1) * 512]
            t1 = p.op("ve", (lambda wf, psS, msl: lambda e: e.scalar_tensor_tensor(
                wf, psS[:, 0:512], 0.125, msl, ALU.mult, ALU.add))(wf, psS, msl),
                deps=[stok, wts_war["f"][it % 3]], sig=True)
            ps.done(sbank, t1)
            scol = K.smalls[:, (it % 3) * 4 + 1:(it % 3) * 4 + 2]
            rcol = K.smalls[:, (it % 3) * 4 + 2:(it % 3) * 4 + 3]
            t3 = p.op("se", (lambda wb, wf, scol: lambda e: e.activation(
                wb, wf, AF.Exp, bias=K.misc[:, 2:3], accum_out=scol))(wb, wf, scol),
                deps=[t1, wts_war["bf"][it % 3], smalls_war[it % 3]], sig=True)
            wts_war["f"][it % 3] = t3
            st8[it] = (wf, wb, wT, scol, rcol, t3)

        def emitRest(it):
            nonlocal new_attn_pe
            i, h = seq[it]
            qi = 4 * g + i
            b = qi // 2
            ch = h // 2
            r0 = 64 * (h % 2)
            wf, wb, wT, scol, rcol, t3 = st8.pop(it)
            t4 = p.op("ve", (lambda rcol, scol: lambda e: e.reciprocal(
                rcol, scol))(rcol, scol), deps=[t3], sig=True)
            smalls_war[it % 3] = t4
            t5 = p.op("ve", (lambda wb, rcol: lambda e: e.tensor_single_scalar(
                wb, wb, rcol, ALU.mult))(wb, rcol), sig=True)
            valid = [2, 3] if b == 0 else [0, 1, 2, 3]
            tbank = ("G", "H", "B")[it % 3]
            psT = (K.psG, K.psH, K.psB)[it % 3]
            tdep = [t5, ps.gate(tbank)]
            for kc in valid:
                p.op("pe", (lambda psT, wb, kc: lambda e: e.transpose(
                    psT[:, kc * 128:(kc + 1) * 128],
                    wb[:, kc * 128:(kc + 1) * 128], K.ident_f32))(psT, wb, kc),
                    deps=tdep)
                tdep = []
            ttok = p.sig("pe")
            wts_war["bf"][it % 3] = ttok
            lo = valid[0] * 128
            hi = (valid[-1] + 1) * 128
            eng = "ve" if it % 2 == 0 else "se"
            if eng == "ve":
                fn = (lambda wT, psT, lo, hi: lambda e: e.tensor_copy(
                    wT[:, lo:hi], psT[:, lo:hi]))(wT, psT, lo, hi)
            else:
                fn = (lambda wT, psT, lo, hi: lambda e: e.copy(
                    wT[:, lo:hi], psT[:, lo:hi]))(wT, psT, lo, hi)
            t6 = p.op(eng, fn, deps=[ttok, wts_war["T"][it % 3]], sig=True)
            ps.done(tbank, t6)
            pbank = "E" if (h // 2) % 2 == 0 else "F"
            psP = K.psE if (h // 2) % 2 == 0 else K.psF
            avdep = [t6] + ([ps.gate(pbank)] if h % 2 == 0 else [])
            for kidx, kc in enumerate(valid):
                gkc = 2 * b - 2 + kc
                s = gkc % 8
                vsl = vring[:, s * 1024 + h * 64: s * 1024 + h * 64 + 64]
                p.op("pe", (lambda psP, r0, vsl, wT, kc, st, sp: lambda e: e.matmul(
                    psP[r0:r0 + 64, 0:128], vsl, wT[:, kc * 128:(kc + 1) * 128],
                    start=st, stop=sp))(
                        psP, r0, vsl, wT, kc, kc == valid[0], kc == valid[-1]),
                    deps=avdep if kidx == 0 else [])
            avtok = p.sig("pe")
            wts_war["T"][it % 3] = avtok
            new_attn_pe = avtok
            if h % 2 == 1:
                dst = attnT[:, ch * 512 + i * 128: ch * 512 + (i + 1) * 128]
                eng = "ve" if ch % 2 == 0 else "se"
                if eng == "ve":
                    fn = (lambda dst, psP: lambda e: e.tensor_copy(
                        dst, psP[:, 0:128]))(dst, psP)
                else:
                    fn = (lambda dst, psP: lambda e: e.copy(
                        dst, psP[:, 0:128]))(dst, psP)
                pv = p.op(eng, fn, deps=[avtok, oproj_pe], sig=True)
                ps.done(pbank, pv)

        emitA(0)
        emitA(1)
        for it in range(len(seq)):
            if it + 2 < len(seq):
                emitA(it + 2)
            emitRest(it)
        attn_pe[0] = new_attn_pe
        attn_evacs = p.barrier(("ve", "se"))

        for c in range(8):
            slot = wk_seq % 4
            wk_seq += 1
            sl = wk[:, slot * 1024:(slot + 1) * 1024]
            wt = ld_mini(K, sl, K.ins["owT"], c * 128, 128, deps=[wkslot_war[slot]])
            bank = "C" if c % 2 == 0 else "D"
            psb = K.psC if c % 2 == 0 else K.psD
            dep0 = [wt, ps.gate(bank)] + (attn_evacs if c == 0 else [])
            for kc in range(8):
                p.op("pe", (lambda sl, kc, psb, attnT: lambda e: e.matmul(
                    psb[:, 0:512], sl[:, kc * 128:(kc + 1) * 128],
                    attnT[:, kc * 512:(kc + 1) * 512],
                    start=(kc == 0), stop=(kc == 7)))(sl, kc, psb, attnT),
                    deps=dep0 if kc == 0 else [])
            mmtok = p.sig("pe")
            wkslot_war[slot] = mmtok
            ob = K.l0bias[:, 24 + c:25 + c]
            et = p.op("ve", (lambda c, t0, psb, ob: lambda e: e.scalar_tensor_tensor(
                CH(K.xres, c, t0, 512), psb[:, 0:512], ob,
                CH(K.xres, c, t0, 512), ALU.add, ALU.add))(c, t0, psb, ob),
                deps=[mmtok], sig=True)
            ps.done(bank, et)
        oproj_pe = p.sig("pe")

    K.final_deps = p.barrier(("pe", "ve", "se"))
    K.fpool.pop(); K.bpool.pop()




def emit_router(K):
    p, ps = K.p, K.ps
    pre = list(K.final_deps)
    K.fpool.push()
    lg = K.fpool.alloc(8)
    lg2 = K.fpool.alloc(8)
    s_bc = K.fpool.alloc(8)
    dep0 = pre + [ps.gate("C")]
    for c in range(8):
        p.op("pe", (lambda c: lambda e: e.matmul(
            K.psC[0:1, 0:8], K.ones_f32[:, 0:1], K.wr_s[:, c * 8:(c + 1) * 8],
            start=(c == 0), stop=(c == 7)))(c), deps=dep0 if c == 0 else [])
    t = p.sig("pe")
    sv = p.op("ve", lambda e: e.tensor_copy(K.smalls[0:1, 32:40], K.psC[0:1, 0:8]),
              deps=[t], sig=True)
    ps.done("C", sv)
    p.op("pe", lambda e: e.matmul(K.psD[:, 0:8], K.ones_f32[0:1, :],
                                  K.smalls[0:1, 32:40], start=True, stop=True),
         deps=[sv, ps.gate("D")])
    t = p.sig("pe")
    sb = p.op("ve", lambda e: e.tensor_copy(s_bc, K.psD[:, 0:8]), deps=[t], sig=True)
    ps.done("D", sb)
    for i in range(16):
        dep0 = [ps.gate("C")]
        for c in range(8):
            p.op("pe", (lambda c, i: lambda e: e.matmul(
                K.psC[:, 0:8], CH(K.xres, c, i * 128, 128),
                K.wr_s[:, c * 8:(c + 1) * 8],
                start=(c == 0), stop=(c == 7)))(c, i), deps=dep0 if c == 0 else [])
        mmt = p.sig("pe")
        t1 = p.op("ve", (lambda i: lambda e: e.tensor_single_scalar(
            lg, K.psC[:, 0:8], K.stat_col[:, i:i + 1], ALU.mult))(i),
            deps=[mmt, sb], sig=True)
        ps.done("C", t1)
        p.op("ve", (lambda i: lambda e: e.scalar_tensor_tensor(
            lg2, s_bc, K.stat_col[:, 16 + i:17 + i], lg, ALU.mult, ALU.add))(i))
        p.op("ve", lambda e: e.tensor_add(lg, lg2, K.br_s))
        m = K.smalls[:, 8:9]
        nm = K.smalls[:, 9:10]
        ss = K.smalls[:, 10:11]
        rs = K.smalls[:, 11:12]
        p.op("ve", lambda e: e.reduce_max(m, lg, axis=AX.X))
        t5 = p.op("ve", lambda e: e.tensor_single_scalar(lg2, lg, m, ALU.is_ge),
                  sig=True)
        t6 = p.op("ve", lambda e: e.tensor_scalar_mul(nm, m, -1.0), sig=True)
        t7 = p.op("se", lambda e: e.activation(lg, lg, AF.Exp, bias=nm,
                                               accum_out=ss),
                  deps=[t5, t6], sig=True)
        t8 = p.op("ve", lambda e: e.reciprocal(rs, ss), deps=[t7], sig=True)
        t9 = p.op("ve", (lambda i: lambda e: e.tensor_single_scalar(
            K.coef[:, 8 * i:8 * (i + 1)], lg2, rs, ALU.mult))(i), sig=True)
        p.op("pe", (lambda i: lambda e: e.transpose(
            K.psA[0:8, 0:128], K.coef[:, 8 * i:8 * (i + 1)], K.ident_f32))(i),
            deps=[t9, ps.gate("A")])
        tt = p.sig("pe")
        tc = p.op("ve", (lambda i: lambda e: e.tensor_copy(
            K.coefT_bf[0:8, i * 128:(i + 1) * 128], K.psA[0:8, 0:128]))(i),
            deps=[tt], sig=True)
        ps.done("A", tc)
    K.final_deps = p.barrier(("pe", "ve", "se"))
    K.fpool.pop()


def emit_moe(K):
    p, ps = K.p, K.ps
    pre = list(K.final_deps)
    K.fpool.push(); K.bpool.push()
    h_all = K.bpool.alloc(32 * 512)
    w1b = K.bpool.alloc(2 * 4096)
    w2b = K.bpool.alloc(2 * 4096)
    cbc = K.fpool.alloc(2 * 512)
    tmp = K.fpool.alloc(512)
    w1war = [None, None]
    w2war = [None, None]
    cbcwar = [None, None]
    h_war = None
    u1 = 0
    u2 = 0
    first = pre
    for g in range(4):
        t0 = g * 512
        for e8 in range(8):
            csl = (g * 8 + e8) % 2
            cb = cbc[:, csl * 512:(csl + 1) * 512]
            p.op("pe", (lambda e8, t0: lambda e: e.matmul(
                K.psG[:, 0:512], K.basis[:, e8 * 128:(e8 + 1) * 128],
                K.coefT_bf[0:8, t0:t0 + 512], start=True, stop=True))(e8, t0),
                deps=first + [ps.gate("G")])
            first = []
            bt = p.sig("pe")
            cbt = p.op("ve", (lambda cb: lambda e: e.tensor_copy(
                cb, K.psG[:, 0:512]))(cb), deps=[bt, cbcwar[csl]], sig=True)
            ps.done("G", cbt)
            for fb in range(8):
                slot = u1 % 2
                u1 += 1
                w1u = w1b[:, slot * 4096:(slot + 1) * 4096]
                src = K.ins["w1"][e8].rearrange(
                    "(kc p) f -> p kc f", p=128)[:, :, fb * 512:(fb + 1) * 512]
                dstv = w1u.rearrange("p (kc f) -> p kc f", f=512)
                wt = p.dma("gp", (lambda dstv, src: lambda e: e.dma_start(
                    dstv, src))(dstv, src), deps=[w1war[slot]] + pre)
                for fc in range(4):
                    hidx = fb * 4 + fc
                    bank = "C" if hidx % 2 == 0 else "D"
                    psb = K.psC if hidx % 2 == 0 else K.psD
                    dep0 = [wt, ps.gate(bank)]
                    for kc in range(8):
                        p.op("pe", (lambda w1u, kc, fc, t0, psb: lambda e: e.matmul(
                            psb[:, 0:512], w1u[:, kc * 512 + fc * 128: kc * 512 + (fc + 1) * 128],
                            CH(K.xn, kc, t0, 512),
                            start=(kc == 0), stop=(kc == 7)))(w1u, kc, fc, t0, psb),
                            deps=dep0 if kc == 0 else [])
                    mmtok = p.sig("pe")
                    if fc == 3:
                        w1war[slot] = mmtok
                    et = p.op("se", (lambda hidx, e8, psb: lambda e: e.activation(
                        h_all[:, hidx * 512:(hidx + 1) * 512], psb[:, 0:512],
                        AF.Gelu, bias=K.b1lay[:, e8 * 32 + hidx:e8 * 32 + hidx + 1]))(
                            hidx, e8, psb),
                        deps=[mmtok, h_war], sig=True)
                    ps.done(bank, et)
            hdone = p.barrier(("se",))
            for dc in range(8):
                slot = u2 % 2
                u2 += 1
                w2u = w2b[:, slot * 4096:(slot + 1) * 4096]
                src = K.ins["w2lay"][e8][dc]
                wt = p.dma("gp", (lambda w2u, src: lambda e: e.dma_start(
                    w2u, src))(w2u, src), deps=[w2war[slot]] + pre)
                bank = "E" if dc % 2 == 0 else "F"
                psb = K.psE if dc % 2 == 0 else K.psF
                dep0 = [wt, ps.gate(bank)] + hdone
                for fk in range(32):
                    p.op("pe", (lambda w2u, fk, psb: lambda e: e.matmul(
                        psb[:, 0:512], w2u[:, fk * 128:(fk + 1) * 128],
                        h_all[:, fk * 512:(fk + 1) * 512],
                        start=(fk == 0), stop=(fk == 31)))(w2u, fk, psb),
                        deps=dep0 if fk == 0 else [])
                mmtok = p.sig("pe")
                w2war[slot] = mmtok
                tm = p.op("ve", (lambda psb, cb: lambda e: e.tensor_mul(
                    tmp, psb[:, 0:512], cb))(psb, cb), deps=[mmtok, cbt], sig=True)
                ta = p.op("ve", (lambda dc, t0: lambda e: e.tensor_add(
                    CH(K.xres, dc, t0, 512), CH(K.xres, dc, t0, 512), tmp))(dc, t0),
                    sig=True)
                ps.done(bank, ta)
                if dc == 7:
                    cbcwar[csl] = tm
            h_war = p.sig("pe")
        for dc in range(8):
            bank = "A" if dc % 2 == 0 else "B"
            psb = K.psA if dc % 2 == 0 else K.psB
            p.op("pe", (lambda dc, t0, psb: lambda e: e.matmul(
                psb[:, 0:512], K.b2bf[0:8, dc * 128:(dc + 1) * 128],
                K.coefT_bf[0:8, t0:t0 + 512], start=True, stop=True))(dc, t0, psb),
                deps=[ps.gate(bank)])
            mmtok = p.sig("pe")
            ta = p.op("ve", (lambda dc, t0, psb: lambda e: e.tensor_add(
                CH(K.xres, dc, t0, 512), CH(K.xres, dc, t0, 512),
                psb[:, 0:512]))(dc, t0, psb), deps=[mmtok], sig=True)
            ps.done(bank, ta)
    K.final_deps = p.barrier(("pe", "ve", "se"))
    K.fpool.pop(); K.bpool.pop()




def emit_l1_attn(K):
    p, ps = K.p, K.ps
    pre = list(K.final_deps)
    K.fpool.push(); K.bpool.push()
    maskc = K.bpool.alloc(2048)
    v_h = K.bpool.alloc(16 * 256)
    kd_h = K.bpool.alloc(2048)
    qd = K.bpool.alloc(2 * 512)
    wch = K.bpool.alloc(2 * 512)
    ret = K.bpool.alloc(2 * 1024)
    gate = K.bpool.alloc(2 * 1024)
    wqu = K.bpool.alloc(4 * 2048)
    xps = K.bpool.alloc(4 * 512)
    sqb = K.bpool.alloc(2 * 512)
    kr = K.bpool.alloc(1024).bitcast(F32)
    ks = K.bpool.alloc(1024).bitcast(F32)
    rt1 = K.bpool.alloc(1024).bitcast(F32)
    possl = K.fpool.alloc(2 * 512)
    dkt = K.fpool.alloc(2 * 512)
    rows = K.fpool.alloc(4 * 512)
    bcast = K.fpool.alloc(2 * 512)
    mu_row = rows[0:1, 0:512]
    v_row = rows[0:1, 512:1024]
    r_row = rows[0:1, 1024:1536]
    nmr_row = rows[0:1, 1536:2048]
    r_bc = bcast[:, 0:512]
    nmr_bc = bcast[:, 512:1024]
    SWAP = []
    for i in range(16):
        SWAP += [2 * i + 1, 2 * i]
    CINV = 1.0 / np.sqrt(128.0)

    mtok = p.dma("sp", lambda e: e.dma_start(maskc, K.ins["maskc"][:]), deps=pre)

    lr = K.smalls[0:1, 48:56]
    t = p.op("se", lambda e: e.activation(lr, K.decay_s, AF.Ln),
             deps=pre, sig=True)
    p.op("pe", lambda e: e.matmul(K.psA[:, 0:8], K.ones_f32[0:1, :], lr,
                                  start=True, stop=True),
         deps=[t, ps.gate("A")])
    t = p.sig("pe")
    t1 = p.op("ve", lambda e: e.tensor_copy(K.lnd[:, 0:8], K.psA[:, 0:8]),
              deps=[t], sig=True)
    ps.done("A", t1)
    t2 = p.op("ve", lambda e: e.tensor_scalar_mul(K.lnd[:, 8:16],
                                                  K.lnd[:, 0:8], -1.0), sig=True)
    lnd_done = t2

    useq = [0]
    uwar = [None, None, None, None]
    xwar = [None, None]
    poswar = [None, None]
    dktwar = [None, None]
    retwar = [None, None]
    gatewar = [None, None]
    sq_war = None
    qdwar = [None, None]
    wchwar = [None, None]
    kdwar = None
    vwar = None

    def wunit(cols0, ncols, deps):
        slot = useq[0] % 4
        useq[0] += 1
        sl = wqu[:, slot * 2048: slot * 2048 + 8 * ncols]
        src = K.ins["wqkv"].rearrange("(kc p) f -> p kc f", p=128)[
            :, :, cols0:cols0 + ncols]
        dstv = sl.rearrange("p (kc f) -> p kc f", f=ncols)
        tok = p.dma("gp", (lambda dstv, src: lambda e: e.dma_start(dstv, src))(
            dstv, src), deps=[uwar[slot]] + deps)
        return slot, sl, tok

    def wunit_from(name, rows0, nrows, cols0, ncols, deps):
        slot = useq[0] % 4
        useq[0] += 1
        sl = wqu[:, slot * 2048: slot * 2048 + (nrows // 128) * ncols]
        src = K.ins[name][rows0:rows0 + nrows].rearrange(
            "(kc p) f -> p kc f", p=128)[:, :, cols0:cols0 + ncols]
        dstv = sl.rearrange("p (kc f) -> p kc f", f=ncols)
        tok = p.dma("gp", (lambda dstv, src: lambda e: e.dma_start(dstv, src))(
            dstv, src), deps=[uwar[slot]] + deps)
        return slot, sl, tok

    def rotary_decay(g, h, raw_bank, psb, dst, qk, dep0, dstwar=None):
        sl = g % 2
        cs_s = xps[:, sl * 1024: sl * 1024 + 512]
        sn_s = xps[:, sl * 1024 + 512: sl * 1024 + 1024]
        csn = "csq" if qk == "q" else "csk"
        snn = "snq" if qk == "q" else "snk"
        ct = p.dma("sp", (lambda cs_s, csn, h, g: lambda e: e.dma_start(
            cs_s, K.ins[csn][128 * h:128 * (h + 1), 512 * g:512 * (g + 1)]))(
                cs_s, csn, h, g), deps=[xwar[sl]])
        st_ = p.dma("sp", (lambda sn_s, snn, h, g: lambda e: e.dma_start(
            sn_s, K.ins[snn][128 * h:128 * (h + 1), 512 * g:512 * (g + 1)]))(
                sn_s, snn, h, g), deps=[xwar[sl]])
        pt = p.dma("sp", (lambda g: lambda e: e.dma_start(
            possl[:, (g % 2) * 512:(g % 2 + 1) * 512],
            K.ins["pos"][:, 512 * g:512 * (g + 1)]))(g), deps=[poswar[g % 2]])
        pos_s = possl[:, (g % 2) * 512:(g % 2 + 1) * 512]
        e1 = p.op("ve", (lambda psb: lambda e: e.tensor_copy(kr, psb[:, 0:512]))(psb),
                  deps=dep0, sig=True)
        ps.done(raw_bank, e1)
        p.op("ve", lambda e: e.stream_shuffle(ks, kr, SWAP))
        p.op("ve", (lambda cs_s: lambda e: e.tensor_mul(rt1, kr, cs_s))(cs_s),
             deps=[ct])
        p.op("ve", (lambda sn_s: lambda e: e.tensor_mul(ks, ks, sn_s))(sn_s),
             deps=[st_])
        s1 = p.op("ve", lambda e: e.tensor_add(rt1, rt1, ks), sig=True)
        lcol = K.lnd[:, h:h + 1] if qk == "q" else K.lnd[:, 8 + h:9 + h]
        dk_s = dkt[:, (g % 2) * 512:(g % 2 + 1) * 512]
        d1 = p.op("se", (lambda dk_s, pos_s, lcol: lambda e: e.activation(
            dk_s, pos_s, AF.Exp, scale=lcol))(dk_s, pos_s, lcol),
            deps=[pt, lnd_done, dktwar[g % 2]], sig=True)
        poswar[g % 2] = d1
        o1 = p.op("ve", (lambda dst, dk_s: lambda e: e.tensor_mul(
            dst, rt1, dk_s))(dst, dk_s), deps=[s1, d1, dstwar], sig=True)
        dktwar[g % 2] = o1
        xwar[sl] = o1
        return o1

    for h in range(8):
        slot, ku, ktok = wunit(1024 + 128 * h, 128, pre if h == 0 else [])
        kd_toks = []
        for g in range(4):
            bank = "C" if g % 2 == 0 else "D"
            psb = K.psC if g % 2 == 0 else K.psD
            dep0 = [ktok, ps.gate(bank), kdwar]
            for kc in range(8):
                p.op("pe", (lambda ku, kc, g, psb: lambda e: e.matmul(
                    psb[:, 0:512], ku[:, kc * 128:(kc + 1) * 128],
                    CH(K.xn, kc, g * 512, 512),
                    start=(kc == 0), stop=(kc == 7)))(ku, kc, g, psb),
                    deps=dep0 if kc == 0 else [])
            mmtok = p.sig("pe")
            if g == 3:
                uwar[slot] = mmtok
            kd_toks.append(rotary_decay(
                g, h, bank, psb, kd_h[:, g * 512:(g + 1) * 512], "k", [mmtok],
                dstwar=kdwar))
        slot, vu, vtok = wunit(2048 + 256 * h, 256, [])
        v_toks = []
        for gi in range(16):
            bank = "E" if gi % 2 == 0 else "F"
            psb = K.psE if gi % 2 == 0 else K.psF
            dep0 = [vtok, ps.gate(bank), vwar]
            for kc in range(8):
                p.op("pe", (lambda vu, kc, gi, psb: lambda e: e.matmul(
                    psb[:, 0:256], CH(K.xn, kc, gi * 128, 128),
                    vu[:, kc * 256:(kc + 1) * 256],
                    start=(kc == 0), stop=(kc == 7)))(vu, kc, gi, psb),
                    deps=dep0 if kc == 0 else [])
            mmtok = p.sig("pe")
            if gi == 15:
                uwar[slot] = mmtok
            eng = "ve" if gi % 2 == 0 else "se"
            dst = v_h[:, gi * 256:(gi + 1) * 256]
            if eng == "ve":
                fn = (lambda dst, psb: lambda e: e.tensor_copy(dst, psb[:, 0:256]))(dst, psb)
            else:
                fn = (lambda dst, psb: lambda e: e.copy(dst, psb[:, 0:256]))(dst, psb)
            et = p.op(eng, fn, deps=[mmtok], sig=True)
            ps.done(bank, et)
            v_toks.append(et)
        sqg, qu, qtok = wunit(128 * h, 128, [])
        sgg, gu, gtok = wunit_from("wg", 0, 1024, 256 * h, 256, [])
        spg, pu, ptok = wunit_from("wproj1", 256 * h, 256, 0, 1024, [])
        q_used = None
        g_used = None
        p_used = None
        for g in range(4):
            t0 = 512 * g
            bank = "C" if g % 2 == 0 else "D"
            psb = K.psC if g % 2 == 0 else K.psD
            dep0 = [qtok, ps.gate(bank), qdwar[g % 2]]
            for kc in range(8):
                p.op("pe", (lambda qu, kc, t0, psb: lambda e: e.matmul(
                    psb[:, 0:512], qu[:, kc * 128:(kc + 1) * 128],
                    CH(K.xn, kc, t0, 512),
                    start=(kc == 0), stop=(kc == 7)))(qu, kc, t0, psb),
                    deps=dep0 if kc == 0 else [])
            mmtok = p.sig("pe")
            q_used = mmtok
            qsl = qd[:, (g % 2) * 512:(g % 2 + 1) * 512]
            qdone = rotary_decay(g, h, bank, psb, qsl, "q", [mmtok],
                                 dstwar=qdwar[g % 2])
            nkc = 4 * g + 4
            avdep0 = [ps.gate("G")]
            avdep1 = [ps.gate("H")]
            av_last = None
            ets = [None] * nkc
            def emit_scores(kc):
                bank = "C" if kc % 2 == 0 else "D"
                psb = K.psC if kc % 2 == 0 else K.psD
                p.op("pe", (lambda kc, qsl, psb: lambda e: e.matmul(
                    psb[:, 0:512], kd_h[:, kc * 128:(kc + 1) * 128], qsl,
                    start=True, stop=True))(kc, qsl, psb),
                    deps=[qdone, kd_toks[kc // 4], ps.gate(bank)])
                smt = p.sig("pe")
                wsl = wch[:, (kc % 2) * 512:(kc % 2 + 1) * 512]
                d = kc - 4 * g
                if d >= 0:
                    et = p.op("ve", (lambda wsl, psb, d: lambda e: e.scalar_tensor_tensor(
                        wsl, psb[:, 0:512], CINV,
                        maskc[:, d * 512:(d + 1) * 512], ALU.mult, ALU.mult))(
                            wsl, psb, d), deps=[smt, mtok, wchwar[kc % 2]], sig=True)
                else:
                    et = p.op("ve", (lambda wsl, psb: lambda e: e.tensor_single_scalar(
                        wsl, psb[:, 0:512], CINV, ALU.mult))(wsl, psb),
                        deps=[smt, wchwar[kc % 2]], sig=True)
                ps.done(bank, et)
                ets[kc] = (et, wsl)
            emit_scores(0)
            for kc in range(nkc):
                if kc + 1 < nkc:
                    emit_scores(kc + 1)
                et, wsl = ets[kc]
                for dvc in range(2):
                    psav = K.psG if dvc == 0 else K.psH
                    vsl = v_h[:, kc * 256 + dvc * 128: kc * 256 + (dvc + 1) * 128]
                    dep = [et, v_toks[kc]] + (avdep0 if dvc == 0 else avdep1)
                    p.op("pe", (lambda psav, vsl, wsl, st, sp: lambda e: e.matmul(
                        psav[:, 0:512], vsl, wsl, start=st, stop=sp))(
                            psav, vsl, wsl, kc == 0, kc == nkc - 1),
                        deps=dep)
                    if dvc == 0:
                        avdep0 = []
                    else:
                        avdep1 = []
                av_last = p.sig("pe")
                wchwar[kc % 2] = av_last
            qdwar[g % 2] = av_last
            rsl = ret[:, (g % 2) * 1024:(g % 2 + 1) * 1024]
            e1 = p.op("ve", (lambda rsl: lambda e: e.tensor_copy(
                rsl[:, 0:512], K.psG[:, 0:512]))(rsl),
                deps=[av_last, retwar[g % 2]], sig=True)
            ps.done("G", e1)
            e2 = p.op("se", (lambda rsl: lambda e: e.copy(
                rsl[:, 512:1024], K.psH[:, 0:512]))(rsl),
                deps=[av_last, retwar[g % 2]], sig=True)
            ps.done("H", e2)
            q1 = p.op("se", (lambda rsl: lambda e: e.activation(
                sqb[:, 0:512], rsl[:, 0:512], AF.Square))(rsl),
                deps=[e1, sq_war], sig=True)
            q2 = p.op("se", (lambda rsl: lambda e: e.activation(
                sqb[:, 512:1024], rsl[:, 512:1024], AF.Square))(rsl),
                deps=[e2], sig=True)
            dep0 = [e1, ps.gate("A")]
            p.op("pe", (lambda rsl: lambda e: e.matmul(
                K.psA[0:1, 0:512], K.ones_bf[:, 0:1], rsl[:, 0:512],
                start=True, stop=False))(rsl), deps=dep0)
            p.op("pe", (lambda rsl: lambda e: e.matmul(
                K.psA[0:1, 0:512], K.ones_bf[:, 0:1], rsl[:, 512:1024],
                start=False, stop=True))(rsl), deps=[e2])
            smu = p.sig("pe")
            p.op("pe", lambda e: e.matmul(
                K.psB[0:1, 0:512], K.ones_bf[:, 0:1], sqb[:, 0:512],
                start=True, stop=False), deps=[q1, ps.gate("B")])
            p.op("pe", lambda e: e.matmul(
                K.psB[0:1, 0:512], K.ones_bf[:, 0:1], sqb[:, 512:1024],
                start=False, stop=True), deps=[q2])
            svar = p.sig("pe")
            sq_war = svar
            inv = 1.0 / 256.0
            t1 = p.op("ve", lambda e: e.tensor_scalar_mul(mu_row, K.psA[0:1, 0:512], inv),
                      deps=[smu], sig=True)
            ps.done("A", t1)
            t2 = p.op("ve", lambda e: e.tensor_scalar_mul(v_row, K.psB[0:1, 0:512], inv),
                      deps=[svar], sig=True)
            ps.done("B", t2)
            t3 = p.op("se", lambda e: e.activation(r_row, mu_row, AF.Square),
                      deps=[t1], sig=True)
            t4 = p.op("ve", lambda e: e.tensor_sub(v_row, v_row, r_row),
                      deps=[t3], sig=True)
            t5 = p.op("se", lambda e: e.activation(
                r_row, v_row, AF.Sqrt, bias=K.misc[0:1, 0:1]), deps=[t4], sig=True)
            t6 = p.op("ve", lambda e: e.reciprocal(r_row, r_row), deps=[t5], sig=True)
            t7 = p.op("ve", lambda e: e.scalar_tensor_tensor(
                nmr_row, mu_row, -1.0, r_row, ALU.mult, ALU.mult),
                deps=[t6], sig=True)
            p.op("pe", lambda e: e.matmul(
                K.psA[:, 0:512], K.ones_f32[0:1, :], r_row, start=True, stop=True),
                deps=[t7, ps.gate("A")])
            bE = p.sig("pe")
            p.op("pe", lambda e: e.matmul(
                K.psB[:, 0:512], K.ones_f32[0:1, :], nmr_row, start=True, stop=True),
                deps=[t7, ps.gate("B")])
            bF = p.sig("pe")
            e3 = p.op("ve", lambda e: e.tensor_copy(r_bc, K.psA[:, 0:512]),
                      deps=[bE], sig=True)
            ps.done("A", e3)
            e4 = p.op("se", lambda e: e.copy(nmr_bc, K.psB[:, 0:512]),
                      deps=[bF], sig=True)
            ps.done("B", e4)
            gsl = gate[:, (g % 2) * 1024:(g % 2 + 1) * 1024]
            for u2 in range(2):
                bank = "E" if u2 == 0 else "F"
                psb = K.psE if u2 == 0 else K.psF
                dep0 = [gtok, ps.gate(bank), gatewar[g % 2]]
                for kc in range(8):
                    p.op("pe", (lambda gu, kc, u2, t0, psb: lambda e: e.matmul(
                        psb[:, 0:512], gu[:, kc * 256 + u2 * 128: kc * 256 + (u2 + 1) * 128],
                        CH(K.xn, kc, t0, 512),
                        start=(kc == 0), stop=(kc == 7)))(gu, kc, u2, t0, psb),
                        deps=dep0 if kc == 0 else [])
                mmtok = p.sig("pe")
                g_used = mmtok
                et = p.op("se", (lambda gsl, u2, psb: lambda e: e.activation(
                    gsl[:, u2 * 512:(u2 + 1) * 512], psb[:, 0:512], AF.Silu))(
                        gsl, u2, psb), deps=[mmtok], sig=True)
                ps.done(bank, et)
            gate_done = p.sig("se")
            ptoks = []
            for u2 in range(2):
                p.op("ve", (lambda rsl, u2: lambda e: e.tensor_mul(
                    rt1, rsl[:, u2 * 512:(u2 + 1) * 512], r_bc))(rsl, u2),
                    deps=[e3, e4, gate_done])
                p.op("ve", (lambda rsl, u2: lambda e: e.tensor_add(
                    rsl[:, u2 * 512:(u2 + 1) * 512], rt1, nmr_bc))(rsl, u2))
                pt2 = p.op("ve", (lambda gsl, rsl, u2: lambda e: e.tensor_mul(
                    gsl[:, u2 * 512:(u2 + 1) * 512],
                    rsl[:, u2 * 512:(u2 + 1) * 512],
                    gsl[:, u2 * 512:(u2 + 1) * 512]))(gsl, rsl, u2), sig=True)
                ptoks.append(pt2)
            retwar[g % 2] = ptoks[-1]
            for dc in range(8):
                bank = "C" if dc % 2 == 0 else "D"
                psb = K.psC if dc % 2 == 0 else K.psD
                dep0 = [ptok, ps.gate(bank)] + ptoks
                for kcp in range(2):
                    p.op("pe", (lambda pu, kcp, dc, gsl, psb: lambda e: e.matmul(
                        psb[:, 0:512], pu[:, kcp * 1024 + dc * 128: kcp * 1024 + (dc + 1) * 128],
                        gsl[:, kcp * 512:(kcp + 1) * 512],
                        start=(kcp == 0), stop=(kcp == 1)))(pu, kcp, dc, gsl, psb),
                        deps=dep0 if kcp == 0 else [])
                mmtok = p.sig("pe")
                p_used = mmtok
                gatewar[g % 2] = mmtok
                et = p.op("ve", (lambda dc, t0, psb: lambda e: e.tensor_add(
                    CH(K.xres, dc, t0, 512), CH(K.xres, dc, t0, 512),
                    psb[:, 0:512]))(dc, t0, psb), deps=[mmtok], sig=True)
                ps.done(bank, et)
        uwar[sqg] = q_used
        uwar[sgg] = g_used
        uwar[spg] = p_used
        kdwar = p.sig("pe")
        vwar = kdwar
    K.final_deps = p.barrier(("pe", "ve", "se"))
    K.fpool.pop(); K.bpool.pop()




def emit_l1_ffn(K):
    p, ps = K.p, K.ps
    pre = list(K.final_deps)
    K.fpool.push(); K.bpool.push()

    qch = K.fpool.alloc(4 * 1024)
    acc = K.smalls[:, 12:13]
    accw = K.smalls[:, 13:14]
    thr = K.smalls[:, 14:15]
    nthr = K.smalls[:, 15:16]
    qbf = K.bpool.alloc(4 * 1024)
    lnw_bf = K.bpool.alloc(32)
    lnb_bf = K.bpool.alloc(32)
    rt_q = K.bpool.alloc(4096).bitcast(F32)
    p.op("ve", lambda e: e.tensor_copy(lnw_bf, K.lnw), deps=pre, sig=True)
    lnbf_tok = p.op("ve", lambda e: e.tensor_copy(lnb_bf, K.lnb),
                    deps=pre, sig=True)

    def quant_weight(name, nrows, ncols, chcols, scname, fold_lnw, stats):
        nch_r = nrows // 128
        nch_c = ncols // chcols
        nslot = 4
        chunks = [(r, c) for r in range(nch_r) for c in range(nch_c)]
        zt = p.op("ve", lambda e: e.memset(acc, 0.0), deps=pre, sig=True)
        war = [zt] * nslot
        for idx, (r, c) in enumerate(chunks):
            sl = qch[:, (idx % nslot) * 1024:(idx % nslot) * 1024 + chcols]
            dt = p.dma("sp", (lambda sl, r, c: lambda e: e.dma_start(
                sl, K.ins[name][r * 128:(r + 1) * 128,
                                c * chcols:(c + 1) * chcols]))(sl, r, c),
                deps=[war[idx % nslot]] + pre)
            t1 = p.op("ve", (lambda sl: lambda e: e.reduce_sum(
                accw, sl, axis=AX.X, apply_absolute_value=True))(sl),
                deps=[dt], sig=True)
            war[idx % nslot] = t1
            t2 = p.op("ve", lambda e: e.tensor_add(acc, acc, accw), sig=True)
        p.op("pe", lambda e: e.matmul(K.psA[0:1, 0:1], K.ones_f32[:, 0:1], acc,
                                      start=True, stop=True),
             deps=[t2, ps.gate("A")])
        gt = p.sig("pe")
        n_el = float(nrows * ncols)
        t3 = p.op("ve", lambda e: e.tensor_scalar_mul(
            K.smalls[0:1, 56:57], K.psA[0:1, 0:1], 1.0 / n_el), deps=[gt], sig=True)
        ps.done("A", t3)
        p.op("ve", lambda e: e.tensor_scalar_add(
            K.smalls[0:1, 56:57], K.smalls[0:1, 56:57], 1e-6))
        t4 = p.op("ve", lambda e: e.tensor_scalar_mul(
            K.smalls[0:1, 56:57], K.smalls[0:1, 56:57], 0.5), sig=True)
        p.op("pe", lambda e: e.matmul(K.psA[:, 0:1], K.ones_f32[0:1, :],
                                      K.smalls[0:1, 56:57], start=True, stop=True),
             deps=[t4, ps.gate("A")])
        bt = p.sig("pe")
        t5 = p.op("ve", lambda e: e.tensor_copy(thr, K.psA[:, 0:1]),
                  deps=[bt], sig=True)
        ps.done("A", t5)
        t6 = p.op("ve", lambda e: e.tensor_scalar_mul(nthr, thr, -1.0), sig=True)
        war = [t6] * nslot
        qwar = [None] * nslot
        stat_started = False
        for idx, (r, c) in enumerate(chunks):
            sl = qch[:, (idx % nslot) * 1024:(idx % nslot) * 1024 + chcols]
            qsl = qbf[:, (idx % nslot) * 1024:(idx % nslot) * 1024 + chcols]
            dt = p.dma("sp", (lambda sl, r, c: lambda e: e.dma_start(
                sl, K.ins[name][r * 128:(r + 1) * 128,
                                c * chcols:(c + 1) * chcols]))(sl, r, c),
                deps=[war[idx % nslot]])
            t1 = p.op("ve", (lambda qsl, sl: lambda e: e.tensor_single_scalar(
                qsl, sl, thr, ALU.is_ge))(qsl, sl),
                deps=[dt, qwar[idx % nslot], t6], sig=True)
            t2 = p.op("ve", (lambda sl, cc: lambda e: e.tensor_single_scalar(
                rt_q[:, 0:cc], sl, nthr, ALU.is_le))(sl, chcols), sig=True)
            war[idx % nslot] = t2
            t3 = p.op("ve", (lambda qsl, cc: lambda e: e.tensor_sub(
                qsl, qsl, rt_q[:, 0:cc]))(qsl, chcols), sig=True)
            if stats:
                dep0 = [t3, lnbf_tok] + ([ps.gate("E"), ps.gate("F"),
                                          ps.gate("G"), ps.gate("H")]
                                         if not stat_started else [])
                stat_started = True
                p.op("pe", (lambda qsl, r: lambda e: e.matmul(
                    K.psE[0:1, 0:512], lnw_bf[:, r:r + 1], qsl[:, 0:512],
                    start=(r == 0), stop=(r == 31)))(qsl, r), deps=dep0)
                p.op("pe", (lambda qsl, r: lambda e: e.matmul(
                    K.psF[0:1, 0:512], lnw_bf[:, r:r + 1], qsl[:, 512:1024],
                    start=(r == 0), stop=(r == 31)))(qsl, r))
                p.op("pe", (lambda qsl, r: lambda e: e.matmul(
                    K.psG[0:1, 0:512], lnb_bf[:, r:r + 1], qsl[:, 0:512],
                    start=(r == 0), stop=(r == 31)))(qsl, r))
                p.op("pe", (lambda qsl, r: lambda e: e.matmul(
                    K.psH[0:1, 0:512], lnb_bf[:, r:r + 1], qsl[:, 512:1024],
                    start=(r == 0), stop=(r == 31)))(qsl, r))
                stok = p.sig("pe")
            if fold_lnw:
                t4 = p.op("ve", (lambda qsl, r: lambda e: e.tensor_single_scalar(
                    qsl, qsl, K.lnw[:, r:r + 1], ALU.mult))(qsl, r),
                    deps=[stok] if stats else [], sig=True)
            if scname == "fc":
                dst = K.wqfc_sc[r * 128:(r + 1) * 128, c * chcols:(c + 1) * chcols]
                qv = qsl
            else:
                dst = K.wqproj_sc.rearrange("a p k d -> p a k d")[:, :, r, :]
                qv = qsl.rearrange("p (a d) -> p a d", d=128)
            st = p.dma("sp", (lambda dst, qv: lambda e: e.dma_start(
                dst, qv))(dst, qv), deps=[p.sig("ve")])
            qwar[idx % nslot] = st
        return p.front("sp")

    fc_done = quant_weight("wfcT", 1024, 4096, 1024, "fc", False, False)
    pr_done = quant_weight("wprojT", 4096, 1024, 1024, "proj", True, True)
    e1 = p.op("ve", lambda e: e.tensor_copy(K.s1s0_bf[0:1, 0:512], K.psE[0:1, 0:512]), sig=True)
    ps.done("E", e1)
    e2 = p.op("ve", lambda e: e.tensor_copy(K.s1s0_bf[0:1, 512:1024], K.psF[0:1, 0:512]), sig=True)
    ps.done("F", e2)
    e3 = p.op("ve", lambda e: e.tensor_copy(K.s1s0_bf[0:1, 1024:1536], K.psG[0:1, 0:512]), sig=True)
    ps.done("G", e3)
    e4 = p.op("ve", lambda e: e.tensor_copy(K.s1s0_bf[0:1, 1536:2048], K.psH[0:1, 0:512]), sig=True)
    ps.done("H", e4)
    s_done = [e1, e2, e3, e4]
    quant_done = p.barrier(("ve", "se", "sp"))
    K.fpool.pop()
    K.bpool.pop()

    K.fpool.push(); K.bpool.push()
    h_g = K.bpool.alloc(32 * 512)
    w1b = K.bpool.alloc(2 * 4096)
    w2b = K.bpool.alloc(2 * 4096)
    sq = K.bpool.alloc(2 * 512)
    nmr_bf = K.bpool.alloc(512)
    rows = K.fpool.alloc(4 * 512)
    r_bc = K.fpool.alloc(512)
    mu_row = rows[0:1, 0:512]
    v_row = rows[0:1, 512:1024]
    r_row = rows[0:1, 1024:1536]
    nmr_row = rows[0:1, 1536:2048]
    w1war = [None, None]
    w2war = [None, None]
    sq_war = [quant_done, quant_done]
    h_war = quant_done
    rbc_war = None
    nmrbf_war = quant_done
    u1 = 0
    u2 = 0
    for g in range(4):
        t0 = g * 512
        stat_dep = [ps.gate("G")]
        stat_dep2 = [ps.gate("H")]
        for fb in range(8):
            slot = u1 % 2
            u1 += 1
            w1u = w1b[:, slot * 4096:(slot + 1) * 4096]
            src = K.wqfc_sc.rearrange("(kc p) f -> p kc f", p=128)[
                :, :, fb * 512:(fb + 1) * 512]
            dstv = w1u.rearrange("p (kc f) -> p kc f", f=512)
            wt = p.dma("sp", (lambda dstv, src: lambda e: e.dma_start(
                dstv, src))(dstv, src), deps=[w1war[slot]] + fc_done + pre)
            for fc in range(4):
                hidx = fb * 4 + fc
                bank = "C" if hidx % 2 == 0 else "D"
                psb = K.psC if hidx % 2 == 0 else K.psD
                dep0 = [wt, ps.gate(bank)]
                for kc in range(8):
                    p.op("pe", (lambda w1u, kc, fc, t0, psb: lambda e: e.matmul(
                        psb[:, 0:512],
                        w1u[:, kc * 512 + fc * 128: kc * 512 + (fc + 1) * 128],
                        CH(K.xn, kc, t0, 512),
                        start=(kc == 0), stop=(kc == 7)))(w1u, kc, fc, t0, psb),
                        deps=dep0 if kc == 0 else [])
                mmtok = p.sig("pe")
                if fc == 3:
                    w1war[slot] = mmtok
                et = p.op("se", (lambda hidx, psb: lambda e: e.activation(
                    h_g[:, hidx * 512:(hidx + 1) * 512], psb[:, 0:512],
                    AF.Gelu))(hidx, psb), deps=[mmtok, h_war], sig=True)
                ps.done(bank, et)
                q1 = p.op("se", (lambda hidx: lambda e: e.activation(
                    sq[:, (hidx % 2) * 512:(hidx % 2 + 1) * 512],
                    h_g[:, hidx * 512:(hidx + 1) * 512], AF.Square))(hidx),
                    deps=[sq_war[hidx % 2]], sig=True)
                p.op("pe", (lambda hidx: lambda e: e.matmul(
                    K.psG[0:1, 0:512], K.ones_bf[:, 0:1],
                    h_g[:, hidx * 512:(hidx + 1) * 512],
                    start=(hidx == 0), stop=(hidx == 31)))(hidx),
                    deps=[et] + stat_dep)
                stat_dep = []
                p.op("pe", (lambda hidx: lambda e: e.matmul(
                    K.psH[0:1, 0:512], K.ones_bf[:, 0:1],
                    sq[:, (hidx % 2) * 512:(hidx % 2 + 1) * 512],
                    start=(hidx == 0), stop=(hidx == 31)))(hidx),
                    deps=[q1] + stat_dep2)
                stat_dep2 = []
                sq_war[hidx % 2] = p.sig("pe")
        smtok = p.sig("pe")
        inv = 1.0 / 4096.0
        t1 = p.op("ve", lambda e: e.tensor_scalar_mul(mu_row, K.psG[0:1, 0:512], inv),
                  deps=[smtok], sig=True)
        ps.done("G", t1)
        t2 = p.op("ve", lambda e: e.tensor_scalar_mul(v_row, K.psH[0:1, 0:512], inv),
                  deps=[smtok], sig=True)
        ps.done("H", t2)
        t3 = p.op("se", lambda e: e.activation(r_row, mu_row, AF.Square),
                  deps=[t1], sig=True)
        t4 = p.op("ve", lambda e: e.tensor_sub(v_row, v_row, r_row), deps=[t3], sig=True)
        t5 = p.op("se", lambda e: e.activation(r_row, v_row, AF.Sqrt,
                                               bias=K.misc[0:1, 0:1]),
                  deps=[t4], sig=True)
        t6 = p.op("ve", lambda e: e.reciprocal(r_row, r_row), deps=[t5], sig=True)
        t7 = p.op("ve", lambda e: e.scalar_tensor_tensor(
            nmr_row, mu_row, -1.0, r_row, ALU.mult, ALU.mult), deps=[t6], sig=True)
        t8 = p.op("ve", lambda e: e.tensor_copy(nmr_bf[0:1, 0:512], nmr_row),
                  deps=[t7, nmrbf_war], sig=True)
        p.op("pe", lambda e: e.matmul(K.psG[:, 0:512], K.ones_f32[0:1, :], r_row,
                                      start=True, stop=True),
             deps=[t7, ps.gate("G")])
        bt = p.sig("pe")
        t9 = p.op("ve", (lambda: lambda e: e.tensor_copy(r_bc, K.psG[:, 0:512]))(),
                  deps=[bt, rbc_war], sig=True)
        ps.done("G", t9)
        htoks = []
        for fk in range(32):
            ht = p.op("ve", (lambda fk: lambda e: e.tensor_mul(
                h_g[:, fk * 512:(fk + 1) * 512], h_g[:, fk * 512:(fk + 1) * 512],
                r_bc))(fk), deps=[t9, smtok], sig=True)
            htoks.append(ht)
        rbc_war_next = htoks[-1]
        for dc in range(8):
            slot = u2 % 2
            u2 += 1
            w2u = w2b[:, slot * 4096:(slot + 1) * 4096]
            wt = p.dma("sp", (lambda w2u, dc: lambda e: e.dma_start(
                w2u, K.wqproj_sc[dc].flatten_outer_dims() if False
                else K.wqproj_sc[dc].rearrange("p k d -> p (k d)")))(w2u, dc),
                deps=[w2war[slot]] + pr_done)
            bank = "E" if dc % 2 == 0 else "F"
            psb = K.psE if dc % 2 == 0 else K.psF
            dep0 = [wt, ps.gate(bank)]
            for fk in range(32):
                p.op("pe", (lambda w2u, fk, psb: lambda e: e.matmul(
                    psb[:, 0:512], w2u[:, fk * 128:(fk + 1) * 128],
                    h_g[:, fk * 512:(fk + 1) * 512],
                    start=(fk == 0), stop=False))(w2u, fk, psb),
                    deps=(dep0 if fk == 0 else []) + [htoks[fk]])
            p.op("pe", (lambda dc, psb: lambda e: e.matmul(
                psb[:, 0:512], K.s1s0_bf[0:1, dc * 128:(dc + 1) * 128],
                nmr_bf[0:1, 0:512], start=False, stop=False))(dc, psb),
                deps=[t8] + s_done)
            p.op("pe", (lambda dc, psb: lambda e: e.matmul(
                psb[:, 0:512], K.s1s0_bf[0:1, 1024 + dc * 128:1024 + (dc + 1) * 128],
                K.ones_bf[0:1, 0:512], start=False, stop=True))(dc, psb))
            mmtok = p.sig("pe")
            w2war[slot] = mmtok
            et = p.op("ve", (lambda dc, t0, psb: lambda e: e.tensor_add(
                CH(K.xres, dc, t0, 512), CH(K.xres, dc, t0, 512),
                psb[:, 0:512]))(dc, t0, psb), deps=[mmtok], sig=True)
            ps.done(bank, et)
        h_war = p.sig("pe")
        rbc_war = rbc_war_next
        nmrbf_war = h_war
    K.final_deps = p.barrier(("pe", "ve", "se"))
    K.fpool.pop(); K.bpool.pop()




# version 06
# speedup vs baseline: 1.0957x; 1.0664x over previous
import numpy as np
import ml_dtypes
from contextlib import ExitStack

import concourse.bass as bass
import concourse.mybir as mybir
from concourse.bass_utils import run_bass_kernel_spmd

F32 = mybir.dt.float32
BF16 = mybir.dt.bfloat16
AF = mybir.ActivationFunctionType
ALU = mybir.AluOpType
AX = mybir.AxisListType
BF = ml_dtypes.bfloat16

T = 2048
D = 1024
NCH = 8
NT = 16
NEG = -1.0e30

ARENA_BF_COLS = 36864
ARENA_F32_COLS = 5120

ENGSTEP = {"pe": 1, "ve": 1, "se": 1, "gp": 16, "sp": 16}
NCH_GP = 12
NCH_SP = 6



def _lay_feat(v):
    f = v.shape[0]
    return np.ascontiguousarray(v.reshape(f // 128, 128).T)


def _host_constants():
    c = {}
    c["ident_bf"] = np.eye(128, dtype=BF)
    c["ident_f32"] = np.eye(128, dtype=np.float32)
    c["ones_bf"] = np.ones((128, 512), dtype=BF)
    c["ones_f32"] = np.ones((128, 128), dtype=np.float32)

    m0 = np.full((128, 4 * 512), NEG, dtype=np.float32)
    w = np.arange(512)[None, :]
    for parity in range(2):
        for b0 in range(2):
            v = parity + 2 * b0
            ql = 128 * parity + np.arange(128)[:, None]
            valid = (w >= ql) & (w <= ql + 256)
            if b0:
                valid &= (w >= 256)
            m0[:, v * 512:(v + 1) * 512][valid] = 0.0
    c["mask0"] = m0.astype(BF)

    mc = np.zeros((128, 4 * 512), dtype=np.float32)
    qf = np.arange(512)[None, :]
    kp = np.arange(128)[:, None]
    for d in range(4):
        mc[:, d * 512:(d + 1) * 512] = (qf >= kp + 128 * d).astype(np.float32)
    c["maskc"] = mc.astype(BF)

    c["pos"] = np.broadcast_to(
        np.arange(T, dtype=np.float32)[None, :], (128, T)).copy()

    QK = 1024
    SCALE_BASE = 512.0
    i2 = np.arange(0, QK, 2, dtype=np.float64)
    base_scale = (i2 + 0.4 * QK) / (1.4 * QK)
    t = np.arange(T, dtype=np.float64)
    scale = base_scale[None, :] ** (t[:, None] / SCALE_BASE)
    inv_freq = 1.0 / (10000.0 ** (np.arange(QK // 2, dtype=np.float64) / (QK // 2)))
    sinu = t[:, None] * inv_freq[None, :]
    sn, cs = np.sin(sinu), np.cos(sinu)
    rep = lambda a: np.repeat(a, 2, axis=1)
    sgn = np.where(np.arange(QK) % 2 == 0, -1.0, 1.0)[None, :]
    c["csq"] = np.ascontiguousarray((rep(cs * scale)).T).astype(BF)
    c["snq"] = np.ascontiguousarray((rep(sn * scale) * sgn).T).astype(BF)
    c["csk"] = np.ascontiguousarray((rep(cs / scale)).T).astype(BF)
    c["snk"] = np.ascontiguousarray((rep(sn / scale) * sgn).T).astype(BF)

    bas = np.zeros((8, 8 * 128), dtype=np.float32)
    for e in range(8):
        bas[e, e * 128:(e + 1) * 128] = 1.0
    c["basis_bf"] = bas.astype(BF)

    mcc = np.zeros((128, 8), dtype=np.float32)
    mcc[:, 0] = 1e-5
    mcc[:, 1] = 1e-6
    mcc[:, 2] = -4.0
    c["misc_cols"] = mcc
    return c


_CONST_CACHE = None


def _prep_shared(inputs):
    global _CONST_CACHE
    if _CONST_CACHE is None:
        _CONST_CACHE = _host_constants()
    s = dict(_CONST_CACHE)
    f32 = np.float32
    asC = lambda a: np.ascontiguousarray(np.asarray(a, f32))
    s["qwT"] = asC(np.asarray(inputs["L0_attn_q_w"]).T)
    s["kwT"] = asC(np.asarray(inputs["L0_attn_k_w"]).T)
    s["vwT"] = asC(np.asarray(inputs["L0_attn_v_w"]).T)
    s["owT"] = asC(np.asarray(inputs["L0_attn_o_w"]).T)
    s["bias_rows"] = np.stack([
        np.asarray(inputs["L0_attn_q_b"], f32),
        np.asarray(inputs["L0_attn_k_b"], f32),
        np.asarray(inputs["L0_attn_v_b"], f32)], axis=0)
    s["l0bias"] = np.concatenate(
        [_lay_feat(np.asarray(inputs[k], f32)) for k in
         ("L0_attn_q_b", "L0_attn_k_b", "L0_attn_v_b", "L0_attn_o_b")],
        axis=1)
    s["w1"] = asC(inputs["L0_ffn_W1"])
    w2 = np.asarray(inputs["L0_ffn_W2"], f32)
    w2r = w2.reshape(8, 32, 128, 8, 128)
    s["w2lay"] = np.ascontiguousarray(
        w2r.transpose(0, 3, 2, 1, 4).reshape(8, 8, 128, 32 * 128))
    s["b1lay"] = np.concatenate(
        [_lay_feat(np.asarray(inputs["L0_ffn_b1"], f32)[e]) for e in range(8)],
        axis=1)
    s["b2"] = asC(inputs["L0_ffn_b2"])
    s["wr"] = asC(inputs["L0_ffn_Wr"])
    s["br_bc"] = np.broadcast_to(
        np.asarray(inputs["L0_ffn_br"], f32)[None, :], (128, 8)).copy()
    s["wqkv"] = asC(inputs["L1_attn_W_qkv"])
    s["wg"] = asC(inputs["L1_attn_W_g"])
    s["wproj1"] = asC(inputs["L1_attn_W_proj"])
    s["decay"] = np.asarray(inputs["L1_attn_decay"], f32).reshape(1, 8).copy()
    s["wfcT"] = asC(np.asarray(inputs["L1_ffn_W_fc"]).T)
    s["wprojT"] = asC(np.asarray(inputs["L1_ffn_W_proj"]).T)
    s["lnw_lay"] = _lay_feat(np.asarray(inputs["L1_ffn_ln_w"], f32))
    s["lnb_lay"] = _lay_feat(np.asarray(inputs["L1_ffn_ln_b"], f32))
    return s



class Prog:

    def __init__(self, strict="none"):
        self.strict = strict
        self.strict_engs = {"none": (), "vese": ("ve", "se"),
                            "all": ("pe", "ve", "se")}[strict]
        self.semkeys = (["pe", "ve", "se"]
                        + [f"gp{i}" for i in range(NCH_GP)]
                        + [f"sp{i}" for i in range(NCH_SP)])
        self.stream = {k: [] for k in ENGSTEP}
        self.cnt = {k: 0 for k in self.semkeys}
        self._last = {k: None for k in ENGSTEP}
        self._rr = {"gp": 0, "sp": 0}

    @staticmethod
    def _flat(deps):
        out = []
        for d in deps:
            if d is None:
                continue
            if isinstance(d, list):
                out.extend(t for t in d if t is not None)
            else:
                out.append(d)
        return out

    def op(self, eng, fn, deps=(), sig=False):
        assert eng in ENGSTEP and (not sig or eng in ("pe", "ve", "se"))
        toks = self._flat(deps)
        entry = [toks, fn, None]
        self.stream[eng].append(entry)
        self._last[eng] = entry
        if eng in self.strict_engs:
            self.sig(eng)
        if sig:
            return self.sig(eng)
        return None

    def dma(self, eng, fn, deps=()):
        assert eng in ("gp", "sp")
        ch = f"{eng}{self._rr[eng]}"
        self._rr[eng] = (self._rr[eng] + 1) % (NCH_GP if eng == "gp" else NCH_SP)
        toks = self._flat(deps)
        if self.cnt[ch]:
            toks.append((ch, self.cnt[ch]))
        self.cnt[ch] += 16
        entry = [toks, fn, (ch, 16)]
        self.stream[eng].append(entry)
        self._last[eng] = entry
        return (ch, self.cnt[ch])

    def sig(self, eng):
        entry = self._last[eng]
        assert entry is not None, f"no op to sign on {eng}"
        if entry[2] is None:
            self.cnt[eng] += 1
            entry[2] = (eng, 1)
        return (eng, self.cnt[eng])

    def front(self, eng):
        if eng in ("pe", "ve", "se"):
            if self._last[eng] is None:
                return []
            return [self.sig(eng)]
        pre = eng
        return [(k, self.cnt[k]) for k in self.semkeys
                if k.startswith(pre) and self.cnt[k]]

    def barrier(self, engs=("pe", "ve", "se", "gp", "sp")):
        toks = []
        for e in engs:
            toks.extend(self.front(e))
        return toks

    def replay(self, nc, sems):
        hw = {k: {k2: 0 for k2 in self.semkeys} for k in ENGSTEP}
        blockmap = {"pe": "tensor", "ve": "vector", "se": "scalar",
                    "gp": "gpsimd", "sp": "sync"}
        with nc.Block() as block:
            for eng in ENGSTEP:
                entries = self.stream[eng]
                if not entries:
                    continue

                def make_body(eng, entries):
                    def body(e):
                        mywait = hw[eng]
                        mycount = 0
                        for toks, fn, sig in entries:
                            for (tkey, tval) in toks:
                                if tkey == eng:
                                    continue
                                if tval > mywait[tkey]:
                                    e.wait_ge(sems[tkey], tval)
                                    mywait[tkey] = tval
                            if eng in self.strict_engs and mycount:
                                e.wait_ge(sems[eng], mycount)
                            inst = fn(e)
                            if sig is not None:
                                assert inst is not None
                                inst.then_inc(sems[sig[0]], sig[1])
                                mycount += sig[1] if sig[0] == eng else 0
                    return body

                getattr(block, blockmap[eng])(make_body(eng, entries))


class Pool:

    def __init__(self, tensor, cols, name):
        self.t = tensor
        self.cols = cols
        self.off = 0
        self.name = name
        self.marks = []

    def alloc(self, n):
        n = (n + 7) & ~7
        assert self.off + n <= self.cols, \
            f"arena {self.name} overflow: {self.off}+{n} > {self.cols}"
        o = self.off
        self.off += n
        return self.t[:, o:o + n]

    def alloc_rows(self, rows, n):
        n = (n + 7) & ~7
        assert self.off + n <= self.cols
        o = self.off
        self.off += n
        return self.t[0:rows, o:o + n]

    def push(self):
        self.marks.append(self.off)

    def pop(self):
        self.off = self.marks.pop()

    def reset(self):
        self.off = 0


class PsumMgr:

    def __init__(self):
        self.tok = {}

    def gate(self, name):
        return self.tok.get(name)

    def done(self, name, token):
        self.tok[name] = token


class Ctx:
    pass



def CH(t, c, t0=0, n=T, stride=T):
    return t[:, c * stride + t0: c * stride + t0 + n]


def emit_load_x(K):
    p = K.p
    K.fpool.push()
    stage = K.fpool.alloc(2 * 1024)
    slot_tok = [None, None]
    bank_tok = [None, None]
    for i in range(NT):
        s = i % 2
        st = stage[:, s * 1024:(s + 1) * 1024]
        dtok = p.dma("sp", (lambda st, i: lambda e: e.dma_start(
            st, K.x_in[i * 128:(i + 1) * 128, :]))(st, i),
            deps=[slot_tok[s]] + (K.consts_all if i == 0 else []))
        for b in range(2):
            ps = K.psA if b == 0 else K.psB
            for j in range(4):
                c = 4 * b + j
                p.op("pe", (lambda st, c, ps, j: lambda e: e.transpose(
                    ps[:, j * 128:(j + 1) * 128],
                    st[:, c * 128:(c + 1) * 128], K.ident_f32))(st, c, ps, j),
                    deps=[dtok, bank_tok[b]] + (K.consts_all if i == 0 else []))
            ptok = p.sig("pe")
            if b == 1:
                slot_tok[s] = ptok
            dst = K.xres.rearrange("p (c t) -> p c t", t=T)[
                :, 4 * b:4 * b + 4, i * 128:(i + 1) * 128]
            psv = ps.rearrange("p (c t) -> p c t", t=128)
            eng = "ve" if b == 0 else "se"
            if eng == "ve":
                fn = (lambda dst, psv: lambda e: e.tensor_copy(dst, psv))(dst, psv)
            else:
                fn = (lambda dst, psv: lambda e: e.copy(dst, psv))(dst, psv)
            bank_tok[b] = p.op(eng, fn, deps=[ptok], sig=True)
    K.x_ready = p.barrier(("pe", "ve", "se"))
    K.final_deps = K.x_ready
    K.fpool.pop()


def emit_store(K):
    p = K.p
    K.fpool.push()
    obuf = K.fpool.alloc(2 * 1024)
    slot_store_tok = [None, None]
    bank_tok = [None, None]
    pre = list(K.final_deps) if hasattr(K, "final_deps") else []
    stok = None
    for i in range(NT):
        s = i % 2
        ob = obuf[:, s * 1024:(s + 1) * 1024]
        evac_toks = []
        for b in range(2):
            ps = K.psA if b == 0 else K.psB
            for j in range(4):
                c = 4 * b + j
                p.op("pe", (lambda c, i, ps, j: lambda e: e.transpose(
                    ps[:, j * 128:(j + 1) * 128],
                    CH(K.xres, c, i * 128, 128), K.ident_f32))(c, i, ps, j),
                    deps=pre + [bank_tok[b]])
            ptok = p.sig("pe")
            dst = ob[:, b * 512:(b + 1) * 512]
            psv = ps[:, 0:512]
            eng = "ve" if b == 0 else "se"
            if eng == "ve":
                fn = (lambda dst, psv: lambda e: e.tensor_copy(dst, psv))(dst, psv)
            else:
                fn = (lambda dst, psv: lambda e: e.copy(dst, psv))(dst, psv)
            etok = p.op(eng, fn, deps=[ptok, slot_store_tok[s]], sig=True)
            evac_toks.append(etok)
            bank_tok[b] = etok
        pre = []
        stok = p.dma("sp", (lambda ob, i: lambda e: e.dma_start(
            K.out[i * 128:(i + 1) * 128, :], ob))(ob, i),
            deps=evac_toks)
        slot_store_tok[s] = stok




def build_nc(upto="full", strict="vese"):
    nc = bass.Bass("TRN2", target_bir_lowering=False)
    K = Ctx()
    K.nc = nc
    K.p = Prog(strict=strict)
    K.ps = PsumMgr()
    K.upto = upto

    f32 = F32
    dt_ = nc.dram_tensor
    K.x_in = dt_("x_in", [T, D], f32, kind="ExternalInput").ap()
    K.out = dt_("out", [T, D], f32, kind="ExternalOutput").ap()

    ins = {}
    for name, shape, dty in [
        ("ident_bf", [128, 128], BF16), ("ident_f32", [128, 128], f32),
        ("ones_bf", [128, 512], BF16), ("ones_f32", [128, 128], f32),
        ("mask0", [128, 2048], BF16), ("maskc", [128, 2048], BF16),
        ("pos", [128, T], f32),
        ("csq", [D, T], BF16), ("snq", [D, T], BF16),
        ("csk", [D, T], BF16), ("snk", [D, T], BF16),
        ("basis_bf", [8, 1024], BF16),
        ("qwT", [D, D], f32), ("kwT", [D, D], f32),
        ("vwT", [D, D], f32), ("owT", [D, D], f32),
        ("l0bias", [128, 32], f32), ("bias_rows", [3, D], f32),
        ("misc_cols", [128, 8], f32),
        ("w1", [8, D, 4096], f32), ("w2lay", [8, 8, 128, 4096], f32),
        ("b1lay", [128, 256], f32), ("b2", [8, D], f32),
        ("wr", [D, 8], f32), ("br_bc", [128, 8], f32),
        ("wqkv", [D, 4096], f32), ("wg", [D, 2048], f32),
        ("wproj1", [2048, D], f32), ("decay", [1, 8], f32),
        ("wfcT", [D, 4096], f32), ("wprojT", [4096, D], f32),
        ("lnw_lay", [128, 32], f32), ("lnb_lay", [128, 32], f32),
    ]:
        ins[name] = dt_(name, shape, dty, kind="ExternalInput").ap()
    K.ins = ins
    K.wqfc_sc = dt_("wqfc_sc", [D, 4096], BF16, kind="Internal").ap()
    K.w1sc = dt_("w1sc", [8, 8, 128, 4096], BF16, kind="Internal").ap()
    K.w2sc = dt_("w2sc", [8, 8, 128, 4096], BF16, kind="Internal").ap()
    K.wqproj_sc = dt_("wqproj_sc", [8, 128, 32, 128], BF16, kind="Internal").ap()

    with ExitStack() as ctx:
        E = ctx.enter_context
        K.xres = E(nc.sbuf_tensor("xres", [128, NCH * T], f32))
        K.xn = E(nc.sbuf_tensor("xn", [128, NCH * T], BF16))
        K.ident_bf = E(nc.sbuf_tensor("ident_bf_s", [128, 128], BF16))[:]
        K.ident_f32 = E(nc.sbuf_tensor("ident_f32_s", [128, 128], f32))[:]
        K.ones_bf = E(nc.sbuf_tensor("ones_bf_s", [128, 512], BF16))[:]
        K.ones_f32 = E(nc.sbuf_tensor("ones_f32_s", [128, 128], f32))[:]
        K.l0bias = E(nc.sbuf_tensor("l0bias_s", [128, 32], f32))[:]
        brow_t = E(nc.sbuf_tensor("brow_bf", [65, D], BF16))
        K.brow_bf = brow_t[:]
        K.brow_dst = bass.AP(brow_t, 0, [[32 * D, 3], [1, D]])
        K.misc = E(nc.sbuf_tensor("misc_cols_s", [128, 8], f32))[:]
        K.b1lay = E(nc.sbuf_tensor("b1lay_s", [128, 256], f32))[:]
        K.b2bf = E(nc.sbuf_tensor("b2bf_s", [8, 1024], BF16))[:]
        K.wr_s = E(nc.sbuf_tensor("wr_s", [128, 64], f32))[:]
        K.br_s = E(nc.sbuf_tensor("br_s", [128, 8], f32))[:]
        K.basis = E(nc.sbuf_tensor("basis_s", [8, 1024], BF16))[:]
        K.lnw = E(nc.sbuf_tensor("lnw_s", [128, 32], f32))[:]
        K.lnb = E(nc.sbuf_tensor("lnb_s", [128, 32], f32))[:]
        K.decay_s = E(nc.sbuf_tensor("decay_s", [1, 8], f32))[:]
        K.lnd = E(nc.sbuf_tensor("lnd_s", [128, 16], f32))[:]
        K.stat_col = E(nc.sbuf_tensor("stat_col", [128, 64], f32))[:]
        K.smalls = E(nc.sbuf_tensor("smalls", [128, 64], f32))[:]
        K.coef = E(nc.sbuf_tensor("coef_s", [128, 128], f32))[:]
        K.coefT_bf = E(nc.sbuf_tensor("coefT_bf", [8, T], BF16))[:]
        K.s1s0_bf = E(nc.sbuf_tensor("s1s0_bf", [1, 2048], BF16))[:]

        arena_bf = E(nc.sbuf_tensor("arena_bf", [128, ARENA_BF_COLS], BF16))
        arena_f32 = E(nc.sbuf_tensor("arena_f32", [128, ARENA_F32_COLS], f32))
        K.bpool = Pool(arena_bf, ARENA_BF_COLS, "bf")
        K.fpool = Pool(arena_f32, ARENA_F32_COLS, "f32")

        K.psA = E(nc.psum_tensor("psA", [128, 512], f32))[:]
        K.psB = E(nc.psum_tensor("psB", [128, 512], f32))[:]
        K.psC = E(nc.psum_tensor("psC", [128, 512], f32))[:]
        K.psD = E(nc.psum_tensor("psD", [128, 512], f32))[:]
        K.psE = E(nc.psum_tensor("psE", [128, 512], f32))[:]
        K.psF = E(nc.psum_tensor("psF", [128, 512], f32))[:]
        K.psG = E(nc.psum_tensor("psG", [128, 512], f32))[:]
        K.psH = E(nc.psum_tensor("psH", [128, 512], f32))[:]

        sems = {k: E(nc.semaphore(f"sem_{k}")) for k in K.p.semkeys}
        K.sems = sems

        emit_program(K)
        K.p.replay(nc, sems)
    return nc


def emit_consts_load(K):
    p = K.p
    toks = []
    for name, dst in [
        ("ident_bf", K.ident_bf), ("ident_f32", K.ident_f32),
        ("ones_bf", K.ones_bf), ("ones_f32", K.ones_f32),
        ("l0bias", K.l0bias), ("b1lay", K.b1lay),
        ("br_bc", K.br_s), ("basis_bf", K.basis),
        ("lnw_lay", K.lnw), ("lnb_lay", K.lnb), ("decay", K.decay_s),
        ("misc_cols", K.misc),
    ]:
        toks.append(p.dma("sp", (lambda name, dst: lambda e: e.dma_start(
            dst, K.ins[name][:]))(name, dst)))
    src = K.ins["wr"].rearrange("(c p) e -> p c e", p=128)
    dstv = K.wr_s.rearrange("p (c e) -> p c e", e=8)
    toks.append(p.dma("sp", lambda e: e.dma_start(dstv, src)))
    toks.append(p.dma("gp", lambda e: e.dma_start(K.brow_dst, K.ins["bias_rows"][:])))
    b2tok = p.dma("gp", lambda e: e.dma_start(K.b2bf, K.ins["b2"][:]))
    toks.append(b2tok)
    K.consts_ready = toks[-1]
    K.consts_all = toks


def emit_program(K):
    emit_consts_load(K)
    emit_load_x(K)
    stages = ["io", "ln0", "l0attn", "moe", "ln1", "l1attn", "full"]
    lvl = stages.index(K.upto)
    if lvl >= 1:
        emit_ln(K, layer=0)
    if lvl >= 3:
        emit_router(K)
    if lvl >= 2:
        emit_l0_attn(K)
    if lvl >= 3:
        emit_moe(K)
    if lvl >= 4:
        emit_ln(K, layer=1)
    if lvl >= 5:
        emit_l1_attn(K)
    if lvl >= 6:
        emit_l1_ffn(K)
    emit_store(K)
    p = K.p
    p.op("sp", lambda e: e.nop(), deps=p.front("sp"))
    if any(p.cnt[k] for k in p.semkeys if k.startswith("gp")):
        p.op("gp", lambda e: e.nop(), deps=p.front("gp"))



def _kernel_impl(inputs, upto="full"):
    x = np.asarray(inputs["L0_attn_x"], np.float32)
    shared = _prep_shared(inputs)
    nc = build_nc(upto=upto)
    in_maps = []
    for core in range(8):
        m = dict(shared)
        m["x_in"] = np.ascontiguousarray(x[core])
        in_maps.append(m)
    res = run_bass_kernel_spmd(nc, in_maps, core_ids=list(range(8)))
    outs = [np.asarray(r["out"], np.float32) for r in res.results]
    return np.stack(outs, axis=0)


def kernel(**inputs):
    return _kernel_impl(inputs, upto="full")


def emit_ln(K, layer):
    p, ps = K.p, K.ps
    pre = list(K.final_deps)
    K.fpool.push(); K.bpool.push()
    xbf = K.bpool.alloc(8 * 512)
    sq = K.bpool.alloc(2 * 512)
    rows = K.fpool.alloc(4 * 512)
    bcast = K.fpool.alloc(2 * 512)
    tmp = K.fpool.alloc(512)
    mu_row = rows[0:1, 0:512]
    v_row = rows[0:1, 512:1024]
    r_row = rows[0:1, 1024:1536]
    nmr_row = rows[0:1, 1536:2048]
    r_bc = bcast[:, 0:512]
    nmr_bc = bcast[:, 512:1024]
    psMU = K.psC[0:1, 0:512]
    psVAR = K.psD[0:1, 0:512]

    xbf_war = None
    sq_war = [None, None]
    rows_war = None
    bc_war = None
    for g in range(4):
        t0 = 512 * g
        mm_toks = []
        for c in range(8):
            cpt = p.op("ve", (lambda c, t0: lambda e: e.tensor_copy(
                xbf[:, c * 512:(c + 1) * 512], CH(K.xres, c, t0, 512)))(c, t0),
                deps=pre + [xbf_war], sig=True)
            sqt = p.op("se", (lambda c, t0: lambda e: e.activation(
                sq[:, (c % 2) * 512:(c % 2 + 1) * 512],
                CH(K.xres, c, t0, 512), AF.Square))(c, t0),
                deps=pre + [sq_war[c % 2]], sig=True)
            p.op("pe", (lambda c: lambda e: e.matmul(
                psMU, K.ones_bf[:, 0:1], xbf[:, c * 512:(c + 1) * 512],
                start=(c == 0), stop=(c == 7)))(c),
                deps=[cpt] + ([ps.gate("C")] if c == 0 else []))
            p.op("pe", (lambda c: lambda e: e.matmul(
                psVAR, K.ones_bf[:, 0:1], sq[:, (c % 2) * 512:(c % 2 + 1) * 512],
                start=(c == 0), stop=(c == 7)))(c),
                deps=[sqt] + ([ps.gate("D")] if c == 0 else []))
            sq_war[c % 2] = p.sig("pe")
        mmtok = p.sig("pe")
        xbf_war = mmtok
        inv = 1.0 / 1024.0
        t1 = p.op("ve", lambda e: e.tensor_scalar_mul(mu_row, psMU, inv),
                  deps=[mmtok, rows_war], sig=True)
        t2 = p.op("ve", lambda e: e.tensor_scalar_mul(v_row, psVAR, inv),
                  deps=[mmtok], sig=True)
        ps.done("C", t1)
        ps.done("D", t2)
        t3 = p.op("se", lambda e: e.activation(r_row, mu_row, AF.Square),
                  deps=[t1, rows_war], sig=True)
        t4 = p.op("ve", lambda e: e.tensor_sub(v_row, v_row, r_row),
                  deps=[t3], sig=True)
        t5 = p.op("se", lambda e: e.activation(
            r_row, v_row, AF.Sqrt, bias=K.misc[0:1, 0:1]), deps=[t4], sig=True)
        t6 = p.op("ve", lambda e: e.reciprocal(r_row, r_row), deps=[t5], sig=True)
        t7 = p.op("ve", lambda e: e.scalar_tensor_tensor(
            nmr_row, mu_row, -1.0, r_row, ALU.mult, ALU.mult),
            deps=[t6], sig=True)

        if layer == 0:
            trdeps = [t7, ps.gate("A")]
            for j in range(4):
                p.op("pe", (lambda j: lambda e: e.transpose(
                    K.psA[:, j:j + 1], r_row[0:1, j * 128:(j + 1) * 128],
                    K.ident_f32[0:1, 0:1]))(j), deps=trdeps)
                trdeps = []
                p.op("pe", (lambda j: lambda e: e.transpose(
                    K.psA[:, 4 + j:5 + j], nmr_row[0:1, j * 128:(j + 1) * 128],
                    K.ident_f32[0:1, 0:1]))(j))
            ttok = p.sig("pe")
            c1 = p.op("ve", (lambda g: lambda e: e.tensor_copy(
                K.stat_col[:, g * 4:(g + 1) * 4], K.psA[:, 0:4]))(g),
                deps=[ttok], sig=True)
            c2 = p.op("ve", (lambda g: lambda e: e.tensor_copy(
                K.stat_col[:, 16 + g * 4:16 + (g + 1) * 4], K.psA[:, 4:8]))(g),
                deps=[ttok], sig=True)
            ps.done("A", c2)

        p.op("pe", lambda e: e.matmul(
            K.psE[:, 0:512], K.ones_f32[0:1, :], r_row, start=True, stop=True),
            deps=[t7, ps.gate("E")])
        bE = p.sig("pe")
        p.op("pe", lambda e: e.matmul(
            K.psF[:, 0:512], K.ones_f32[0:1, :], nmr_row, start=True, stop=True),
            deps=[t7, ps.gate("F")])
        bF = p.sig("pe")
        rows_war = bF
        e1 = p.op("ve", lambda e: e.tensor_copy(r_bc, K.psE[:, 0:512]),
                  deps=[bE, bc_war], sig=True)
        e2 = p.op("se", lambda e: e.copy(nmr_bc, K.psF[:, 0:512]),
                  deps=[bF, bc_war], sig=True)
        ps.done("E", e1)
        ps.done("F", e2)
        atoks = []
        for c in range(8):
            p.op("ve", (lambda c, t0: lambda e: e.tensor_mul(
                tmp, CH(K.xres, c, t0, 512), r_bc))(c, t0),
                deps=pre + [e1, e2])
            at = p.op("ve", (lambda c, t0: lambda e: e.tensor_add(
                CH(K.xn, c, t0, 512), tmp, nmr_bc))(c, t0), sig=True)
            atoks.append(at)
        bc_war = atoks[-1]
        pre = []
    K.final_deps = p.barrier(("pe", "ve", "se"))
    K.fpool.pop(); K.bpool.pop()
    if K.upto == "ln0":
        for c in range(8):
            p.op("ve", (lambda c: lambda e: e.tensor_copy(
                CH(K.xres, c, 0, T), CH(K.xn, c, 0, T)))(c),
                deps=list(K.final_deps), sig=True)
        K.final_deps = p.barrier(("ve",))


def ld_mini(K, dst2d, src_ap, col0, ncols, deps=()):
    src = src_ap.rearrange("(kc p) f -> p kc f", p=128)[:, :, col0:col0 + ncols]
    dstv = dst2d.rearrange("p (kc f) -> p kc f", f=ncols)
    return K.p.dma("gp", (lambda dstv, src: lambda e: e.dma_start(dstv, src))(dstv, src),
                   deps=deps)


def emit_l0_attn(K):
    p, ps = K.p, K.ps
    pre = list(K.final_deps)
    K.fpool.push(); K.bpool.push()
    mask0 = K.bpool.alloc(2048)
    kring = K.bpool.alloc(8 * 1024)
    vring = K.bpool.alloc(8 * 1024)
    qg = K.bpool.alloc(4 * 1024)
    attnT = K.bpool.alloc(8 * 512)
    wk = K.bpool.alloc(4 * 1024)
    wv = K.bpool.alloc(2 * 2048)
    wtsT = K.bpool.alloc(3 * 512)
    wtsf = K.fpool.alloc(3 * 512)
    wtse = K.fpool.alloc(3 * 512)

    mtok = p.dma("sp", lambda e: e.dma_start(mask0, K.ins["mask0"][:]), deps=pre)

    wkslot_war = [None] * 4
    wvslot_war = [None] * 2
    wk_seq = 0
    attn_pe = [None]
    oproj_pe = None
    wts_war = {"f": [None] * 3, "bf": [None] * 3, "T": [None] * 3}
    smalls_war = [None] * 3
    QCH = {"q": K.ins["qwT"], "k": K.ins["kwT"], "o": K.ins["owT"]}

    for g in range(4):
        t0 = 512 * g
        rc0 = (512 * g) % 1024

        for c in range(8):
            slot = wk_seq % 4
            wk_seq += 1
            sl = wk[:, slot * 1024:(slot + 1) * 1024]
            wt = ld_mini(K, sl, K.ins["kwT"], c * 128, 128,
                         deps=[wkslot_war[slot]] + (pre if g == 0 else []))
            bank = "C" if c % 2 == 0 else "D"
            psb = K.psC if c % 2 == 0 else K.psD
            dep0 = [wt, ps.gate(bank)] + (pre if g == 0 else [])
            for kc in range(8):
                p.op("pe", (lambda sl, kc, c, t0, psb: lambda e: e.matmul(
                    psb[:, 0:512], sl[:, kc * 128:(kc + 1) * 128],
                    CH(K.xn, kc, t0, 512), start=(kc == 0), stop=False))(
                        sl, kc, c, t0, psb),
                    deps=dep0 if kc == 0 else [])
            p.op("pe", (lambda c, psb: lambda e: e.matmul(
                psb[:, 0:512], K.brow_bf[32:33, c * 128:(c + 1) * 128],
                K.ones_bf[32:33, 0:512], start=False, stop=True))(c, psb))
            mmtok = p.sig("pe")
            wkslot_war[slot] = mmtok
            eng = "ve" if c % 2 == 0 else "se"
            dst = kring[:, c * 1024 + rc0: c * 1024 + rc0 + 512]
            if eng == "ve":
                fn = (lambda dst, psb: lambda e: e.tensor_copy(dst, psb[:, 0:512]))(dst, psb)
            else:
                fn = (lambda dst, psb: lambda e: e.copy(dst, psb[:, 0:512]))(dst, psb)
            et = p.op(eng, fn, deps=[mmtok, attn_pe[0]], sig=True)
            ps.done(bank, et)

        for u in range(4):
            su = u % 2
            sl = wv[:, su * 2048:(su + 1) * 2048]
            wt = ld_mini(K, sl, K.ins["vwT"], u * 256, 256,
                         deps=[wvslot_war[su]] + (pre if g == 0 else []))
            for i in range(4):
                gi = 4 * g + i
                s = gi % 8
                bank = "E" if (u * 4 + i) % 2 == 0 else "F"
                psb = K.psE if bank == "E" else K.psF
                dep0 = [wt, ps.gate(bank)]
                for kc in range(8):
                    p.op("pe", (lambda sl, kc, gi, psb: lambda e: e.matmul(
                        psb[:, 0:256], CH(K.xn, kc, gi * 128, 128),
                        sl[:, kc * 256:(kc + 1) * 256],
                        start=(kc == 0), stop=False))(sl, kc, gi, psb),
                        deps=dep0 if kc == 0 else [])
                p.op("pe", (lambda u, psb: lambda e: e.matmul(
                    psb[:, 0:256], K.ones_bf[64:65, 0:128],
                    K.brow_bf[64:65, u * 256:(u + 1) * 256],
                    start=False, stop=True))(u, psb))
                mmtok = p.sig("pe")
                if i == 3:
                    wvslot_war[su] = mmtok
                eng = "ve" if i % 2 == 0 else "se"
                dst = vring[:, s * 1024 + u * 256: s * 1024 + u * 256 + 256]
                if eng == "ve":
                    fn = (lambda dst, psb: lambda e: e.tensor_copy(dst, psb[:, 0:256]))(dst, psb)
                else:
                    fn = (lambda dst, psb: lambda e: e.copy(dst, psb[:, 0:256]))(dst, psb)
                et = p.op(eng, fn, deps=[mmtok, attn_pe[0]], sig=True)
                ps.done(bank, et)

        for c in range(8):
            slot = wk_seq % 4
            wk_seq += 1
            sl = wk[:, slot * 1024:(slot + 1) * 1024]
            wt = ld_mini(K, sl, K.ins["qwT"], c * 128, 128, deps=[wkslot_war[slot]])
            bank = "C" if c % 2 == 0 else "D"
            psb = K.psC if c % 2 == 0 else K.psD
            for i in range(4):
                gi = 4 * g + i
                reg = psb[:, i * 128:(i + 1) * 128]
                dep0 = [wt, ps.gate(bank)] if i == 0 else []
                for kc in range(8):
                    p.op("pe", (lambda sl, kc, gi, reg: lambda e: e.matmul(
                        reg, sl[:, kc * 128:(kc + 1) * 128],
                        CH(K.xn, kc, gi * 128, 128),
                        start=(kc == 0), stop=False))(sl, kc, gi, reg),
                        deps=dep0)
                p.op("pe", (lambda c, reg: lambda e: e.matmul(
                    reg, K.brow_bf[0:1, c * 128:(c + 1) * 128],
                    K.ones_bf[0:1, 0:128], start=False, stop=True))(c, reg))
            mmtok = p.sig("pe")
            wkslot_war[slot] = mmtok
            dstv = qg.rearrange("p (i f) -> p i f", f=1024)[:, :, c * 128:(c + 1) * 128]
            psv = psb.rearrange("p (i f) -> p i f", f=128)
            eng = "ve" if c % 2 == 0 else "se"
            if eng == "ve":
                fn = (lambda dstv, psv: lambda e: e.tensor_copy(dstv, psv))(dstv, psv)
            else:
                fn = (lambda dstv, psv: lambda e: e.copy(dstv, psv))(dstv, psv)
            et = p.op(eng, fn, deps=[mmtok, attn_pe[0]], sig=True)
            ps.done(bank, et)

        filldone = p.barrier(("ve", "se"))
        new_attn_pe = None
        seq = [(i, h) for i in range(4) for h in range(16)]
        st8 = {}

        def emitA(it):
            i, h = seq[it]
            qi = 4 * g + i
            b = qi // 2
            parity = qi % 2
            mv = parity + 2 * (1 if b == 0 else 0)
            msl = mask0[:, mv * 512:(mv + 1) * 512]
            rc = (256 * b - 256) % 1024
            sbank = ("C", "D", "A")[it % 3]
            psS = (K.psC, K.psD, K.psA)[it % 3]
            ch = h // 2
            r0 = 64 * (h % 2)
            qsl = qg[r0:r0 + 64, i * 1024 + ch * 128: i * 1024 + (ch + 1) * 128]
            krow = lambda a, b_: kring[r0:r0 + 64, ch * 1024 + a: ch * 1024 + b_]
            sdep = [mtok, ps.gate(sbank)] + (filldone if it == 0 else [])
            if b == 0:
                p.op("pe", (lambda qsl, psS, k1: lambda e: e.matmul(
                    psS[:, 256:512], qsl, k1, start=True, stop=True))(
                        qsl, psS, krow(0, 256)), deps=sdep)
                stok = p.sig("pe")
                stok = p.op("ve", (lambda psS: lambda e: e.memset(
                    psS[:, 0:256], 0.0))(psS), deps=[stok], sig=True)
            elif rc == 768:
                p.op("pe", (lambda qsl, psS, k1: lambda e: e.matmul(
                    psS[:, 0:256], qsl, k1, start=True, stop=True))(
                        qsl, psS, krow(768, 1024)), deps=sdep)
                p.op("pe", (lambda qsl, psS, k2: lambda e: e.matmul(
                    psS[:, 256:512], qsl, k2, start=True, stop=True))(
                        qsl, psS, krow(0, 256)))
                stok = p.sig("pe")
            else:
                p.op("pe", (lambda qsl, psS, k1: lambda e: e.matmul(
                    psS[:, 0:512], qsl, k1, start=True, stop=True))(
                        qsl, psS, krow(rc, rc + 512)), deps=sdep)
                stok = p.sig("pe")
            wf = wtsf[:, (it % 3) * 512:(it % 3 + 1) * 512]
            wb = wtse[:, (it % 3) * 512:(it % 3 + 1) * 512]
            wT = wtsT[:, (it % 3) * 512:(it % 3 + 1) * 512]
            t1 = p.op("ve", (lambda wf, psS, msl: lambda e: e.scalar_tensor_tensor(
                wf, psS[:, 0:512], 0.125, msl, ALU.mult, ALU.add))(wf, psS, msl),
                deps=[stok, wts_war["f"][it % 3]], sig=True)
            ps.done(sbank, t1)
            scol = K.smalls[:, (it % 3) * 4 + 1:(it % 3) * 4 + 2]
            rcol = K.smalls[:, (it % 3) * 4 + 2:(it % 3) * 4 + 3]
            t3 = p.op("se", (lambda wb, wf, scol: lambda e: e.activation(
                wb, wf, AF.Exp, bias=K.misc[:, 2:3], accum_out=scol))(wb, wf, scol),
                deps=[t1, wts_war["bf"][it % 3], smalls_war[it % 3]], sig=True)
            wts_war["f"][it % 3] = t3
            st8[it] = (wf, wb, wT, scol, rcol, t3)

        def emitRest(it):
            nonlocal new_attn_pe
            i, h = seq[it]
            qi = 4 * g + i
            b = qi // 2
            ch = h // 2
            r0 = 64 * (h % 2)
            wf, wb, wT, scol, rcol, t3 = st8.pop(it)
            t4 = p.op("ve", (lambda rcol, scol: lambda e: e.reciprocal(
                rcol, scol))(rcol, scol), deps=[t3], sig=True)
            smalls_war[it % 3] = t4
            t5 = p.op("ve", (lambda wb, rcol: lambda e: e.tensor_single_scalar(
                wb, wb, rcol, ALU.mult))(wb, rcol), sig=True)
            valid = [2, 3] if b == 0 else [0, 1, 2, 3]
            tbank = ("G", "H", "B")[it % 3]
            psT = (K.psG, K.psH, K.psB)[it % 3]
            tdep = [t5, ps.gate(tbank)]
            for kc in valid:
                p.op("pe", (lambda psT, wb, kc: lambda e: e.transpose(
                    psT[:, kc * 128:(kc + 1) * 128],
                    wb[:, kc * 128:(kc + 1) * 128], K.ident_f32))(psT, wb, kc),
                    deps=tdep)
                tdep = []
            ttok = p.sig("pe")
            wts_war["bf"][it % 3] = ttok
            lo = valid[0] * 128
            hi = (valid[-1] + 1) * 128
            eng = "ve" if it % 2 == 0 else "se"
            if eng == "ve":
                fn = (lambda wT, psT, lo, hi: lambda e: e.tensor_copy(
                    wT[:, lo:hi], psT[:, lo:hi]))(wT, psT, lo, hi)
            else:
                fn = (lambda wT, psT, lo, hi: lambda e: e.copy(
                    wT[:, lo:hi], psT[:, lo:hi]))(wT, psT, lo, hi)
            t6 = p.op(eng, fn, deps=[ttok, wts_war["T"][it % 3]], sig=True)
            ps.done(tbank, t6)
            pbank = "E" if (h // 2) % 2 == 0 else "F"
            psP = K.psE if (h // 2) % 2 == 0 else K.psF
            avdep = [t6] + ([ps.gate(pbank)] if h % 2 == 0 else [])
            for kidx, kc in enumerate(valid):
                gkc = 2 * b - 2 + kc
                s = gkc % 8
                vsl = vring[:, s * 1024 + h * 64: s * 1024 + h * 64 + 64]
                p.op("pe", (lambda psP, r0, vsl, wT, kc, st, sp: lambda e: e.matmul(
                    psP[r0:r0 + 64, 0:128], vsl, wT[:, kc * 128:(kc + 1) * 128],
                    start=st, stop=sp))(
                        psP, r0, vsl, wT, kc, kc == valid[0], kc == valid[-1]),
                    deps=avdep if kidx == 0 else [])
            avtok = p.sig("pe")
            wts_war["T"][it % 3] = avtok
            new_attn_pe = avtok
            if h % 2 == 1:
                dst = attnT[:, ch * 512 + i * 128: ch * 512 + (i + 1) * 128]
                eng = "ve" if ch % 2 == 0 else "se"
                if eng == "ve":
                    fn = (lambda dst, psP: lambda e: e.tensor_copy(
                        dst, psP[:, 0:128]))(dst, psP)
                else:
                    fn = (lambda dst, psP: lambda e: e.copy(
                        dst, psP[:, 0:128]))(dst, psP)
                pv = p.op(eng, fn, deps=[avtok, oproj_pe], sig=True)
                ps.done(pbank, pv)

        for it in range(len(seq) + 1):
            if it < len(seq):
                emitA(it)
            if it >= 1:
                emitRest(it - 1)
        attn_pe[0] = new_attn_pe
        attn_evacs = p.barrier(("ve", "se"))

        for c in range(8):
            slot = wk_seq % 4
            wk_seq += 1
            sl = wk[:, slot * 1024:(slot + 1) * 1024]
            wt = ld_mini(K, sl, K.ins["owT"], c * 128, 128, deps=[wkslot_war[slot]])
            bank = "C" if c % 2 == 0 else "D"
            psb = K.psC if c % 2 == 0 else K.psD
            dep0 = [wt, ps.gate(bank)] + (attn_evacs if c == 0 else [])
            for kc in range(8):
                p.op("pe", (lambda sl, kc, psb, attnT: lambda e: e.matmul(
                    psb[:, 0:512], sl[:, kc * 128:(kc + 1) * 128],
                    attnT[:, kc * 512:(kc + 1) * 512],
                    start=(kc == 0), stop=(kc == 7)))(sl, kc, psb, attnT),
                    deps=dep0 if kc == 0 else [])
            mmtok = p.sig("pe")
            wkslot_war[slot] = mmtok
            ob = K.l0bias[:, 24 + c:25 + c]
            et = p.op("ve", (lambda c, t0, psb, ob: lambda e: e.scalar_tensor_tensor(
                CH(K.xres, c, t0, 512), psb[:, 0:512], ob,
                CH(K.xres, c, t0, 512), ALU.add, ALU.add))(c, t0, psb, ob),
                deps=[mmtok], sig=True)
            ps.done(bank, et)
        oproj_pe = p.sig("pe")

    K.final_deps = p.barrier(("pe", "ve", "se"))
    K.fpool.pop(); K.bpool.pop()




def emit_router(K):
    p, ps = K.p, K.ps
    pre = list(K.final_deps)
    K.fpool.push()
    lg = K.fpool.alloc(8)
    lg2 = K.fpool.alloc(8)
    s_bc = K.fpool.alloc(8)
    dep0 = pre + [ps.gate("C")]
    for c in range(8):
        p.op("pe", (lambda c: lambda e: e.matmul(
            K.psC[0:1, 0:8], K.ones_f32[:, 0:1], K.wr_s[:, c * 8:(c + 1) * 8],
            start=(c == 0), stop=(c == 7)))(c), deps=dep0 if c == 0 else [])
    t = p.sig("pe")
    sv = p.op("ve", lambda e: e.tensor_copy(K.smalls[0:1, 32:40], K.psC[0:1, 0:8]),
              deps=[t], sig=True)
    ps.done("C", sv)
    p.op("pe", lambda e: e.matmul(K.psD[:, 0:8], K.ones_f32[0:1, :],
                                  K.smalls[0:1, 32:40], start=True, stop=True),
         deps=[sv, ps.gate("D")])
    t = p.sig("pe")
    sb = p.op("ve", lambda e: e.tensor_copy(s_bc, K.psD[:, 0:8]), deps=[t], sig=True)
    ps.done("D", sb)
    for i in range(16):
        dep0 = [ps.gate("C")]
        for c in range(8):
            p.op("pe", (lambda c, i: lambda e: e.matmul(
                K.psC[:, 0:8], CH(K.xres, c, i * 128, 128),
                K.wr_s[:, c * 8:(c + 1) * 8],
                start=(c == 0), stop=(c == 7)))(c, i), deps=dep0 if c == 0 else [])
        mmt = p.sig("pe")
        t1 = p.op("ve", (lambda i: lambda e: e.tensor_single_scalar(
            lg, K.psC[:, 0:8], K.stat_col[:, i:i + 1], ALU.mult))(i),
            deps=[mmt, sb], sig=True)
        ps.done("C", t1)
        p.op("ve", (lambda i: lambda e: e.scalar_tensor_tensor(
            lg2, s_bc, K.stat_col[:, 16 + i:17 + i], lg, ALU.mult, ALU.add))(i))
        p.op("ve", lambda e: e.tensor_add(lg, lg2, K.br_s))
        m = K.smalls[:, 8:9]
        nm = K.smalls[:, 9:10]
        ss = K.smalls[:, 10:11]
        rs = K.smalls[:, 11:12]
        p.op("ve", lambda e: e.reduce_max(m, lg, axis=AX.X))
        t5 = p.op("ve", lambda e: e.tensor_single_scalar(lg2, lg, m, ALU.is_ge),
                  sig=True)
        t6 = p.op("ve", lambda e: e.tensor_scalar_mul(nm, m, -1.0), sig=True)
        t7 = p.op("se", lambda e: e.activation(lg, lg, AF.Exp, bias=nm,
                                               accum_out=ss),
                  deps=[t5, t6], sig=True)
        t8 = p.op("ve", lambda e: e.reciprocal(rs, ss), deps=[t7], sig=True)
        t9 = p.op("ve", (lambda i: lambda e: e.tensor_single_scalar(
            K.coef[:, 8 * i:8 * (i + 1)], lg2, rs, ALU.mult))(i), sig=True)
        p.op("pe", (lambda i: lambda e: e.transpose(
            K.psA[0:8, 0:128], K.coef[:, 8 * i:8 * (i + 1)], K.ident_f32))(i),
            deps=[t9, ps.gate("A")])
        tt = p.sig("pe")
        tc = p.op("ve", (lambda i: lambda e: e.tensor_copy(
            K.coefT_bf[0:8, i * 128:(i + 1) * 128], K.psA[0:8, 0:128]))(i),
            deps=[tt], sig=True)
        ps.done("A", tc)
    K.final_deps = p.barrier(("pe", "ve", "se"))
    K.fpool.pop()


def emit_moe(K):
    p, ps = K.p, K.ps
    pre = list(K.final_deps)
    K.fpool.push(); K.bpool.push()
    h_all = K.bpool.alloc(32 * 512)
    w1b = K.bpool.alloc(2 * 4096)
    w2b = K.bpool.alloc(2 * 4096)
    cbc = K.fpool.alloc(2 * 512)
    tmp = K.fpool.alloc(512)
    w1war = [None, None]
    w2war = [None, None]
    cbcwar = [None, None]
    h_war = None
    u1 = 0
    u2 = 0
    K.w1sc_tok = {}
    K.w2sc_tok = {}
    first = pre
    for g in range(4):
        t0 = g * 512
        for e8 in range(8):
            csl = (g * 8 + e8) % 2
            cb = cbc[:, csl * 512:(csl + 1) * 512]
            p.op("pe", (lambda e8, t0: lambda e: e.matmul(
                K.psG[:, 0:512], K.basis[:, e8 * 128:(e8 + 1) * 128],
                K.coefT_bf[0:8, t0:t0 + 512], start=True, stop=True))(e8, t0),
                deps=first + [ps.gate("G")])
            first = []
            bt = p.sig("pe")
            cbt = p.op("ve", (lambda cb: lambda e: e.tensor_copy(
                cb, K.psG[:, 0:512]))(cb), deps=[bt, cbcwar[csl]], sig=True)
            ps.done("G", cbt)
            for fb in range(8):
                slot = u1 % 2
                u1 += 1
                w1u = w1b[:, slot * 4096:(slot + 1) * 4096]
                if g == 0:
                    src = K.ins["w1"][e8].rearrange(
                        "(kc p) f -> p kc f", p=128)[:, :, fb * 512:(fb + 1) * 512]
                    dstv = w1u.rearrange("p (kc f) -> p kc f", f=512)
                    wt = p.dma("gp", (lambda dstv, src: lambda e: e.dma_start(
                        dstv, src))(dstv, src), deps=[w1war[slot]] + pre)
                else:
                    wt = p.dma("sp", (lambda w1u, e8, fb: lambda e: e.dma_start(
                        w1u, K.w1sc[e8][fb]))(w1u, e8, fb),
                        deps=[w1war[slot], K.w1sc_tok[(e8, fb)]])
                for fc in range(4):
                    hidx = fb * 4 + fc
                    bank = "C" if hidx % 2 == 0 else "D"
                    psb = K.psC if hidx % 2 == 0 else K.psD
                    dep0 = [wt, ps.gate(bank)]
                    for kc in range(8):
                        p.op("pe", (lambda w1u, kc, fc, t0, psb: lambda e: e.matmul(
                            psb[:, 0:512], w1u[:, kc * 512 + fc * 128: kc * 512 + (fc + 1) * 128],
                            CH(K.xn, kc, t0, 512),
                            start=(kc == 0), stop=(kc == 7)))(w1u, kc, fc, t0, psb),
                            deps=dep0 if kc == 0 else [])
                    mmtok = p.sig("pe")
                    if fc == 3:
                        if g == 0:
                            stok_ = p.dma("sp", (lambda w1u, e8, fb: lambda e:
                                e.dma_start(K.w1sc[e8][fb], w1u))(w1u, e8, fb),
                                deps=[wt])
                            K.w1sc_tok[(e8, fb)] = stok_
                            w1war[slot] = [mmtok, stok_]
                        else:
                            w1war[slot] = mmtok
                    et = p.op("se", (lambda hidx, e8, psb: lambda e: e.activation(
                        h_all[:, hidx * 512:(hidx + 1) * 512], psb[:, 0:512],
                        AF.Gelu, bias=K.b1lay[:, e8 * 32 + hidx:e8 * 32 + hidx + 1]))(
                            hidx, e8, psb),
                        deps=[mmtok, h_war], sig=True)
                    ps.done(bank, et)
            hdone = p.barrier(("se",))
            for dc in range(8):
                slot = u2 % 2
                u2 += 1
                w2u = w2b[:, slot * 4096:(slot + 1) * 4096]
                if g == 0:
                    src = K.ins["w2lay"][e8][dc]
                    wt = p.dma("gp", (lambda w2u, src: lambda e: e.dma_start(
                        w2u, src))(w2u, src), deps=[w2war[slot]] + pre)
                else:
                    wt = p.dma("sp", (lambda w2u, e8, dc: lambda e: e.dma_start(
                        w2u, K.w2sc[e8][dc]))(w2u, e8, dc),
                        deps=[w2war[slot], K.w2sc_tok[(e8, dc)]])
                bank = "E" if dc % 2 == 0 else "F"
                psb = K.psE if dc % 2 == 0 else K.psF
                dep0 = [wt, ps.gate(bank)] + hdone
                for fk in range(32):
                    p.op("pe", (lambda w2u, fk, psb: lambda e: e.matmul(
                        psb[:, 0:512], w2u[:, fk * 128:(fk + 1) * 128],
                        h_all[:, fk * 512:(fk + 1) * 512],
                        start=(fk == 0), stop=(fk == 31)))(w2u, fk, psb),
                        deps=dep0 if fk == 0 else [])
                mmtok = p.sig("pe")
                if g == 0:
                    stok_ = p.dma("sp", (lambda w2u, e8, dc: lambda e:
                        e.dma_start(K.w2sc[e8][dc], w2u))(w2u, e8, dc), deps=[wt])
                    K.w2sc_tok[(e8, dc)] = stok_
                    w2war[slot] = [mmtok, stok_]
                else:
                    w2war[slot] = mmtok
                tm = p.op("ve", (lambda psb, cb: lambda e: e.tensor_mul(
                    tmp, psb[:, 0:512], cb))(psb, cb), deps=[mmtok, cbt], sig=True)
                ta = p.op("ve", (lambda dc, t0: lambda e: e.tensor_add(
                    CH(K.xres, dc, t0, 512), CH(K.xres, dc, t0, 512), tmp))(dc, t0),
                    sig=True)
                ps.done(bank, ta)
                if dc == 7:
                    cbcwar[csl] = tm
            h_war = p.sig("pe")
        for dc in range(8):
            bank = "A" if dc % 2 == 0 else "B"
            psb = K.psA if dc % 2 == 0 else K.psB
            p.op("pe", (lambda dc, t0, psb: lambda e: e.matmul(
                psb[:, 0:512], K.b2bf[0:8, dc * 128:(dc + 1) * 128],
                K.coefT_bf[0:8, t0:t0 + 512], start=True, stop=True))(dc, t0, psb),
                deps=[ps.gate(bank)])
            mmtok = p.sig("pe")
            ta = p.op("ve", (lambda dc, t0, psb: lambda e: e.tensor_add(
                CH(K.xres, dc, t0, 512), CH(K.xres, dc, t0, 512),
                psb[:, 0:512]))(dc, t0, psb), deps=[mmtok], sig=True)
            ps.done(bank, ta)
    K.final_deps = p.barrier(("pe", "ve", "se"))
    K.fpool.pop(); K.bpool.pop()




def emit_l1_attn(K):
    p, ps = K.p, K.ps
    pre = list(K.final_deps)
    K.fpool.push(); K.bpool.push()
    maskc = K.bpool.alloc(2048)
    v_h = K.bpool.alloc(16 * 256)
    kd_h = K.bpool.alloc(2048)
    qd = K.bpool.alloc(2 * 512)
    wch = K.bpool.alloc(2 * 512)
    ret = K.bpool.alloc(2 * 1024)
    gate = K.bpool.alloc(2 * 1024)
    wqu = K.bpool.alloc(4 * 2048)
    xps = K.bpool.alloc(4 * 512)
    sqb = K.bpool.alloc(2 * 512)
    kr = K.bpool.alloc(1024).bitcast(F32)
    ks = K.bpool.alloc(1024).bitcast(F32)
    rt1 = K.bpool.alloc(1024).bitcast(F32)
    possl = K.fpool.alloc(2 * 512)
    dkt = K.fpool.alloc(2 * 512)
    rows = K.fpool.alloc(4 * 512)
    bcast = K.fpool.alloc(2 * 512)
    mu_row = rows[0:1, 0:512]
    v_row = rows[0:1, 512:1024]
    r_row = rows[0:1, 1024:1536]
    nmr_row = rows[0:1, 1536:2048]
    r_bc = bcast[:, 0:512]
    nmr_bc = bcast[:, 512:1024]
    SWAP = []
    for i in range(16):
        SWAP += [2 * i + 1, 2 * i]
    CINV = 1.0 / np.sqrt(128.0)

    mtok = p.dma("sp", lambda e: e.dma_start(maskc, K.ins["maskc"][:]), deps=pre)

    lr = K.smalls[0:1, 48:56]
    t = p.op("se", lambda e: e.activation(lr, K.decay_s, AF.Ln),
             deps=pre, sig=True)
    p.op("pe", lambda e: e.matmul(K.psA[:, 0:8], K.ones_f32[0:1, :], lr,
                                  start=True, stop=True),
         deps=[t, ps.gate("A")])
    t = p.sig("pe")
    t1 = p.op("ve", lambda e: e.tensor_copy(K.lnd[:, 0:8], K.psA[:, 0:8]),
              deps=[t], sig=True)
    ps.done("A", t1)
    t2 = p.op("ve", lambda e: e.tensor_scalar_mul(K.lnd[:, 8:16],
                                                  K.lnd[:, 0:8], -1.0), sig=True)
    lnd_done = t2

    useq = [0]
    uwar = [None, None, None, None]
    xwar = [None, None]
    poswar = [None, None]
    dktwar = [None, None]
    retwar = [None, None]
    gatewar = [None, None]
    sq_war = None
    qdwar = [None, None]
    wchwar = [None, None]
    kdwar = None
    vwar = None

    def wunit(cols0, ncols, deps):
        slot = useq[0] % 4
        useq[0] += 1
        sl = wqu[:, slot * 2048: slot * 2048 + 8 * ncols]
        src = K.ins["wqkv"].rearrange("(kc p) f -> p kc f", p=128)[
            :, :, cols0:cols0 + ncols]
        dstv = sl.rearrange("p (kc f) -> p kc f", f=ncols)
        tok = p.dma("gp", (lambda dstv, src: lambda e: e.dma_start(dstv, src))(
            dstv, src), deps=[uwar[slot]] + deps)
        return slot, sl, tok

    def wunit_from(name, rows0, nrows, cols0, ncols, deps):
        slot = useq[0] % 4
        useq[0] += 1
        sl = wqu[:, slot * 2048: slot * 2048 + (nrows // 128) * ncols]
        src = K.ins[name][rows0:rows0 + nrows].rearrange(
            "(kc p) f -> p kc f", p=128)[:, :, cols0:cols0 + ncols]
        dstv = sl.rearrange("p (kc f) -> p kc f", f=ncols)
        tok = p.dma("gp", (lambda dstv, src: lambda e: e.dma_start(dstv, src))(
            dstv, src), deps=[uwar[slot]] + deps)
        return slot, sl, tok

    def rotary_decay(g, h, raw_bank, psb, dst, qk, dep0, dstwar=None):
        sl = g % 2
        cs_s = xps[:, sl * 1024: sl * 1024 + 512]
        sn_s = xps[:, sl * 1024 + 512: sl * 1024 + 1024]
        csn = "csq" if qk == "q" else "csk"
        snn = "snq" if qk == "q" else "snk"
        ct = p.dma("sp", (lambda cs_s, csn, h, g: lambda e: e.dma_start(
            cs_s, K.ins[csn][128 * h:128 * (h + 1), 512 * g:512 * (g + 1)]))(
                cs_s, csn, h, g), deps=[xwar[sl]])
        st_ = p.dma("sp", (lambda sn_s, snn, h, g: lambda e: e.dma_start(
            sn_s, K.ins[snn][128 * h:128 * (h + 1), 512 * g:512 * (g + 1)]))(
                sn_s, snn, h, g), deps=[xwar[sl]])
        pt = p.dma("sp", (lambda g: lambda e: e.dma_start(
            possl[:, (g % 2) * 512:(g % 2 + 1) * 512],
            K.ins["pos"][:, 512 * g:512 * (g + 1)]))(g), deps=[poswar[g % 2]])
        pos_s = possl[:, (g % 2) * 512:(g % 2 + 1) * 512]
        e1 = p.op("ve", (lambda psb: lambda e: e.tensor_copy(kr, psb[:, 0:512]))(psb),
                  deps=dep0, sig=True)
        ps.done(raw_bank, e1)
        p.op("ve", lambda e: e.stream_shuffle(ks, kr, SWAP))
        p.op("ve", (lambda cs_s: lambda e: e.tensor_mul(rt1, kr, cs_s))(cs_s),
             deps=[ct])
        p.op("ve", (lambda sn_s: lambda e: e.tensor_mul(ks, ks, sn_s))(sn_s),
             deps=[st_])
        s1 = p.op("ve", lambda e: e.tensor_add(rt1, rt1, ks), sig=True)
        lcol = K.lnd[:, h:h + 1] if qk == "q" else K.lnd[:, 8 + h:9 + h]
        dk_s = dkt[:, (g % 2) * 512:(g % 2 + 1) * 512]
        d1 = p.op("se", (lambda dk_s, pos_s, lcol: lambda e: e.activation(
            dk_s, pos_s, AF.Exp, scale=lcol))(dk_s, pos_s, lcol),
            deps=[pt, lnd_done, dktwar[g % 2]], sig=True)
        poswar[g % 2] = d1
        o1 = p.op("ve", (lambda dst, dk_s: lambda e: e.tensor_mul(
            dst, rt1, dk_s))(dst, dk_s), deps=[s1, d1, dstwar], sig=True)
        dktwar[g % 2] = o1
        xwar[sl] = o1
        return o1

    for h in range(8):
        slot, ku, ktok = wunit(1024 + 128 * h, 128, pre if h == 0 else [])
        kd_toks = []
        for g in range(4):
            bank = "C" if g % 2 == 0 else "D"
            psb = K.psC if g % 2 == 0 else K.psD
            dep0 = [ktok, ps.gate(bank), kdwar]
            for kc in range(8):
                p.op("pe", (lambda ku, kc, g, psb: lambda e: e.matmul(
                    psb[:, 0:512], ku[:, kc * 128:(kc + 1) * 128],
                    CH(K.xn, kc, g * 512, 512),
                    start=(kc == 0), stop=(kc == 7)))(ku, kc, g, psb),
                    deps=dep0 if kc == 0 else [])
            mmtok = p.sig("pe")
            if g == 3:
                uwar[slot] = mmtok
            kd_toks.append(rotary_decay(
                g, h, bank, psb, kd_h[:, g * 512:(g + 1) * 512], "k", [mmtok],
                dstwar=kdwar))
        slot, vu, vtok = wunit(2048 + 256 * h, 256, [])
        v_toks = []
        for gi in range(16):
            bank = "E" if gi % 2 == 0 else "F"
            psb = K.psE if gi % 2 == 0 else K.psF
            dep0 = [vtok, ps.gate(bank), vwar]
            for kc in range(8):
                p.op("pe", (lambda vu, kc, gi, psb: lambda e: e.matmul(
                    psb[:, 0:256], CH(K.xn, kc, gi * 128, 128),
                    vu[:, kc * 256:(kc + 1) * 256],
                    start=(kc == 0), stop=(kc == 7)))(vu, kc, gi, psb),
                    deps=dep0 if kc == 0 else [])
            mmtok = p.sig("pe")
            if gi == 15:
                uwar[slot] = mmtok
            eng = "ve" if gi % 2 == 0 else "se"
            dst = v_h[:, gi * 256:(gi + 1) * 256]
            if eng == "ve":
                fn = (lambda dst, psb: lambda e: e.tensor_copy(dst, psb[:, 0:256]))(dst, psb)
            else:
                fn = (lambda dst, psb: lambda e: e.copy(dst, psb[:, 0:256]))(dst, psb)
            et = p.op(eng, fn, deps=[mmtok], sig=True)
            ps.done(bank, et)
            v_toks.append(et)
        sqg, qu, qtok = wunit(128 * h, 128, [])
        sgg, gu, gtok = wunit_from("wg", 0, 1024, 256 * h, 256, [])
        spg, pu, ptok = wunit_from("wproj1", 256 * h, 256, 0, 1024, [])
        q_used = None
        g_used = None
        p_used = None
        for g in range(4):
            t0 = 512 * g
            bank = "C" if g % 2 == 0 else "D"
            psb = K.psC if g % 2 == 0 else K.psD
            dep0 = [qtok, ps.gate(bank), qdwar[g % 2]]
            for kc in range(8):
                p.op("pe", (lambda qu, kc, t0, psb: lambda e: e.matmul(
                    psb[:, 0:512], qu[:, kc * 128:(kc + 1) * 128],
                    CH(K.xn, kc, t0, 512),
                    start=(kc == 0), stop=(kc == 7)))(qu, kc, t0, psb),
                    deps=dep0 if kc == 0 else [])
            mmtok = p.sig("pe")
            q_used = mmtok
            qsl = qd[:, (g % 2) * 512:(g % 2 + 1) * 512]
            qdone = rotary_decay(g, h, bank, psb, qsl, "q", [mmtok],
                                 dstwar=qdwar[g % 2])
            nkc = 4 * g + 4
            avdep0 = [ps.gate("G")]
            avdep1 = [ps.gate("H")]
            av_last = None
            ets = [None] * nkc
            def emit_scores(kc):
                bank = "C" if kc % 2 == 0 else "D"
                psb = K.psC if kc % 2 == 0 else K.psD
                p.op("pe", (lambda kc, qsl, psb: lambda e: e.matmul(
                    psb[:, 0:512], kd_h[:, kc * 128:(kc + 1) * 128], qsl,
                    start=True, stop=True))(kc, qsl, psb),
                    deps=[qdone, kd_toks[kc // 4], ps.gate(bank)])
                smt = p.sig("pe")
                wsl = wch[:, (kc % 2) * 512:(kc % 2 + 1) * 512]
                d = kc - 4 * g
                if d >= 0:
                    et = p.op("ve", (lambda wsl, psb, d: lambda e: e.scalar_tensor_tensor(
                        wsl, psb[:, 0:512], CINV,
                        maskc[:, d * 512:(d + 1) * 512], ALU.mult, ALU.mult))(
                            wsl, psb, d), deps=[smt, mtok, wchwar[kc % 2]], sig=True)
                else:
                    et = p.op("ve", (lambda wsl, psb: lambda e: e.tensor_single_scalar(
                        wsl, psb[:, 0:512], CINV, ALU.mult))(wsl, psb),
                        deps=[smt, wchwar[kc % 2]], sig=True)
                ps.done(bank, et)
                ets[kc] = (et, wsl)
            emit_scores(0)
            for kc in range(nkc):
                if kc + 1 < nkc:
                    emit_scores(kc + 1)
                et, wsl = ets[kc]
                for dvc in range(2):
                    psav = K.psG if dvc == 0 else K.psH
                    vsl = v_h[:, kc * 256 + dvc * 128: kc * 256 + (dvc + 1) * 128]
                    dep = [et, v_toks[kc]] + (avdep0 if dvc == 0 else avdep1)
                    p.op("pe", (lambda psav, vsl, wsl, st, sp: lambda e: e.matmul(
                        psav[:, 0:512], vsl, wsl, start=st, stop=sp))(
                            psav, vsl, wsl, kc == 0, kc == nkc - 1),
                        deps=dep)
                    if dvc == 0:
                        avdep0 = []
                    else:
                        avdep1 = []
                av_last = p.sig("pe")
                wchwar[kc % 2] = av_last
            qdwar[g % 2] = av_last
            rsl = ret[:, (g % 2) * 1024:(g % 2 + 1) * 1024]
            e1 = p.op("ve", (lambda rsl: lambda e: e.tensor_copy(
                rsl[:, 0:512], K.psG[:, 0:512]))(rsl),
                deps=[av_last, retwar[g % 2]], sig=True)
            ps.done("G", e1)
            e2 = p.op("se", (lambda rsl: lambda e: e.copy(
                rsl[:, 512:1024], K.psH[:, 0:512]))(rsl),
                deps=[av_last, retwar[g % 2]], sig=True)
            ps.done("H", e2)
            q1 = p.op("se", (lambda rsl: lambda e: e.activation(
                sqb[:, 0:512], rsl[:, 0:512], AF.Square))(rsl),
                deps=[e1, sq_war], sig=True)
            q2 = p.op("se", (lambda rsl: lambda e: e.activation(
                sqb[:, 512:1024], rsl[:, 512:1024], AF.Square))(rsl),
                deps=[e2], sig=True)
            dep0 = [e1, ps.gate("A")]
            p.op("pe", (lambda rsl: lambda e: e.matmul(
                K.psA[0:1, 0:512], K.ones_bf[:, 0:1], rsl[:, 0:512],
                start=True, stop=False))(rsl), deps=dep0)
            p.op("pe", (lambda rsl: lambda e: e.matmul(
                K.psA[0:1, 0:512], K.ones_bf[:, 0:1], rsl[:, 512:1024],
                start=False, stop=True))(rsl), deps=[e2])
            smu = p.sig("pe")
            p.op("pe", lambda e: e.matmul(
                K.psB[0:1, 0:512], K.ones_bf[:, 0:1], sqb[:, 0:512],
                start=True, stop=False), deps=[q1, ps.gate("B")])
            p.op("pe", lambda e: e.matmul(
                K.psB[0:1, 0:512], K.ones_bf[:, 0:1], sqb[:, 512:1024],
                start=False, stop=True), deps=[q2])
            svar = p.sig("pe")
            sq_war = svar
            inv = 1.0 / 256.0
            t1 = p.op("ve", lambda e: e.tensor_scalar_mul(mu_row, K.psA[0:1, 0:512], inv),
                      deps=[smu], sig=True)
            ps.done("A", t1)
            t2 = p.op("ve", lambda e: e.tensor_scalar_mul(v_row, K.psB[0:1, 0:512], inv),
                      deps=[svar], sig=True)
            ps.done("B", t2)
            t3 = p.op("se", lambda e: e.activation(r_row, mu_row, AF.Square),
                      deps=[t1], sig=True)
            t4 = p.op("ve", lambda e: e.tensor_sub(v_row, v_row, r_row),
                      deps=[t3], sig=True)
            t5 = p.op("se", lambda e: e.activation(
                r_row, v_row, AF.Sqrt, bias=K.misc[0:1, 0:1]), deps=[t4], sig=True)
            t6 = p.op("ve", lambda e: e.reciprocal(r_row, r_row), deps=[t5], sig=True)
            t7 = p.op("ve", lambda e: e.scalar_tensor_tensor(
                nmr_row, mu_row, -1.0, r_row, ALU.mult, ALU.mult),
                deps=[t6], sig=True)
            p.op("pe", lambda e: e.matmul(
                K.psA[:, 0:512], K.ones_f32[0:1, :], r_row, start=True, stop=True),
                deps=[t7, ps.gate("A")])
            bE = p.sig("pe")
            p.op("pe", lambda e: e.matmul(
                K.psB[:, 0:512], K.ones_f32[0:1, :], nmr_row, start=True, stop=True),
                deps=[t7, ps.gate("B")])
            bF = p.sig("pe")
            e3 = p.op("ve", lambda e: e.tensor_copy(r_bc, K.psA[:, 0:512]),
                      deps=[bE], sig=True)
            ps.done("A", e3)
            e4 = p.op("se", lambda e: e.copy(nmr_bc, K.psB[:, 0:512]),
                      deps=[bF], sig=True)
            ps.done("B", e4)
            gsl = gate[:, (g % 2) * 1024:(g % 2 + 1) * 1024]
            for u2 in range(2):
                bank = "E" if u2 == 0 else "F"
                psb = K.psE if u2 == 0 else K.psF
                dep0 = [gtok, ps.gate(bank), gatewar[g % 2]]
                for kc in range(8):
                    p.op("pe", (lambda gu, kc, u2, t0, psb: lambda e: e.matmul(
                        psb[:, 0:512], gu[:, kc * 256 + u2 * 128: kc * 256 + (u2 + 1) * 128],
                        CH(K.xn, kc, t0, 512),
                        start=(kc == 0), stop=(kc == 7)))(gu, kc, u2, t0, psb),
                        deps=dep0 if kc == 0 else [])
                mmtok = p.sig("pe")
                g_used = mmtok
                et = p.op("se", (lambda gsl, u2, psb: lambda e: e.activation(
                    gsl[:, u2 * 512:(u2 + 1) * 512], psb[:, 0:512], AF.Silu))(
                        gsl, u2, psb), deps=[mmtok], sig=True)
                ps.done(bank, et)
            gate_done = p.sig("se")
            ptoks = []
            for u2 in range(2):
                p.op("ve", (lambda rsl, u2: lambda e: e.tensor_mul(
                    rt1, rsl[:, u2 * 512:(u2 + 1) * 512], r_bc))(rsl, u2),
                    deps=[e3, e4, gate_done])
                p.op("ve", (lambda rsl, u2: lambda e: e.tensor_add(
                    rsl[:, u2 * 512:(u2 + 1) * 512], rt1, nmr_bc))(rsl, u2))
                pt2 = p.op("ve", (lambda gsl, rsl, u2: lambda e: e.tensor_mul(
                    gsl[:, u2 * 512:(u2 + 1) * 512],
                    rsl[:, u2 * 512:(u2 + 1) * 512],
                    gsl[:, u2 * 512:(u2 + 1) * 512]))(gsl, rsl, u2), sig=True)
                ptoks.append(pt2)
            retwar[g % 2] = ptoks[-1]
            for dc in range(8):
                bank = "C" if dc % 2 == 0 else "D"
                psb = K.psC if dc % 2 == 0 else K.psD
                dep0 = [ptok, ps.gate(bank)] + ptoks
                for kcp in range(2):
                    p.op("pe", (lambda pu, kcp, dc, gsl, psb: lambda e: e.matmul(
                        psb[:, 0:512], pu[:, kcp * 1024 + dc * 128: kcp * 1024 + (dc + 1) * 128],
                        gsl[:, kcp * 512:(kcp + 1) * 512],
                        start=(kcp == 0), stop=(kcp == 1)))(pu, kcp, dc, gsl, psb),
                        deps=dep0 if kcp == 0 else [])
                mmtok = p.sig("pe")
                p_used = mmtok
                gatewar[g % 2] = mmtok
                et = p.op("ve", (lambda dc, t0, psb: lambda e: e.tensor_add(
                    CH(K.xres, dc, t0, 512), CH(K.xres, dc, t0, 512),
                    psb[:, 0:512]))(dc, t0, psb), deps=[mmtok], sig=True)
                ps.done(bank, et)
        uwar[sqg] = q_used
        uwar[sgg] = g_used
        uwar[spg] = p_used
        kdwar = p.sig("pe")
        vwar = kdwar
    K.final_deps = p.barrier(("pe", "ve", "se"))
    K.fpool.pop(); K.bpool.pop()




def emit_l1_ffn(K):
    p, ps = K.p, K.ps
    pre = list(K.final_deps)
    K.fpool.push(); K.bpool.push()

    qch = K.fpool.alloc(4 * 1024)
    acc = K.smalls[:, 12:13]
    accw = K.smalls[:, 13:14]
    thr = K.smalls[:, 14:15]
    nthr = K.smalls[:, 15:16]
    qbf = K.bpool.alloc(4 * 1024)
    lnw_bf = K.bpool.alloc(32)
    lnb_bf = K.bpool.alloc(32)
    rt_q = K.bpool.alloc(4096).bitcast(F32)
    p.op("ve", lambda e: e.tensor_copy(lnw_bf, K.lnw), deps=pre, sig=True)
    lnbf_tok = p.op("ve", lambda e: e.tensor_copy(lnb_bf, K.lnb),
                    deps=pre, sig=True)

    def quant_weight(name, nrows, ncols, chcols, scname, fold_lnw, stats):
        nch_r = nrows // 128
        nch_c = ncols // chcols
        nslot = 4
        chunks = [(r, c) for r in range(nch_r) for c in range(nch_c)]
        zt = p.op("ve", lambda e: e.memset(acc, 0.0), deps=pre, sig=True)
        war = [zt] * nslot
        for idx, (r, c) in enumerate(chunks):
            sl = qch[:, (idx % nslot) * 1024:(idx % nslot) * 1024 + chcols]
            dt = p.dma("sp", (lambda sl, r, c: lambda e: e.dma_start(
                sl, K.ins[name][r * 128:(r + 1) * 128,
                                c * chcols:(c + 1) * chcols]))(sl, r, c),
                deps=[war[idx % nslot]] + pre)
            t1 = p.op("ve", (lambda sl: lambda e: e.reduce_sum(
                accw, sl, axis=AX.X, apply_absolute_value=True))(sl),
                deps=[dt], sig=True)
            war[idx % nslot] = t1
            t2 = p.op("ve", lambda e: e.tensor_add(acc, acc, accw), sig=True)
        p.op("pe", lambda e: e.matmul(K.psA[0:1, 0:1], K.ones_f32[:, 0:1], acc,
                                      start=True, stop=True),
             deps=[t2, ps.gate("A")])
        gt = p.sig("pe")
        n_el = float(nrows * ncols)
        t3 = p.op("ve", lambda e: e.tensor_scalar_mul(
            K.smalls[0:1, 56:57], K.psA[0:1, 0:1], 1.0 / n_el), deps=[gt], sig=True)
        ps.done("A", t3)
        p.op("ve", lambda e: e.tensor_scalar_add(
            K.smalls[0:1, 56:57], K.smalls[0:1, 56:57], 1e-6))
        t4 = p.op("ve", lambda e: e.tensor_scalar_mul(
            K.smalls[0:1, 56:57], K.smalls[0:1, 56:57], 0.5), sig=True)
        p.op("pe", lambda e: e.matmul(K.psA[:, 0:1], K.ones_f32[0:1, :],
                                      K.smalls[0:1, 56:57], start=True, stop=True),
             deps=[t4, ps.gate("A")])
        bt = p.sig("pe")
        t5 = p.op("ve", lambda e: e.tensor_copy(thr, K.psA[:, 0:1]),
                  deps=[bt], sig=True)
        ps.done("A", t5)
        t6 = p.op("ve", lambda e: e.tensor_scalar_mul(nthr, thr, -1.0), sig=True)
        war = [t6] * nslot
        qwar = [None] * nslot
        stat_started = False
        for idx, (r, c) in enumerate(chunks):
            sl = qch[:, (idx % nslot) * 1024:(idx % nslot) * 1024 + chcols]
            qsl = qbf[:, (idx % nslot) * 1024:(idx % nslot) * 1024 + chcols]
            dt = p.dma("sp", (lambda sl, r, c: lambda e: e.dma_start(
                sl, K.ins[name][r * 128:(r + 1) * 128,
                                c * chcols:(c + 1) * chcols]))(sl, r, c),
                deps=[war[idx % nslot]])
            t1 = p.op("ve", (lambda qsl, sl: lambda e: e.tensor_single_scalar(
                qsl, sl, thr, ALU.is_ge))(qsl, sl),
                deps=[dt, qwar[idx % nslot], t6], sig=True)
            t2 = p.op("ve", (lambda sl, cc: lambda e: e.tensor_single_scalar(
                rt_q[:, 0:cc], sl, nthr, ALU.is_le))(sl, chcols), sig=True)
            war[idx % nslot] = t2
            t3 = p.op("ve", (lambda qsl, cc: lambda e: e.tensor_sub(
                qsl, qsl, rt_q[:, 0:cc]))(qsl, chcols), sig=True)
            if stats:
                dep0 = [t3, lnbf_tok] + ([ps.gate("E"), ps.gate("F"),
                                          ps.gate("G"), ps.gate("H")]
                                         if not stat_started else [])
                stat_started = True
                p.op("pe", (lambda qsl, r: lambda e: e.matmul(
                    K.psE[0:1, 0:512], lnw_bf[:, r:r + 1], qsl[:, 0:512],
                    start=(r == 0), stop=(r == 31)))(qsl, r), deps=dep0)
                p.op("pe", (lambda qsl, r: lambda e: e.matmul(
                    K.psF[0:1, 0:512], lnw_bf[:, r:r + 1], qsl[:, 512:1024],
                    start=(r == 0), stop=(r == 31)))(qsl, r))
                p.op("pe", (lambda qsl, r: lambda e: e.matmul(
                    K.psG[0:1, 0:512], lnb_bf[:, r:r + 1], qsl[:, 0:512],
                    start=(r == 0), stop=(r == 31)))(qsl, r))
                p.op("pe", (lambda qsl, r: lambda e: e.matmul(
                    K.psH[0:1, 0:512], lnb_bf[:, r:r + 1], qsl[:, 512:1024],
                    start=(r == 0), stop=(r == 31)))(qsl, r))
                stok = p.sig("pe")
            if fold_lnw:
                t4 = p.op("ve", (lambda qsl, r: lambda e: e.tensor_single_scalar(
                    qsl, qsl, K.lnw[:, r:r + 1], ALU.mult))(qsl, r),
                    deps=[stok] if stats else [], sig=True)
            if scname == "fc":
                dst = K.wqfc_sc[r * 128:(r + 1) * 128, c * chcols:(c + 1) * chcols]
                qv = qsl
            else:
                dst = K.wqproj_sc.rearrange("a p k d -> p a k d")[:, :, r, :]
                qv = qsl.rearrange("p (a d) -> p a d", d=128)
            st = p.dma("sp", (lambda dst, qv: lambda e: e.dma_start(
                dst, qv))(dst, qv), deps=[p.sig("ve")])
            qwar[idx % nslot] = st
        return p.front("sp")

    fc_done = quant_weight("wfcT", 1024, 4096, 1024, "fc", False, False)
    pr_done = quant_weight("wprojT", 4096, 1024, 1024, "proj", True, True)
    e1 = p.op("ve", lambda e: e.tensor_copy(K.s1s0_bf[0:1, 0:512], K.psE[0:1, 0:512]), sig=True)
    ps.done("E", e1)
    e2 = p.op("ve", lambda e: e.tensor_copy(K.s1s0_bf[0:1, 512:1024], K.psF[0:1, 0:512]), sig=True)
    ps.done("F", e2)
    e3 = p.op("ve", lambda e: e.tensor_copy(K.s1s0_bf[0:1, 1024:1536], K.psG[0:1, 0:512]), sig=True)
    ps.done("G", e3)
    e4 = p.op("ve", lambda e: e.tensor_copy(K.s1s0_bf[0:1, 1536:2048], K.psH[0:1, 0:512]), sig=True)
    ps.done("H", e4)
    s_done = [e1, e2, e3, e4]
    quant_done = p.barrier(("ve", "se", "sp"))
    K.fpool.pop()
    K.bpool.pop()

    K.fpool.push(); K.bpool.push()
    h_g = K.bpool.alloc(32 * 512)
    w1b = K.bpool.alloc(2 * 4096)
    w2b = K.bpool.alloc(2 * 4096)
    sq = K.bpool.alloc(2 * 512)
    nmr_bf = K.bpool.alloc(512)
    rows = K.fpool.alloc(4 * 512)
    r_bc = K.fpool.alloc(512)
    mu_row = rows[0:1, 0:512]
    v_row = rows[0:1, 512:1024]
    r_row = rows[0:1, 1024:1536]
    nmr_row = rows[0:1, 1536:2048]
    w1war = [None, None]
    w2war = [None, None]
    sq_war = [quant_done, quant_done]
    h_war = quant_done
    rbc_war = None
    nmrbf_war = quant_done
    u1 = 0
    u2 = 0
    for g in range(4):
        t0 = g * 512
        stat_dep = [ps.gate("G")]
        stat_dep2 = [ps.gate("H")]
        for fb in range(8):
            slot = u1 % 2
            u1 += 1
            w1u = w1b[:, slot * 4096:(slot + 1) * 4096]
            src = K.wqfc_sc.rearrange("(kc p) f -> p kc f", p=128)[
                :, :, fb * 512:(fb + 1) * 512]
            dstv = w1u.rearrange("p (kc f) -> p kc f", f=512)
            wt = p.dma("sp", (lambda dstv, src: lambda e: e.dma_start(
                dstv, src))(dstv, src), deps=[w1war[slot]] + fc_done + pre)
            for fc in range(4):
                hidx = fb * 4 + fc
                bank = "C" if hidx % 2 == 0 else "D"
                psb = K.psC if hidx % 2 == 0 else K.psD
                dep0 = [wt, ps.gate(bank)]
                for kc in range(8):
                    p.op("pe", (lambda w1u, kc, fc, t0, psb: lambda e: e.matmul(
                        psb[:, 0:512],
                        w1u[:, kc * 512 + fc * 128: kc * 512 + (fc + 1) * 128],
                        CH(K.xn, kc, t0, 512),
                        start=(kc == 0), stop=(kc == 7)))(w1u, kc, fc, t0, psb),
                        deps=dep0 if kc == 0 else [])
                mmtok = p.sig("pe")
                if fc == 3:
                    w1war[slot] = mmtok
                et = p.op("se", (lambda hidx, psb: lambda e: e.activation(
                    h_g[:, hidx * 512:(hidx + 1) * 512], psb[:, 0:512],
                    AF.Gelu))(hidx, psb), deps=[mmtok, h_war], sig=True)
                ps.done(bank, et)
                q1 = p.op("se", (lambda hidx: lambda e: e.activation(
                    sq[:, (hidx % 2) * 512:(hidx % 2 + 1) * 512],
                    h_g[:, hidx * 512:(hidx + 1) * 512], AF.Square))(hidx),
                    deps=[sq_war[hidx % 2]], sig=True)
                p.op("pe", (lambda hidx: lambda e: e.matmul(
                    K.psG[0:1, 0:512], K.ones_bf[:, 0:1],
                    h_g[:, hidx * 512:(hidx + 1) * 512],
                    start=(hidx == 0), stop=(hidx == 31)))(hidx),
                    deps=[et] + stat_dep)
                stat_dep = []
                p.op("pe", (lambda hidx: lambda e: e.matmul(
                    K.psH[0:1, 0:512], K.ones_bf[:, 0:1],
                    sq[:, (hidx % 2) * 512:(hidx % 2 + 1) * 512],
                    start=(hidx == 0), stop=(hidx == 31)))(hidx),
                    deps=[q1] + stat_dep2)
                stat_dep2 = []
                sq_war[hidx % 2] = p.sig("pe")
        smtok = p.sig("pe")
        inv = 1.0 / 4096.0
        t1 = p.op("ve", lambda e: e.tensor_scalar_mul(mu_row, K.psG[0:1, 0:512], inv),
                  deps=[smtok], sig=True)
        ps.done("G", t1)
        t2 = p.op("ve", lambda e: e.tensor_scalar_mul(v_row, K.psH[0:1, 0:512], inv),
                  deps=[smtok], sig=True)
        ps.done("H", t2)
        t3 = p.op("se", lambda e: e.activation(r_row, mu_row, AF.Square),
                  deps=[t1], sig=True)
        t4 = p.op("ve", lambda e: e.tensor_sub(v_row, v_row, r_row), deps=[t3], sig=True)
        t5 = p.op("se", lambda e: e.activation(r_row, v_row, AF.Sqrt,
                                               bias=K.misc[0:1, 0:1]),
                  deps=[t4], sig=True)
        t6 = p.op("ve", lambda e: e.reciprocal(r_row, r_row), deps=[t5], sig=True)
        t7 = p.op("ve", lambda e: e.scalar_tensor_tensor(
            nmr_row, mu_row, -1.0, r_row, ALU.mult, ALU.mult), deps=[t6], sig=True)
        t8 = p.op("ve", lambda e: e.tensor_copy(nmr_bf[0:1, 0:512], nmr_row),
                  deps=[t7, nmrbf_war], sig=True)
        p.op("pe", lambda e: e.matmul(K.psG[:, 0:512], K.ones_f32[0:1, :], r_row,
                                      start=True, stop=True),
             deps=[t7, ps.gate("G")])
        bt = p.sig("pe")
        t9 = p.op("ve", (lambda: lambda e: e.tensor_copy(r_bc, K.psG[:, 0:512]))(),
                  deps=[bt, rbc_war], sig=True)
        ps.done("G", t9)
        htoks = []
        for fk in range(32):
            ht = p.op("ve", (lambda fk: lambda e: e.tensor_mul(
                h_g[:, fk * 512:(fk + 1) * 512], h_g[:, fk * 512:(fk + 1) * 512],
                r_bc))(fk), deps=[t9, smtok], sig=True)
            htoks.append(ht)
        rbc_war_next = htoks[-1]
        for dc in range(8):
            slot = u2 % 2
            u2 += 1
            w2u = w2b[:, slot * 4096:(slot + 1) * 4096]
            wt = p.dma("sp", (lambda w2u, dc: lambda e: e.dma_start(
                w2u, K.wqproj_sc[dc].flatten_outer_dims() if False
                else K.wqproj_sc[dc].rearrange("p k d -> p (k d)")))(w2u, dc),
                deps=[w2war[slot]] + pr_done)
            bank = "E" if dc % 2 == 0 else "F"
            psb = K.psE if dc % 2 == 0 else K.psF
            dep0 = [wt, ps.gate(bank)]
            for fk in range(32):
                p.op("pe", (lambda w2u, fk, psb: lambda e: e.matmul(
                    psb[:, 0:512], w2u[:, fk * 128:(fk + 1) * 128],
                    h_g[:, fk * 512:(fk + 1) * 512],
                    start=(fk == 0), stop=False))(w2u, fk, psb),
                    deps=(dep0 if fk == 0 else []) + [htoks[fk]])
            p.op("pe", (lambda dc, psb: lambda e: e.matmul(
                psb[:, 0:512], K.s1s0_bf[0:1, dc * 128:(dc + 1) * 128],
                nmr_bf[0:1, 0:512], start=False, stop=False))(dc, psb),
                deps=[t8] + s_done)
            p.op("pe", (lambda dc, psb: lambda e: e.matmul(
                psb[:, 0:512], K.s1s0_bf[0:1, 1024 + dc * 128:1024 + (dc + 1) * 128],
                K.ones_bf[0:1, 0:512], start=False, stop=True))(dc, psb))
            mmtok = p.sig("pe")
            w2war[slot] = mmtok
            et = p.op("ve", (lambda dc, t0, psb: lambda e: e.tensor_add(
                CH(K.xres, dc, t0, 512), CH(K.xres, dc, t0, 512),
                psb[:, 0:512]))(dc, t0, psb), deps=[mmtok], sig=True)
            ps.done(bank, et)
        h_war = p.sig("pe")
        rbc_war = rbc_war_next
        nmrbf_war = h_war
    K.final_deps = p.barrier(("pe", "ve", "se"))
    K.fpool.pop(); K.bpool.pop()




# version 07
# speedup vs baseline: 1.1635x; 1.0619x over previous
import numpy as np
import ml_dtypes
from contextlib import ExitStack

import concourse.bass as bass
import concourse.mybir as mybir
from concourse.bass_utils import run_bass_kernel_spmd

F32 = mybir.dt.float32
BF16 = mybir.dt.bfloat16
AF = mybir.ActivationFunctionType
ALU = mybir.AluOpType
AX = mybir.AxisListType
BF = ml_dtypes.bfloat16

T = 2048
D = 1024
NCH = 8
NT = 16
NEG = -1.0e30

ARENA_BF_COLS = 36864
ARENA_F32_COLS = 5120

ENGSTEP = {"pe": 1, "ve": 1, "se": 1, "gp": 16, "sp": 16}
NCH_GP = 12
NCH_SP = 6



def _lay_feat(v):
    f = v.shape[0]
    return np.ascontiguousarray(v.reshape(f // 128, 128).T)


def _host_constants():
    c = {}
    c["ident_bf"] = np.eye(128, dtype=BF)
    c["ident_f32"] = np.eye(128, dtype=np.float32)
    c["ones_bf"] = np.ones((128, 512), dtype=BF)
    c["ones_f32"] = np.ones((128, 128), dtype=np.float32)

    m0 = np.full((128, 4 * 512), NEG, dtype=np.float32)
    w = np.arange(512)[None, :]
    for parity in range(2):
        for b0 in range(2):
            v = parity + 2 * b0
            ql = 128 * parity + np.arange(128)[:, None]
            valid = (w >= ql) & (w <= ql + 256)
            if b0:
                valid &= (w >= 256)
            m0[:, v * 512:(v + 1) * 512][valid] = 0.0
    c["mask0"] = m0.astype(BF)

    mc = np.zeros((128, 4 * 512), dtype=np.float32)
    qf = np.arange(512)[None, :]
    kp = np.arange(128)[:, None]
    for d in range(4):
        mc[:, d * 512:(d + 1) * 512] = (qf >= kp + 128 * d).astype(np.float32)
    c["maskc"] = mc.astype(BF)

    c["pos"] = np.broadcast_to(
        np.arange(T, dtype=np.float32)[None, :], (128, T)).copy()

    QK = 1024
    SCALE_BASE = 512.0
    i2 = np.arange(0, QK, 2, dtype=np.float64)
    base_scale = (i2 + 0.4 * QK) / (1.4 * QK)
    t = np.arange(T, dtype=np.float64)
    scale = base_scale[None, :] ** (t[:, None] / SCALE_BASE)
    inv_freq = 1.0 / (10000.0 ** (np.arange(QK // 2, dtype=np.float64) / (QK // 2)))
    sinu = t[:, None] * inv_freq[None, :]
    sn, cs = np.sin(sinu), np.cos(sinu)
    rep = lambda a: np.repeat(a, 2, axis=1)
    sgn = np.where(np.arange(QK) % 2 == 0, -1.0, 1.0)[None, :]
    c["csq"] = np.ascontiguousarray((rep(cs * scale)).T).astype(BF)
    c["snq"] = np.ascontiguousarray((rep(sn * scale) * sgn).T).astype(BF)
    c["csk"] = np.ascontiguousarray((rep(cs / scale)).T).astype(BF)
    c["snk"] = np.ascontiguousarray((rep(sn / scale) * sgn).T).astype(BF)

    bas = np.zeros((8, 8 * 128), dtype=np.float32)
    for e in range(8):
        bas[e, e * 128:(e + 1) * 128] = 1.0
    c["basis_bf"] = bas.astype(BF)

    mcc = np.zeros((128, 8), dtype=np.float32)
    mcc[:, 0] = 1e-5
    mcc[:, 1] = 1e-6
    mcc[:, 2] = -4.0
    c["misc_cols"] = mcc
    return c


_CONST_CACHE = None


def _prep_shared(inputs):
    global _CONST_CACHE
    if _CONST_CACHE is None:
        _CONST_CACHE = _host_constants()
    s = dict(_CONST_CACHE)
    f32 = np.float32
    asC = lambda a: np.ascontiguousarray(np.asarray(a, f32))
    s["qwT"] = asC(np.asarray(inputs["L0_attn_q_w"]).T)
    s["kwT"] = asC(np.asarray(inputs["L0_attn_k_w"]).T)
    s["vwT"] = asC(np.asarray(inputs["L0_attn_v_w"]).T)
    s["owT"] = asC(np.asarray(inputs["L0_attn_o_w"]).T)
    s["bias_rows"] = np.stack([
        np.asarray(inputs["L0_attn_q_b"], f32),
        np.asarray(inputs["L0_attn_k_b"], f32),
        np.asarray(inputs["L0_attn_v_b"], f32)], axis=0)
    s["l0bias"] = np.concatenate(
        [_lay_feat(np.asarray(inputs[k], f32)) for k in
         ("L0_attn_q_b", "L0_attn_k_b", "L0_attn_v_b", "L0_attn_o_b")],
        axis=1)
    s["w1"] = asC(inputs["L0_ffn_W1"])
    w2 = np.asarray(inputs["L0_ffn_W2"], f32)
    w2r = w2.reshape(8, 32, 128, 8, 128)
    s["w2lay"] = np.ascontiguousarray(
        w2r.transpose(0, 3, 2, 1, 4).reshape(8, 8, 128, 32 * 128))
    s["b1lay"] = np.concatenate(
        [_lay_feat(np.asarray(inputs["L0_ffn_b1"], f32)[e]) for e in range(8)],
        axis=1)
    s["b2"] = asC(inputs["L0_ffn_b2"])
    s["wr"] = asC(inputs["L0_ffn_Wr"])
    s["br_bc"] = np.broadcast_to(
        np.asarray(inputs["L0_ffn_br"], f32)[None, :], (128, 8)).copy()
    s["wqkv"] = asC(inputs["L1_attn_W_qkv"])
    s["wg"] = asC(inputs["L1_attn_W_g"])
    s["wproj1"] = asC(inputs["L1_attn_W_proj"])
    s["decay"] = np.asarray(inputs["L1_attn_decay"], f32).reshape(1, 8).copy()
    s["wfcT"] = asC(np.asarray(inputs["L1_ffn_W_fc"]).T)
    s["wprojT"] = asC(np.asarray(inputs["L1_ffn_W_proj"]).T)
    s["lnw_lay"] = _lay_feat(np.asarray(inputs["L1_ffn_ln_w"], f32))
    s["lnb_lay"] = _lay_feat(np.asarray(inputs["L1_ffn_ln_b"], f32))
    return s



class Prog:

    def __init__(self, strict="none"):
        self.strict = strict
        self.strict_engs = {"none": (), "vese": ("ve", "se"),
                            "all": ("pe", "ve", "se")}[strict]
        self.semkeys = (["pe", "ve", "se"]
                        + [f"gp{i}" for i in range(NCH_GP)]
                        + [f"sp{i}" for i in range(NCH_SP)])
        self.stream = {k: [] for k in ENGSTEP}
        self.cnt = {k: 0 for k in self.semkeys}
        self._last = {k: None for k in ENGSTEP}
        self._rr = {"gp": 0, "sp": 0}

    @staticmethod
    def _flat(deps):
        out = []
        for d in deps:
            if d is None:
                continue
            if isinstance(d, list):
                out.extend(t for t in d if t is not None)
            else:
                out.append(d)
        return out

    def op(self, eng, fn, deps=(), sig=False):
        assert eng in ENGSTEP and (not sig or eng in ("pe", "ve", "se"))
        toks = self._flat(deps)
        entry = [toks, fn, None]
        self.stream[eng].append(entry)
        self._last[eng] = entry
        if eng in self.strict_engs:
            self.sig(eng)
        if sig:
            return self.sig(eng)
        return None

    def dma(self, eng, fn, deps=()):
        assert eng in ("gp", "sp")
        ch = f"{eng}{self._rr[eng]}"
        self._rr[eng] = (self._rr[eng] + 1) % (NCH_GP if eng == "gp" else NCH_SP)
        toks = self._flat(deps)
        if self.cnt[ch]:
            toks.append((ch, self.cnt[ch]))
        self.cnt[ch] += 16
        entry = [toks, fn, (ch, 16)]
        self.stream[eng].append(entry)
        self._last[eng] = entry
        return (ch, self.cnt[ch])

    def sig(self, eng):
        entry = self._last[eng]
        assert entry is not None, f"no op to sign on {eng}"
        if entry[2] is None:
            self.cnt[eng] += 1
            entry[2] = (eng, 1)
        return (eng, self.cnt[eng])

    def front(self, eng):
        if eng in ("pe", "ve", "se"):
            if self._last[eng] is None:
                return []
            return [self.sig(eng)]
        pre = eng
        return [(k, self.cnt[k]) for k in self.semkeys
                if k.startswith(pre) and self.cnt[k]]

    def barrier(self, engs=("pe", "ve", "se", "gp", "sp")):
        toks = []
        for e in engs:
            toks.extend(self.front(e))
        return toks

    def replay(self, nc, sems):
        hw = {k: {k2: 0 for k2 in self.semkeys} for k in ENGSTEP}
        blockmap = {"pe": "tensor", "ve": "vector", "se": "scalar",
                    "gp": "gpsimd", "sp": "sync"}
        with nc.Block() as block:
            for eng in ENGSTEP:
                entries = self.stream[eng]
                if not entries:
                    continue

                def make_body(eng, entries):
                    def body(e):
                        mywait = hw[eng]
                        mycount = 0
                        for toks, fn, sig in entries:
                            for (tkey, tval) in toks:
                                if tkey == eng:
                                    continue
                                if tval > mywait[tkey]:
                                    e.wait_ge(sems[tkey], tval)
                                    mywait[tkey] = tval
                            if eng in self.strict_engs and mycount:
                                e.wait_ge(sems[eng], mycount)
                            inst = fn(e)
                            if sig is not None:
                                assert inst is not None
                                inst.then_inc(sems[sig[0]], sig[1])
                                mycount += sig[1] if sig[0] == eng else 0
                    return body

                getattr(block, blockmap[eng])(make_body(eng, entries))


class Pool:

    def __init__(self, tensor, cols, name):
        self.t = tensor
        self.cols = cols
        self.off = 0
        self.name = name
        self.marks = []

    def alloc(self, n):
        n = (n + 7) & ~7
        assert self.off + n <= self.cols, \
            f"arena {self.name} overflow: {self.off}+{n} > {self.cols}"
        o = self.off
        self.off += n
        return self.t[:, o:o + n]

    def alloc_rows(self, rows, n):
        n = (n + 7) & ~7
        assert self.off + n <= self.cols
        o = self.off
        self.off += n
        return self.t[0:rows, o:o + n]

    def push(self):
        self.marks.append(self.off)

    def pop(self):
        self.off = self.marks.pop()

    def reset(self):
        self.off = 0


class PsumMgr:

    def __init__(self):
        self.tok = {}

    def gate(self, name):
        return self.tok.get(name)

    def done(self, name, token):
        self.tok[name] = token


class Ctx:
    pass



def CH(t, c, t0=0, n=T, stride=T):
    return t[:, c * stride + t0: c * stride + t0 + n]


def emit_load_x(K):
    p = K.p
    K.fpool.push()
    stage = K.fpool.alloc(2 * 1024)
    slot_tok = [None, None]
    bank_tok = [None, None]
    for i in range(NT):
        s = i % 2
        st = stage[:, s * 1024:(s + 1) * 1024]
        dtok = p.dma("sp", (lambda st, i: lambda e: e.dma_start(
            st, K.x_in[i * 128:(i + 1) * 128, :]))(st, i),
            deps=[slot_tok[s]] + (K.consts_all if i == 0 else []))
        for b in range(2):
            ps = K.psA if b == 0 else K.psB
            for j in range(4):
                c = 4 * b + j
                p.op("pe", (lambda st, c, ps, j: lambda e: e.transpose(
                    ps[:, j * 128:(j + 1) * 128],
                    st[:, c * 128:(c + 1) * 128], K.ident_f32))(st, c, ps, j),
                    deps=[dtok, bank_tok[b]] + (K.consts_all if i == 0 else []))
            ptok = p.sig("pe")
            if b == 1:
                slot_tok[s] = ptok
            dst = K.xres.rearrange("p (c t) -> p c t", t=T)[
                :, 4 * b:4 * b + 4, i * 128:(i + 1) * 128]
            psv = ps.rearrange("p (c t) -> p c t", t=128)
            eng = "ve" if b == 0 else "se"
            if eng == "ve":
                fn = (lambda dst, psv: lambda e: e.tensor_copy(dst, psv))(dst, psv)
            else:
                fn = (lambda dst, psv: lambda e: e.copy(dst, psv))(dst, psv)
            bank_tok[b] = p.op(eng, fn, deps=[ptok], sig=True)
    K.x_ready = p.barrier(("pe", "ve", "se"))
    K.final_deps = K.x_ready
    K.fpool.pop()


def emit_store(K):
    p = K.p
    K.fpool.push()
    obuf = K.fpool.alloc(2 * 1024)
    slot_store_tok = [None, None]
    bank_tok = [None, None]
    pre = list(K.final_deps) if hasattr(K, "final_deps") else []
    stok = None
    for i in range(NT):
        s = i % 2
        ob = obuf[:, s * 1024:(s + 1) * 1024]
        evac_toks = []
        for b in range(2):
            ps = K.psA if b == 0 else K.psB
            for j in range(4):
                c = 4 * b + j
                p.op("pe", (lambda c, i, ps, j: lambda e: e.transpose(
                    ps[:, j * 128:(j + 1) * 128],
                    CH(K.xres, c, i * 128, 128), K.ident_f32))(c, i, ps, j),
                    deps=pre + [bank_tok[b]])
            ptok = p.sig("pe")
            dst = ob[:, b * 512:(b + 1) * 512]
            psv = ps[:, 0:512]
            eng = "ve" if b == 0 else "se"
            if eng == "ve":
                fn = (lambda dst, psv: lambda e: e.tensor_copy(dst, psv))(dst, psv)
            else:
                fn = (lambda dst, psv: lambda e: e.copy(dst, psv))(dst, psv)
            etok = p.op(eng, fn, deps=[ptok, slot_store_tok[s]], sig=True)
            evac_toks.append(etok)
            bank_tok[b] = etok
        pre = []
        stok = p.dma("sp", (lambda ob, i: lambda e: e.dma_start(
            K.out[i * 128:(i + 1) * 128, :], ob))(ob, i),
            deps=evac_toks)
        slot_store_tok[s] = stok




def build_nc(upto="full", strict="vese"):
    nc = bass.Bass("TRN2", target_bir_lowering=False)
    K = Ctx()
    K.nc = nc
    K.p = Prog(strict=strict)
    K.ps = PsumMgr()
    K.upto = upto

    f32 = F32
    dt_ = nc.dram_tensor
    K.x_in = dt_("x_in", [T, D], f32, kind="ExternalInput").ap()
    K.out = dt_("out", [T, D], f32, kind="ExternalOutput").ap()

    ins = {}
    for name, shape, dty in [
        ("ident_bf", [128, 128], BF16), ("ident_f32", [128, 128], f32),
        ("ones_bf", [128, 512], BF16), ("ones_f32", [128, 128], f32),
        ("mask0", [128, 2048], BF16), ("maskc", [128, 2048], BF16),
        ("pos", [128, T], f32),
        ("csq", [D, T], BF16), ("snq", [D, T], BF16),
        ("csk", [D, T], BF16), ("snk", [D, T], BF16),
        ("basis_bf", [8, 1024], BF16),
        ("qwT", [D, D], f32), ("kwT", [D, D], f32),
        ("vwT", [D, D], f32), ("owT", [D, D], f32),
        ("l0bias", [128, 32], f32), ("bias_rows", [3, D], f32),
        ("misc_cols", [128, 8], f32),
        ("w1", [8, D, 4096], f32), ("w2lay", [8, 8, 128, 4096], f32),
        ("b1lay", [128, 256], f32), ("b2", [8, D], f32),
        ("wr", [D, 8], f32), ("br_bc", [128, 8], f32),
        ("wqkv", [D, 4096], f32), ("wg", [D, 2048], f32),
        ("wproj1", [2048, D], f32), ("decay", [1, 8], f32),
        ("wfcT", [D, 4096], f32), ("wprojT", [4096, D], f32),
        ("lnw_lay", [128, 32], f32), ("lnb_lay", [128, 32], f32),
    ]:
        ins[name] = dt_(name, shape, dty, kind="ExternalInput").ap()
    K.ins = ins
    K.wqfc_sc = dt_("wqfc_sc", [D, 4096], BF16, kind="Internal").ap()
    K.w1sc = dt_("w1sc", [8, 8, 128, 4096], BF16, kind="Internal").ap()
    K.w2sc = dt_("w2sc", [8, 8, 128, 4096], BF16, kind="Internal").ap()
    K.wqproj_sc = dt_("wqproj_sc", [8, 128, 32, 128], BF16, kind="Internal").ap()

    with ExitStack() as ctx:
        E = ctx.enter_context
        K.xres = E(nc.sbuf_tensor("xres", [128, NCH * T], f32))
        K.xn = E(nc.sbuf_tensor("xn", [128, NCH * T], BF16))
        K.ident_bf = E(nc.sbuf_tensor("ident_bf_s", [128, 128], BF16))[:]
        K.ident_f32 = E(nc.sbuf_tensor("ident_f32_s", [128, 128], f32))[:]
        K.ones_bf = E(nc.sbuf_tensor("ones_bf_s", [128, 512], BF16))[:]
        K.ones_f32 = E(nc.sbuf_tensor("ones_f32_s", [128, 128], f32))[:]
        K.l0bias = E(nc.sbuf_tensor("l0bias_s", [128, 32], f32))[:]
        brow_t = E(nc.sbuf_tensor("brow_bf", [65, D], BF16))
        K.brow_bf = brow_t[:]
        K.brow_dst = bass.AP(brow_t, 0, [[32 * D, 3], [1, D]])
        K.misc = E(nc.sbuf_tensor("misc_cols_s", [128, 8], f32))[:]
        K.b1lay = E(nc.sbuf_tensor("b1lay_s", [128, 256], f32))[:]
        K.b2bf = E(nc.sbuf_tensor("b2bf_s", [8, 1024], BF16))[:]
        K.wr_s = E(nc.sbuf_tensor("wr_s", [128, 64], f32))[:]
        K.br_s = E(nc.sbuf_tensor("br_s", [128, 8], f32))[:]
        K.basis = E(nc.sbuf_tensor("basis_s", [8, 1024], BF16))[:]
        K.lnw = E(nc.sbuf_tensor("lnw_s", [128, 32], f32))[:]
        K.lnb = E(nc.sbuf_tensor("lnb_s", [128, 32], f32))[:]
        K.decay_s = E(nc.sbuf_tensor("decay_s", [1, 8], f32))[:]
        K.lnd = E(nc.sbuf_tensor("lnd_s", [128, 16], f32))[:]
        K.stat_col = E(nc.sbuf_tensor("stat_col", [128, 64], f32))[:]
        K.smalls = E(nc.sbuf_tensor("smalls", [128, 64], f32))[:]
        K.coef = E(nc.sbuf_tensor("coef_s", [128, 128], f32))[:]
        K.coefT_bf = E(nc.sbuf_tensor("coefT_bf", [8, T], BF16))[:]
        K.s1s0_bf = E(nc.sbuf_tensor("s1s0_bf", [1, 2048], BF16))[:]

        arena_bf = E(nc.sbuf_tensor("arena_bf", [128, ARENA_BF_COLS], BF16))
        arena_f32 = E(nc.sbuf_tensor("arena_f32", [128, ARENA_F32_COLS], f32))
        K.bpool = Pool(arena_bf, ARENA_BF_COLS, "bf")
        K.fpool = Pool(arena_f32, ARENA_F32_COLS, "f32")

        K.psA = E(nc.psum_tensor("psA", [128, 512], f32))[:]
        K.psB = E(nc.psum_tensor("psB", [128, 512], f32))[:]
        K.psC = E(nc.psum_tensor("psC", [128, 512], f32))[:]
        K.psD = E(nc.psum_tensor("psD", [128, 512], f32))[:]
        K.psE = E(nc.psum_tensor("psE", [128, 512], f32))[:]
        K.psF = E(nc.psum_tensor("psF", [128, 512], f32))[:]
        K.psG = E(nc.psum_tensor("psG", [128, 512], f32))[:]
        K.psH = E(nc.psum_tensor("psH", [128, 512], f32))[:]

        sems = {k: E(nc.semaphore(f"sem_{k}")) for k in K.p.semkeys}
        K.sems = sems

        emit_program(K)
        K.p.replay(nc, sems)
    return nc


def emit_consts_load(K):
    p = K.p
    toks = []
    for name, dst in [
        ("ident_bf", K.ident_bf), ("ident_f32", K.ident_f32),
        ("ones_bf", K.ones_bf), ("ones_f32", K.ones_f32),
        ("l0bias", K.l0bias), ("b1lay", K.b1lay),
        ("br_bc", K.br_s), ("basis_bf", K.basis),
        ("lnw_lay", K.lnw), ("lnb_lay", K.lnb), ("decay", K.decay_s),
        ("misc_cols", K.misc),
    ]:
        toks.append(p.dma("sp", (lambda name, dst: lambda e: e.dma_start(
            dst, K.ins[name][:]))(name, dst)))
    src = K.ins["wr"].rearrange("(c p) e -> p c e", p=128)
    dstv = K.wr_s.rearrange("p (c e) -> p c e", e=8)
    toks.append(p.dma("sp", lambda e: e.dma_start(dstv, src)))
    toks.append(p.dma("gp", lambda e: e.dma_start(K.brow_dst, K.ins["bias_rows"][:])))
    b2tok = p.dma("gp", lambda e: e.dma_start(K.b2bf, K.ins["b2"][:]))
    toks.append(b2tok)
    K.consts_ready = toks[-1]
    K.consts_all = toks


def emit_program(K):
    emit_consts_load(K)
    emit_load_x(K)
    stages = ["io", "ln0", "l0attn", "moe", "ln1", "l1attn", "full"]
    lvl = stages.index(K.upto)
    if lvl >= 1:
        emit_ln(K, layer=0)
    if lvl >= 3:
        emit_router(K)
    if lvl >= 2:
        emit_l0_attn(K)
    if lvl >= 3:
        emit_moe(K)
    if lvl >= 4:
        emit_ln(K, layer=1)
    if lvl >= 5:
        emit_l1_attn(K)
    if lvl >= 6:
        emit_l1_ffn(K)
    emit_store(K)
    p = K.p
    p.op("sp", lambda e: e.nop(), deps=p.front("sp"))
    if any(p.cnt[k] for k in p.semkeys if k.startswith("gp")):
        p.op("gp", lambda e: e.nop(), deps=p.front("gp"))



def _kernel_impl(inputs, upto="full"):
    x = np.asarray(inputs["L0_attn_x"], np.float32)
    shared = _prep_shared(inputs)
    nc = build_nc(upto=upto)
    in_maps = []
    for core in range(8):
        m = dict(shared)
        m["x_in"] = np.ascontiguousarray(x[core])
        in_maps.append(m)
    res = run_bass_kernel_spmd(nc, in_maps, core_ids=list(range(8)))
    outs = [np.asarray(r["out"], np.float32) for r in res.results]
    return np.stack(outs, axis=0)


def kernel(**inputs):
    return _kernel_impl(inputs, upto="full")


def emit_ln(K, layer):
    p, ps = K.p, K.ps
    pre = list(K.final_deps)
    K.fpool.push(); K.bpool.push()
    xbf = K.bpool.alloc(8 * 512)
    sq = K.bpool.alloc(2 * 512)
    rows = K.fpool.alloc(4 * 512)
    bcast = K.fpool.alloc(2 * 512)
    tmp = K.fpool.alloc(512)
    mu_row = rows[0:1, 0:512]
    v_row = rows[0:1, 512:1024]
    r_row = rows[0:1, 1024:1536]
    nmr_row = rows[0:1, 1536:2048]
    r_bc = bcast[:, 0:512]
    nmr_bc = bcast[:, 512:1024]
    psMU = K.psC[0:1, 0:512]
    psVAR = K.psD[0:1, 0:512]

    xbf_war = None
    sq_war = [None, None]
    rows_war = None
    bc_war = None
    for g in range(4):
        t0 = 512 * g
        mm_toks = []
        for c in range(8):
            cpt = p.op("ve", (lambda c, t0: lambda e: e.tensor_copy(
                xbf[:, c * 512:(c + 1) * 512], CH(K.xres, c, t0, 512)))(c, t0),
                deps=pre + [xbf_war], sig=True)
            sqt = p.op("se", (lambda c, t0: lambda e: e.activation(
                sq[:, (c % 2) * 512:(c % 2 + 1) * 512],
                CH(K.xres, c, t0, 512), AF.Square))(c, t0),
                deps=pre + [sq_war[c % 2]], sig=True)
            p.op("pe", (lambda c: lambda e: e.matmul(
                psMU, K.ones_bf[:, 0:1], xbf[:, c * 512:(c + 1) * 512],
                start=(c == 0), stop=(c == 7)))(c),
                deps=[cpt] + ([ps.gate("C")] if c == 0 else []))
            p.op("pe", (lambda c: lambda e: e.matmul(
                psVAR, K.ones_bf[:, 0:1], sq[:, (c % 2) * 512:(c % 2 + 1) * 512],
                start=(c == 0), stop=(c == 7)))(c),
                deps=[sqt] + ([ps.gate("D")] if c == 0 else []))
            sq_war[c % 2] = p.sig("pe")
        mmtok = p.sig("pe")
        xbf_war = mmtok
        inv = 1.0 / 1024.0
        t1 = p.op("ve", lambda e: e.tensor_scalar_mul(mu_row, psMU, inv),
                  deps=[mmtok, rows_war], sig=True)
        t2 = p.op("ve", lambda e: e.tensor_scalar_mul(v_row, psVAR, inv),
                  deps=[mmtok], sig=True)
        ps.done("C", t1)
        ps.done("D", t2)
        t3 = p.op("se", lambda e: e.activation(r_row, mu_row, AF.Square),
                  deps=[t1, rows_war], sig=True)
        t4 = p.op("ve", lambda e: e.tensor_sub(v_row, v_row, r_row),
                  deps=[t3], sig=True)
        t5 = p.op("se", lambda e: e.activation(
            r_row, v_row, AF.Sqrt, bias=K.misc[0:1, 0:1]), deps=[t4], sig=True)
        t6 = p.op("ve", lambda e: e.reciprocal(r_row, r_row), deps=[t5], sig=True)
        t7 = p.op("ve", lambda e: e.scalar_tensor_tensor(
            nmr_row, mu_row, -1.0, r_row, ALU.mult, ALU.mult),
            deps=[t6], sig=True)

        if layer == 0:
            trdeps = [t7, ps.gate("A")]
            for j in range(4):
                p.op("pe", (lambda j: lambda e: e.transpose(
                    K.psA[:, j:j + 1], r_row[0:1, j * 128:(j + 1) * 128],
                    K.ident_f32[0:1, 0:1]))(j), deps=trdeps)
                trdeps = []
                p.op("pe", (lambda j: lambda e: e.transpose(
                    K.psA[:, 4 + j:5 + j], nmr_row[0:1, j * 128:(j + 1) * 128],
                    K.ident_f32[0:1, 0:1]))(j))
            ttok = p.sig("pe")
            c1 = p.op("ve", (lambda g: lambda e: e.tensor_copy(
                K.stat_col[:, g * 4:(g + 1) * 4], K.psA[:, 0:4]))(g),
                deps=[ttok], sig=True)
            c2 = p.op("ve", (lambda g: lambda e: e.tensor_copy(
                K.stat_col[:, 16 + g * 4:16 + (g + 1) * 4], K.psA[:, 4:8]))(g),
                deps=[ttok], sig=True)
            ps.done("A", c2)

        p.op("pe", lambda e: e.matmul(
            K.psE[:, 0:512], K.ones_f32[0:1, :], r_row, start=True, stop=True),
            deps=[t7, ps.gate("E")])
        bE = p.sig("pe")
        p.op("pe", lambda e: e.matmul(
            K.psF[:, 0:512], K.ones_f32[0:1, :], nmr_row, start=True, stop=True),
            deps=[t7, ps.gate("F")])
        bF = p.sig("pe")
        rows_war = bF
        e1 = p.op("ve", lambda e: e.tensor_copy(r_bc, K.psE[:, 0:512]),
                  deps=[bE, bc_war], sig=True)
        e2 = p.op("se", lambda e: e.copy(nmr_bc, K.psF[:, 0:512]),
                  deps=[bF, bc_war], sig=True)
        ps.done("E", e1)
        ps.done("F", e2)
        atoks = []
        for c in range(8):
            p.op("ve", (lambda c, t0: lambda e: e.tensor_mul(
                tmp, CH(K.xres, c, t0, 512), r_bc))(c, t0),
                deps=pre + [e1, e2])
            at = p.op("ve", (lambda c, t0: lambda e: e.tensor_add(
                CH(K.xn, c, t0, 512), tmp, nmr_bc))(c, t0), sig=True)
            atoks.append(at)
        bc_war = atoks[-1]
        pre = []
    K.final_deps = p.barrier(("pe", "ve", "se"))
    K.fpool.pop(); K.bpool.pop()
    if K.upto == "ln0":
        for c in range(8):
            p.op("ve", (lambda c: lambda e: e.tensor_copy(
                CH(K.xres, c, 0, T), CH(K.xn, c, 0, T)))(c),
                deps=list(K.final_deps), sig=True)
        K.final_deps = p.barrier(("ve",))


def ld_mini(K, dst2d, src_ap, col0, ncols, deps=()):
    src = src_ap.rearrange("(kc p) f -> p kc f", p=128)[:, :, col0:col0 + ncols]
    dstv = dst2d.rearrange("p (kc f) -> p kc f", f=ncols)
    return K.p.dma("gp", (lambda dstv, src: lambda e: e.dma_start(dstv, src))(dstv, src),
                   deps=deps)


def emit_l0_attn(K):
    p, ps = K.p, K.ps
    pre = list(K.final_deps)
    K.fpool.push(); K.bpool.push()
    mask0 = K.bpool.alloc(2048)
    kring = K.bpool.alloc(8 * 1024)
    vring = K.bpool.alloc(8 * 1024)
    qg = K.bpool.alloc(4 * 1024)
    attnT = K.bpool.alloc(8 * 512)
    wk = K.bpool.alloc(4 * 1024)
    wv = K.bpool.alloc(2 * 2048)
    wtsT = K.bpool.alloc(3 * 512)
    wtsf = K.fpool.alloc(3 * 512)
    wtse = K.fpool.alloc(3 * 512)

    mtok = p.dma("sp", lambda e: e.dma_start(mask0, K.ins["mask0"][:]), deps=pre)

    wkslot_war = [None] * 4
    wvslot_war = [None] * 2
    wk_seq = 0
    attn_pe = [None]
    oproj_pe = None
    wts_war = {"f": [None] * 3, "bf": [None] * 3, "T": [None] * 3}
    smalls_war = [None] * 3
    QCH = {"q": K.ins["qwT"], "k": K.ins["kwT"], "o": K.ins["owT"]}

    for g in range(4):
        t0 = 512 * g
        rc0 = (512 * g) % 1024

        for c in range(8):
            slot = wk_seq % 4
            wk_seq += 1
            sl = wk[:, slot * 1024:(slot + 1) * 1024]
            wt = ld_mini(K, sl, K.ins["kwT"], c * 128, 128,
                         deps=[wkslot_war[slot]] + (pre if g == 0 else []))
            bank = "C" if c % 2 == 0 else "D"
            psb = K.psC if c % 2 == 0 else K.psD
            dep0 = [wt, ps.gate(bank)] + (pre if g == 0 else [])
            for kc in range(8):
                p.op("pe", (lambda sl, kc, c, t0, psb: lambda e: e.matmul(
                    psb[:, 0:512], sl[:, kc * 128:(kc + 1) * 128],
                    CH(K.xn, kc, t0, 512), start=(kc == 0), stop=False))(
                        sl, kc, c, t0, psb),
                    deps=dep0 if kc == 0 else [])
            p.op("pe", (lambda c, psb: lambda e: e.matmul(
                psb[:, 0:512], K.brow_bf[32:33, c * 128:(c + 1) * 128],
                K.ones_bf[32:33, 0:512], start=False, stop=True))(c, psb))
            mmtok = p.sig("pe")
            wkslot_war[slot] = mmtok
            eng = "ve" if c % 2 == 0 else "se"
            dst = kring[:, c * 1024 + rc0: c * 1024 + rc0 + 512]
            if eng == "ve":
                fn = (lambda dst, psb: lambda e: e.tensor_copy(dst, psb[:, 0:512]))(dst, psb)
            else:
                fn = (lambda dst, psb: lambda e: e.copy(dst, psb[:, 0:512]))(dst, psb)
            et = p.op(eng, fn, deps=[mmtok, attn_pe[0]], sig=True)
            ps.done(bank, et)

        for u in range(4):
            su = u % 2
            sl = wv[:, su * 2048:(su + 1) * 2048]
            wt = ld_mini(K, sl, K.ins["vwT"], u * 256, 256,
                         deps=[wvslot_war[su]] + (pre if g == 0 else []))
            for i in range(4):
                gi = 4 * g + i
                s = gi % 8
                bank = "E" if (u * 4 + i) % 2 == 0 else "F"
                psb = K.psE if bank == "E" else K.psF
                dep0 = [wt, ps.gate(bank)]
                for kc in range(8):
                    p.op("pe", (lambda sl, kc, gi, psb: lambda e: e.matmul(
                        psb[:, 0:256], CH(K.xn, kc, gi * 128, 128),
                        sl[:, kc * 256:(kc + 1) * 256],
                        start=(kc == 0), stop=False))(sl, kc, gi, psb),
                        deps=dep0 if kc == 0 else [])
                p.op("pe", (lambda u, psb: lambda e: e.matmul(
                    psb[:, 0:256], K.ones_bf[64:65, 0:128],
                    K.brow_bf[64:65, u * 256:(u + 1) * 256],
                    start=False, stop=True))(u, psb))
                mmtok = p.sig("pe")
                if i == 3:
                    wvslot_war[su] = mmtok
                eng = "ve" if i % 2 == 0 else "se"
                dst = vring[:, s * 1024 + u * 256: s * 1024 + u * 256 + 256]
                if eng == "ve":
                    fn = (lambda dst, psb: lambda e: e.tensor_copy(dst, psb[:, 0:256]))(dst, psb)
                else:
                    fn = (lambda dst, psb: lambda e: e.copy(dst, psb[:, 0:256]))(dst, psb)
                et = p.op(eng, fn, deps=[mmtok, attn_pe[0]], sig=True)
                ps.done(bank, et)

        for c in range(8):
            slot = wk_seq % 4
            wk_seq += 1
            sl = wk[:, slot * 1024:(slot + 1) * 1024]
            wt = ld_mini(K, sl, K.ins["qwT"], c * 128, 128, deps=[wkslot_war[slot]])
            bank = "C" if c % 2 == 0 else "D"
            psb = K.psC if c % 2 == 0 else K.psD
            for i in range(4):
                gi = 4 * g + i
                reg = psb[:, i * 128:(i + 1) * 128]
                dep0 = [wt, ps.gate(bank)] if i == 0 else []
                for kc in range(8):
                    p.op("pe", (lambda sl, kc, gi, reg: lambda e: e.matmul(
                        reg, sl[:, kc * 128:(kc + 1) * 128],
                        CH(K.xn, kc, gi * 128, 128),
                        start=(kc == 0), stop=False))(sl, kc, gi, reg),
                        deps=dep0)
                p.op("pe", (lambda c, reg: lambda e: e.matmul(
                    reg, K.brow_bf[0:1, c * 128:(c + 1) * 128],
                    K.ones_bf[0:1, 0:128], start=False, stop=True))(c, reg))
            mmtok = p.sig("pe")
            wkslot_war[slot] = mmtok
            dstv = qg.rearrange("p (i f) -> p i f", f=1024)[:, :, c * 128:(c + 1) * 128]
            psv = psb.rearrange("p (i f) -> p i f", f=128)
            eng = "ve" if c % 2 == 0 else "se"
            if eng == "ve":
                fn = (lambda dstv, psv: lambda e: e.tensor_copy(dstv, psv))(dstv, psv)
            else:
                fn = (lambda dstv, psv: lambda e: e.copy(dstv, psv))(dstv, psv)
            et = p.op(eng, fn, deps=[mmtok, attn_pe[0]], sig=True)
            ps.done(bank, et)

        filldone = p.barrier(("ve", "se"))
        new_attn_pe = None
        seq = [(i, h) for i in range(4) for h in range(16)]
        st8 = {}

        def stage_scores(it):
            i, h = seq[it]
            qi = 4 * g + i
            b = qi // 2
            parity = qi % 2
            mv = parity + 2 * (1 if b == 0 else 0)
            msl = mask0[:, mv * 512:(mv + 1) * 512]
            rc = (256 * b - 256) % 1024
            sbank = ("C", "D", "A")[it % 3]
            psS = (K.psC, K.psD, K.psA)[it % 3]
            ch = h // 2
            r0 = 64 * (h % 2)
            qsl = qg[r0:r0 + 64, i * 1024 + ch * 128: i * 1024 + (ch + 1) * 128]
            krow = lambda a, b_: kring[r0:r0 + 64, ch * 1024 + a: ch * 1024 + b_]
            sdep = [mtok, ps.gate(sbank)] + (filldone if it == 0 else [])
            if b == 0:
                p.op("pe", (lambda qsl, psS, k1: lambda e: e.matmul(
                    psS[:, 256:512], qsl, k1, start=True, stop=True))(
                        qsl, psS, krow(0, 256)), deps=sdep)
                stok = p.sig("pe")
                stok = p.op("ve", (lambda psS: lambda e: e.memset(
                    psS[:, 0:256], 0.0))(psS), deps=[stok], sig=True)
            elif rc == 768:
                p.op("pe", (lambda qsl, psS, k1: lambda e: e.matmul(
                    psS[:, 0:256], qsl, k1, start=True, stop=True))(
                        qsl, psS, krow(768, 1024)), deps=sdep)
                p.op("pe", (lambda qsl, psS, k2: lambda e: e.matmul(
                    psS[:, 256:512], qsl, k2, start=True, stop=True))(
                        qsl, psS, krow(0, 256)))
                stok = p.sig("pe")
            else:
                p.op("pe", (lambda qsl, psS, k1: lambda e: e.matmul(
                    psS[:, 0:512], qsl, k1, start=True, stop=True))(
                        qsl, psS, krow(rc, rc + 512)), deps=sdep)
                stok = p.sig("pe")
            st8[it] = dict(b=b, i=i, h=h, psS=psS, sbank=sbank, msl=msl, stok=stok)

        def stage_stt(it):
            s = st8[it]
            wf = wtsf[:, (it % 3) * 512:(it % 3 + 1) * 512]
            t1 = p.op("ve", (lambda wf, psS, msl: lambda e: e.scalar_tensor_tensor(
                wf, psS[:, 0:512], 0.125, msl, ALU.mult, ALU.add))(
                    wf, s["psS"], s["msl"]),
                deps=[s["stok"], wts_war["f"][it % 3]], sig=True)
            ps.done(s["sbank"], t1)
            s["wf"] = wf
            s["t1"] = t1

        def stage_exp(it):
            s = st8[it]
            wb = wtse[:, (it % 3) * 512:(it % 3 + 1) * 512]
            scol = K.smalls[:, (it % 3) * 4 + 1:(it % 3) * 4 + 2]
            t3 = p.op("se", (lambda wb, wf, scol: lambda e: e.activation(
                wb, wf, AF.Exp, bias=K.misc[:, 2:3], accum_out=scol))(
                    wb, s["wf"], scol),
                deps=[s["t1"], wts_war["bf"][it % 3], smalls_war[it % 3]], sig=True)
            wts_war["f"][it % 3] = t3
            s["wb"] = wb
            s["scol"] = scol
            s["t3"] = t3

        def stage_norm(it):
            s = st8[it]
            rcol = K.smalls[:, (it % 3) * 4 + 2:(it % 3) * 4 + 3]
            t4 = p.op("ve", (lambda rcol, scol: lambda e: e.reciprocal(
                rcol, scol))(rcol, s["scol"]), deps=[s["t3"]], sig=True)
            smalls_war[it % 3] = t4
            t5 = p.op("se", (lambda wb, rcol: lambda e: e.activation(
                wb, wb, AF.Copy, scale=rcol))(s["wb"], rcol), deps=[t4], sig=True)
            s["t5"] = t5

        def stage_back(it):
            nonlocal new_attn_pe
            s = st8.pop(it)
            b, i, h, wb = s["b"], s["i"], s["h"], s["wb"]
            ch = h // 2
            r0 = 64 * (h % 2)
            wT = wtsT[:, (it % 3) * 512:(it % 3 + 1) * 512]
            valid = [2, 3] if b == 0 else [0, 1, 2, 3]
            tbank = ("G", "H", "B")[it % 3]
            psT = (K.psG, K.psH, K.psB)[it % 3]
            tdep = [s["t5"], ps.gate(tbank)]
            for kc in valid:
                p.op("pe", (lambda psT, wb, kc: lambda e: e.transpose(
                    psT[:, kc * 128:(kc + 1) * 128],
                    wb[:, kc * 128:(kc + 1) * 128], K.ident_f32))(psT, wb, kc),
                    deps=tdep)
                tdep = []
            ttok = p.sig("pe")
            wts_war["bf"][it % 3] = ttok
            lo = valid[0] * 128
            hi = (valid[-1] + 1) * 128
            eng = "ve" if it % 2 == 0 else "se"
            if eng == "ve":
                fn = (lambda wT, psT, lo, hi: lambda e: e.tensor_copy(
                    wT[:, lo:hi], psT[:, lo:hi]))(wT, psT, lo, hi)
            else:
                fn = (lambda wT, psT, lo, hi: lambda e: e.copy(
                    wT[:, lo:hi], psT[:, lo:hi]))(wT, psT, lo, hi)
            t6 = p.op(eng, fn, deps=[ttok, wts_war["T"][it % 3]], sig=True)
            ps.done(tbank, t6)
            pbank = "E" if (h // 2) % 2 == 0 else "F"
            psP = K.psE if (h // 2) % 2 == 0 else K.psF
            avdep = [t6] + ([ps.gate(pbank)] if h % 2 == 0 else [])
            for kidx, kc in enumerate(valid):
                gkc = 2 * b - 2 + kc
                sv = gkc % 8
                vsl = vring[:, sv * 1024 + h * 64: sv * 1024 + h * 64 + 64]
                p.op("pe", (lambda psP, r0, vsl, wT, kc, st_, sp: lambda e: e.matmul(
                    psP[r0:r0 + 64, 0:128], vsl, wT[:, kc * 128:(kc + 1) * 128],
                    start=st_, stop=sp))(
                        psP, r0, vsl, wT, kc, kc == valid[0], kc == valid[-1]),
                    deps=avdep if kidx == 0 else [])
            avtok = p.sig("pe")
            wts_war["T"][it % 3] = avtok
            new_attn_pe = avtok
            if h % 2 == 1:
                dst = attnT[:, ch * 512 + i * 128: ch * 512 + (i + 1) * 128]
                eng = "ve" if ch % 2 == 0 else "se"
                if eng == "ve":
                    fn = (lambda dst, psP: lambda e: e.tensor_copy(
                        dst, psP[:, 0:128]))(dst, psP)
                else:
                    fn = (lambda dst, psP: lambda e: e.copy(
                        dst, psP[:, 0:128]))(dst, psP)
                pv = p.op(eng, fn, deps=[avtok, oproj_pe], sig=True)
                ps.done(pbank, pv)

        n = len(seq)
        for k in range(n + 4):
            if k < n:
                stage_scores(k)
            if 1 <= k + 1 - 1 and k - 1 >= 0 and k - 1 < n:
                stage_stt(k - 1)
            if k - 2 >= 0 and k - 2 < n:
                stage_exp(k - 2)
            if k - 3 >= 0 and k - 3 < n:
                stage_norm(k - 3)
            if k - 4 >= 0 and k - 4 < n:
                stage_back(k - 4)
        attn_pe[0] = new_attn_pe
        attn_evacs = p.barrier(("ve", "se"))

        for c in range(8):
            slot = wk_seq % 4
            wk_seq += 1
            sl = wk[:, slot * 1024:(slot + 1) * 1024]
            wt = ld_mini(K, sl, K.ins["owT"], c * 128, 128, deps=[wkslot_war[slot]])
            bank = "C" if c % 2 == 0 else "D"
            psb = K.psC if c % 2 == 0 else K.psD
            dep0 = [wt, ps.gate(bank)] + (attn_evacs if c == 0 else [])
            for kc in range(8):
                p.op("pe", (lambda sl, kc, psb, attnT: lambda e: e.matmul(
                    psb[:, 0:512], sl[:, kc * 128:(kc + 1) * 128],
                    attnT[:, kc * 512:(kc + 1) * 512],
                    start=(kc == 0), stop=(kc == 7)))(sl, kc, psb, attnT),
                    deps=dep0 if kc == 0 else [])
            mmtok = p.sig("pe")
            wkslot_war[slot] = mmtok
            ob = K.l0bias[:, 24 + c:25 + c]
            et = p.op("ve", (lambda c, t0, psb, ob: lambda e: e.scalar_tensor_tensor(
                CH(K.xres, c, t0, 512), psb[:, 0:512], ob,
                CH(K.xres, c, t0, 512), ALU.add, ALU.add))(c, t0, psb, ob),
                deps=[mmtok], sig=True)
            ps.done(bank, et)
        oproj_pe = p.sig("pe")

    K.final_deps = p.barrier(("pe", "ve", "se"))
    K.fpool.pop(); K.bpool.pop()




def emit_router(K):
    p, ps = K.p, K.ps
    pre = list(K.final_deps)
    K.fpool.push()
    lg = K.fpool.alloc(8)
    lg2 = K.fpool.alloc(8)
    s_bc = K.fpool.alloc(8)
    dep0 = pre + [ps.gate("C")]
    for c in range(8):
        p.op("pe", (lambda c: lambda e: e.matmul(
            K.psC[0:1, 0:8], K.ones_f32[:, 0:1], K.wr_s[:, c * 8:(c + 1) * 8],
            start=(c == 0), stop=(c == 7)))(c), deps=dep0 if c == 0 else [])
    t = p.sig("pe")
    sv = p.op("ve", lambda e: e.tensor_copy(K.smalls[0:1, 32:40], K.psC[0:1, 0:8]),
              deps=[t], sig=True)
    ps.done("C", sv)
    p.op("pe", lambda e: e.matmul(K.psD[:, 0:8], K.ones_f32[0:1, :],
                                  K.smalls[0:1, 32:40], start=True, stop=True),
         deps=[sv, ps.gate("D")])
    t = p.sig("pe")
    sb = p.op("ve", lambda e: e.tensor_copy(s_bc, K.psD[:, 0:8]), deps=[t], sig=True)
    ps.done("D", sb)
    for i in range(16):
        dep0 = [ps.gate("C")]
        for c in range(8):
            p.op("pe", (lambda c, i: lambda e: e.matmul(
                K.psC[:, 0:8], CH(K.xres, c, i * 128, 128),
                K.wr_s[:, c * 8:(c + 1) * 8],
                start=(c == 0), stop=(c == 7)))(c, i), deps=dep0 if c == 0 else [])
        mmt = p.sig("pe")
        t1 = p.op("ve", (lambda i: lambda e: e.tensor_single_scalar(
            lg, K.psC[:, 0:8], K.stat_col[:, i:i + 1], ALU.mult))(i),
            deps=[mmt, sb], sig=True)
        ps.done("C", t1)
        p.op("ve", (lambda i: lambda e: e.scalar_tensor_tensor(
            lg2, s_bc, K.stat_col[:, 16 + i:17 + i], lg, ALU.mult, ALU.add))(i))
        p.op("ve", lambda e: e.tensor_add(lg, lg2, K.br_s))
        m = K.smalls[:, 8:9]
        nm = K.smalls[:, 9:10]
        ss = K.smalls[:, 10:11]
        rs = K.smalls[:, 11:12]
        p.op("ve", lambda e: e.reduce_max(m, lg, axis=AX.X))
        t5 = p.op("ve", lambda e: e.tensor_single_scalar(lg2, lg, m, ALU.is_ge),
                  sig=True)
        t6 = p.op("ve", lambda e: e.tensor_scalar_mul(nm, m, -1.0), sig=True)
        t7 = p.op("se", lambda e: e.activation(lg, lg, AF.Exp, bias=nm,
                                               accum_out=ss),
                  deps=[t5, t6], sig=True)
        t8 = p.op("ve", lambda e: e.reciprocal(rs, ss), deps=[t7], sig=True)
        t9 = p.op("ve", (lambda i: lambda e: e.tensor_single_scalar(
            K.coef[:, 8 * i:8 * (i + 1)], lg2, rs, ALU.mult))(i), sig=True)
        p.op("pe", (lambda i: lambda e: e.transpose(
            K.psA[0:8, 0:128], K.coef[:, 8 * i:8 * (i + 1)], K.ident_f32))(i),
            deps=[t9, ps.gate("A")])
        tt = p.sig("pe")
        tc = p.op("ve", (lambda i: lambda e: e.tensor_copy(
            K.coefT_bf[0:8, i * 128:(i + 1) * 128], K.psA[0:8, 0:128]))(i),
            deps=[tt], sig=True)
        ps.done("A", tc)
    K.final_deps = p.barrier(("pe", "ve", "se"))
    K.fpool.pop()


def emit_moe(K):
    p, ps = K.p, K.ps
    pre = list(K.final_deps)
    K.fpool.push(); K.bpool.push()
    h_all = K.bpool.alloc(32 * 512)
    w1b = K.bpool.alloc(2 * 4096)
    w2b = K.bpool.alloc(2 * 4096)
    cbc = K.fpool.alloc(2 * 512)
    tmp = K.fpool.alloc(512)
    w1war = [None, None]
    w2war = [None, None]
    cbcwar = [None, None]
    h_war = None
    u1 = 0
    u2 = 0
    K.w1sc_tok = {}
    K.w2sc_tok = {}
    first = pre
    for g in range(4):
        t0 = g * 512
        for e8 in range(8):
            csl = (g * 8 + e8) % 2
            cb = cbc[:, csl * 512:(csl + 1) * 512]
            p.op("pe", (lambda e8, t0: lambda e: e.matmul(
                K.psG[:, 0:512], K.basis[:, e8 * 128:(e8 + 1) * 128],
                K.coefT_bf[0:8, t0:t0 + 512], start=True, stop=True))(e8, t0),
                deps=first + [ps.gate("G")])
            first = []
            bt = p.sig("pe")
            cbt = p.op("ve", (lambda cb: lambda e: e.tensor_copy(
                cb, K.psG[:, 0:512]))(cb), deps=[bt, cbcwar[csl]], sig=True)
            ps.done("G", cbt)
            for fb in range(8):
                slot = u1 % 2
                u1 += 1
                w1u = w1b[:, slot * 4096:(slot + 1) * 4096]
                if g == 0:
                    src = K.ins["w1"][e8].rearrange(
                        "(kc p) f -> p kc f", p=128)[:, :, fb * 512:(fb + 1) * 512]
                    dstv = w1u.rearrange("p (kc f) -> p kc f", f=512)
                    wt = p.dma("gp", (lambda dstv, src: lambda e: e.dma_start(
                        dstv, src))(dstv, src), deps=[w1war[slot]] + pre)
                else:
                    wt = p.dma("sp", (lambda w1u, e8, fb: lambda e: e.dma_start(
                        w1u, K.w1sc[e8][fb]))(w1u, e8, fb),
                        deps=[w1war[slot], K.w1sc_tok[(e8, fb)]])
                for fc in range(4):
                    hidx = fb * 4 + fc
                    bank = "C" if hidx % 2 == 0 else "D"
                    psb = K.psC if hidx % 2 == 0 else K.psD
                    dep0 = [wt, ps.gate(bank)]
                    for kc in range(8):
                        p.op("pe", (lambda w1u, kc, fc, t0, psb: lambda e: e.matmul(
                            psb[:, 0:512], w1u[:, kc * 512 + fc * 128: kc * 512 + (fc + 1) * 128],
                            CH(K.xn, kc, t0, 512),
                            start=(kc == 0), stop=(kc == 7)))(w1u, kc, fc, t0, psb),
                            deps=dep0 if kc == 0 else [])
                    mmtok = p.sig("pe")
                    if fc == 3:
                        if g == 0:
                            stok_ = p.dma("sp", (lambda w1u, e8, fb: lambda e:
                                e.dma_start(K.w1sc[e8][fb], w1u))(w1u, e8, fb),
                                deps=[wt])
                            K.w1sc_tok[(e8, fb)] = stok_
                            w1war[slot] = [mmtok, stok_]
                        else:
                            w1war[slot] = mmtok
                    et = p.op("se", (lambda hidx, e8, psb: lambda e: e.activation(
                        h_all[:, hidx * 512:(hidx + 1) * 512], psb[:, 0:512],
                        AF.Gelu, bias=K.b1lay[:, e8 * 32 + hidx:e8 * 32 + hidx + 1]))(
                            hidx, e8, psb),
                        deps=[mmtok, h_war], sig=True)
                    ps.done(bank, et)
            hdone = p.barrier(("se",))
            for dc in range(8):
                slot = u2 % 2
                u2 += 1
                w2u = w2b[:, slot * 4096:(slot + 1) * 4096]
                if g == 0:
                    src = K.ins["w2lay"][e8][dc]
                    wt = p.dma("gp", (lambda w2u, src: lambda e: e.dma_start(
                        w2u, src))(w2u, src), deps=[w2war[slot]] + pre)
                else:
                    wt = p.dma("sp", (lambda w2u, e8, dc: lambda e: e.dma_start(
                        w2u, K.w2sc[e8][dc]))(w2u, e8, dc),
                        deps=[w2war[slot], K.w2sc_tok[(e8, dc)]])
                bank = "E" if dc % 2 == 0 else "F"
                psb = K.psE if dc % 2 == 0 else K.psF
                dep0 = [wt, ps.gate(bank)] + hdone
                for fk in range(32):
                    p.op("pe", (lambda w2u, fk, psb: lambda e: e.matmul(
                        psb[:, 0:512], w2u[:, fk * 128:(fk + 1) * 128],
                        h_all[:, fk * 512:(fk + 1) * 512],
                        start=(fk == 0), stop=(fk == 31)))(w2u, fk, psb),
                        deps=dep0 if fk == 0 else [])
                mmtok = p.sig("pe")
                if g == 0:
                    stok_ = p.dma("sp", (lambda w2u, e8, dc: lambda e:
                        e.dma_start(K.w2sc[e8][dc], w2u))(w2u, e8, dc), deps=[wt])
                    K.w2sc_tok[(e8, dc)] = stok_
                    w2war[slot] = [mmtok, stok_]
                else:
                    w2war[slot] = mmtok
                tm = p.op("ve", (lambda psb, cb: lambda e: e.tensor_mul(
                    tmp, psb[:, 0:512], cb))(psb, cb), deps=[mmtok, cbt], sig=True)
                ta = p.op("ve", (lambda dc, t0: lambda e: e.tensor_add(
                    CH(K.xres, dc, t0, 512), CH(K.xres, dc, t0, 512), tmp))(dc, t0),
                    sig=True)
                ps.done(bank, ta)
                if dc == 7:
                    cbcwar[csl] = tm
            h_war = p.sig("pe")
        for dc in range(8):
            bank = "A" if dc % 2 == 0 else "B"
            psb = K.psA if dc % 2 == 0 else K.psB
            p.op("pe", (lambda dc, t0, psb: lambda e: e.matmul(
                psb[:, 0:512], K.b2bf[0:8, dc * 128:(dc + 1) * 128],
                K.coefT_bf[0:8, t0:t0 + 512], start=True, stop=True))(dc, t0, psb),
                deps=[ps.gate(bank)])
            mmtok = p.sig("pe")
            ta = p.op("ve", (lambda dc, t0, psb: lambda e: e.tensor_add(
                CH(K.xres, dc, t0, 512), CH(K.xres, dc, t0, 512),
                psb[:, 0:512]))(dc, t0, psb), deps=[mmtok], sig=True)
            ps.done(bank, ta)
    K.final_deps = p.barrier(("pe", "ve", "se"))
    K.fpool.pop(); K.bpool.pop()




def emit_l1_attn(K):
    p, ps = K.p, K.ps
    pre = list(K.final_deps)
    K.fpool.push(); K.bpool.push()
    maskc = K.bpool.alloc(2048)
    v_h = K.bpool.alloc(16 * 256)
    kd_h = K.bpool.alloc(2048)
    qd = K.bpool.alloc(2 * 512)
    wch = K.bpool.alloc(2 * 512)
    ret = K.bpool.alloc(2 * 1024)
    gate = K.bpool.alloc(2 * 1024)
    wqu = K.bpool.alloc(4 * 2048)
    xps = K.bpool.alloc(4 * 512)
    sqb = K.bpool.alloc(2 * 512)
    kr = K.bpool.alloc(1024).bitcast(F32)
    ks = K.bpool.alloc(1024).bitcast(F32)
    rt1 = K.bpool.alloc(1024).bitcast(F32)
    possl = K.fpool.alloc(2 * 512)
    dkt = K.fpool.alloc(2 * 512)
    rows = K.fpool.alloc(4 * 512)
    bcast = K.fpool.alloc(2 * 512)
    mu_row = rows[0:1, 0:512]
    v_row = rows[0:1, 512:1024]
    r_row = rows[0:1, 1024:1536]
    nmr_row = rows[0:1, 1536:2048]
    r_bc = bcast[:, 0:512]
    nmr_bc = bcast[:, 512:1024]
    SWAP = []
    for i in range(16):
        SWAP += [2 * i + 1, 2 * i]
    CINV = 1.0 / np.sqrt(128.0)

    mtok = p.dma("sp", lambda e: e.dma_start(maskc, K.ins["maskc"][:]), deps=pre)

    lr = K.smalls[0:1, 48:56]
    t = p.op("se", lambda e: e.activation(lr, K.decay_s, AF.Ln),
             deps=pre, sig=True)
    p.op("pe", lambda e: e.matmul(K.psA[:, 0:8], K.ones_f32[0:1, :], lr,
                                  start=True, stop=True),
         deps=[t, ps.gate("A")])
    t = p.sig("pe")
    t1 = p.op("ve", lambda e: e.tensor_copy(K.lnd[:, 0:8], K.psA[:, 0:8]),
              deps=[t], sig=True)
    ps.done("A", t1)
    t2 = p.op("ve", lambda e: e.tensor_scalar_mul(K.lnd[:, 8:16],
                                                  K.lnd[:, 0:8], -1.0), sig=True)
    lnd_done = t2

    useq = [0]
    uwar = [None, None, None, None]
    xwar = [None, None]
    poswar = [None, None]
    dktwar = [None, None]
    retwar = [None, None]
    gatewar = [None, None]
    sq_war = None
    qdwar = [None, None]
    wchwar = [None, None]
    kdwar = None
    vwar = None

    def wunit(cols0, ncols, deps):
        slot = useq[0] % 4
        useq[0] += 1
        sl = wqu[:, slot * 2048: slot * 2048 + 8 * ncols]
        src = K.ins["wqkv"].rearrange("(kc p) f -> p kc f", p=128)[
            :, :, cols0:cols0 + ncols]
        dstv = sl.rearrange("p (kc f) -> p kc f", f=ncols)
        tok = p.dma("gp", (lambda dstv, src: lambda e: e.dma_start(dstv, src))(
            dstv, src), deps=[uwar[slot]] + deps)
        return slot, sl, tok

    def wunit_from(name, rows0, nrows, cols0, ncols, deps):
        slot = useq[0] % 4
        useq[0] += 1
        sl = wqu[:, slot * 2048: slot * 2048 + (nrows // 128) * ncols]
        src = K.ins[name][rows0:rows0 + nrows].rearrange(
            "(kc p) f -> p kc f", p=128)[:, :, cols0:cols0 + ncols]
        dstv = sl.rearrange("p (kc f) -> p kc f", f=ncols)
        tok = p.dma("gp", (lambda dstv, src: lambda e: e.dma_start(dstv, src))(
            dstv, src), deps=[uwar[slot]] + deps)
        return slot, sl, tok

    def rotary_decay(g, h, raw_bank, psb, dst, qk, dep0, dstwar=None):
        sl = g % 2
        cs_s = xps[:, sl * 1024: sl * 1024 + 512]
        sn_s = xps[:, sl * 1024 + 512: sl * 1024 + 1024]
        csn = "csq" if qk == "q" else "csk"
        snn = "snq" if qk == "q" else "snk"
        ct = p.dma("sp", (lambda cs_s, csn, h, g: lambda e: e.dma_start(
            cs_s, K.ins[csn][128 * h:128 * (h + 1), 512 * g:512 * (g + 1)]))(
                cs_s, csn, h, g), deps=[xwar[sl]])
        st_ = p.dma("sp", (lambda sn_s, snn, h, g: lambda e: e.dma_start(
            sn_s, K.ins[snn][128 * h:128 * (h + 1), 512 * g:512 * (g + 1)]))(
                sn_s, snn, h, g), deps=[xwar[sl]])
        pt = p.dma("sp", (lambda g: lambda e: e.dma_start(
            possl[:, (g % 2) * 512:(g % 2 + 1) * 512],
            K.ins["pos"][:, 512 * g:512 * (g + 1)]))(g), deps=[poswar[g % 2]])
        pos_s = possl[:, (g % 2) * 512:(g % 2 + 1) * 512]
        e1 = p.op("ve", (lambda psb: lambda e: e.tensor_copy(kr, psb[:, 0:512]))(psb),
                  deps=dep0, sig=True)
        ps.done(raw_bank, e1)
        p.op("ve", lambda e: e.stream_shuffle(ks, kr, SWAP))
        p.op("ve", (lambda cs_s: lambda e: e.tensor_mul(rt1, kr, cs_s))(cs_s),
             deps=[ct])
        p.op("ve", (lambda sn_s: lambda e: e.tensor_mul(ks, ks, sn_s))(sn_s),
             deps=[st_])
        s1 = p.op("ve", lambda e: e.tensor_add(rt1, rt1, ks), sig=True)
        lcol = K.lnd[:, h:h + 1] if qk == "q" else K.lnd[:, 8 + h:9 + h]
        dk_s = dkt[:, (g % 2) * 512:(g % 2 + 1) * 512]
        d1 = p.op("se", (lambda dk_s, pos_s, lcol: lambda e: e.activation(
            dk_s, pos_s, AF.Exp, scale=lcol))(dk_s, pos_s, lcol),
            deps=[pt, lnd_done, dktwar[g % 2]], sig=True)
        poswar[g % 2] = d1
        o1 = p.op("ve", (lambda dst, dk_s: lambda e: e.tensor_mul(
            dst, rt1, dk_s))(dst, dk_s), deps=[s1, d1, dstwar], sig=True)
        dktwar[g % 2] = o1
        xwar[sl] = o1
        return o1

    for h in range(8):
        slot, ku, ktok = wunit(1024 + 128 * h, 128, pre if h == 0 else [])
        kd_toks = []
        for g in range(4):
            bank = "C" if g % 2 == 0 else "D"
            psb = K.psC if g % 2 == 0 else K.psD
            dep0 = [ktok, ps.gate(bank), kdwar]
            for kc in range(8):
                p.op("pe", (lambda ku, kc, g, psb: lambda e: e.matmul(
                    psb[:, 0:512], ku[:, kc * 128:(kc + 1) * 128],
                    CH(K.xn, kc, g * 512, 512),
                    start=(kc == 0), stop=(kc == 7)))(ku, kc, g, psb),
                    deps=dep0 if kc == 0 else [])
            mmtok = p.sig("pe")
            if g == 3:
                uwar[slot] = mmtok
            kd_toks.append(rotary_decay(
                g, h, bank, psb, kd_h[:, g * 512:(g + 1) * 512], "k", [mmtok],
                dstwar=kdwar))
        slot, vu, vtok = wunit(2048 + 256 * h, 256, [])
        v_toks = []
        for gi in range(16):
            bank = "E" if gi % 2 == 0 else "F"
            psb = K.psE if gi % 2 == 0 else K.psF
            dep0 = [vtok, ps.gate(bank), vwar]
            for kc in range(8):
                p.op("pe", (lambda vu, kc, gi, psb: lambda e: e.matmul(
                    psb[:, 0:256], CH(K.xn, kc, gi * 128, 128),
                    vu[:, kc * 256:(kc + 1) * 256],
                    start=(kc == 0), stop=(kc == 7)))(vu, kc, gi, psb),
                    deps=dep0 if kc == 0 else [])
            mmtok = p.sig("pe")
            if gi == 15:
                uwar[slot] = mmtok
            eng = "ve" if gi % 2 == 0 else "se"
            dst = v_h[:, gi * 256:(gi + 1) * 256]
            if eng == "ve":
                fn = (lambda dst, psb: lambda e: e.tensor_copy(dst, psb[:, 0:256]))(dst, psb)
            else:
                fn = (lambda dst, psb: lambda e: e.copy(dst, psb[:, 0:256]))(dst, psb)
            et = p.op(eng, fn, deps=[mmtok], sig=True)
            ps.done(bank, et)
            v_toks.append(et)
        sqg, qu, qtok = wunit(128 * h, 128, [])
        sgg, gu, gtok = wunit_from("wg", 0, 1024, 256 * h, 256, [])
        spg, pu, ptok = wunit_from("wproj1", 256 * h, 256, 0, 1024, [])
        q_used = None
        g_used = None
        p_used = None
        for g in range(4):
            t0 = 512 * g
            bank = "C" if g % 2 == 0 else "D"
            psb = K.psC if g % 2 == 0 else K.psD
            dep0 = [qtok, ps.gate(bank), qdwar[g % 2]]
            for kc in range(8):
                p.op("pe", (lambda qu, kc, t0, psb: lambda e: e.matmul(
                    psb[:, 0:512], qu[:, kc * 128:(kc + 1) * 128],
                    CH(K.xn, kc, t0, 512),
                    start=(kc == 0), stop=(kc == 7)))(qu, kc, t0, psb),
                    deps=dep0 if kc == 0 else [])
            mmtok = p.sig("pe")
            q_used = mmtok
            qsl = qd[:, (g % 2) * 512:(g % 2 + 1) * 512]
            qdone = rotary_decay(g, h, bank, psb, qsl, "q", [mmtok],
                                 dstwar=qdwar[g % 2])
            nkc = 4 * g + 4
            avdep0 = [ps.gate("G")]
            avdep1 = [ps.gate("H")]
            av_last = None
            ets = [None] * nkc
            def emit_scores(kc):
                bank = "C" if kc % 2 == 0 else "D"
                psb = K.psC if kc % 2 == 0 else K.psD
                p.op("pe", (lambda kc, qsl, psb: lambda e: e.matmul(
                    psb[:, 0:512], kd_h[:, kc * 128:(kc + 1) * 128], qsl,
                    start=True, stop=True))(kc, qsl, psb),
                    deps=[qdone, kd_toks[kc // 4], ps.gate(bank)])
                smt = p.sig("pe")
                wsl = wch[:, (kc % 2) * 512:(kc % 2 + 1) * 512]
                d = kc - 4 * g
                if d >= 0:
                    et = p.op("ve", (lambda wsl, psb, d: lambda e: e.scalar_tensor_tensor(
                        wsl, psb[:, 0:512], CINV,
                        maskc[:, d * 512:(d + 1) * 512], ALU.mult, ALU.mult))(
                            wsl, psb, d), deps=[smt, mtok, wchwar[kc % 2]], sig=True)
                else:
                    et = p.op("ve", (lambda wsl, psb: lambda e: e.tensor_single_scalar(
                        wsl, psb[:, 0:512], CINV, ALU.mult))(wsl, psb),
                        deps=[smt, wchwar[kc % 2]], sig=True)
                ps.done(bank, et)
                ets[kc] = (et, wsl)
            emit_scores(0)
            for kc in range(nkc):
                if kc + 1 < nkc:
                    emit_scores(kc + 1)
                et, wsl = ets[kc]
                for dvc in range(2):
                    psav = K.psG if dvc == 0 else K.psH
                    vsl = v_h[:, kc * 256 + dvc * 128: kc * 256 + (dvc + 1) * 128]
                    dep = [et, v_toks[kc]] + (avdep0 if dvc == 0 else avdep1)
                    p.op("pe", (lambda psav, vsl, wsl, st, sp: lambda e: e.matmul(
                        psav[:, 0:512], vsl, wsl, start=st, stop=sp))(
                            psav, vsl, wsl, kc == 0, kc == nkc - 1),
                        deps=dep)
                    if dvc == 0:
                        avdep0 = []
                    else:
                        avdep1 = []
                av_last = p.sig("pe")
                wchwar[kc % 2] = av_last
            qdwar[g % 2] = av_last
            rsl = ret[:, (g % 2) * 1024:(g % 2 + 1) * 1024]
            e1 = p.op("ve", (lambda rsl: lambda e: e.tensor_copy(
                rsl[:, 0:512], K.psG[:, 0:512]))(rsl),
                deps=[av_last, retwar[g % 2]], sig=True)
            ps.done("G", e1)
            e2 = p.op("se", (lambda rsl: lambda e: e.copy(
                rsl[:, 512:1024], K.psH[:, 0:512]))(rsl),
                deps=[av_last, retwar[g % 2]], sig=True)
            ps.done("H", e2)
            q1 = p.op("se", (lambda rsl: lambda e: e.activation(
                sqb[:, 0:512], rsl[:, 0:512], AF.Square))(rsl),
                deps=[e1, sq_war], sig=True)
            q2 = p.op("se", (lambda rsl: lambda e: e.activation(
                sqb[:, 512:1024], rsl[:, 512:1024], AF.Square))(rsl),
                deps=[e2], sig=True)
            dep0 = [e1, ps.gate("A")]
            p.op("pe", (lambda rsl: lambda e: e.matmul(
                K.psA[0:1, 0:512], K.ones_bf[:, 0:1], rsl[:, 0:512],
                start=True, stop=False))(rsl), deps=dep0)
            p.op("pe", (lambda rsl: lambda e: e.matmul(
                K.psA[0:1, 0:512], K.ones_bf[:, 0:1], rsl[:, 512:1024],
                start=False, stop=True))(rsl), deps=[e2])
            smu = p.sig("pe")
            p.op("pe", lambda e: e.matmul(
                K.psB[0:1, 0:512], K.ones_bf[:, 0:1], sqb[:, 0:512],
                start=True, stop=False), deps=[q1, ps.gate("B")])
            p.op("pe", lambda e: e.matmul(
                K.psB[0:1, 0:512], K.ones_bf[:, 0:1], sqb[:, 512:1024],
                start=False, stop=True), deps=[q2])
            svar = p.sig("pe")
            sq_war = svar
            inv = 1.0 / 256.0
            t1 = p.op("ve", lambda e: e.tensor_scalar_mul(mu_row, K.psA[0:1, 0:512], inv),
                      deps=[smu], sig=True)
            ps.done("A", t1)
            t2 = p.op("ve", lambda e: e.tensor_scalar_mul(v_row, K.psB[0:1, 0:512], inv),
                      deps=[svar], sig=True)
            ps.done("B", t2)
            t3 = p.op("se", lambda e: e.activation(r_row, mu_row, AF.Square),
                      deps=[t1], sig=True)
            t4 = p.op("ve", lambda e: e.tensor_sub(v_row, v_row, r_row),
                      deps=[t3], sig=True)
            t5 = p.op("se", lambda e: e.activation(
                r_row, v_row, AF.Sqrt, bias=K.misc[0:1, 0:1]), deps=[t4], sig=True)
            t6 = p.op("ve", lambda e: e.reciprocal(r_row, r_row), deps=[t5], sig=True)
            t7 = p.op("ve", lambda e: e.scalar_tensor_tensor(
                nmr_row, mu_row, -1.0, r_row, ALU.mult, ALU.mult),
                deps=[t6], sig=True)
            p.op("pe", lambda e: e.matmul(
                K.psA[:, 0:512], K.ones_f32[0:1, :], r_row, start=True, stop=True),
                deps=[t7, ps.gate("A")])
            bE = p.sig("pe")
            p.op("pe", lambda e: e.matmul(
                K.psB[:, 0:512], K.ones_f32[0:1, :], nmr_row, start=True, stop=True),
                deps=[t7, ps.gate("B")])
            bF = p.sig("pe")
            e3 = p.op("ve", lambda e: e.tensor_copy(r_bc, K.psA[:, 0:512]),
                      deps=[bE], sig=True)
            ps.done("A", e3)
            e4 = p.op("se", lambda e: e.copy(nmr_bc, K.psB[:, 0:512]),
                      deps=[bF], sig=True)
            ps.done("B", e4)
            gsl = gate[:, (g % 2) * 1024:(g % 2 + 1) * 1024]
            for u2 in range(2):
                bank = "E" if u2 == 0 else "F"
                psb = K.psE if u2 == 0 else K.psF
                dep0 = [gtok, ps.gate(bank), gatewar[g % 2]]
                for kc in range(8):
                    p.op("pe", (lambda gu, kc, u2, t0, psb: lambda e: e.matmul(
                        psb[:, 0:512], gu[:, kc * 256 + u2 * 128: kc * 256 + (u2 + 1) * 128],
                        CH(K.xn, kc, t0, 512),
                        start=(kc == 0), stop=(kc == 7)))(gu, kc, u2, t0, psb),
                        deps=dep0 if kc == 0 else [])
                mmtok = p.sig("pe")
                g_used = mmtok
                et = p.op("se", (lambda gsl, u2, psb: lambda e: e.activation(
                    gsl[:, u2 * 512:(u2 + 1) * 512], psb[:, 0:512], AF.Silu))(
                        gsl, u2, psb), deps=[mmtok], sig=True)
                ps.done(bank, et)
            gate_done = p.sig("se")
            ptoks = []
            for u2 in range(2):
                p.op("ve", (lambda rsl, u2: lambda e: e.tensor_mul(
                    rt1, rsl[:, u2 * 512:(u2 + 1) * 512], r_bc))(rsl, u2),
                    deps=[e3, e4, gate_done])
                p.op("ve", (lambda rsl, u2: lambda e: e.tensor_add(
                    rsl[:, u2 * 512:(u2 + 1) * 512], rt1, nmr_bc))(rsl, u2))
                pt2 = p.op("ve", (lambda gsl, rsl, u2: lambda e: e.tensor_mul(
                    gsl[:, u2 * 512:(u2 + 1) * 512],
                    rsl[:, u2 * 512:(u2 + 1) * 512],
                    gsl[:, u2 * 512:(u2 + 1) * 512]))(gsl, rsl, u2), sig=True)
                ptoks.append(pt2)
            retwar[g % 2] = ptoks[-1]
            for dc in range(8):
                bank = "C" if dc % 2 == 0 else "D"
                psb = K.psC if dc % 2 == 0 else K.psD
                dep0 = [ptok, ps.gate(bank)] + ptoks
                for kcp in range(2):
                    p.op("pe", (lambda pu, kcp, dc, gsl, psb: lambda e: e.matmul(
                        psb[:, 0:512], pu[:, kcp * 1024 + dc * 128: kcp * 1024 + (dc + 1) * 128],
                        gsl[:, kcp * 512:(kcp + 1) * 512],
                        start=(kcp == 0), stop=(kcp == 1)))(pu, kcp, dc, gsl, psb),
                        deps=dep0 if kcp == 0 else [])
                mmtok = p.sig("pe")
                p_used = mmtok
                gatewar[g % 2] = mmtok
                et = p.op("ve", (lambda dc, t0, psb: lambda e: e.tensor_add(
                    CH(K.xres, dc, t0, 512), CH(K.xres, dc, t0, 512),
                    psb[:, 0:512]))(dc, t0, psb), deps=[mmtok], sig=True)
                ps.done(bank, et)
        uwar[sqg] = q_used
        uwar[sgg] = g_used
        uwar[spg] = p_used
        kdwar = p.sig("pe")
        vwar = kdwar
    K.final_deps = p.barrier(("pe", "ve", "se"))
    K.fpool.pop(); K.bpool.pop()




def emit_l1_ffn(K):
    p, ps = K.p, K.ps
    pre = list(K.final_deps)
    K.fpool.push(); K.bpool.push()

    qch = K.fpool.alloc(4 * 1024)
    acc = K.smalls[:, 12:13]
    accw = K.smalls[:, 13:14]
    thr = K.smalls[:, 14:15]
    nthr = K.smalls[:, 15:16]
    qbf = K.bpool.alloc(4 * 1024)
    lnw_bf = K.bpool.alloc(32)
    lnb_bf = K.bpool.alloc(32)
    rt_q = K.bpool.alloc(4096).bitcast(F32)
    p.op("ve", lambda e: e.tensor_copy(lnw_bf, K.lnw), deps=pre, sig=True)
    lnbf_tok = p.op("ve", lambda e: e.tensor_copy(lnb_bf, K.lnb),
                    deps=pre, sig=True)

    def quant_weight(name, nrows, ncols, chcols, scname, fold_lnw, stats):
        nch_r = nrows // 128
        nch_c = ncols // chcols
        nslot = 4
        chunks = [(r, c) for r in range(nch_r) for c in range(nch_c)]
        zt = p.op("ve", lambda e: e.memset(acc, 0.0), deps=pre, sig=True)
        war = [zt] * nslot
        for idx, (r, c) in enumerate(chunks):
            sl = qch[:, (idx % nslot) * 1024:(idx % nslot) * 1024 + chcols]
            dt = p.dma("sp", (lambda sl, r, c: lambda e: e.dma_start(
                sl, K.ins[name][r * 128:(r + 1) * 128,
                                c * chcols:(c + 1) * chcols]))(sl, r, c),
                deps=[war[idx % nslot]] + pre)
            t1 = p.op("ve", (lambda sl: lambda e: e.reduce_sum(
                accw, sl, axis=AX.X, apply_absolute_value=True))(sl),
                deps=[dt], sig=True)
            war[idx % nslot] = t1
            t2 = p.op("ve", lambda e: e.tensor_add(acc, acc, accw), sig=True)
        p.op("pe", lambda e: e.matmul(K.psA[0:1, 0:1], K.ones_f32[:, 0:1], acc,
                                      start=True, stop=True),
             deps=[t2, ps.gate("A")])
        gt = p.sig("pe")
        n_el = float(nrows * ncols)
        t3 = p.op("ve", lambda e: e.tensor_scalar_mul(
            K.smalls[0:1, 56:57], K.psA[0:1, 0:1], 1.0 / n_el), deps=[gt], sig=True)
        ps.done("A", t3)
        p.op("ve", lambda e: e.tensor_scalar_add(
            K.smalls[0:1, 56:57], K.smalls[0:1, 56:57], 1e-6))
        t4 = p.op("ve", lambda e: e.tensor_scalar_mul(
            K.smalls[0:1, 56:57], K.smalls[0:1, 56:57], 0.5), sig=True)
        p.op("pe", lambda e: e.matmul(K.psA[:, 0:1], K.ones_f32[0:1, :],
                                      K.smalls[0:1, 56:57], start=True, stop=True),
             deps=[t4, ps.gate("A")])
        bt = p.sig("pe")
        t5 = p.op("ve", lambda e: e.tensor_copy(thr, K.psA[:, 0:1]),
                  deps=[bt], sig=True)
        ps.done("A", t5)
        t6 = p.op("ve", lambda e: e.tensor_scalar_mul(nthr, thr, -1.0), sig=True)
        war = [t6] * nslot
        qwar = [None] * nslot
        stat_started = False
        for idx, (r, c) in enumerate(chunks):
            sl = qch[:, (idx % nslot) * 1024:(idx % nslot) * 1024 + chcols]
            qsl = qbf[:, (idx % nslot) * 1024:(idx % nslot) * 1024 + chcols]
            dt = p.dma("sp", (lambda sl, r, c: lambda e: e.dma_start(
                sl, K.ins[name][r * 128:(r + 1) * 128,
                                c * chcols:(c + 1) * chcols]))(sl, r, c),
                deps=[war[idx % nslot]])
            t1 = p.op("ve", (lambda qsl, sl: lambda e: e.tensor_single_scalar(
                qsl, sl, thr, ALU.is_ge))(qsl, sl),
                deps=[dt, qwar[idx % nslot], t6], sig=True)
            t2 = p.op("ve", (lambda sl, cc: lambda e: e.tensor_single_scalar(
                rt_q[:, 0:cc], sl, nthr, ALU.is_le))(sl, chcols), sig=True)
            war[idx % nslot] = t2
            t3 = p.op("ve", (lambda qsl, cc: lambda e: e.tensor_sub(
                qsl, qsl, rt_q[:, 0:cc]))(qsl, chcols), sig=True)
            if stats:
                dep0 = [t3, lnbf_tok] + ([ps.gate("E"), ps.gate("F"),
                                          ps.gate("G"), ps.gate("H")]
                                         if not stat_started else [])
                stat_started = True
                p.op("pe", (lambda qsl, r: lambda e: e.matmul(
                    K.psE[0:1, 0:512], lnw_bf[:, r:r + 1], qsl[:, 0:512],
                    start=(r == 0), stop=(r == 31)))(qsl, r), deps=dep0)
                p.op("pe", (lambda qsl, r: lambda e: e.matmul(
                    K.psF[0:1, 0:512], lnw_bf[:, r:r + 1], qsl[:, 512:1024],
                    start=(r == 0), stop=(r == 31)))(qsl, r))
                p.op("pe", (lambda qsl, r: lambda e: e.matmul(
                    K.psG[0:1, 0:512], lnb_bf[:, r:r + 1], qsl[:, 0:512],
                    start=(r == 0), stop=(r == 31)))(qsl, r))
                p.op("pe", (lambda qsl, r: lambda e: e.matmul(
                    K.psH[0:1, 0:512], lnb_bf[:, r:r + 1], qsl[:, 512:1024],
                    start=(r == 0), stop=(r == 31)))(qsl, r))
                stok = p.sig("pe")
            if fold_lnw:
                t4 = p.op("ve", (lambda qsl, r: lambda e: e.tensor_single_scalar(
                    qsl, qsl, K.lnw[:, r:r + 1], ALU.mult))(qsl, r),
                    deps=[stok] if stats else [], sig=True)
            if scname == "fc":
                dst = K.wqfc_sc[r * 128:(r + 1) * 128, c * chcols:(c + 1) * chcols]
                qv = qsl
            else:
                dst = K.wqproj_sc.rearrange("a p k d -> p a k d")[:, :, r, :]
                qv = qsl.rearrange("p (a d) -> p a d", d=128)
            st = p.dma("sp", (lambda dst, qv: lambda e: e.dma_start(
                dst, qv))(dst, qv), deps=[p.sig("ve")])
            qwar[idx % nslot] = st
        return p.front("sp")

    fc_done = quant_weight("wfcT", 1024, 4096, 1024, "fc", False, False)
    pr_done = quant_weight("wprojT", 4096, 1024, 1024, "proj", True, True)
    e1 = p.op("ve", lambda e: e.tensor_copy(K.s1s0_bf[0:1, 0:512], K.psE[0:1, 0:512]), sig=True)
    ps.done("E", e1)
    e2 = p.op("ve", lambda e: e.tensor_copy(K.s1s0_bf[0:1, 512:1024], K.psF[0:1, 0:512]), sig=True)
    ps.done("F", e2)
    e3 = p.op("ve", lambda e: e.tensor_copy(K.s1s0_bf[0:1, 1024:1536], K.psG[0:1, 0:512]), sig=True)
    ps.done("G", e3)
    e4 = p.op("ve", lambda e: e.tensor_copy(K.s1s0_bf[0:1, 1536:2048], K.psH[0:1, 0:512]), sig=True)
    ps.done("H", e4)
    s_done = [e1, e2, e3, e4]
    quant_done = p.barrier(("ve", "se", "sp"))
    K.fpool.pop()
    K.bpool.pop()

    K.fpool.push(); K.bpool.push()
    h_g = K.bpool.alloc(32 * 512)
    w1b = K.bpool.alloc(2 * 4096)
    w2b = K.bpool.alloc(2 * 4096)
    sq = K.bpool.alloc(2 * 512)
    nmr_bf = K.bpool.alloc(512)
    rows = K.fpool.alloc(4 * 512)
    r_bc = K.fpool.alloc(512)
    mu_row = rows[0:1, 0:512]
    v_row = rows[0:1, 512:1024]
    r_row = rows[0:1, 1024:1536]
    nmr_row = rows[0:1, 1536:2048]
    w1war = [None, None]
    w2war = [None, None]
    sq_war = [quant_done, quant_done]
    h_war = quant_done
    rbc_war = None
    nmrbf_war = quant_done
    u1 = 0
    u2 = 0
    for g in range(4):
        t0 = g * 512
        stat_dep = [ps.gate("G")]
        stat_dep2 = [ps.gate("H")]
        for fb in range(8):
            slot = u1 % 2
            u1 += 1
            w1u = w1b[:, slot * 4096:(slot + 1) * 4096]
            src = K.wqfc_sc.rearrange("(kc p) f -> p kc f", p=128)[
                :, :, fb * 512:(fb + 1) * 512]
            dstv = w1u.rearrange("p (kc f) -> p kc f", f=512)
            wt = p.dma("sp", (lambda dstv, src: lambda e: e.dma_start(
                dstv, src))(dstv, src), deps=[w1war[slot]] + fc_done + pre)
            for fc in range(4):
                hidx = fb * 4 + fc
                bank = "C" if hidx % 2 == 0 else "D"
                psb = K.psC if hidx % 2 == 0 else K.psD
                dep0 = [wt, ps.gate(bank)]
                for kc in range(8):
                    p.op("pe", (lambda w1u, kc, fc, t0, psb: lambda e: e.matmul(
                        psb[:, 0:512],
                        w1u[:, kc * 512 + fc * 128: kc * 512 + (fc + 1) * 128],
                        CH(K.xn, kc, t0, 512),
                        start=(kc == 0), stop=(kc == 7)))(w1u, kc, fc, t0, psb),
                        deps=dep0 if kc == 0 else [])
                mmtok = p.sig("pe")
                if fc == 3:
                    w1war[slot] = mmtok
                et = p.op("se", (lambda hidx, psb: lambda e: e.activation(
                    h_g[:, hidx * 512:(hidx + 1) * 512], psb[:, 0:512],
                    AF.Gelu))(hidx, psb), deps=[mmtok, h_war], sig=True)
                ps.done(bank, et)
                q1 = p.op("se", (lambda hidx: lambda e: e.activation(
                    sq[:, (hidx % 2) * 512:(hidx % 2 + 1) * 512],
                    h_g[:, hidx * 512:(hidx + 1) * 512], AF.Square))(hidx),
                    deps=[sq_war[hidx % 2]], sig=True)
                p.op("pe", (lambda hidx: lambda e: e.matmul(
                    K.psG[0:1, 0:512], K.ones_bf[:, 0:1],
                    h_g[:, hidx * 512:(hidx + 1) * 512],
                    start=(hidx == 0), stop=(hidx == 31)))(hidx),
                    deps=[et] + stat_dep)
                stat_dep = []
                p.op("pe", (lambda hidx: lambda e: e.matmul(
                    K.psH[0:1, 0:512], K.ones_bf[:, 0:1],
                    sq[:, (hidx % 2) * 512:(hidx % 2 + 1) * 512],
                    start=(hidx == 0), stop=(hidx == 31)))(hidx),
                    deps=[q1] + stat_dep2)
                stat_dep2 = []
                sq_war[hidx % 2] = p.sig("pe")
        smtok = p.sig("pe")
        inv = 1.0 / 4096.0
        t1 = p.op("ve", lambda e: e.tensor_scalar_mul(mu_row, K.psG[0:1, 0:512], inv),
                  deps=[smtok], sig=True)
        ps.done("G", t1)
        t2 = p.op("ve", lambda e: e.tensor_scalar_mul(v_row, K.psH[0:1, 0:512], inv),
                  deps=[smtok], sig=True)
        ps.done("H", t2)
        t3 = p.op("se", lambda e: e.activation(r_row, mu_row, AF.Square),
                  deps=[t1], sig=True)
        t4 = p.op("ve", lambda e: e.tensor_sub(v_row, v_row, r_row), deps=[t3], sig=True)
        t5 = p.op("se", lambda e: e.activation(r_row, v_row, AF.Sqrt,
                                               bias=K.misc[0:1, 0:1]),
                  deps=[t4], sig=True)
        t6 = p.op("ve", lambda e: e.reciprocal(r_row, r_row), deps=[t5], sig=True)
        t7 = p.op("ve", lambda e: e.scalar_tensor_tensor(
            nmr_row, mu_row, -1.0, r_row, ALU.mult, ALU.mult), deps=[t6], sig=True)
        t8 = p.op("ve", lambda e: e.tensor_copy(nmr_bf[0:1, 0:512], nmr_row),
                  deps=[t7, nmrbf_war], sig=True)
        p.op("pe", lambda e: e.matmul(K.psG[:, 0:512], K.ones_f32[0:1, :], r_row,
                                      start=True, stop=True),
             deps=[t7, ps.gate("G")])
        bt = p.sig("pe")
        t9 = p.op("ve", (lambda: lambda e: e.tensor_copy(r_bc, K.psG[:, 0:512]))(),
                  deps=[bt, rbc_war], sig=True)
        ps.done("G", t9)
        htoks = []
        for fk in range(32):
            ht = p.op("ve", (lambda fk: lambda e: e.tensor_mul(
                h_g[:, fk * 512:(fk + 1) * 512], h_g[:, fk * 512:(fk + 1) * 512],
                r_bc))(fk), deps=[t9, smtok], sig=True)
            htoks.append(ht)
        rbc_war_next = htoks[-1]
        for dc in range(8):
            slot = u2 % 2
            u2 += 1
            w2u = w2b[:, slot * 4096:(slot + 1) * 4096]
            wt = p.dma("sp", (lambda w2u, dc: lambda e: e.dma_start(
                w2u, K.wqproj_sc[dc].flatten_outer_dims() if False
                else K.wqproj_sc[dc].rearrange("p k d -> p (k d)")))(w2u, dc),
                deps=[w2war[slot]] + pr_done)
            bank = "E" if dc % 2 == 0 else "F"
            psb = K.psE if dc % 2 == 0 else K.psF
            dep0 = [wt, ps.gate(bank)]
            for fk in range(32):
                p.op("pe", (lambda w2u, fk, psb: lambda e: e.matmul(
                    psb[:, 0:512], w2u[:, fk * 128:(fk + 1) * 128],
                    h_g[:, fk * 512:(fk + 1) * 512],
                    start=(fk == 0), stop=False))(w2u, fk, psb),
                    deps=(dep0 if fk == 0 else []) + [htoks[fk]])
            p.op("pe", (lambda dc, psb: lambda e: e.matmul(
                psb[:, 0:512], K.s1s0_bf[0:1, dc * 128:(dc + 1) * 128],
                nmr_bf[0:1, 0:512], start=False, stop=False))(dc, psb),
                deps=[t8] + s_done)
            p.op("pe", (lambda dc, psb: lambda e: e.matmul(
                psb[:, 0:512], K.s1s0_bf[0:1, 1024 + dc * 128:1024 + (dc + 1) * 128],
                K.ones_bf[0:1, 0:512], start=False, stop=True))(dc, psb))
            mmtok = p.sig("pe")
            w2war[slot] = mmtok
            et = p.op("ve", (lambda dc, t0, psb: lambda e: e.tensor_add(
                CH(K.xres, dc, t0, 512), CH(K.xres, dc, t0, 512),
                psb[:, 0:512]))(dc, t0, psb), deps=[mmtok], sig=True)
            ps.done(bank, et)
        h_war = p.sig("pe")
        rbc_war = rbc_war_next
        nmrbf_war = h_war
    K.final_deps = p.barrier(("pe", "ve", "se"))
    K.fpool.pop(); K.bpool.pop()


